# version 03
# baseline (speedup 1.0000x reference)
import contextlib
import numpy as np
import concourse.bass as bass
import concourse.mybir as mybir
from concourse.bass_utils import run_bass_kernel_spmd

F32 = mybir.dt.float32
BF16 = mybir.dt.bfloat16
AF = mybir.ActivationFunctionType
ALU = mybir.AluOpType

H = 1024
NCI = 8
HALO = 128
TCORE = 4096
NTOK = TCORE + HALO
NCORES = 8
EPS_LN = 1e-5
EPS_A = 1e-6
SSCALE = 16384.0
DBG = False


def tiles_of(total, step, base=0):
    out = []
    t = 0
    while t < total:
        out.append((base + t, min(step, total - t)))
        t += step
    return out


class Sem:
    def __init__(self, ctx, name):
        self.h = ctx.es.enter_context(ctx.nc.semaphore(name))
        self.n = 0

    def inc(self, instr, k=1):
        instr.then_inc(self.h, k)
        self.n += k
        return self.n

    def wait(self, eng, val):
        if val > 0:
            eng.wait_ge(self.h, val)

    def dma(self, eng, *args, **kw):
        self.wait(eng, self.n)
        return self.inc(eng.dma_start(*args, **kw), 16)


class Ctx:
    def __init__(self, nc, es):
        self.nc = nc
        self.es = es
        self.scope = es

    def sem(self, name):
        return Sem(self, name)

    def sb(self, name, shape, dtype):
        return self.scope.enter_context(self.nc.sbuf_tensor(name, shape, dtype))

    def ps(self, name, shape, dtype=F32):
        return self.scope.enter_context(self.nc.psum_tensor(name, shape, dtype))


ENGS = ('sync', 'gpsimd', 'scalar', 'vector', 'tensor')


class Phases:

    def __init__(self, ctx, sems):
        self.ctx = ctx
        self.sems = sems
        self.guard = {}

    def snapshot(self):
        self.guard = {k: s.n for k, s in self.sems.items()}

    def emit_guard(self, block):
        g = dict(self.guard)

        def f(eng):
            for k, v in g.items():
                self.sems[k].wait(eng, v)
        for e in ENGS:
            getattr(block, e)(f)


def load_phase_weights(block, sems, pairs):
    sW = sems['wld']

    def f(g):
        for dst, src in pairs:
            sW.dma(g, out=dst, in_=src)
    block.gpsimd(f)
    return sW.n



def emit_phase_l0(ctx, block, d_src, d_dst, W1, W2, W3, vb, sems, kind):
    TP = 256
    tls = tiles_of(NTOK, TP)
    XT = [ctx.sb(f"{kind}_xt{b}", [128, NCI, TP], F32) for b in range(2)]
    SQB = [ctx.sb(f"{kind}_sqb{b}", [128, NCI, TP], BF16) for b in range(2)]
    XB = [ctx.sb(f"{kind}_xb{b}", [128, NCI, TP], BF16) for b in range(2)]
    LNT = ctx.sb(f"{kind}_lnt", [128, NCI, TP], F32)
    LNF = ctx.sb(f"{kind}_lnf", [128, NCI, TP + 1], F32)
    PREV = ctx.sb(f"{kind}_prev", [128, NCI, TP + 1], F32)
    UT = ctx.sb(f"{kind}_ut", [128, NCI, TP], F32)
    VIN = [ctx.sb(f"{kind}_vin{b}", [128, NCI, TP], BF16) for b in range(2)]
    RIN = [ctx.sb(f"{kind}_rin{b}", [128, NCI, TP], BF16) for b in range(2)]
    VB = [ctx.sb(f"{kind}_vb{b}", [128, NCI, TP], BF16) for b in range(2)]
    KAF = ctx.sb(f"{kind}_kaf", [128, TP], F32) if kind == 'ffn' else None
    RS = ctx.sb(f"{kind}_rs", [128, NCI, TP], F32)
    OTX = ctx.sb(f"{kind}_otx", [128, TP], F32)
    OT = [ctx.sb(f"{kind}_ot{b}", [128, NCI, TP], F32) for b in range(2)]
    CARRY = ctx.sb(f"{kind}_carry", [128, NCI], F32)
    ROWS = ctx.sb(f"{kind}_rows", [1, 2, 5, TP], F32)
    p_sum = ctx.ps(f"{kind}_psum", [1, TP])
    p_sq = ctx.ps(f"{kind}_psq", [1, TP])
    p_a = ctx.ps(f"{kind}_pa", [128, TP])
    p_b = ctx.ps(f"{kind}_pb", [128, TP])
    p_mm = [ctx.ps(f"{kind}_pmm{b}", [128, TP]) for b in range(2)]

    sPE, sACT, sDVE = sems['pe'], sems['act'], sems['dve']
    sLDp = [sems['ld0'], sems['ld1']]
    sSTp = [sems['st0'], sems['st1']]
    ev = {}

    def on(name, f):
        getattr(block, name)(f)

    def mkV(vector):
        def V(fn, *args, **kw):
            sDVE.wait(vector, sDVE.n)
            return sDVE.inc(getattr(vector, fn)(*args, **kw))
        return V

    def mkA(scalar):
        def A(fn, *args, **kw):
            sACT.wait(scalar, sACT.n)
            return sACT.inc(getattr(scalar, fn)(*args, **kw))
        return A

    eps_ln = vb['consts'][0:1, 0:1]
    if kind == 'attn':
        m1, m2c, dcol = vb['tmv'], vb['tmr'], vb['d']
    else:
        m1, m2c = vb['muk'], vb['mur']

    def f_init(vector):
        V = mkV(vector)
        V('memset', CARRY[:], 0.0)
        V('memset', PREV[:, :, 0:1], 0.0)
        ev['init'] = V('memset', LNF[:, :, 0:1], 0.0)
    on('vector', f_init)

    for i, (t0, T) in enumerate(tls):
        b = i % 2
        r = i % 2

        def f_load(sync, i=i, b=b, t0=t0, T=T):
            if i >= 2:
                sDVE.wait(sync, ev[('ep', i - 2)])
                sACT.wait(sync, ev[('sq', i - 2)])
            ev[('xld', i)] = sLDp[b].inc(
                sync.dma_start(out=XT[b][:, :, 0:T], in_=d_src[:, :, t0:t0 + T]), 16)
        on('sync', f_load)

        def f_sq(scalar, i=i, b=b, T=T):
            sLDp[b].wait(scalar, ev[('xld', i)])
            if i >= 2:
                sPE.wait(scalar, ev[('stats', i - 2)])
            ev[('sq', i)] = mkA(scalar)(
                'activation', SQB[b][:, :, 0:T], XT[b][:, :, 0:T], AF.Square)
        on('scalar', f_sq)

        def f_cast(vector, i=i, b=b, T=T):
            sLDp[b].wait(vector, ev[('xld', i)])
            if i >= 2:
                sPE.wait(vector, ev[('stats', i - 2)])
            ev[('xb', i)] = mkV(vector)(
                'tensor_copy', XB[b][:, :, 0:T], XT[b][:, :, 0:T])
        on('vector', f_cast)

        def f_stats(ten, i=i, b=b, T=T):
            sACT.wait(ten, ev[('sq', i)])
            sDVE.wait(ten, ev[('xb', i)])
            if i >= 1:
                sACT.wait(ten, ev[('m2', i - 1)])
                sDVE.wait(ten, ev[('rows', i - 1)])
            for ci in range(NCI):
                ten.matmul(p_sq[0:1, 0:T], vb['ones_inv'][:], SQB[b][:, ci, 0:T],
                           start=(ci == 0), stop=(ci == NCI - 1))
            for ci in range(NCI):
                ins = ten.matmul(p_sum[0:1, 0:T], vb['ones_inv'][:], XB[b][:, ci, 0:T],
                                 start=(ci == 0), stop=(ci == NCI - 1))
            ev[('stats', i)] = sPE.inc(ins)
        on('tensor', f_stats)

        m2 = ROWS[0:1, r, 0, 0:T]
        var = ROWS[0:1, r, 1, 0:T]
        sqv = ROWS[0:1, r, 2, 0:T]
        a_ = ROWS[0:1, r, 3, 0:T]
        mr = ROWS[0:1, r, 4, 0:T]

        def f_m2(scalar, i=i, T=T, m2=m2):
            sPE.wait(scalar, ev[('stats', i)])
            if i >= 1:
                sDVE.wait(scalar, ev[('rows', i - 1)])
            ev[('m2', i)] = mkA(scalar)('activation', m2, p_sum[0:1, 0:T], AF.Square)
        on('scalar', f_m2)

        def f_var(vector, i=i, T=T, m2=m2, var=var):
            sACT.wait(vector, ev[('m2', i)])
            ev[('var', i)] = mkV(vector)('tensor_sub', var, p_sq[0:1, 0:T], m2)
        on('vector', f_var)

        def f_sqv(scalar, i=i, T=T, var=var, sqv=sqv):
            sDVE.wait(scalar, ev[('var', i)])
            ev[('sqv', i)] = mkA(scalar)('activation', sqv, var, AF.Sqrt, bias=eps_ln)
        on('scalar', f_sqv)

        def f_rows(vector, i=i, T=T, sqv=sqv, a_=a_, mr=mr):
            sACT.wait(vector, ev[('sqv', i)])
            if i >= 1:
                sPE.wait(vector, ev[('bcast', i - 1)])
            V = mkV(vector)
            V('reciprocal', a_, sqv)
            ev[('rows', i)] = V('tensor_mul', mr, p_sum[0:1, 0:T], a_)
        on('vector', f_rows)

        def f_bcast(ten, i=i, T=T, a_=a_, mr=mr):
            sDVE.wait(ten, ev[('rows', i)])
            if i >= 1:
                sDVE.wait(ten, ev[('ln2', i - 1)])
            ten.matmul(p_a[:, 0:T], vb['ones1'][:], a_, start=True, stop=True)
            ins = ten.matmul(p_b[:, 0:T], vb['nones1'][:], mr, start=True, stop=True)
            ev[('bcast', i)] = sPE.inc(ins)
        on('tensor', f_bcast)

        def f_ln(vector, i=i, b=b, T=T):
            sPE.wait(vector, ev[('bcast', i)])
            V = mkV(vector)
            for ci in range(NCI):
                V('tensor_mul', LNT[:, ci, 0:T], XT[b][:, ci, 0:T], p_a[:, 0:T])
            for ci in range(NCI):
                ev[('ln2', i)] = V('tensor_add', LNF[:, ci, 1:T + 1],
                                   LNT[:, ci, 0:T], p_b[:, 0:T])
        on('vector', f_ln)

        if kind == 'attn':
            def f_scan(vector, i=i, T=T):
                V = mkV(vector)
                for ci in range(NCI):
                    V('tensor_copy', PREV[:, ci, 0:1], CARRY[:, ci:ci + 1])
                    V('tensor_tensor_scan',
                      PREV[:, ci, 1:T + 1],
                      dcol[:, ci:ci + 1].broadcast_to([128, T]),
                      LNF[:, ci, 1:T + 1],
                      CARRY[:, ci:ci + 1], ALU.mult, ALU.add)
                    V('tensor_copy', CARRY[:, ci:ci + 1], PREV[:, ci, T:T + 1])
            on('vector', f_scan)
            def base_ap(ci, T):
                return PREV[:, ci, 0:T]
        else:
            def base_ap(ci, T):
                return LNF[:, ci, 0:T]

        def f_mix(vector, i=i, b=b, T=T):
            if i >= 2:
                sPE.wait(vector, ev[('projr', i - 2)])
            V = mkV(vector)
            for ci in range(NCI):
                V('tensor_sub', UT[:, ci, 0:T], LNF[:, ci, 1:T + 1], base_ap(ci, T))
            for ci in range(NCI):
                V('scalar_tensor_tensor',
                  VIN[b][:, ci, 0:T], UT[:, ci, 0:T], m1[:, ci:ci + 1],
                  base_ap(ci, T), ALU.mult, ALU.add)
            for ci in range(NCI):
                ev[('mix', i)] = V('scalar_tensor_tensor',
                                   RIN[b][:, ci, 0:T], UT[:, ci, 0:T], m2c[:, ci:ci + 1],
                                   base_ap(ci, T), ALU.mult, ALU.add)
            if kind == 'ffn':
                for ci in range(NCI):
                    ev[('mix', i)] = V('tensor_copy', LNF[:, ci, 0:1], LNF[:, ci, T:T + 1])
        on('vector', f_mix)

        def f_projv_pre(ten, i=i):
            sDVE.wait(ten, ev[('mix', i)])
        on('tensor', f_projv_pre)

        for oc in range(8):
            def f_projv(ten, i=i, b=b, T=T, oc=oc):
                if oc >= 2:
                    sACT.wait(ten, ev[('evacv', i, oc - 2)])
                elif i >= 1:
                    sDVE.wait(ten, ev[('epoc', i - 1, 6 + oc)])
                for ci in range(NCI):
                    ins = ten.matmul(p_mm[oc % 2][:, 0:T], W1[:, ci, oc * 128:(oc + 1) * 128],
                                     VIN[b][:, ci, 0:T], start=(ci == 0), stop=(ci == NCI - 1))
                ev[('projv', i, oc)] = sPE.inc(ins)
            on('tensor', f_projv)

            def f_evacv(scalar, i=i, b=b, T=T, oc=oc):
                sPE.wait(scalar, ev[('projv', i, oc)])
                if oc == 0 and i >= 2:
                    sPE.wait(scalar, ev[('projo', i - 2)])
                A = mkA(scalar)
                if kind == 'attn':
                    ev[('evacv', i, oc)] = A('copy', VB[b][:, oc, 0:T], p_mm[oc % 2][:, 0:T])
                else:
                    A('activation', KAF[:, 0:T], p_mm[oc % 2][:, 0:T], AF.Relu)
                    ev[('evacv', i, oc)] = A('activation', VB[b][:, oc, 0:T],
                                             KAF[:, 0:T], AF.Square)
            on('scalar', f_evacv)

        for oc in range(8):
            def f_projr(ten, i=i, b=b, T=T, oc=oc):
                if oc >= 2:
                    sACT.wait(ten, ev[('sigr', i, oc - 2)])
                else:
                    sACT.wait(ten, ev[('evacv', i, 6 + oc)])
                for ci in range(NCI):
                    ins = ten.matmul(p_mm[oc % 2][:, 0:T], W2[:, ci, oc * 128:(oc + 1) * 128],
                                     RIN[b][:, ci, 0:T], start=(ci == 0), stop=(ci == NCI - 1))
                ev[('projr', i, oc)] = sPE.inc(ins)
            on('tensor', f_projr)

            def f_sigr(scalar, i=i, T=T, oc=oc):
                sPE.wait(scalar, ev[('projr', i, oc)])
                if oc == 0 and i >= 1:
                    sDVE.wait(scalar, ev[('ep', i - 1)])
                ev[('sigr', i, oc)] = mkA(scalar)(
                    'activation', RS[:, oc, 0:T], p_mm[oc % 2][:, 0:T], AF.Sigmoid)
            on('scalar', f_sigr)
        ev[('projr', i)] = ev[('projr', i, 7)]

        for oc in range(8):
            def f_projo(ten, i=i, b=b, T=T, oc=oc):
                if oc == 0:
                    sACT.wait(ten, ev[('evacv', i, 7)])
                if oc >= 2:
                    sDVE.wait(ten, ev[('epoc', i, oc - 2)])
                else:
                    sACT.wait(ten, ev[('sigr', i, 6 + oc)])
                for ci in range(NCI):
                    ins = ten.matmul(p_mm[oc % 2][:, 0:T], W3[:, ci, oc * 128:(oc + 1) * 128],
                                     VB[b][:, ci, 0:T], start=(ci == 0), stop=(ci == NCI - 1))
                ev[('projo', i, oc)] = sPE.inc(ins)
            on('tensor', f_projo)

            def f_ep(vector, i=i, b=b, T=T, oc=oc):
                sPE.wait(vector, ev[('projo', i, oc)])
                sACT.wait(vector, ev[('sigr', i, oc)])
                if oc == 0 and i >= 2:
                    sSTp[b].wait(vector, ev[('stored', i - 2)])
                V = mkV(vector)
                V('tensor_mul', OTX[:, 0:T], RS[:, oc, 0:T], p_mm[oc % 2][:, 0:T])
                ev[('epoc', i, oc)] = V('tensor_add', OT[b][:, oc, 0:T],
                                        OTX[:, 0:T], XT[b][:, oc, 0:T])
            on('vector', f_ep)
        ev[('ep', i)] = ev[('epoc', i, 7)]
        ev[('projo', i)] = ev[('projo', i, 7)]

        def f_store(sync, i=i, b=b, t0=t0, T=T):
            sDVE.wait(sync, ev[('ep', i)])
            ev[('stored', i)] = sSTp[b].inc(
                sync.dma_start(out=d_dst[:, :, t0:t0 + T], in_=OT[b][:, :, 0:T]), 16)
        on('sync', f_store)

    def f_fin(sync):
        n = len(tls)
        sSTp[(n - 1) % 2].wait(sync, ev[('stored', n - 1)])
        sSTp[(n - 2) % 2].wait(sync, ev[('stored', n - 2)])
    on('sync', f_fin)




class LNHelper:

    def __init__(self, ctx, sems, vb, prefix, TP):
        self.vb = vb
        self.sems = sems
        self.TP = TP
        self.SQB = [ctx.sb(f"{prefix}_sqb{b}", [128, NCI, TP], BF16) for b in range(2)]
        self.XB = [ctx.sb(f"{prefix}_xb{b}", [128, NCI, TP], BF16) for b in range(2)]
        self.LNT = ctx.sb(f"{prefix}_lnt", [128, NCI, TP], BF16)
        self.ROWS = ctx.sb(f"{prefix}_rows", [1, 1, 5, TP], F32)
        self.p_sum = ctx.ps(f"{prefix}_psum", [1, TP])
        self.p_sq = ctx.ps(f"{prefix}_psq", [1, TP])
        self.p_a = ctx.ps(f"{prefix}_pa", [128, TP])
        self.p_b = ctx.ps(f"{prefix}_pb", [128, TP])

    def emit(self, block, ev, i, b, XT_b, T, tgt, ld_wait, pe_busy_wait=None):
        sems = self.sems
        sPE, sACT, sDVE = sems['pe'], sems['act'], sems['dve']
        vb = self.vb
        SQB, XB, LNT, ROWS = self.SQB, self.XB, self.LNT, self.ROWS
        p_sum, p_sq, p_a, p_b = self.p_sum, self.p_sq, self.p_a, self.p_b
        eps_ln = vb['consts'][0:1, 0:1]
        r = 0

        def on(name, f):
            getattr(block, name)(f)

        def mkV(vector):
            def V(fn, *args, **kw):
                sDVE.wait(vector, sDVE.n)
                return sDVE.inc(getattr(vector, fn)(*args, **kw))
            return V

        def mkA(scalar):
            def A(fn, *args, **kw):
                sACT.wait(scalar, sACT.n)
                return sACT.inc(getattr(scalar, fn)(*args, **kw))
            return A

        def f_sq(scalar, i=i, b=b, T=T):
            ld_wait(scalar)
            if i >= 2:
                sPE.wait(scalar, ev[('stats', i - 2)])
            ev[('sq', i)] = mkA(scalar)(
                'activation', SQB[b][:, :, 0:T], XT_b[:, :, 0:T], AF.Square)
        on('scalar', f_sq)

        def f_cast(vector, i=i, b=b, T=T):
            ld_wait(vector)
            if i >= 2:
                sPE.wait(vector, ev[('stats', i - 2)])
            ev[('xb', i)] = mkV(vector)(
                'tensor_copy', XB[b][:, :, 0:T], XT_b[:, :, 0:T])
        on('vector', f_cast)

        def f_stats(ten, i=i, b=b, T=T):
            sACT.wait(ten, ev[('sq', i)])
            sDVE.wait(ten, ev[('xb', i)])
            if i >= 1:
                sACT.wait(ten, ev[('m2', i - 1)])
                sDVE.wait(ten, ev[('rows', i - 1)])
            if pe_busy_wait is not None:
                pe_busy_wait(ten, i)
            for ci in range(NCI):
                ten.matmul(p_sq[0:1, 0:T], vb['ones_inv'][:], SQB[b][:, ci, 0:T],
                           start=(ci == 0), stop=(ci == NCI - 1))
            for ci in range(NCI):
                ins = ten.matmul(p_sum[0:1, 0:T], vb['ones_inv'][:], XB[b][:, ci, 0:T],
                                 start=(ci == 0), stop=(ci == NCI - 1))
            ev[('stats', i)] = sPE.inc(ins)
        on('tensor', f_stats)

        m2 = ROWS[0:1, r, 0, 0:T]
        var = ROWS[0:1, r, 1, 0:T]
        sqv = ROWS[0:1, r, 2, 0:T]
        a_ = ROWS[0:1, r, 3, 0:T]
        mr = ROWS[0:1, r, 4, 0:T]

        def f_m2(scalar, i=i, T=T, m2=m2):
            sPE.wait(scalar, ev[('stats', i)])
            if i >= 1:
                sDVE.wait(scalar, ev[('rows', i - 1)])
            ev[('m2', i)] = mkA(scalar)('activation', m2, p_sum[0:1, 0:T], AF.Square)
        on('scalar', f_m2)

        def f_var(vector, i=i, T=T, m2=m2, var=var):
            sACT.wait(vector, ev[('m2', i)])
            ev[('var', i)] = mkV(vector)('tensor_sub', var, p_sq[0:1, 0:T], m2)
        on('vector', f_var)

        def f_sqv(scalar, i=i, T=T, var=var, sqv=sqv):
            sDVE.wait(scalar, ev[('var', i)])
            ev[('sqv', i)] = mkA(scalar)('activation', sqv, var, AF.Sqrt, bias=eps_ln)
        on('scalar', f_sqv)

        def f_rows(vector, i=i, T=T, sqv=sqv, a_=a_, mr=mr):
            sACT.wait(vector, ev[('sqv', i)])
            if i >= 1:
                sPE.wait(vector, ev[('bcast', i - 1)])
            V = mkV(vector)
            V('reciprocal', a_, sqv)
            ev[('rows', i)] = V('tensor_mul', mr, p_sum[0:1, 0:T], a_)
        on('vector', f_rows)

        def f_bcast(ten, i=i, T=T, a_=a_, mr=mr):
            sDVE.wait(ten, ev[('rows', i)])
            if i >= 1:
                sDVE.wait(ten, ev[('ln2', i - 1)])
            ten.matmul(p_a[:, 0:T], vb['ones1'][:], a_, start=True, stop=True)
            ins = ten.matmul(p_b[:, 0:T], vb['nones1'][:], mr, start=True, stop=True)
            ev[('bcast', i)] = sPE.inc(ins)
        on('tensor', f_bcast)

        def f_ln(vector, i=i, b=b, T=T):
            sPE.wait(vector, ev[('bcast', i)])
            V = mkV(vector)
            for ci in range(NCI):
                V('tensor_mul', LNT[:, ci, 0:T], XT_b[:, ci, 0:T], p_a[:, 0:T])
            for ci in range(NCI):
                ev[('ln2', i)] = tgt(vector, V, ci)
        on('vector', f_ln)


def emit_phase_l1a_A(ctx, block, io, WQ, WK, WV_, vb, sems, QSP, KSP):
    TP = 256
    tls = tiles_of(TCORE, TP)
    d_src = io['r2']
    XT = [ctx.sb(f"a_xt{b}", [128, NCI, TP], F32) for b in range(2)]
    LNB = [ctx.sb(f"a_lnb{b}", [128, NCI, TP], BF16) for b in range(2)]
    QFB = [ctx.sb(f"a_qfb{b}", [128, NCI, TP], BF16) for b in range(2)]
    KFB = [ctx.sb(f"a_kfb{b}", [128, NCI, TP], BF16) for b in range(2)]
    VFB = [ctx.sb(f"a_vfb{b}", [128, NCI, TP], BF16) for b in range(2)]
    KTS = [ctx.sb(f"a_kts{b}", [128, 2, H], BF16) for b in range(2)]
    VTS = [ctx.sb(f"a_vts{b}", [128, 2, H], BF16) for b in range(2)]
    nt = len(tls)
    ln = LNHelper(ctx, sems, vb, "a", TP)
    p_mm = [ctx.ps(f"a_pmm{b}", [128, TP]) for b in range(2)]
    p_tp = [ctx.ps(f"a_ptp{k}", [128, 128], BF16) for k in range(2)]

    sPE, sACT, sDVE = sems['pe'], sems['act'], sems['dve']
    sLDp = [sems['ld0'], sems['ld1']]
    sSTp = [sems['st0'], sems['st1']]
    sSTa = [sems['sta0'], sems['sta1']]
    sSTb = [sems['stb0'], sems['stb1']]
    sSTc = [sems['stc0'], sems['stc1']]
    ev = {}

    def on(name, f):
        getattr(block, name)(f)

    def mkV(vector):
        def V(fn, *args, **kw):
            sDVE.wait(vector, sDVE.n)
            return sDVE.inc(getattr(vector, fn)(*args, **kw))
        return V

    def mkA(scalar):
        def A(fn, *args, **kw):
            sACT.wait(scalar, sACT.n)
            return sACT.inc(getattr(scalar, fn)(*args, **kw))
        return A

    def bank_prev(i, st, oc):
        g = st * 8 + oc - 2
        if g >= 0:
            return i, g // 8, g % 8
        if i == 0:
            return None
        return i - 1, (24 + g) // 8, (24 + g) % 8

    EV_ENG = {0: 'act', 1: 'act', 2: 'dve'}

    for i, (t0, T) in enumerate(tls):
        b = i % 2

        def f_load(sync, i=i, b=b, t0=t0, T=T):
            if i >= 2:
                sACT.wait(sync, ev[('sq', i - 2)])
                sDVE.wait(sync, ev[('xb', i - 2)])
            ev[('xld', i)] = sLDp[b].inc(
                sync.dma_start(out=XT[b][:, :, 0:T],
                               in_=d_src[:, :, HALO + t0:HALO + t0 + T]), 16)
        on('sync', f_load)

        def ld_wait(eng, i=i, b=b):
            sLDp[b].wait(eng, ev[('xld', i)])

        def tgt(vector, V, ci, i=i, b=b, T=T):
            if ci == 0 and i >= 2:
                sPE.wait(vector, ev[('projV', i - 2)])
            return V('tensor_add', LNB[b][:, ci, 0:T], ln.LNT[:, ci, 0:T],
                     ln.p_b[:, 0:T])

        ln.emit(block, ev, i, b, XT[b], T, tgt, ld_wait)

        for st, Wmat, OUT, bias_col, acc in ((0, WQ, QFB, 'bq', QSP),
                                             (1, WK, KFB, 'bk', KSP)):
            for oc in range(8):
                def f_proj(ten, i=i, b=b, T=T, oc=oc, st=st, Wmat=Wmat):
                    if st == 0 and oc == 0:
                        sDVE.wait(ten, ev[('ln2', i)])
                    prev = bank_prev(i, st, oc)
                    if prev is not None:
                        pi, pst, poc = prev
                        s = sACT if EV_ENG[pst] == 'act' else sDVE
                        s.wait(ten, ev[('evx', pi, pst, poc)])
                    for ci in range(NCI):
                        ins = ten.matmul(p_mm[oc % 2][:, 0:T],
                                         Wmat[:, ci, oc * 128:(oc + 1) * 128],
                                         LNB[b][:, ci, 0:T],
                                         start=(ci == 0), stop=(ci == NCI - 1))
                    ev[('proj', i, st, oc)] = sPE.inc(ins)
                on('tensor', f_proj)

                def f_evac(scalar, i=i, b=b, T=T, oc=oc, st=st, OUT=OUT,
                           bias_col=bias_col, acc=acc):
                    sPE.wait(scalar, ev[('proj', i, st, oc)])
                    if st == 0 and oc == 0 and i >= 2:
                        sSTp[b].wait(scalar, ev[('stq', i - 2)])
                    if st == 1 and oc == 0 and i >= 2:
                        sSTa[b].wait(scalar, ev[('stk', i - 2)])
                    ev[('evx', i, st, oc)] = mkA(scalar)(
                        'activation', OUT[b][:, oc, 0:T], p_mm[oc % 2][:, 0:T],
                        AF.Sigmoid, bias=vb[bias_col][:, oc:oc + 1],
                        accum_out=acc[:, oc, i:i + 1])
                on('scalar', f_evac)

        for oc in range(8):
            def f_projv(ten, i=i, b=b, T=T, oc=oc):
                pi, pst, poc = bank_prev(i, 2, oc)
                s = sACT if EV_ENG[pst] == 'act' else sDVE
                s.wait(ten, ev[('evx', pi, pst, poc)])
                for ci in range(NCI):
                    ins = ten.matmul(p_mm[oc % 2][:, 0:T],
                                     WV_[:, ci, oc * 128:(oc + 1) * 128],
                                     LNB[b][:, ci, 0:T],
                                     start=(ci == 0), stop=(ci == NCI - 1))
                ev[('proj', i, 2, oc)] = sPE.inc(ins)
            on('tensor', f_projv)

            def f_evacv(vector, i=i, b=b, T=T, oc=oc):
                sPE.wait(vector, ev[('proj', i, 2, oc)])
                ev[('evx', i, 2, oc)] = mkV(vector)(
                    'tensor_scalar_add', VFB[b][:, oc, 0:T], p_mm[oc % 2][:, 0:T],
                    vb['bv'][:, oc:oc + 1])
            on('vector', f_evacv)
        ev[('projV', i)] = ev[('proj', i, 2, 7)]

        tp_seq = [(tch, sid, oc) for tch in range(2) for sid in range(2)
                  for oc in range(8)]
        for n, (tch, sid, oc) in enumerate(tp_seq):
            SRC = KFB if sid == 0 else VFB
            DST = KTS if sid == 0 else VTS

            def f_tp(ten, i=i, b=b, tch=tch, oc=oc, sid=sid, n=n, SRC=SRC):
                if n == 0:
                    sACT.wait(ten, ev[('evx', i, 1, 7)])
                    sDVE.wait(ten, ev[('evx', i, 2, 7)])
                if n >= 2:
                    sACT.wait(ten, ev[('tpc', i, n - 2)])
                elif i >= 1:
                    sACT.wait(ten, ev[('tpc', i - 1, 30 + n)])
                ins = ten.transpose(p_tp[n % 2][:],
                                    SRC[b][:, oc, tch * 128:(tch + 1) * 128],
                                    vb['idb'][:])
                ev[('tp', i, n)] = sPE.inc(ins)
            on('tensor', f_tp)

            def f_tpc(scalar, i=i, b=b, tch=tch, oc=oc, n=n, DST=DST):
                sPE.wait(scalar, ev[('tp', i, n)])
                if n == 0 and i >= 2:
                    sSTb[b].wait(scalar, ev[('stkt', i - 2)])
                    sSTc[b].wait(scalar, ev[('stvt', i - 2)])
                ev[('tpc', i, n)] = mkA(scalar)(
                    'copy', DST[b][:, tch, oc * 128:(oc + 1) * 128], p_tp[n % 2][:])
            on('scalar', f_tpc)

        def f_store(sync, i=i, b=b, t0=t0, T=T):
            sACT.wait(sync, ev[('tpc', i, 31)])
            sDVE.wait(sync, ev[('evx', i, 2, 7)])
            ev[('stq', i)] = sSTp[b].dma(sync, 
                out=io['qf'][:, :, t0:t0 + T], in_=QFB[b][:, :, 0:T])
            ev[('stk', i)] = sSTa[b].dma(sync, 
                out=io['kf'][:, :, t0:t0 + T], in_=KFB[b][:, :, 0:T])
            tc0 = 2 * i
            ev[('stkt', i)] = sSTb[b].dma(sync, 
                out=io['kt'][tc0:tc0 + 2, :, :].rearrange("tc p d -> p tc d"),
                in_=KTS[b][:])
            ev[('stvt', i)] = sSTc[b].dma(sync, 
                out=io['vt'][tc0:tc0 + 2, :, :].rearrange("tc p d -> p tc d"),
                in_=VTS[b][:])
        on('sync', f_store)

    def f_fin(sync):
        for j in (nt - 1, nt - 2):
            sSTp[j % 2].wait(sync, ev[('stq', j)])
            sSTa[j % 2].wait(sync, ev[('stk', j)])
            sSTb[j % 2].wait(sync, ev[('stkt', j)])
            sSTc[j % 2].wait(sync, ev[('stvt', j)])
    on('sync', f_fin)
    ev['nt'] = nt
    return ev


def emit_l1a_BCD(ctx, block, io, vb, sems, QSP, KSP, nt_a, cc_groups):
    TP = 256
    WO = ctx.sb("d_wo", [128, NCI, H], BF16)
    load_phase_weights(block, sems, [(WO[:], io['dw_l1a_wo'])])
    wo_ld = sems['wld'].n
    tls = tiles_of(TCORE, TP)
    nt = len(tls)
    sPE, sACT, sDVE, sGPS, sCC = (sems[k] for k in
                                  ('pe', 'act', 'dve', 'gps', 'cc'))
    sLDp = [sems['ld0'], sems['ld1']]
    sLDa = [sems['lda0'], sems['lda1']]
    sLDb = [sems['ldb0'], sems['ldb1']]
    sLDc = [sems['ldc0'], sems['ldc1']]
    sSTp = [sems['st0'], sems['st1']]
    ev = {}

    def on(name, f):
        getattr(block, name)(f)

    def mkV(vector):
        def V(fn, *args, **kw):
            sDVE.wait(vector, sDVE.n)
            return sDVE.inc(getattr(vector, fn)(*args, **kw))
        return V

    def mkA(scalar):
        def A(fn, *args, **kw):
            sACT.wait(scalar, sACT.n)
            return sACT.inc(getattr(scalar, fn)(*args, **kw))
        return A

    SUMS = ctx.sb("b_sums", [128, 2, NCI], F32)
    GS = ctx.sb("b_gs", [128, 2, NCI], F32)
    AK = ctx.sb("b_ak", [128, NCI, 16], BF16)
    AQ = ctx.sb("b_aq", [128, NCI, 16], BF16)
    EC = ctx.sb("b_ec", [16, 4], F32)
    T12 = ctx.sb("b_t12", [128, 2, NCI], F32)
    T12P = ctx.sb("b_t12p", [128, 2, NCI, nt], F32)
    AT1 = ctx.sb("b_at1", [128, NCI, 16], BF16)
    AT2 = ctx.sb("b_at2", [128, NCI, 16], BF16)
    SA = ctx.sb("c_sa", [16, TCORE], F32)
    ZP = ctx.sb("c_zp", [16, nt], F32)
    KVF = ctx.sb("c_kvf", [128, NCI, 128], F32)
    KVG = ctx.sb("c_kvg", [128, NCI, 128], F32)
    KVSB = ctx.sb("c_kvsb", [128, NCI, 128], BF16)
    ZROW = ctx.sb("c_zrow", [16, 2], F32)
    ZB = ctx.sb("c_zb", [16, 1], BF16)
    ZREP = ctx.sb("c_zrep", [128, NCI], F32)
    ROWS = ctx.sb("b_rows", [16, 2, 6, TP], F32)
    eps_a = vb['consts'][0:1, 1:2]

    def emit_ar(tag, sb_in, d_in, d_out, sb_out, wait_evs):
        def f(g):
            for s, v in wait_evs:
                s.wait(g, v)
            sGPS.dma(g, out=d_in, in_=sb_in)
            sGPS.wait(g, sGPS.n)
            sCC.inc(g.collective_compute(
                "AllReduce", ALU.add, replica_groups=cc_groups,
                ins=[d_in.opt()], outs=[d_out.opt()]))
            sCC.wait(g, sCC.n)
            sGPS.dma(g, out=sb_out, in_=d_out)
            sGPS.wait(g, sGPS.n)
        on('gpsimd', f)
        return sGPS.n

    def f_r1red(vector):
        V = mkV(vector)
        for ci in range(NCI):
            V('reduce_sum', SUMS[:, 0, ci:ci + 1], QSP[:, ci, 0:nt_a], mybir.AxisListType.X)
        for ci in range(NCI):
            ev['r1red'] = V('reduce_sum', SUMS[:, 1, ci:ci + 1], KSP[:, ci, 0:nt_a], mybir.AxisListType.X)
    on('vector', f_r1red)
    g1 = emit_ar('r1', SUMS[:], io['cc1i'], io['cc1o'], GS[:],
                 [(sDVE, sems['dve'].n)])

    def f_build_a(vector):
        sGPS.wait(vector, g1)
        V = mkV(vector)
        V('memset', AK[:], 0.0)
        V('memset', AQ[:], 0.0)
        for ci in range(NCI):
            V('tensor_scalar_add', AK[0:64, ci, 2 * ci:2 * ci + 1], GS[0:64, 1, ci:ci + 1], EPS_A)
            V('tensor_scalar_add', AK[64:128, ci, 2 * ci + 1:2 * ci + 2], GS[64:128, 1, ci:ci + 1], EPS_A)
            V('tensor_scalar_add', AQ[0:64, ci, 2 * ci:2 * ci + 1], GS[0:64, 0, ci:ci + 1], EPS_A)
            ev['builda'] = V('tensor_scalar_add', AQ[64:128, ci, 2 * ci + 1:2 * ci + 2], GS[64:128, 0, ci:ci + 1], EPS_A)
    on('vector', f_build_a)

    p_row = [ctx.ps(f"b_prow{k}", [16, TP]) for k in range(2)]
    p_bc = [ctx.ps(f"b_pbc{k}", [128, TP]) for k in range(2)]
    bc_scope = contextlib.ExitStack()
    outer_scope = ctx.scope
    ctx.scope = bc_scope
    p_e = ctx.ps("c_pe", [128, H])
    p_kv = ctx.ps("c_pkv", [128, NCI, 128])
    ctx.scope = outer_scope

    def f_ec_mm(ten):
        sDVE.wait(ten, ev['builda'])
        for ci in range(NCI):
            ten.matmul(p_row[0][0:16, 0:1], AK[:, ci, :], vb['onesb'][:],
                       start=(ci == 0), stop=(ci == NCI - 1))
        for ci in range(NCI):
            ins = ten.matmul(p_row[1][0:16, 0:1], AQ[:, ci, :], vb['onesb'][:],
                             start=(ci == 0), stop=(ci == NCI - 1))
        ev['ecmm'] = sPE.inc(ins)
    on('tensor', f_ec_mm)

    def f_ec(vector):
        sPE.wait(vector, ev['ecmm'])
        V = mkV(vector)
        V('tensor_scalar_mul', EC[:, 0:1], p_row[0][0:16, 0:1], EPS_A)
        ev['ec'] = V('tensor_scalar_mul', EC[:, 1:2], p_row[1][0:16, 0:1], EPS_A)
    on('vector', f_ec)

    QFT = [ctx.sb(f"b_qft{b}", [128, NCI, TP], BF16) for b in range(2)]
    KFT = [ctx.sb(f"b_kft{b}", [128, NCI, TP], BF16) for b in range(2)]
    SIB = ctx.sb("b_sib", [16, 2, TP], BF16)
    SOB = ctx.sb("b_sob", [16, 2, TP], BF16)
    QSIB = [ctx.sb(f"b_qsib{b}", [128, NCI, TP], BF16) for b in range(2)]
    DUMB = ctx.sb("b_dumb", [128, TP], BF16)

    for i, (t0, T) in enumerate(tls):
        b = i % 2

        def f_load(sync, i=i, b=b, t0=t0, T=T):
            if i >= 2:
                sDVE.wait(sync, ev[('bstt', i - 2)])
            ev[('bldq', i)] = sLDp[b].inc(
                sync.dma_start(out=QFT[b][:, :, 0:T], in_=io['qf'][:, :, t0:t0 + T]), 16)
            ev[('bldk', i)] = sLDa[b].inc(
                sync.dma_start(out=KFT[b][:, :, 0:T], in_=io['kf'][:, :, t0:t0 + T]), 16)
        on('sync', f_load)

        def f_rowmm(ten, i=i, b=b, T=T):
            sLDp[b].wait(ten, ev[('bldq', i)])
            sLDa[b].wait(ten, ev[('bldk', i)])
            if i == 0:
                sDVE.wait(ten, ev['ec'])
            if i >= 1:
                sDVE.wait(ten, ev[('brow', i - 1)])
            for ci in range(NCI):
                ten.matmul(p_row[0][:, 0:T], AK[:, ci, :], QFT[b][:, ci, 0:T],
                           start=(ci == 0), stop=(ci == NCI - 1))
            for ci in range(NCI):
                ins = ten.matmul(p_row[1][:, 0:T], AQ[:, ci, :], KFT[b][:, ci, 0:T],
                                 start=(ci == 0), stop=(ci == NCI - 1))
            ev[('browmm', i)] = sPE.inc(ins)
        on('tensor', f_rowmm)

        def f_brow(vector, i=i, b=b, T=T):
            sPE.wait(vector, ev[('browmm', i)])
            if i >= 2:
                sPE.wait(vector, ev[('bbc', i - 2, 1, 7)])
            V = mkV(vector)
            r = i % 2
            si_f = ROWS[:, r, 0, 0:T]
            so_f = ROWS[:, r, 1, 0:T]
            V('tensor_scalar_add', ROWS[:, r, 2, 0:T], p_row[0][:, 0:T], EC[:, 0:1])
            V('reciprocal', si_f, ROWS[:, r, 2, 0:T])
            V('tensor_copy', SIB[:, b, 0:T], si_f)
            V('tensor_scalar_add', ROWS[:, r, 3, 0:T], p_row[1][:, 0:T], EC[:, 1:2])
            V('reciprocal', so_f, ROWS[:, r, 3, 0:T])
            ev[('brow', i)] = V('tensor_copy', SOB[:, b, 0:T], so_f)
        on('vector', f_brow)

        for which in range(2):
            SB = SIB if which == 0 else SOB
            FT = QFT if which == 0 else KFT
            for ci in range(NCI):
                def f_bc(ten, i=i, b=b, T=T, ci=ci, which=which, SB=SB):
                    if ci == 0 and which == 0:
                        sDVE.wait(ten, ev[('brow', i)])
                    k = ('bstt', i, which, ci - 2)
                    if ci >= 2:
                        sDVE.wait(ten, ev[k])
                    elif which == 1:
                        sDVE.wait(ten, ev[('bstt', i, 0, 6 + ci)])
                    elif i >= 1:
                        sDVE.wait(ten, ev[('bstt', i - 1, 1, 6 + ci)])
                    ins = ten.matmul(p_bc[ci % 2][:, 0:T], vb['selb'][:, ci, :],
                                     SB[:, b, 0:T], start=True, stop=True)
                    ev[('bbc', i, which, ci)] = sPE.inc(ins)
                on('tensor', f_bc)

                def f_stt(vector, i=i, b=b, T=T, ci=ci, which=which, FT=FT):
                    sPE.wait(vector, ev[('bbc', i, which, ci)])
                    V = mkV(vector)
                    if which == 0:
                        out = QSIB[b][:, ci, 0:T]
                        if ci == 0 and i >= 2:
                            sSTp[b].wait(vector, ev[('bst', i - 2)])
                        acc = T12P[:, 1, ci, i:i + 1]
                    else:
                        out = DUMB[:, 0:T]
                        acc = T12P[:, 0, ci, i:i + 1]
                    ev[('bstt', i, which, ci)] = V(
                        'scalar_tensor_tensor', out, FT[b][:, ci, 0:T], 1.0,
                        p_bc[ci % 2][:, 0:T], ALU.mult, ALU.mult, accum_out=acc)
                on('vector', f_stt)
        ev[('bstt', i)] = ev[('bstt', i, 1, 7)]

        def f_bst(sync, i=i, b=b, t0=t0, T=T):
            sDVE.wait(sync, ev[('bstt', i, 0, 7)])
            ev[('bst', i)] = sSTp[b].inc(
                sync.dma_start(out=io['qsi'][:, :, t0:t0 + T], in_=QSIB[b][:, :, 0:T]), 16)
        on('sync', f_bst)

    bst_done = (sSTp[0].n, sSTp[1].n)

    def f_r2red(vector):
        V = mkV(vector)
        for w in range(2):
            for ci in range(NCI):
                ev['r2red'] = V('reduce_sum', T12[:, w, ci:ci + 1],
                                T12P[:, w, ci, 0:nt], mybir.AxisListType.X)
    on('vector', f_r2red)
    g2 = emit_ar('r2', T12[:], io['cc2i'], io['cc2o'], T12[:],
                 [(sDVE, sems['dve'].n)])

    def f_build_at(vector):
        sGPS.wait(vector, g2)
        V = mkV(vector)
        V('memset', AT1[:], 0.0)
        V('memset', AT2[:], 0.0)
        for ci in range(NCI):
            V('tensor_scalar_add', AT1[0:64, ci, 2 * ci:2 * ci + 1], T12[0:64, 0, ci:ci + 1], EPS_A)
            V('tensor_scalar_add', AT1[64:128, ci, 2 * ci + 1:2 * ci + 2], T12[64:128, 0, ci:ci + 1], EPS_A)
            V('tensor_scalar_add', AT2[0:64, ci, 2 * ci:2 * ci + 1], T12[0:64, 1, ci:ci + 1], EPS_A)
            ev['buildat'] = V('tensor_scalar_add', AT2[64:128, ci, 2 * ci + 1:2 * ci + 2], T12[64:128, 1, ci:ci + 1], EPS_A)
    on('vector', f_build_at)

    def f_ec2_mm(ten):
        sDVE.wait(ten, ev['buildat'])
        for ci in range(NCI):
            ten.matmul(p_row[0][0:16, 0:1], AT1[:, ci, :], vb['onesb'][:],
                       start=(ci == 0), stop=(ci == NCI - 1))
        for ci in range(NCI):
            ins = ten.matmul(p_row[1][0:16, 0:1], AT2[:, ci, :], vb['onesb'][:],
                             start=(ci == 0), stop=(ci == NCI - 1))
        ev['ec2mm'] = sPE.inc(ins)
    on('tensor', f_ec2_mm)

    def f_ec2(vector):
        sPE.wait(vector, ev['ec2mm'])
        V = mkV(vector)
        V('tensor_scalar_mul', EC[:, 2:3], p_row[0][0:16, 0:1], EPS_A)
        ev['ec2'] = V('tensor_scalar_mul', EC[:, 3:4], p_row[1][0:16, 0:1], EPS_A)
    on('vector', f_ec2)

    DBGETM = ctx.sb("c_dbgetm", [128, H], F32) if DBG else None
    KTT = [ctx.sb(f"c_ktt{b}", [128, 2, H], BF16) for b in range(2)]
    VTT = [ctx.sb(f"c_vtt{b}", [128, 2, H], BF16) for b in range(2)]
    ESB = ctx.sb("c_esb", [16, 2, TP], BF16)
    VSCB = ctx.sb("c_vscb", [128, 2, H], BF16)

    for i, (t0, T) in enumerate(tls):
        b = i % 2

        def f_cload(sync, i=i, b=b, t0=t0, T=T):
            if i >= 2:
                sPE.wait(sync, ev[('ckv', i - 2)])
                sDVE.wait(sync, ev[('cvsc', i - 2, 1)])
            elif ('bstt', nt - 1) in ev:
                sDVE.wait(sync, ev[('bstt', nt - 1)])
            tc0 = 2 * i
            ev[('cldq', i)] = sLDp[b].dma(sync, 
                out=QFT[b][:, :, 0:T], in_=io['qf'][:, :, t0:t0 + T])
            ev[('cldk', i)] = sLDa[b].dma(sync, 
                out=KFT[b][:, :, 0:T], in_=io['kf'][:, :, t0:t0 + T])
            ev[('cldkt', i)] = sLDb[b].dma(sync, 
                out=KTT[b][:], in_=io['kt'][tc0:tc0 + 2, :, :].rearrange("tc p d -> p tc d"))
            ev[('cldvt', i)] = sLDc[b].dma(sync, 
                out=VTT[b][:], in_=io['vt'][tc0:tc0 + 2, :, :].rearrange("tc p d -> p tc d"))
        on('sync', f_cload)

        def f_crowmm(ten, i=i, b=b, T=T):
            sLDp[b].wait(ten, ev[('cldq', i)])
            sLDa[b].wait(ten, ev[('cldk', i)])
            sLDb[b].wait(ten, ev[('cldkt', i)])
            if i == 0:
                sDVE.wait(ten, ev['ec2'])
            if i >= 1:
                sDVE.wait(ten, ev[('ces', i - 1)])
            for ci in range(NCI):
                ten.matmul(p_row[0][:, 0:T], AT1[:, ci, :], QFT[b][:, ci, 0:T],
                           start=(ci == 0), stop=(ci == NCI - 1))
            for ci in range(NCI):
                ins = ten.matmul(p_row[1][:, 0:T], AT2[:, ci, :], KFT[b][:, ci, 0:T],
                                 start=(ci == 0), stop=(ci == NCI - 1))
            ev[('crowmm', i)] = sPE.inc(ins)
        on('tensor', f_crowmm)

        def f_crow(vector, i=i, b=b, T=T, t0=t0):
            sPE.wait(vector, ev[('crowmm', i)])
            V = mkV(vector)
            r = i % 2
            V('tensor_scalar_add', ROWS[:, r, 0, 0:T], p_row[0][:, 0:T], EC[:, 2:3])
            ev[('cct', i)] = V('tensor_scalar', ROWS[:, r, 1, 0:T], p_row[1][:, 0:T],
                               EC[:, 3:4], None, ALU.add)
            ev[('cclip', i)] = V('tensor_scalar', ROWS[:, r, 2, 0:T], ROWS[:, r, 1, 0:T],
                                 1.0, -1.0, ALU.min, ALU.max)
        on('vector', f_crow)

        def f_cact(scalar, i=i, b=b, T=T, t0=t0):
            sDVE.wait(scalar, ev[('cclip', i)])
            A = mkA(scalar)
            r = i % 2
            ev[('csa', i)] = A('activation', SA[:, t0:t0 + T], ROWS[:, r, 0, 0:T], AF.Sigmoid)
            ev[('cexp', i)] = A('activation', ROWS[:, r, 3, 0:T], ROWS[:, r, 2, 0:T],
                                AF.Exp, accum_out=ZP[:, i:i + 1])
        on('scalar', f_cact)

        def f_ces(vector, i=i, b=b, T=T):
            sACT.wait(vector, ev[('cexp', i)])
            if i >= 2:
                sPE.wait(vector, ev[('ce', i - 2, 1)])
            r = i % 2
            ev[('ces', i)] = mkV(vector)('tensor_copy', ESB[:, b, 0:T], ROWS[:, r, 3, 0:T])
        on('vector', f_ces)

        for tch in range(2):
            def f_ce(ten, i=i, b=b, tch=tch):
                sDVE.wait(ten, ev[('ces', i)])
                if ('cvsc', i, tch - 1) in ev:
                    sDVE.wait(ten, ev[('cvsc', i, tch - 1)])
                elif i >= 1:
                    sDVE.wait(ten, ev[('cvsc', i - 1, 1)])
                ten.matmul(p_e[:, 0:512], ESB[:, b, tch * 128:(tch + 1) * 128],
                           vb['selrow'][:, 0:512], start=True, stop=True)
                ins = ten.matmul(p_e[:, 512:1024], ESB[:, b, tch * 128:(tch + 1) * 128],
                                 vb['selrow'][:, 512:1024], start=True, stop=True)
                ev[('ce', i, tch)] = sPE.inc(ins)
            on('tensor', f_ce)

            def f_cvsc(vector, i=i, b=b, tch=tch):
                sPE.wait(vector, ev[('ce', i, tch)])
                sLDc[b].wait(vector, ev[('cldvt', i)])
                ev[('cvsc', i, tch)] = mkV(vector)(
                    'tensor_mul', VSCB[:, tch, :], VTT[b][:, tch, :], p_e[:, 0:H])
            on('vector', f_cvsc)

            if DBG and i == nt - 1 and tch == 1:
                def f_dbg_etm(scalar, i=i):
                    sPE.wait(scalar, ev[('ce', i, 1)])
                    ev['dbgetm'] = mkA(scalar)('copy', DBGETM[:], p_e[:, 0:H])
                on('scalar', f_dbg_etm)

            def f_ckv(ten, i=i, b=b, tch=tch):
                sDVE.wait(ten, ev[('cvsc', i, tch)])
                for ci in range(NCI):
                    ins = ten.matmul(
                        p_kv[:, ci, :], KTT[b][:, tch, ci * 128:(ci + 1) * 128],
                        VSCB[:, tch, ci * 128:(ci + 1) * 128],
                        start=(i == 0 and tch == 0 and ci % 4 == 0),
                        stop=(i == nt - 1 and tch == 1 and ci % 4 == 3))
                ev[('ckv', i, tch)] = sPE.inc(ins)
            on('tensor', f_ckv)
        ev[('ckv', i)] = ev[('ckv', i, 1)]

    def f_kvev(scalar):
        sPE.wait(scalar, ev[('ckv', nt - 1)])
        ev['kvev'] = mkA(scalar)('copy', KVF[:], p_kv[:])
    on('scalar', f_kvev)

    def f_kvzero(vector):
        sACT.wait(vector, ev['kvev'])
        V = mkV(vector)
        for ci in range(NCI):
            V('memset', KVF[0:64, ci, 64:128], 0.0)
            V('memset', KVF[64:128, ci, 0:64], 0.0)
        for _ in (0,):
            ev['zred'] = V('reduce_sum', ZROW[:, 0:1], ZP[:, 0:nt], mybir.AxisListType.X)
    on('vector', f_kvzero)

    g3 = emit_ar('r3', KVF[:].rearrange("p c d -> p (c d)"), io['cc3i'], io['cc3o'],
                 KVG[:].rearrange("p c d -> p (c d)"), [(sDVE, sems['dve'].n)])
    g3z = emit_ar('r3z', ZROW[:, 0:1], io['cc4i'], io['cc4o'], ZROW[:, 0:1],
                  [(sGPS, g3)])

    def f_zinv(vector):
        sGPS.wait(vector, g3z)
        V = mkV(vector)
        V('reciprocal', ZROW[:, 1:2], ZROW[:, 0:1])
        V('tensor_scalar_mul', ZROW[:, 1:2], ZROW[:, 1:2], SSCALE)
        ev['zb'] = V('tensor_copy', ZB[:], ZROW[:, 1:2])
    on('vector', f_zinv)

    def f_zrep_mm(ten):
        sDVE.wait(ten, ev['zb'])
        for ci in range(NCI):
            ins = ten.matmul(p_bc[0][:, ci:ci + 1],
                             vb['selb'][:, ci, :], ZB[:], start=True, stop=True)
        ev['zrepmm'] = sPE.inc(ins)
    on('tensor', f_zrep_mm)

    def f_zrep(scalar):
        sPE.wait(scalar, ev['zrepmm'])
        ev['zrep'] = mkA(scalar)('copy', ZREP[:], p_bc[0][:, 0:NCI])
    on('scalar', f_zrep)

    def f_kvs(vector):
        sACT.wait(vector, ev['zrep'])
        V = mkV(vector)
        for ci in range(NCI):
            ev['kvs'] = V('tensor_scalar_mul', KVSB[:, ci, :], KVG[:, ci, :],
                          ZREP[:, ci:ci + 1])
    on('vector', f_kvs)

    if DBG:
        def f_dbg(g):
            sDVE.wait(g, sems['dve'].n)
            sACT.wait(g, sems['act'].n)
            for nm, ap in (("dbg_sa", SA[:]), ("dbg_kvg", KVG[:]),
                           ("dbg_kvsb", KVSB[:]), ("dbg_zrow", ZROW[:]),
                           ("dbg_zrep", ZREP[:]), ("dbg_t12", T12[:]),
                           ("dbg_at1", AT1[:]), ("dbg_at2", AT2[:]),
                           ("dbg_ec", EC[:]), ("dbg_vscb", VSCB[:]),
                           ("dbg_esb", ESB[:]), ("dbg_ktt", KTT[(nt - 1) % 2][:]),
                           ("dbg_etm", DBGETM[:])):
                sGPS.dma(g, out=io[nm], in_=ap)
        on('gpsimd', f_dbg)

    bc_scope.close()
    QST = [ctx.sb(f"d_qst{b}", [128, NCI, TP], BF16) for b in range(2)]
    XT = [ctx.sb(f"d_xt{b}", [128, NCI, TP], F32) for b in range(2)]
    SAB = ctx.sb("d_sab", [16, 2, TP], BF16)
    QSA = ctx.sb("d_qsa", [128, NCI, TP], BF16)
    OB = ctx.sb("d_ob", [128, NCI, TP], BF16)
    OX = ctx.sb("d_ox", [128, TP], F32)
    OT = [ctx.sb(f"d_ot{b}", [128, NCI, TP], F32) for b in range(2)]
    p_mm2 = [ctx.ps(f"d_pmm{k}", [128, TP]) for k in range(2)]

    for i, (t0, T) in enumerate(tls):
        b = i % 2

        def f_dload(sync, i=i, b=b, t0=t0, T=T):
            if i >= 2:
                sDVE.wait(sync, ev[('dep', i - 2)])
            else:
                sSTp[0].wait(sync, bst_done[0])
                sSTp[1].wait(sync, bst_done[1])
            ev[('dldq', i)] = sLDp[b].dma(sync, 
                out=QST[b][:, :, 0:T], in_=io['qsi'][:, :, t0:t0 + T])
            ev[('dld', i)] = sLDa[b].dma(sync, 
                out=XT[b][:, :, 0:T], in_=io['r2'][:, :, HALO + t0:HALO + t0 + T])
        on('sync', f_dload)

        def f_dsab(vector, i=i, b=b, T=T, t0=t0):
            if i == 0:
                sDVE.wait(vector, ev['kvs'])
            if i >= 1:
                sPE.wait(vector, ev[('dsabmm', i - 1, 7)])
            ev[('dsab', i)] = mkV(vector)('tensor_copy', SAB[:, b, 0:T], SA[:, t0:t0 + T])
        on('vector', f_dsab)

        for ci in range(NCI):
            def f_dbc(ten, i=i, b=b, T=T, ci=ci):
                sDVE.wait(ten, ev[('dsab', i)])
                if ci >= 2:
                    sDVE.wait(ten, ev[('dqsa', i, ci - 2)])
                elif i >= 1:
                    sDVE.wait(ten, ev[('dqsa', i - 1, 6 + ci)])
                ins = ten.matmul(p_bc[ci % 2][:, 0:T], vb['selb'][:, ci, :],
                                 SAB[:, b, 0:T], start=True, stop=True)
                ev[('dsabmm', i, ci)] = sPE.inc(ins)
            on('tensor', f_dbc)

            def f_dqsa(vector, i=i, b=b, T=T, ci=ci):
                sPE.wait(vector, ev[('dsabmm', i, ci)])
                sLDp[b].wait(vector, ev[('dldq', i)])
                if ci == 0 and i >= 1:
                    sACT.wait(vector, ev[('dob', i - 1, 7)])
                ev[('dqsa', i, ci)] = mkV(vector)(
                    'tensor_mul', QSA[:, ci, 0:T], QST[b][:, ci, 0:T], p_bc[ci % 2][:, 0:T])
            on('vector', f_dqsa)

            def f_dkv(ten, i=i, T=T, ci=ci):
                sDVE.wait(ten, ev[('dqsa', i, ci)])
                if ci >= 2:
                    sACT.wait(ten, ev[('dob', i, ci - 2)])
                elif i >= 1:
                    sACT.wait(ten, ev[('dob', i - 1, 6 + ci)])
                ins = ten.matmul(p_mm2[ci % 2][:, 0:T], KVSB[:, ci, :],
                                 QSA[:, ci, 0:T], start=True, stop=True)
                ev[('dkv', i, ci)] = sPE.inc(ins)
            on('tensor', f_dkv)

            def f_dob(scalar, i=i, T=T, ci=ci):
                sPE.wait(scalar, ev[('dkv', i, ci)])
                if ci == 0 and i >= 1:
                    sPE.wait(scalar, ev[('dproj', i - 1, 7)])
                ev[('dob', i, ci)] = mkA(scalar)('copy', OB[:, ci, 0:T],
                                                 p_mm2[ci % 2][:, 0:T])
            on('scalar', f_dob)

        for oc in range(8):
            def f_dproj(ten, i=i, b=b, T=T, oc=oc):
                if oc == 0:
                    sACT.wait(ten, ev[('dob', i, 7)])
                    sems['wld'].wait(ten, wo_ld)
                if oc >= 2:
                    sDVE.wait(ten, ev[('dep', i, oc - 2)])
                elif i >= 1:
                    sDVE.wait(ten, ev[('dep', i - 1, 6 + oc)])
                for ci in range(NCI):
                    ins = ten.matmul(p_mm2[oc % 2][:, 0:T], WO[:, ci, oc * 128:(oc + 1) * 128],
                                     OB[:, ci, 0:T], start=(ci == 0), stop=(ci == NCI - 1))
                ev[('dproj', i, oc)] = sPE.inc(ins)
            on('tensor', f_dproj)

            def f_dox(scalar, i=i, T=T, oc=oc):
                sPE.wait(scalar, ev[('dproj', i, oc)])
                if oc >= 1:
                    sDVE.wait(scalar, ev[('dep', i, oc - 1)])
                elif i >= 1:
                    sDVE.wait(scalar, ev[('dep', i - 1, 7)])
                ev[('dox', i, oc)] = mkA(scalar)(
                    'activation', OX[:, 0:T], p_mm2[oc % 2][:, 0:T], AF.Identity,
                    bias=vb['bo'][:, oc:oc + 1])
            on('scalar', f_dox)

            def f_dep(vector, i=i, b=b, T=T, oc=oc):
                sACT.wait(vector, ev[('dox', i, oc)])
                sLDa[b].wait(vector, ev[('dld', i)])
                if oc == 0 and i >= 2:
                    sSTp[b].wait(vector, ev[('dst', i - 2)])
                ev[('dep', i, oc)] = mkV(vector)(
                    'tensor_add', OT[b][:, oc, 0:T], OX[:, 0:T], XT[b][:, oc, 0:T])
            on('vector', f_dep)
        ev[('dep', i)] = ev[('dep', i, 7)]

        def f_dst(sync, i=i, b=b, t0=t0, T=T):
            sDVE.wait(sync, ev[('dep', i)])
            ev[('dst', i)] = sSTp[b].inc(
                sync.dma_start(out=io['r3'][:, :, t0:t0 + T], in_=OT[b][:, :, 0:T]), 16)
        on('sync', f_dst)

    def f_dfin(sync):
        sSTp[(nt - 1) % 2].wait(sync, ev[('dst', nt - 1)])
        sSTp[(nt - 2) % 2].wait(sync, ev[('dst', nt - 2)])
    on('sync', f_dfin)


def emit_phase_l1f(ctx, block, io, vb, sems):
    TS = min(512, TCORE)
    TT = TS // 2
    NS = TCORE // TS
    NH = 32
    sPE, sACT, sDVE, sGPS = sems['pe'], sems['act'], sems['dve'], sems['gps']
    sLDp = [sems['ld0'], sems['ld1']]
    sSTp = [sems['st0'], sems['st1']]
    sWL = [sems['wl0'], sems['wl1']]
    ev = {}

    LNB = ctx.sb("f_lnb", [128, NCI, TS], BF16)
    GX = ctx.sb("f_gx", [128, NH, TS], BF16)
    GY = ctx.sb("f_gy", [128, NH, TS], BF16)
    XT = ctx.sb("f_xt", [128, NCI, TT], F32)
    WB1 = [ctx.sb(f"f_wb1{k}", [128, NCI, 128], BF16) for k in range(2)]
    WB2 = [ctx.sb(f"f_wb2{k}", [128, NCI, 128], BF16) for k in range(2)]
    WH = [ctx.sb(f"f_wh{k}", [128, NH, 128], BF16) for k in range(2)]
    W3B = [ctx.sb(f"f_w3b{k}", [128, NH, 128], BF16) for k in range(2)]
    SILT = ctx.sb("f_silt", [128, TT], BF16)
    OX = ctx.sb("f_ox", [128, TT], F32)
    OTS = [ctx.sb(f"f_ots{k}", [128, TT], F32) for k in range(2)]
    ln = LNHelper(ctx, sems, vb, "f", TT)
    p1 = ctx.ps("f_p1", [128, TT])
    p2 = ctx.ps("f_p2", [128, TT])

    def on(name, f):
        getattr(block, name)(f)

    def mkV(vector):
        def V(fn, *args, **kw):
            sDVE.wait(vector, sDVE.n)
            return sDVE.inc(getattr(vector, fn)(*args, **kw))
        return V

    def mkA(scalar):
        def A(fn, *args, **kw):
            sACT.wait(scalar, sACT.n)
            return sACT.inc(getattr(scalar, fn)(*args, **kw))
        return A

    for s in range(NS):
        S0 = s * TS

        for tt in range(2):
            i = s * 2 + tt
            t0 = S0 + tt * TT

            def f_load(sync, i=i, t0=t0, s=s, tt=tt):
                if i >= 1:
                    sDVE.wait(sync, ev[('ln2', i - 1)])
                    sACT.wait(sync, ev[('sq', i - 1)])
                if s >= 1 and tt == 0:
                    sDVE.wait(sync, ev[('f4ep', s - 1, 1)])
                ev[('fxld', i)] = sLDp[0].dma(
                    sync, out=XT[:], in_=io['r3'][:, :, t0:t0 + TT])
            on('sync', f_load)

            def ld_wait(eng, i=i):
                sLDp[0].wait(eng, ev[('fxld', i)])

            def tgt(vector, V, ci, i=i, s=s, tt=tt):
                if ci == 0 and s >= 1:
                    sPE.wait(vector, ev[('f1done', s - 1)])
                return V('tensor_add', LNB[:, ci, tt * TT:(tt + 1) * TT],
                         ln.LNT[:, ci, 0:TT], ln.p_b[:, 0:TT])

            ln.emit(block, ev, i, 0, XT, TT, tgt, ld_wait)

        for oc in range(NH):
            kb = oc % 2

            def f_wl(g, s=s, oc=oc, kb=kb):
                if oc >= 2:
                    sPE.wait(g, ev[('f1mm', s, oc - 2)])
                elif s >= 1:
                    sPE.wait(g, ev[('f1mm', s - 1, NH - 2 + oc)])
                sWL[kb].dma(g, out=WB1[kb][:], in_=io['w1'][oc, :, :, :])
                ev[('f1w', s, oc)] = sWL[kb].dma(g, out=WB2[kb][:], in_=io['w2'][oc, :, :, :])
            on('gpsimd', f_wl)

            for tt in range(2):
                def f_mm(ten, s=s, oc=oc, tt=tt, kb=kb):
                    if tt == 1:
                        sACT.wait(ten, ev[('f1silu', s, oc, 0)])
                        sDVE.wait(ten, ev[('f1stt', s, oc, 0)])
                    if tt == 0:
                        sWL[kb].wait(ten, ev[('f1w', s, oc)])
                        if oc == 0:
                            sDVE.wait(ten, ev[('ln2', s * 2 + 1)])
                        if oc >= 1:
                            sDVE.wait(ten, ev[('f1stt', s, oc - 1, 1)])
                        elif s >= 1:
                            sDVE.wait(ten, ev[('f1stt', s - 1, NH - 1, 1)])
                    for ci in range(NCI):
                        ten.matmul(p1[:, 0:TT], WB1[kb][:, ci, :],
                                   LNB[:, ci, tt * TT:(tt + 1) * TT],
                                   start=(ci == 0), stop=(ci == NCI - 1))
                    for ci in range(NCI):
                        ins = ten.matmul(p2[:, 0:TT], WB2[kb][:, ci, :],
                                         LNB[:, ci, tt * TT:(tt + 1) * TT],
                                         start=(ci == 0), stop=(ci == NCI - 1))
                    ev[('f1mm', s, oc)] = ev[('f1mmt', s, oc, tt)] = sPE.inc(ins)
                on('tensor', f_mm)

                def f_silu(scalar, s=s, oc=oc, tt=tt):
                    sPE.wait(scalar, ev[('f1mmt', s, oc, tt)])
                    ev[('f1silu', s, oc, tt)] = mkA(scalar)(
                        'activation', SILT[:, 0:TT], p1[:, 0:TT], AF.Silu,
                        scale=vb['s1v'][:, oc:oc + 1])
                on('scalar', f_silu)

                def f_stt(vector, s=s, oc=oc, tt=tt):
                    sACT.wait(vector, ev[('f1silu', s, oc, tt)])
                    if oc == 0 and tt == 0 and s >= 1:
                        sPE.wait(vector, ev[('f4mm', s - 1, 1, 7)])
                    ev[('f1stt', s, oc, tt)] = mkV(vector)(
                        'scalar_tensor_tensor', GX[:, oc, tt * TT:(tt + 1) * TT],
                        p2[:, 0:TT], vb['s2v'][:, oc:oc + 1], SILT[:, 0:TT],
                        ALU.mult, ALU.mult)
                on('vector', f_stt)
        ev[('f1done', s)] = ev[('f1mm', s, NH - 1)]

        for stage, (SRC, DST, wkey, skey) in enumerate((
                (GX, GY, 'h1', 'h1s'), (GY, GX, 'h2', 'h2s'))):
            for oc in range(NH):
                kb = oc % 2

                def f_whl(g, s=s, oc=oc, kb=kb, stage=stage, wkey=wkey):
                    if oc >= 2:
                        sPE.wait(g, ev[('fhmm', s, stage, oc - 2)])
                    elif stage >= 1:
                        sPE.wait(g, ev[('fhmm', s, 0, NH - 2 + oc)])
                    elif s >= 1:
                        sPE.wait(g, ev[('f4mm', s - 1, 1, 6 + oc)])
                    ev[('fhw', s, stage, oc)] = sWL[kb].dma(
                        g, out=WH[kb][:], in_=io[wkey][oc, :, :, :])
                on('gpsimd', f_whl)

                for tt in range(2):
                    def f_hmm(ten, s=s, oc=oc, tt=tt, kb=kb, stage=stage, SRC=SRC):
                        if tt == 1:
                            sACT.wait(ten, ev[('fg', s, stage, oc, 0)])
                        if tt == 0:
                            sWL[kb].wait(ten, ev[('fhw', s, stage, oc)])
                            if oc == 0:
                                if stage == 0:
                                    sDVE.wait(ten, ev[('f1stt', s, NH - 1, 1)])
                                else:
                                    sACT.wait(ten, ev[('fg', s, 0, NH - 1, 1)])
                            if oc >= 1:
                                sACT.wait(ten, ev[('fg', s, stage, oc - 1, 1)])
                            elif stage == 1:
                                sACT.wait(ten, ev[('fg', s, 0, NH - 1, 1)])
                        for hc in range(NH):
                            ins = ten.matmul(p1[:, 0:TT], WH[kb][:, hc, :],
                                             SRC[:, hc, tt * TT:(tt + 1) * TT],
                                             start=(hc == 0), stop=(hc == NH - 1))
                        ev[('fhmm', s, stage, oc)] = ev[('fhmmt', s, stage, oc, tt)] = sPE.inc(ins)
                    on('tensor', f_hmm)

                    def f_gelu(scalar, s=s, oc=oc, tt=tt, stage=stage, DST=DST, skey=skey):
                        sPE.wait(scalar, ev[('fhmmt', s, stage, oc, tt)])
                        if oc == 0 and tt == 0:
                            if stage == 0 and s >= 1:
                                sPE.wait(scalar, ev[('fhmm', s - 1, 1, NH - 1)])
                            elif stage == 1:
                                sPE.wait(scalar, ev[('fhmm', s, 0, NH - 1)])
                        ev[('fg', s, stage, oc, tt)] = mkA(scalar)(
                            'activation', DST[:, oc, tt * TT:(tt + 1) * TT],
                            p1[:, 0:TT], AF.Gelu, scale=vb[skey][:, oc:oc + 1])
                    on('scalar', f_gelu)
        ev[('f2done', s)] = ev[('fhmm', s, 1, NH - 1)]

        for tt in range(2):
            t0 = S0 + tt * TT

            def f_x4(sync, s=s, tt=tt, t0=t0):
                sDVE.wait(sync, ev[('ln2', s * 2 + 1)])
                sACT.wait(sync, ev[('sq', s * 2 + 1)])
                if s + tt >= 1:
                    sDVE.wait(sync, ev[('f4ep', s, tt - 1) if tt >= 1 else ('f4ep', s - 1, 1)])
                ev[('f4x', s, tt)] = sLDp[1].dma(
                    sync, out=XT[:], in_=io['r3'][:, :, t0:t0 + TT])
            on('sync', f_x4)

            for oc in range(NCI):
                kb = oc % 2

                def f_w3l(g, s=s, tt=tt, oc=oc, kb=kb):
                    if oc >= 2:
                        sPE.wait(g, ev[('f4mm', s, tt, oc - 2)])
                    elif tt == 1:
                        sPE.wait(g, ev[('f4mm', s, 0, 6 + oc)])
                    elif s >= 1:
                        sPE.wait(g, ev[('f4mm', s - 1, 1, 6 + oc)])
                    ev[('f4w', s, tt, oc)] = sWL[kb].dma(
                        g, out=W3B[kb][:], in_=io['w3'][oc, :, :, :])
                on('gpsimd', f_w3l)

                def f_mm3(ten, s=s, tt=tt, oc=oc, kb=kb):
                    sWL[kb].wait(ten, ev[('f4w', s, tt, oc)])
                    if oc == 0 and tt == 0:
                        sACT.wait(ten, ev[('fg', s, 1, NH - 1, 1)])
                    if oc >= 1:
                        sACT.wait(ten, ev[('f4ox', s, tt, oc - 1)])
                    elif tt == 1:
                        sACT.wait(ten, ev[('f4ox', s, 0, 7)])
                    elif s >= 1:
                        sACT.wait(ten, ev[('f4ox', s - 1, 1, 7)])
                    for hc in range(NH):
                        ins = ten.matmul(p1[:, 0:TT], W3B[kb][:, hc, :],
                                         GX[:, hc, tt * TT:(tt + 1) * TT],
                                         start=(hc == 0), stop=(hc == NH - 1))
                    ev[('f4mm', s, tt, oc)] = sPE.inc(ins)
                on('tensor', f_mm3)

                def f_ox(scalar, s=s, tt=tt, oc=oc):
                    sPE.wait(scalar, ev[('f4mm', s, tt, oc)])
                    if oc >= 1:
                        sDVE.wait(scalar, ev[('f4add', s, tt, oc - 1)])
                    elif tt == 1:
                        sDVE.wait(scalar, ev[('f4add', s, 0, 7)])
                    elif s >= 1:
                        sDVE.wait(scalar, ev[('f4add', s - 1, 1, 7)])
                    ev[('f4ox', s, tt, oc)] = mkA(scalar)(
                        'activation', OX[:, 0:TT], p1[:, 0:TT], AF.Identity,
                        scale=vb['s3v'][:, oc:oc + 1])
                on('scalar', f_ox)

                def f_add(vector, s=s, tt=tt, oc=oc):
                    sACT.wait(vector, ev[('f4ox', s, tt, oc)])
                    sLDp[1].wait(vector, ev[('f4x', s, tt)])
                    ob = oc % 2
                    prev = None
                    if oc >= 2:
                        prev = (s, tt, oc - 2)
                    elif tt == 1:
                        prev = (s, 0, 6 + oc)
                    elif s >= 1:
                        prev = (s - 1, 1, 6 + oc)
                    if prev is not None:
                        sSTp[ob].wait(vector, ev[('fyst',) + prev])
                    ev[('f4add', s, tt, oc)] = ev[('f4ep', s, tt)] = mkV(vector)(
                        'tensor_add', OTS[ob][:, 0:TT], OX[:, 0:TT], XT[:, oc, 0:TT])
                on('vector', f_add)

                def f_yst(sync, s=s, tt=tt, oc=oc, t0=t0):
                    ob = oc % 2
                    sDVE.wait(sync, ev[('f4add', s, tt, oc)])
                    ev[('fyst', s, tt, oc)] = sSTp[ob].dma(
                        sync, out=io['y'][:, oc, t0:t0 + TT], in_=OTS[ob][:, 0:TT])
                on('sync', f_yst)

    def f_fin(sync):
        sSTp[0].wait(sync, ev[('fyst', NS - 1, 1, 6)])
        sSTp[1].wait(sync, ev[('fyst', NS - 1, 1, 7)])
        if DBG:
            sLDp[0].dma(sync, out=io['dbg_lnb'][:, :, 0:TS], in_=LNB[:])
            sLDp[0].dma(sync, out=io['dbg_gx'][:, :, 0:TS], in_=GX[:])
            sLDp[0].dma(sync, out=io['dbg_gy'][:, :, 0:TS], in_=GY[:])
            sLDp[0].dma(sync, out=io['dbg_flnt'][:, :, 0:TT], in_=ln.LNT[:, :, 0:TT])
            sLDp[0].dma(sync, out=io['dbg_frows'][:, :, :, 0:TT], in_=ln.ROWS[:, :, :, 0:TT])
            sLDp[0].wait(sync, sLDp[0].n)
    on('sync', f_fin)



ALL_PHASES = ('l0a', 'l0f', 'l1aA', 'l1aBCD', 'l1f')


def build_program(phases=ALL_PHASES, io_external=(), io_extin=(), cc_groups=None):
    if cc_groups is None:
        cc_groups = [[0, 1, 2, 3], [4, 5, 6, 7]]
    nc = bass.Bass("TRN2", target_bir_lowering=False)
    es = contextlib.ExitStack()
    ctx = Ctx(nc, es)
    with es:
        def dram(name, shape, dtype=F32, ext_in=False):
            if ext_in:
                kind = "ExternalInput"
            elif name in io_external or (DBG and name.startswith("dbg_")):
                kind = "ExternalOutput"
            else:
                kind = "Internal"
            return nc.dram_tensor(name, shape, dtype, kind=kind).ap()

        io = {}
        io['x'] = dram("x", [128, NCI, NTOK], ext_in=True)
        io['r1'] = dram("r1", [128, NCI, NTOK])
        io['r2'] = dram("r2", [128, NCI, NTOK], ext_in=('r2' in io_extin))
        io['qf'] = dram("qf", [128, NCI, TCORE], BF16)
        io['kf'] = dram("kf", [128, NCI, TCORE], BF16)
        io['kt'] = dram("kt", [TCORE // 128, 128, H], BF16)
        io['vt'] = dram("vt", [TCORE // 128, 128, H], BF16)
        io['qsi'] = dram("qsi", [128, NCI, TCORE], BF16)
        io['r3'] = dram("r3", [128, NCI, TCORE], ext_in=('r3' in io_extin))
        io['y'] = dram("y", [128, NCI, TCORE], ext_in=False)
        io['cc1i'] = dram("cc1i", [128, 2 * NCI])
        io['cc1o'] = dram("cc1o", [128, 2 * NCI])
        io['cc2i'] = dram("cc2i", [128, 2 * NCI])
        io['cc2o'] = dram("cc2o", [128, 2 * NCI])
        io['cc3i'] = dram("cc3i", [128, NCI * 128])
        io['cc3o'] = dram("cc3o", [128, NCI * 128])
        io['cc4i'] = dram("cc4i", [16, 1])
        io['cc4o'] = dram("cc4o", [16, 1])
        if DBG:
            io['dbg_sa'] = dram("dbg_sa", [16, TCORE], ext_in=False)
            io['dbg_kvg'] = dram("dbg_kvg", [128, NCI, 128])
            io['dbg_kvsb'] = dram("dbg_kvsb", [128, NCI, 128], BF16)
            io['dbg_zrow'] = dram("dbg_zrow", [16, 2])
            io['dbg_zrep'] = dram("dbg_zrep", [128, NCI])
            io['dbg_t12'] = dram("dbg_t12", [128, 2, NCI])
            io['dbg_at1'] = dram("dbg_at1", [128, NCI, 16], BF16)
            io['dbg_at2'] = dram("dbg_at2", [128, NCI, 16], BF16)
            io['dbg_ec'] = dram("dbg_ec", [16, 4])
            io['dbg_vscb'] = dram("dbg_vscb", [128, 2, H], BF16)
            io['dbg_esb'] = dram("dbg_esb", [16, 2, 256], BF16)
            io['dbg_ktt'] = dram("dbg_ktt", [128, 2, H], BF16)
            io['dbg_etm'] = dram("dbg_etm", [128, H])
            io['dbg_lnb'] = dram("dbg_lnb", [128, NCI, 1024], BF16)
            io['dbg_gx'] = dram("dbg_gx", [128, 32, 1024], BF16)
            io['dbg_gy'] = dram("dbg_gy", [128, 32, 1024], BF16)
            io['dbg_flnt'] = dram("dbg_flnt", [128, NCI, 512])
            io['dbg_frows'] = dram("dbg_frows", [1, 1, 5, 512])

        wnames = []
        if 'l0a' in phases:
            wnames += ['l0a_wv', 'l0a_wr', 'l0a_wo']
        if 'l0f' in phases:
            wnames += ['l0f_wk', 'l0f_wr', 'l0f_wv']
        if 'l1aA' in phases:
            wnames += ['l1a_wq', 'l1a_wk', 'l1a_wv']
        if 'l1aBCD' in phases:
            wnames += ['l1a_wo']
        dw = {n: dram(n, [128, NCI, H], ext_in=True) for n in wnames}
        if 'l1f' in phases:
            io['w1'] = dram("w1", [32, 128, NCI, 128], ext_in=True)
            io['w2'] = dram("w2", [32, 128, NCI, 128], ext_in=True)
            io['h1'] = dram("h1", [32, 128, 32, 128], ext_in=True)
            io['h2'] = dram("h2", [32, 128, 32, 128], ext_in=True)
            io['w3'] = dram("w3", [NCI, 128, 32, 128], ext_in=True)
            io['s1v'] = dram("s1v", [128, 32], ext_in=True)
            io['s2v'] = dram("s2v", [128, 32], ext_in=True)
            io['h1sv'] = dram("h1sv", [128, 32], ext_in=True)
            io['h2sv'] = dram("h2sv", [128, 32], ext_in=True)
            io['s3v'] = dram("s3v", [128, NCI], ext_in=True)
        d_ident = dram("ident", [128, 128], ext_in=True)
        d_selb = dram("selb", [NCI, 16, 128], ext_in=True)
        d_selrow = dram("selrow", [16, H], ext_in=True)

        vnames = ['td', 'tmv', 'tmr', 'mur', 'muk', 'bq', 'bk', 'bv', 'bo']
        dv = {n: dram("vec_" + n, [128, NCI], ext_in=True) for n in vnames}

        vb = {n: ctx.sb("v_" + n, [128, NCI], F32) for n in vnames}
        vb['d'] = ctx.sb("v_d", [128, NCI], F32)
        vb['dneg'] = ctx.sb("v_dneg", [128, NCI], F32)
        vb['consts'] = ctx.sb("v_consts", [128, 4], F32)
        vb['ones_inv'] = ctx.sb("v_onesinv", [128, 1], BF16)
        vb['ones1'] = ctx.sb("v_ones1", [1, 128], F32)
        vb['nones1'] = ctx.sb("v_nones1", [1, 128], F32)
        vb['idf'] = ctx.sb("v_idf", [128, 128], F32)
        vb['idb'] = ctx.sb("v_idb", [128, 128], BF16)
        vb['selbf'] = ctx.sb("v_selbf", [16, NCI, 128], F32)
        vb['selb'] = ctx.sb("v_selb", [16, NCI, 128], BF16)
        vb['selrowf'] = ctx.sb("v_selrowf", [16, H], F32)
        vb['selrow'] = ctx.sb("v_selrow", [16, H], BF16)
        vb['onesb'] = ctx.sb("v_onesb", [128, 1], BF16)
        if 'l1f' in phases:
            vb['s1v'] = ctx.sb("v_s1v", [128, 32], F32)
            vb['s2v'] = ctx.sb("v_s2v", [128, 32], F32)
            vb['h1s'] = ctx.sb("v_h1s", [128, 32], F32)
            vb['h2s'] = ctx.sb("v_h2s", [128, 32], F32)
            vb['s3v'] = ctx.sb("v_s3v", [128, NCI], F32)

        if 'l1aA' in phases:
            nt_a = len(tiles_of(TCORE, 256))
            QSP = ctx.sb("a_qsp", [128, NCI, nt_a], F32)
            KSP = ctx.sb("a_ksp", [128, NCI, nt_a], F32)

        wscope = contextlib.ExitStack()
        wscope_closed = False
        ctx.scope = wscope
        wslot = [ctx.sb(f"wslot{k}", [128, NCI, H], BF16) for k in range(4)]
        ctx.scope = es

        semnames = ('pe', 'act', 'dve', 'gps', 'wld',
                    'ld0', 'ld1', 'st0', 'st1', 'cc',
                    'lda0', 'lda1', 'ldb0', 'ldb1', 'ldc0', 'ldc1',
                    'sta0', 'sta1', 'stb0', 'stb1', 'stc0', 'stc1',
                    'wl0', 'wl1')
        sems = {k: ctx.sem("s_" + k) for k in semnames}
        ph = Phases(ctx, sems)

        with nc.Block() as blk0:
            def f_vload(s):
                for n in vnames:
                    sems['ld0'].dma(s, out=vb[n][:], in_=dv[n])
                sems['ld0'].dma(s, out=vb['idf'][:], in_=d_ident)
                sems['ld0'].dma(s, 
                    out=vb['selbf'][:], in_=d_selb.rearrange("c h j -> h c j"))
                sems['ld0'].dma(s, out=vb['selrowf'][:], in_=d_selrow)
                if 'l1f' in phases:
                    for a, bnm in (('s1v', 's1v'), ('s2v', 's2v'), ('h1s', 'h1sv'),
                                   ('h2s', 'h2sv'), ('s3v', 's3v')):
                        sems['ld0'].dma(s, out=vb[a][:], in_=io[bnm])
            blk0.sync(f_vload)

            def f_consts(v):
                sems['ld0'].wait(v, sems['ld0'].n)
                sDVE = sems['dve']
                for ins in (
                    v.memset(vb['consts'][:, 0:1], EPS_LN),
                    v.memset(vb['consts'][:, 1:2], EPS_A),
                    v.memset(vb['ones_inv'][:], 1.0 / H),
                    v.memset(vb['ones1'][:], 1.0),
                    v.memset(vb['nones1'][:], -1.0),
                    v.tensor_copy(vb['idb'][:], vb['idf'][:]),
                    v.tensor_copy(vb['selb'][:], vb['selbf'][:]),
                    v.tensor_copy(vb['selrow'][:], vb['selrowf'][:]),
                    v.memset(vb['onesb'][:], 1.0),
                ):
                    sDVE.inc(ins)
            blk0.vector(f_consts)

            def f_d(sc):
                sems['dve'].wait(sc, sems['dve'].n)
                sems['act'].inc(sc.activation(vb['dneg'][:], vb['td'][:], AF.Exp))
                sems['act'].wait(sc, sems['act'].n)
                sems['act'].inc(sc.activation(vb['d'][:], vb['dneg'][:], AF.Exp, scale=-1.0))
            blk0.scalar(f_d)
        ph.snapshot()

        if 'l0a' in phases:
            with contextlib.ExitStack() as ps:
                ctx.scope = ps
                with nc.Block() as blk:
                    ph.emit_guard(blk)
                    load_phase_weights(blk, sems, [
                        (wslot[0][:], dw['l0a_wv']),
                        (wslot[1][:], dw['l0a_wr']),
                        (wslot[2][:], dw['l0a_wo'])])
                    wv = sems['wld'].n

                    def f_w(t):
                        sems['wld'].wait(t, wv)
                    blk.tensor(f_w)
                    emit_phase_l0(ctx, blk, io['x'], io['r1'],
                                  wslot[0], wslot[1], wslot[2], vb, sems, 'attn')
                ctx.scope = es
            ph.snapshot()

        if 'l0f' in phases:
            with contextlib.ExitStack() as ps:
                ctx.scope = ps
                with nc.Block() as blk:
                    ph.emit_guard(blk)
                    load_phase_weights(blk, sems, [
                        (wslot[0][:], dw['l0f_wk']),
                        (wslot[1][:], dw['l0f_wr']),
                        (wslot[2][:], dw['l0f_wv'])])
                    wv = sems['wld'].n

                    def f_w(t):
                        sems['wld'].wait(t, wv)
                    blk.tensor(f_w)
                    emit_phase_l0(ctx, blk, io['r1'], io['r2'],
                                  wslot[0], wslot[1], wslot[2], vb, sems, 'ffn')
                ctx.scope = es
            ph.snapshot()

        if 'l1aA' in phases:
            with contextlib.ExitStack() as ps:
                ctx.scope = ps
                with nc.Block() as blk:
                    ph.emit_guard(blk)
                    wpairs = [
                        (wslot[0][:], dw['l1a_wq']),
                        (wslot[1][:], dw['l1a_wk']),
                        (wslot[2][:], dw['l1a_wv'])]
                    load_phase_weights(blk, sems, wpairs)
                    wv = sems['wld'].n

                    def f_w(t):
                        sems['wld'].wait(t, wv)
                    blk.tensor(f_w)
                    emit_phase_l1a_A(ctx, blk, io, wslot[0], wslot[1],
                                     wslot[2], vb, sems, QSP, KSP)
                ctx.scope = es
            ph.snapshot()

        if 'l1aBCD' in phases:
            wscope.close()
            wscope_closed = True
            io['dw_l1a_wo'] = dw['l1a_wo']
            with contextlib.ExitStack() as ps:
                ctx.scope = ps
                with nc.Block() as blk:
                    ph.emit_guard(blk)
                    emit_l1a_BCD(ctx, blk, io, vb, sems,
                                 QSP, KSP, nt_a, cc_groups)
                ctx.scope = es
            ph.snapshot()

        if 'l1f' in phases:
            if not wscope_closed:
                wscope.close()
                wscope_closed = True
            with contextlib.ExitStack() as ps:
                ctx.scope = ps
                with nc.Block() as blk:
                    ph.emit_guard(blk)
                    emit_phase_l1f(ctx, blk, io, vb, sems)
                ctx.scope = es
            ph.snapshot()

        return nc



def _fm(x2d):
    T = x2d.shape[0]
    return np.ascontiguousarray(x2d.T.reshape(NCI, 128, T).transpose(1, 0, 2))


def _unfm(a):
    return a.transpose(1, 0, 2).reshape(H, -1).T


def _wT(W):
    return np.ascontiguousarray(W.T.reshape(NCI, 128, W.shape[0]).transpose(1, 0, 2))


def _vec(v):
    return np.ascontiguousarray(np.asarray(v, np.float32).reshape(NCI, 128).T)


def make_core_inputs(full, phases=ALL_PHASES):
    x = np.asarray(full['L0_attn_x'])
    per = {}
    if 'l0a' in phases:
        per['l0a_wv'] = _wT(np.asarray(full['L0_attn_Wv']))
        per['l0a_wr'] = _wT(np.asarray(full['L0_attn_Wr']))
        per['l0a_wo'] = _wT(np.asarray(full['L0_attn_Wo']))
    if 'l0f' in phases:
        per['l0f_wk'] = _wT(np.asarray(full['L0_ffn_W_k']))
        per['l0f_wr'] = _wT(np.asarray(full['L0_ffn_W_r']))
        per['l0f_wv'] = _wT(np.asarray(full['L0_ffn_W_v']))
    if 'l1aA' in phases:
        per['l1a_wq'] = _wT(np.asarray(full['L1_attn_Wq']))
        per['l1a_wk'] = _wT(np.asarray(full['L1_attn_Wk']))
        per['l1a_wv'] = _wT(np.asarray(full['L1_attn_Wv']))
    if 'l1aBCD' in phases:
        per['l1a_wo'] = _wT(np.asarray(full['L1_attn_Wo']))
    per['vec_td'] = _vec(full['L0_attn_time_decay'])
    per['vec_tmv'] = _vec(full['L0_attn_time_mix_v'])
    per['vec_tmr'] = _vec(full['L0_attn_time_mix_r'])
    per['vec_mur'] = _vec(full['L0_ffn_mu_r'])
    per['vec_muk'] = _vec(full['L0_ffn_mu_k'])
    per['vec_bq'] = _vec(full['L1_attn_bq'])
    per['vec_bk'] = _vec(full['L1_attn_bk'])
    per['vec_bv'] = _vec(full['L1_attn_bv'])
    per['vec_bo'] = _vec(full['L1_attn_bo'])
    per['ident'] = np.eye(128, dtype=np.float32)
    selb = np.zeros((NCI, 16, 128), np.float32)
    for ci in range(NCI):
        selb[ci, 2 * ci, 0:64] = 1.0
        selb[ci, 2 * ci + 1, 64:128] = 1.0
    per['selb'] = selb
    selrow = np.zeros((16, H), np.float32)
    for hh in range(16):
        selrow[hh, hh * 64:(hh + 1) * 64] = 1.0
    per['selrow'] = selrow
    if 'l1f' in phases:
        def blk4(W, no):
            W = np.asarray(W, np.float32)
            return np.ascontiguousarray(
                W.reshape(no, 128, W.shape[1] // 128, 128).transpose(0, 3, 2, 1))
        per['w1'] = blk4(full['L1_ffn_w1_w'], 32)
        per['w2'] = blk4(full['L1_ffn_w2_w'], 32)
        per['h1'] = blk4(full['L1_ffn_h1_w'], 32)
        per['h2'] = blk4(full['L1_ffn_h2_w'], 32)
        per['w3'] = blk4(full['L1_ffn_w3_w'], NCI)
        per['s1v'] = np.ascontiguousarray(np.asarray(full['L1_ffn_w1_s'], np.float32).reshape(32, 128).T)
        per['s2v'] = np.ascontiguousarray(np.asarray(full['L1_ffn_w2_s'], np.float32).reshape(32, 128).T)
        per['h1sv'] = np.ascontiguousarray(np.asarray(full['L1_ffn_h1_s'], np.float32).reshape(32, 128).T)
        per['h2sv'] = np.ascontiguousarray(np.asarray(full['L1_ffn_h2_s'], np.float32).reshape(32, 128).T)
        per['s3v'] = np.ascontiguousarray(np.asarray(full['L1_ffn_w3_s'], np.float32).reshape(NCI, 128).T)
    maps = []
    for c in range(NCORES):
        bidx, s = divmod(c, 4)
        s0 = s * TCORE
        if s0 == 0:
            halo = np.zeros((HALO, H), np.float32)
        else:
            halo = x[bidx, s0 - HALO:s0]
        xs = np.concatenate([halo, x[bidx, s0:s0 + TCORE]], axis=0)
        m = {'x': _fm(xs)}
        m.update(per)
        maps.append(m)
    return maps


def kernel(**inputs):
    maps = make_core_inputs(inputs, phases=ALL_PHASES)
    nc = build_program(phases=ALL_PHASES, io_external=('y',))
    res = run_bass_kernel_spmd(nc, maps, core_ids=list(range(NCORES)))
    out = np.empty((2, 4 * TCORE, H), np.float32)
    for c in range(NCORES):
        y = np.asarray(res.results[c]['y']).reshape(128, NCI, TCORE)
        bidx, s = divmod(c, 4)
        out[bidx, s * TCORE:(s + 1) * TCORE] = _unfm(y.astype(np.float32))
    return out


# version 04
# speedup vs baseline: 1.1687x; 1.1687x over previous
import contextlib
import numpy as np
import concourse.bass as bass
import concourse.mybir as mybir
from concourse.bass_utils import run_bass_kernel_spmd

F32 = mybir.dt.float32
BF16 = mybir.dt.bfloat16
AF = mybir.ActivationFunctionType
ALU = mybir.AluOpType

H = 1024
NCI = 8
HALO = 128
TCORE = 4096
NTOK = TCORE + HALO
NCORES = 8
EPS_LN = 1e-5
EPS_A = 1e-6
SSCALE = 16384.0
DBG = False


def tiles_of(total, step, base=0):
    out = []
    t = 0
    while t < total:
        out.append((base + t, min(step, total - t)))
        t += step
    return out


class Sem:
    def __init__(self, ctx, name):
        self.h = ctx.es.enter_context(ctx.nc.semaphore(name))
        self.n = 0

    def inc(self, instr, k=1):
        instr.then_inc(self.h, k)
        self.n += k
        return self.n

    def wait(self, eng, val):
        if val > 0:
            eng.wait_ge(self.h, val)

    def dma(self, eng, *args, **kw):
        self.wait(eng, self.n)
        return self.inc(eng.dma_start(*args, **kw), 16)


class Ctx:
    def __init__(self, nc, es):
        self.nc = nc
        self.es = es
        self.scope = es

    def sem(self, name):
        return Sem(self, name)

    def sb(self, name, shape, dtype):
        return self.scope.enter_context(self.nc.sbuf_tensor(name, shape, dtype))

    def ps(self, name, shape, dtype=F32):
        return self.scope.enter_context(self.nc.psum_tensor(name, shape, dtype))


ENGS = ('sync', 'gpsimd', 'scalar', 'vector', 'tensor')


class Phases:

    def __init__(self, ctx, sems):
        self.ctx = ctx
        self.sems = sems
        self.guard = {}

    def snapshot(self):
        self.guard = {k: s.n for k, s in self.sems.items()}

    def emit_guard(self, block):
        g = dict(self.guard)

        def f(eng):
            for k, v in g.items():
                self.sems[k].wait(eng, v)
        for e in ENGS:
            getattr(block, e)(f)


def load_phase_weights(block, sems, pairs):
    sW = sems['wld']

    def f(g):
        for dst, src in pairs:
            sW.dma(g, out=dst, in_=src)
    block.gpsimd(f)
    return sW.n



def emit_phase_l0(ctx, block, d_src, d_dst, W1, W2, W3, vb, sems, kind):
    TP = 256
    tls = tiles_of(NTOK, TP)
    XT = [ctx.sb(f"{kind}_xt{b}", [128, NCI, TP], F32) for b in range(2)]
    SQB = [ctx.sb(f"{kind}_sqb{b}", [128, NCI, TP], BF16) for b in range(2)]
    XB = [ctx.sb(f"{kind}_xb{b}", [128, NCI, TP], BF16) for b in range(2)]
    LNT = ctx.sb(f"{kind}_lnt", [128, NCI, TP], F32)
    LNF = ctx.sb(f"{kind}_lnf", [128, NCI, TP + 1], F32)
    PREV = ctx.sb(f"{kind}_prev", [128, NCI, TP + 1], F32)
    UT = ctx.sb(f"{kind}_ut", [128, NCI, TP], F32)
    VIN = [ctx.sb(f"{kind}_vin{b}", [128, NCI, TP], BF16) for b in range(2)]
    RIN = [ctx.sb(f"{kind}_rin{b}", [128, NCI, TP], BF16) for b in range(2)]
    VB = [ctx.sb(f"{kind}_vb{b}", [128, NCI, TP], BF16) for b in range(2)]
    KAF = ctx.sb(f"{kind}_kaf", [128, TP], F32) if kind == 'ffn' else None
    RS = ctx.sb(f"{kind}_rs", [128, NCI, TP], F32)
    OTX = ctx.sb(f"{kind}_otx", [128, TP], F32)
    OT = [ctx.sb(f"{kind}_ot{b}", [128, NCI, TP], F32) for b in range(2)]
    CARRY = ctx.sb(f"{kind}_carry", [128, NCI], F32)
    ROWS = ctx.sb(f"{kind}_rows", [1, 2, 5, TP], F32)
    p_sum = ctx.ps(f"{kind}_psum", [1, TP])
    p_sq = ctx.ps(f"{kind}_psq", [1, TP])
    p_a = ctx.ps(f"{kind}_pa", [128, TP])
    p_b = ctx.ps(f"{kind}_pb", [128, TP])
    p_mm = [ctx.ps(f"{kind}_pmm{b}", [128, TP]) for b in range(2)]

    sPE, sACT, sDVE = sems['pe'], sems['act'], sems['dve']
    sLDp = [sems['ld0'], sems['ld1']]
    sSTp = [sems['st0'], sems['st1']]
    ev = {}

    def on(name, f):
        getattr(block, name)(f)

    def mkV(vector):
        def V(fn, *args, **kw):
            sDVE.wait(vector, sDVE.n)
            return sDVE.inc(getattr(vector, fn)(*args, **kw))
        return V

    def mkA(scalar):
        def A(fn, *args, **kw):
            sACT.wait(scalar, sACT.n)
            return sACT.inc(getattr(scalar, fn)(*args, **kw))
        return A

    eps_ln = vb['consts'][0:1, 0:1]
    if kind == 'attn':
        m1, m2c, dcol = vb['tmv'], vb['tmr'], vb['d']
    else:
        m1, m2c = vb['muk'], vb['mur']

    def f_init(vector):
        V = mkV(vector)
        V('memset', CARRY[:], 0.0)
        V('memset', PREV[:, :, 0:1], 0.0)
        ev['init'] = V('memset', LNF[:, :, 0:1], 0.0)
    on('vector', f_init)

    for i, (t0, T) in enumerate(tls):
        b = i % 2
        r = i % 2

        def f_load(sync, i=i, b=b, t0=t0, T=T):
            if i >= 2:
                sDVE.wait(sync, ev[('ep', i - 2)])
                sACT.wait(sync, ev[('sq', i - 2)])
            ev[('xld', i)] = sLDp[b].inc(
                sync.dma_start(out=XT[b][:, :, 0:T], in_=d_src[:, :, t0:t0 + T]), 16)
        on('sync', f_load)

        def f_sq(scalar, i=i, b=b, T=T):
            sLDp[b].wait(scalar, ev[('xld', i)])
            if i >= 2:
                sPE.wait(scalar, ev[('stats', i - 2)])
            ev[('sq', i)] = mkA(scalar)(
                'activation', SQB[b][:, :, 0:T], XT[b][:, :, 0:T], AF.Square)
        on('scalar', f_sq)

        def f_cast(vector, i=i, b=b, T=T):
            sLDp[b].wait(vector, ev[('xld', i)])
            if i >= 2:
                sPE.wait(vector, ev[('stats', i - 2)])
            ev[('xb', i)] = mkV(vector)(
                'tensor_copy', XB[b][:, :, 0:T], XT[b][:, :, 0:T])
        on('vector', f_cast)

        def f_stats(ten, i=i, b=b, T=T):
            sACT.wait(ten, ev[('sq', i)])
            sDVE.wait(ten, ev[('xb', i)])
            if i >= 1:
                sACT.wait(ten, ev[('m2', i - 1)])
                sDVE.wait(ten, ev[('rows', i - 1)])
            for ci in range(NCI):
                ten.matmul(p_sq[0:1, 0:T], vb['ones_inv'][:], SQB[b][:, ci, 0:T],
                           start=(ci == 0), stop=(ci == NCI - 1))
            for ci in range(NCI):
                ins = ten.matmul(p_sum[0:1, 0:T], vb['ones_inv'][:], XB[b][:, ci, 0:T],
                                 start=(ci == 0), stop=(ci == NCI - 1))
            ev[('stats', i)] = sPE.inc(ins)
        on('tensor', f_stats)

        m2 = ROWS[0:1, r, 0, 0:T]
        var = ROWS[0:1, r, 1, 0:T]
        sqv = ROWS[0:1, r, 2, 0:T]
        a_ = ROWS[0:1, r, 3, 0:T]
        mr = ROWS[0:1, r, 4, 0:T]

        def f_m2(scalar, i=i, T=T, m2=m2):
            sPE.wait(scalar, ev[('stats', i)])
            if i >= 1:
                sDVE.wait(scalar, ev[('rows', i - 1)])
            ev[('m2', i)] = mkA(scalar)('activation', m2, p_sum[0:1, 0:T], AF.Square)
        on('scalar', f_m2)

        def f_var(vector, i=i, T=T, m2=m2, var=var):
            sACT.wait(vector, ev[('m2', i)])
            ev[('var', i)] = mkV(vector)('tensor_sub', var, p_sq[0:1, 0:T], m2)
        on('vector', f_var)

        def f_sqv(scalar, i=i, T=T, var=var, sqv=sqv):
            sDVE.wait(scalar, ev[('var', i)])
            ev[('sqv', i)] = mkA(scalar)('activation', sqv, var, AF.Sqrt, bias=eps_ln)
        on('scalar', f_sqv)

        def f_rows(vector, i=i, T=T, sqv=sqv, a_=a_, mr=mr):
            sACT.wait(vector, ev[('sqv', i)])
            if i >= 1:
                sPE.wait(vector, ev[('bcast', i - 1)])
            V = mkV(vector)
            V('reciprocal', a_, sqv)
            ev[('rows', i)] = V('tensor_mul', mr, p_sum[0:1, 0:T], a_)
        on('vector', f_rows)

        def f_bcast(ten, i=i, T=T, a_=a_, mr=mr):
            sDVE.wait(ten, ev[('rows', i)])
            if i >= 1:
                sDVE.wait(ten, ev[('ln2', i - 1)])
            ten.matmul(p_a[:, 0:T], vb['ones1'][:], a_, start=True, stop=True)
            ins = ten.matmul(p_b[:, 0:T], vb['nones1'][:], mr, start=True, stop=True)
            ev[('bcast', i)] = sPE.inc(ins)
        on('tensor', f_bcast)

        def f_ln(vector, i=i, b=b, T=T):
            sPE.wait(vector, ev[('bcast', i)])
            V = mkV(vector)
            for ci in range(NCI):
                V('tensor_mul', LNT[:, ci, 0:T], XT[b][:, ci, 0:T], p_a[:, 0:T])
            for ci in range(NCI):
                ev[('ln2', i)] = V('tensor_add', LNF[:, ci, 1:T + 1],
                                   LNT[:, ci, 0:T], p_b[:, 0:T])
        on('vector', f_ln)

        if kind == 'attn':
            def f_scan(vector, i=i, T=T):
                V = mkV(vector)
                for ci in range(NCI):
                    V('tensor_copy', PREV[:, ci, 0:1], CARRY[:, ci:ci + 1])
                    V('tensor_tensor_scan',
                      PREV[:, ci, 1:T + 1],
                      dcol[:, ci:ci + 1].broadcast_to([128, T]),
                      LNF[:, ci, 1:T + 1],
                      CARRY[:, ci:ci + 1], ALU.mult, ALU.add)
                    V('tensor_copy', CARRY[:, ci:ci + 1], PREV[:, ci, T:T + 1])
            on('vector', f_scan)
            def base_ap(ci, T):
                return PREV[:, ci, 0:T]
        else:
            def base_ap(ci, T):
                return LNF[:, ci, 0:T]

        def f_mix(vector, i=i, b=b, T=T):
            if i >= 2:
                sPE.wait(vector, ev[('projr', i - 2)])
            V = mkV(vector)
            for ci in range(NCI):
                V('tensor_sub', UT[:, ci, 0:T], LNF[:, ci, 1:T + 1], base_ap(ci, T))
            for ci in range(NCI):
                V('scalar_tensor_tensor',
                  VIN[b][:, ci, 0:T], UT[:, ci, 0:T], m1[:, ci:ci + 1],
                  base_ap(ci, T), ALU.mult, ALU.add)
            for ci in range(NCI):
                ev[('mix', i)] = V('scalar_tensor_tensor',
                                   RIN[b][:, ci, 0:T], UT[:, ci, 0:T], m2c[:, ci:ci + 1],
                                   base_ap(ci, T), ALU.mult, ALU.add)
            if kind == 'ffn':
                for ci in range(NCI):
                    ev[('mix', i)] = V('tensor_copy', LNF[:, ci, 0:1], LNF[:, ci, T:T + 1])
        on('vector', f_mix)

        def f_projv_pre(ten, i=i):
            sDVE.wait(ten, ev[('mix', i)])
        on('tensor', f_projv_pre)

        for oc in range(8):
            def f_projv(ten, i=i, b=b, T=T, oc=oc):
                if oc >= 2:
                    sACT.wait(ten, ev[('evacv', i, oc - 2)])
                elif i >= 1:
                    sDVE.wait(ten, ev[('epoc', i - 1, 6 + oc)])
                for ci in range(NCI):
                    ins = ten.matmul(p_mm[oc % 2][:, 0:T], W1[:, ci, oc * 128:(oc + 1) * 128],
                                     VIN[b][:, ci, 0:T], start=(ci == 0), stop=(ci == NCI - 1))
                ev[('projv', i, oc)] = sPE.inc(ins)
            on('tensor', f_projv)

            def f_evacv(scalar, i=i, b=b, T=T, oc=oc):
                sPE.wait(scalar, ev[('projv', i, oc)])
                if oc == 0 and i >= 2:
                    sPE.wait(scalar, ev[('projo', i - 2)])
                A = mkA(scalar)
                if kind == 'attn':
                    ev[('evacv', i, oc)] = A('copy', VB[b][:, oc, 0:T], p_mm[oc % 2][:, 0:T])
                else:
                    A('activation', KAF[:, 0:T], p_mm[oc % 2][:, 0:T], AF.Relu)
                    ev[('evacv', i, oc)] = A('activation', VB[b][:, oc, 0:T],
                                             KAF[:, 0:T], AF.Square)
            on('scalar', f_evacv)

        for oc in range(8):
            def f_projr(ten, i=i, b=b, T=T, oc=oc):
                if oc >= 2:
                    sACT.wait(ten, ev[('sigr', i, oc - 2)])
                else:
                    sACT.wait(ten, ev[('evacv', i, 6 + oc)])
                for ci in range(NCI):
                    ins = ten.matmul(p_mm[oc % 2][:, 0:T], W2[:, ci, oc * 128:(oc + 1) * 128],
                                     RIN[b][:, ci, 0:T], start=(ci == 0), stop=(ci == NCI - 1))
                ev[('projr', i, oc)] = sPE.inc(ins)
            on('tensor', f_projr)

            def f_sigr(scalar, i=i, T=T, oc=oc):
                sPE.wait(scalar, ev[('projr', i, oc)])
                if oc == 0 and i >= 1:
                    sDVE.wait(scalar, ev[('ep', i - 1)])
                ev[('sigr', i, oc)] = mkA(scalar)(
                    'activation', RS[:, oc, 0:T], p_mm[oc % 2][:, 0:T], AF.Sigmoid)
            on('scalar', f_sigr)
        ev[('projr', i)] = ev[('projr', i, 7)]

        for oc in range(8):
            def f_projo(ten, i=i, b=b, T=T, oc=oc):
                if oc == 0:
                    sACT.wait(ten, ev[('evacv', i, 7)])
                if oc >= 2:
                    sDVE.wait(ten, ev[('epoc', i, oc - 2)])
                else:
                    sACT.wait(ten, ev[('sigr', i, 6 + oc)])
                for ci in range(NCI):
                    ins = ten.matmul(p_mm[oc % 2][:, 0:T], W3[:, ci, oc * 128:(oc + 1) * 128],
                                     VB[b][:, ci, 0:T], start=(ci == 0), stop=(ci == NCI - 1))
                ev[('projo', i, oc)] = sPE.inc(ins)
            on('tensor', f_projo)

            def f_ep(vector, i=i, b=b, T=T, oc=oc):
                sPE.wait(vector, ev[('projo', i, oc)])
                sACT.wait(vector, ev[('sigr', i, oc)])
                if oc == 0 and i >= 2:
                    sSTp[b].wait(vector, ev[('stored', i - 2)])
                V = mkV(vector)
                V('tensor_mul', OTX[:, 0:T], RS[:, oc, 0:T], p_mm[oc % 2][:, 0:T])
                ev[('epoc', i, oc)] = V('tensor_add', OT[b][:, oc, 0:T],
                                        OTX[:, 0:T], XT[b][:, oc, 0:T])
            on('vector', f_ep)
        ev[('ep', i)] = ev[('epoc', i, 7)]
        ev[('projo', i)] = ev[('projo', i, 7)]

        def f_store(sync, i=i, b=b, t0=t0, T=T):
            sDVE.wait(sync, ev[('ep', i)])
            ev[('stored', i)] = sSTp[b].inc(
                sync.dma_start(out=d_dst[:, :, t0:t0 + T], in_=OT[b][:, :, 0:T]), 16)
        on('sync', f_store)

    def f_fin(sync):
        n = len(tls)
        sSTp[(n - 1) % 2].wait(sync, ev[('stored', n - 1)])
        sSTp[(n - 2) % 2].wait(sync, ev[('stored', n - 2)])
    on('sync', f_fin)




class LNHelper:

    def __init__(self, ctx, sems, vb, prefix, TP):
        self.vb = vb
        self.sems = sems
        self.TP = TP
        self.SQB = [ctx.sb(f"{prefix}_sqb{b}", [128, NCI, TP], BF16) for b in range(2)]
        self.XB = [ctx.sb(f"{prefix}_xb{b}", [128, NCI, TP], BF16) for b in range(2)]
        self.LNT = ctx.sb(f"{prefix}_lnt", [128, NCI, TP], BF16)
        self.ROWS = ctx.sb(f"{prefix}_rows", [1, 1, 5, TP], F32)
        self.p_sum = ctx.ps(f"{prefix}_psum", [1, TP])
        self.p_sq = ctx.ps(f"{prefix}_psq", [1, TP])
        self.p_a = ctx.ps(f"{prefix}_pa", [128, TP])
        self.p_b = ctx.ps(f"{prefix}_pb", [128, TP])

    def emit(self, block, ev, i, b, XT_b, T, tgt, ld_wait, pe_busy_wait=None):
        sems = self.sems
        sPE, sACT, sDVE = sems['pe'], sems['act'], sems['dve']
        vb = self.vb
        SQB, XB, LNT, ROWS = self.SQB, self.XB, self.LNT, self.ROWS
        p_sum, p_sq, p_a, p_b = self.p_sum, self.p_sq, self.p_a, self.p_b
        eps_ln = vb['consts'][0:1, 0:1]
        r = 0

        def on(name, f):
            getattr(block, name)(f)

        def mkV(vector):
            def V(fn, *args, **kw):
                sDVE.wait(vector, sDVE.n)
                return sDVE.inc(getattr(vector, fn)(*args, **kw))
            return V

        def mkA(scalar):
            def A(fn, *args, **kw):
                sACT.wait(scalar, sACT.n)
                return sACT.inc(getattr(scalar, fn)(*args, **kw))
            return A

        def f_sq(scalar, i=i, b=b, T=T):
            ld_wait(scalar)
            if i >= 2:
                sPE.wait(scalar, ev[('stats', i - 2)])
            ev[('sq', i)] = mkA(scalar)(
                'activation', SQB[b][:, :, 0:T], XT_b[:, :, 0:T], AF.Square)
        on('scalar', f_sq)

        def f_cast(vector, i=i, b=b, T=T):
            ld_wait(vector)
            if i >= 2:
                sPE.wait(vector, ev[('stats', i - 2)])
            ev[('xb', i)] = mkV(vector)(
                'tensor_copy', XB[b][:, :, 0:T], XT_b[:, :, 0:T])
        on('vector', f_cast)

        def f_stats(ten, i=i, b=b, T=T):
            sACT.wait(ten, ev[('sq', i)])
            sDVE.wait(ten, ev[('xb', i)])
            if i >= 1:
                sACT.wait(ten, ev[('m2', i - 1)])
                sDVE.wait(ten, ev[('rows', i - 1)])
            if pe_busy_wait is not None:
                pe_busy_wait(ten, i)
            for ci in range(NCI):
                ten.matmul(p_sq[0:1, 0:T], vb['ones_inv'][:], SQB[b][:, ci, 0:T],
                           start=(ci == 0), stop=(ci == NCI - 1))
            for ci in range(NCI):
                ins = ten.matmul(p_sum[0:1, 0:T], vb['ones_inv'][:], XB[b][:, ci, 0:T],
                                 start=(ci == 0), stop=(ci == NCI - 1))
            ev[('stats', i)] = sPE.inc(ins)
        on('tensor', f_stats)

        m2 = ROWS[0:1, r, 0, 0:T]
        var = ROWS[0:1, r, 1, 0:T]
        sqv = ROWS[0:1, r, 2, 0:T]
        a_ = ROWS[0:1, r, 3, 0:T]
        mr = ROWS[0:1, r, 4, 0:T]

        def f_m2(scalar, i=i, T=T, m2=m2):
            sPE.wait(scalar, ev[('stats', i)])
            if i >= 1:
                sDVE.wait(scalar, ev[('rows', i - 1)])
            ev[('m2', i)] = mkA(scalar)('activation', m2, p_sum[0:1, 0:T], AF.Square)
        on('scalar', f_m2)

        def f_var(vector, i=i, T=T, m2=m2, var=var):
            sACT.wait(vector, ev[('m2', i)])
            ev[('var', i)] = mkV(vector)('tensor_sub', var, p_sq[0:1, 0:T], m2)
        on('vector', f_var)

        def f_sqv(scalar, i=i, T=T, var=var, sqv=sqv):
            sDVE.wait(scalar, ev[('var', i)])
            ev[('sqv', i)] = mkA(scalar)('activation', sqv, var, AF.Sqrt, bias=eps_ln)
        on('scalar', f_sqv)

        def f_rows(vector, i=i, T=T, sqv=sqv, a_=a_, mr=mr):
            sACT.wait(vector, ev[('sqv', i)])
            if i >= 1:
                sPE.wait(vector, ev[('bcast', i - 1)])
            V = mkV(vector)
            V('reciprocal', a_, sqv)
            ev[('rows', i)] = V('tensor_mul', mr, p_sum[0:1, 0:T], a_)
        on('vector', f_rows)

        def f_bcast(ten, i=i, T=T, a_=a_, mr=mr):
            sDVE.wait(ten, ev[('rows', i)])
            if i >= 1:
                sDVE.wait(ten, ev[('ln2', i - 1)])
            ten.matmul(p_a[:, 0:T], vb['ones1'][:], a_, start=True, stop=True)
            ins = ten.matmul(p_b[:, 0:T], vb['nones1'][:], mr, start=True, stop=True)
            ev[('bcast', i)] = sPE.inc(ins)
        on('tensor', f_bcast)

        def f_ln(vector, i=i, b=b, T=T):
            sPE.wait(vector, ev[('bcast', i)])
            V = mkV(vector)
            for ci in range(NCI):
                V('tensor_mul', LNT[:, ci, 0:T], XT_b[:, ci, 0:T], p_a[:, 0:T])
            for ci in range(NCI):
                ev[('ln2', i)] = tgt(vector, V, ci)
        on('vector', f_ln)


def emit_phase_l1a_A(ctx, block, io, WQ, WK, WV_, vb, sems, QSP, KSP):
    TP = 256
    tls = tiles_of(TCORE, TP)
    d_src = io['r2']
    XT = [ctx.sb(f"a_xt{b}", [128, NCI, TP], F32) for b in range(2)]
    LNB = [ctx.sb(f"a_lnb{b}", [128, NCI, TP], BF16) for b in range(2)]
    QFB = [ctx.sb(f"a_qfb{b}", [128, NCI, TP], BF16) for b in range(2)]
    KFB = [ctx.sb(f"a_kfb{b}", [128, NCI, TP], BF16) for b in range(2)]
    VFB = [ctx.sb(f"a_vfb{b}", [128, NCI, TP], BF16) for b in range(2)]
    KTS = [ctx.sb(f"a_kts{b}", [128, 2, H], BF16) for b in range(2)]
    VTS = [ctx.sb(f"a_vts{b}", [128, 2, H], BF16) for b in range(2)]
    nt = len(tls)
    ln = LNHelper(ctx, sems, vb, "a", TP)
    p_mm = [ctx.ps(f"a_pmm{b}", [128, TP]) for b in range(2)]
    p_tp = [ctx.ps(f"a_ptp{k}", [128, 128], BF16) for k in range(2)]

    sPE, sACT, sDVE = sems['pe'], sems['act'], sems['dve']
    sLDp = [sems['ld0'], sems['ld1']]
    sSTp = [sems['st0'], sems['st1']]
    sSTa = [sems['sta0'], sems['sta1']]
    sSTb = [sems['stb0'], sems['stb1']]
    sSTc = [sems['stc0'], sems['stc1']]
    ev = {}

    def on(name, f):
        getattr(block, name)(f)

    def mkV(vector):
        def V(fn, *args, **kw):
            sDVE.wait(vector, sDVE.n)
            return sDVE.inc(getattr(vector, fn)(*args, **kw))
        return V

    def mkA(scalar):
        def A(fn, *args, **kw):
            sACT.wait(scalar, sACT.n)
            return sACT.inc(getattr(scalar, fn)(*args, **kw))
        return A

    def bank_prev(i, st, oc):
        g = st * 8 + oc - 2
        if g >= 0:
            return i, g // 8, g % 8
        if i == 0:
            return None
        return i - 1, (24 + g) // 8, (24 + g) % 8

    EV_ENG = {0: 'act', 1: 'act', 2: 'dve'}

    for i, (t0, T) in enumerate(tls):
        b = i % 2

        def f_load(sync, i=i, b=b, t0=t0, T=T):
            if i >= 2:
                sACT.wait(sync, ev[('sq', i - 2)])
                sDVE.wait(sync, ev[('xb', i - 2)])
            ev[('xld', i)] = sLDp[b].inc(
                sync.dma_start(out=XT[b][:, :, 0:T],
                               in_=d_src[:, :, HALO + t0:HALO + t0 + T]), 16)
        on('sync', f_load)

        def ld_wait(eng, i=i, b=b):
            sLDp[b].wait(eng, ev[('xld', i)])

        def tgt(vector, V, ci, i=i, b=b, T=T):
            if ci == 0 and i >= 2:
                sPE.wait(vector, ev[('projV', i - 2)])
            return V('tensor_add', LNB[b][:, ci, 0:T], ln.LNT[:, ci, 0:T],
                     ln.p_b[:, 0:T])

        ln.emit(block, ev, i, b, XT[b], T, tgt, ld_wait)

        for st, Wmat, OUT, bias_col, acc in ((0, WQ, QFB, 'bq', QSP),
                                             (1, WK, KFB, 'bk', KSP)):
            for oc in range(8):
                def f_proj(ten, i=i, b=b, T=T, oc=oc, st=st, Wmat=Wmat):
                    if st == 0 and oc == 0:
                        sDVE.wait(ten, ev[('ln2', i)])
                    prev = bank_prev(i, st, oc)
                    if prev is not None:
                        pi, pst, poc = prev
                        s = sACT if EV_ENG[pst] == 'act' else sDVE
                        s.wait(ten, ev[('evx', pi, pst, poc)])
                    for ci in range(NCI):
                        ins = ten.matmul(p_mm[oc % 2][:, 0:T],
                                         Wmat[:, ci, oc * 128:(oc + 1) * 128],
                                         LNB[b][:, ci, 0:T],
                                         start=(ci == 0), stop=(ci == NCI - 1))
                    ev[('proj', i, st, oc)] = sPE.inc(ins)
                on('tensor', f_proj)

                def f_evac(scalar, i=i, b=b, T=T, oc=oc, st=st, OUT=OUT,
                           bias_col=bias_col, acc=acc):
                    sPE.wait(scalar, ev[('proj', i, st, oc)])
                    if st == 0 and oc == 0 and i >= 2:
                        sSTp[b].wait(scalar, ev[('stq', i - 2)])
                    if st == 1 and oc == 0 and i >= 2:
                        sSTa[b].wait(scalar, ev[('stk', i - 2)])
                    ev[('evx', i, st, oc)] = mkA(scalar)(
                        'activation', OUT[b][:, oc, 0:T], p_mm[oc % 2][:, 0:T],
                        AF.Sigmoid, bias=vb[bias_col][:, oc:oc + 1],
                        accum_out=acc[:, oc, i:i + 1])
                on('scalar', f_evac)

        for oc in range(8):
            def f_projv(ten, i=i, b=b, T=T, oc=oc):
                pi, pst, poc = bank_prev(i, 2, oc)
                s = sACT if EV_ENG[pst] == 'act' else sDVE
                s.wait(ten, ev[('evx', pi, pst, poc)])
                for ci in range(NCI):
                    ins = ten.matmul(p_mm[oc % 2][:, 0:T],
                                     WV_[:, ci, oc * 128:(oc + 1) * 128],
                                     LNB[b][:, ci, 0:T],
                                     start=(ci == 0), stop=(ci == NCI - 1))
                ev[('proj', i, 2, oc)] = sPE.inc(ins)
            on('tensor', f_projv)

            def f_evacv(vector, i=i, b=b, T=T, oc=oc):
                sPE.wait(vector, ev[('proj', i, 2, oc)])
                ev[('evx', i, 2, oc)] = mkV(vector)(
                    'tensor_scalar_add', VFB[b][:, oc, 0:T], p_mm[oc % 2][:, 0:T],
                    vb['bv'][:, oc:oc + 1])
            on('vector', f_evacv)
        ev[('projV', i)] = ev[('proj', i, 2, 7)]

        tp_seq = [(tch, sid, oc) for tch in range(2) for sid in range(2)
                  for oc in range(8)]
        for n, (tch, sid, oc) in enumerate(tp_seq):
            SRC = KFB if sid == 0 else VFB
            DST = KTS if sid == 0 else VTS

            def f_tp(ten, i=i, b=b, tch=tch, oc=oc, sid=sid, n=n, SRC=SRC):
                if n == 0:
                    sACT.wait(ten, ev[('evx', i, 1, 7)])
                    sDVE.wait(ten, ev[('evx', i, 2, 7)])
                if n >= 2:
                    sACT.wait(ten, ev[('tpc', i, n - 2)])
                elif i >= 1:
                    sACT.wait(ten, ev[('tpc', i - 1, 30 + n)])
                ins = ten.transpose(p_tp[n % 2][:],
                                    SRC[b][:, oc, tch * 128:(tch + 1) * 128],
                                    vb['idb'][:])
                ev[('tp', i, n)] = sPE.inc(ins)
            on('tensor', f_tp)

            def f_tpc(scalar, i=i, b=b, tch=tch, oc=oc, n=n, DST=DST):
                sPE.wait(scalar, ev[('tp', i, n)])
                if n == 0 and i >= 2:
                    sSTb[b].wait(scalar, ev[('stkt', i - 2)])
                    sSTc[b].wait(scalar, ev[('stvt', i - 2)])
                ev[('tpc', i, n)] = mkA(scalar)(
                    'copy', DST[b][:, tch, oc * 128:(oc + 1) * 128], p_tp[n % 2][:])
            on('scalar', f_tpc)

        def f_store(sync, i=i, b=b, t0=t0, T=T):
            sACT.wait(sync, ev[('tpc', i, 31)])
            sDVE.wait(sync, ev[('evx', i, 2, 7)])
            ev[('stq', i)] = sSTp[b].dma(sync, 
                out=io['qf'][:, :, t0:t0 + T], in_=QFB[b][:, :, 0:T])
            ev[('stk', i)] = sSTa[b].dma(sync, 
                out=io['kf'][:, :, t0:t0 + T], in_=KFB[b][:, :, 0:T])
            tc0 = 2 * i
            ev[('stkt', i)] = sSTb[b].dma(sync, 
                out=io['kt'][tc0:tc0 + 2, :, :].rearrange("tc p d -> p tc d"),
                in_=KTS[b][:])
            ev[('stvt', i)] = sSTc[b].dma(sync, 
                out=io['vt'][tc0:tc0 + 2, :, :].rearrange("tc p d -> p tc d"),
                in_=VTS[b][:])
        on('sync', f_store)

    def f_fin(sync):
        for j in (nt - 1, nt - 2):
            sSTp[j % 2].wait(sync, ev[('stq', j)])
            sSTa[j % 2].wait(sync, ev[('stk', j)])
            sSTb[j % 2].wait(sync, ev[('stkt', j)])
            sSTc[j % 2].wait(sync, ev[('stvt', j)])
    on('sync', f_fin)
    ev['nt'] = nt
    return ev


def emit_l1a_BCD(ctx, block, io, vb, sems, QSP, KSP, nt_a, cc_groups):
    TP = 256
    WO = ctx.sb("d_wo", [128, NCI, H], BF16)
    load_phase_weights(block, sems, [(WO[:], io['dw_l1a_wo'])])
    wo_ld = sems['wld'].n
    tls = tiles_of(TCORE, TP)
    nt = len(tls)
    sPE, sACT, sDVE, sGPS, sCC = (sems[k] for k in
                                  ('pe', 'act', 'dve', 'gps', 'cc'))
    sLDp = [sems['ld0'], sems['ld1']]
    sLDa = [sems['lda0'], sems['lda1']]
    sLDb = [sems['ldb0'], sems['ldb1']]
    sLDc = [sems['ldc0'], sems['ldc1']]
    sSTp = [sems['st0'], sems['st1']]
    ev = {}

    def on(name, f):
        getattr(block, name)(f)

    def mkV(vector):
        def V(fn, *args, **kw):
            sDVE.wait(vector, sDVE.n)
            return sDVE.inc(getattr(vector, fn)(*args, **kw))
        return V

    def mkA(scalar):
        def A(fn, *args, **kw):
            sACT.wait(scalar, sACT.n)
            return sACT.inc(getattr(scalar, fn)(*args, **kw))
        return A

    SUMS = ctx.sb("b_sums", [128, 2, NCI], F32)
    GS = ctx.sb("b_gs", [128, 2, NCI], F32)
    AK = ctx.sb("b_ak", [128, NCI, 16], BF16)
    AQ = ctx.sb("b_aq", [128, NCI, 16], BF16)
    EC = ctx.sb("b_ec", [16, 4], F32)
    T12 = ctx.sb("b_t12", [128, 2, NCI], F32)
    T12P = ctx.sb("b_t12p", [128, 2, NCI, nt], F32)
    AT1 = ctx.sb("b_at1", [128, NCI, 16], BF16)
    AT2 = ctx.sb("b_at2", [128, NCI, 16], BF16)
    SA = ctx.sb("c_sa", [16, TCORE], F32)
    ZP = ctx.sb("c_zp", [16, nt], F32)
    KVF = ctx.sb("c_kvf", [128, NCI, 128], F32)
    KVG = ctx.sb("c_kvg", [128, NCI, 128], F32)
    KVSB = ctx.sb("c_kvsb", [128, NCI, 128], BF16)
    ZROW = ctx.sb("c_zrow", [16, 2], F32)
    ZB = ctx.sb("c_zb", [16, 1], BF16)
    ZREP = ctx.sb("c_zrep", [128, NCI], F32)
    ROWS = ctx.sb("b_rows", [16, 2, 6, TP], F32)
    eps_a = vb['consts'][0:1, 1:2]

    def emit_ar(tag, sb_in, d_in, d_out, sb_out, wait_evs):
        def f(g):
            for s, v in wait_evs:
                s.wait(g, v)
            sGPS.dma(g, out=d_in, in_=sb_in)
            sGPS.wait(g, sGPS.n)
            sCC.inc(g.collective_compute(
                "AllReduce", ALU.add, replica_groups=cc_groups,
                ins=[d_in.opt()], outs=[d_out.opt()]))
            sCC.wait(g, sCC.n)
            sGPS.dma(g, out=sb_out, in_=d_out)
            sGPS.wait(g, sGPS.n)
        on('gpsimd', f)
        return sGPS.n

    def f_r1red(vector):
        V = mkV(vector)
        for ci in range(NCI):
            V('reduce_sum', SUMS[:, 0, ci:ci + 1], QSP[:, ci, 0:nt_a], mybir.AxisListType.X)
        for ci in range(NCI):
            ev['r1red'] = V('reduce_sum', SUMS[:, 1, ci:ci + 1], KSP[:, ci, 0:nt_a], mybir.AxisListType.X)
    on('vector', f_r1red)
    g1 = emit_ar('r1', SUMS[:], io['cc1i'], io['cc1o'], GS[:],
                 [(sDVE, sems['dve'].n)])

    def f_build_a(vector):
        sGPS.wait(vector, g1)
        V = mkV(vector)
        V('memset', AK[:], 0.0)
        V('memset', AQ[:], 0.0)
        for ci in range(NCI):
            V('tensor_scalar_add', AK[0:64, ci, 2 * ci:2 * ci + 1], GS[0:64, 1, ci:ci + 1], EPS_A)
            V('tensor_scalar_add', AK[64:128, ci, 2 * ci + 1:2 * ci + 2], GS[64:128, 1, ci:ci + 1], EPS_A)
            V('tensor_scalar_add', AQ[0:64, ci, 2 * ci:2 * ci + 1], GS[0:64, 0, ci:ci + 1], EPS_A)
            ev['builda'] = V('tensor_scalar_add', AQ[64:128, ci, 2 * ci + 1:2 * ci + 2], GS[64:128, 0, ci:ci + 1], EPS_A)
    on('vector', f_build_a)

    p_row = [ctx.ps(f"b_prow{k}", [16, TP]) for k in range(2)]
    p_bc = [ctx.ps(f"b_pbc{k}", [128, TP]) for k in range(2)]
    bc_scope = contextlib.ExitStack()
    outer_scope = ctx.scope
    ctx.scope = bc_scope
    p_e = ctx.ps("c_pe", [128, H])
    p_kv = ctx.ps("c_pkv", [128, NCI, 128])
    ctx.scope = outer_scope

    def f_ec_mm(ten):
        sDVE.wait(ten, ev['builda'])
        for ci in range(NCI):
            ten.matmul(p_row[0][0:16, 0:1], AK[:, ci, :], vb['onesb'][:],
                       start=(ci == 0), stop=(ci == NCI - 1))
        for ci in range(NCI):
            ins = ten.matmul(p_row[1][0:16, 0:1], AQ[:, ci, :], vb['onesb'][:],
                             start=(ci == 0), stop=(ci == NCI - 1))
        ev['ecmm'] = sPE.inc(ins)
    on('tensor', f_ec_mm)

    def f_ec(vector):
        sPE.wait(vector, ev['ecmm'])
        V = mkV(vector)
        V('tensor_scalar_mul', EC[:, 0:1], p_row[0][0:16, 0:1], EPS_A)
        ev['ec'] = V('tensor_scalar_mul', EC[:, 1:2], p_row[1][0:16, 0:1], EPS_A)
    on('vector', f_ec)

    QFT = [ctx.sb(f"b_qft{b}", [128, NCI, TP], BF16) for b in range(2)]
    KFT = [ctx.sb(f"b_kft{b}", [128, NCI, TP], BF16) for b in range(2)]
    SIB = ctx.sb("b_sib", [16, 2, TP], BF16)
    SOB = ctx.sb("b_sob", [16, 2, TP], BF16)
    QSIB = [ctx.sb(f"b_qsib{b}", [128, NCI, TP], BF16) for b in range(2)]
    DUMB = ctx.sb("b_dumb", [128, TP], BF16)

    for i, (t0, T) in enumerate(tls):
        b = i % 2

        def f_load(sync, i=i, b=b, t0=t0, T=T):
            if i >= 2:
                sDVE.wait(sync, ev[('bstt', i - 2)])
            ev[('bldq', i)] = sLDp[b].inc(
                sync.dma_start(out=QFT[b][:, :, 0:T], in_=io['qf'][:, :, t0:t0 + T]), 16)
            ev[('bldk', i)] = sLDa[b].inc(
                sync.dma_start(out=KFT[b][:, :, 0:T], in_=io['kf'][:, :, t0:t0 + T]), 16)
        on('sync', f_load)

        def f_rowmm(ten, i=i, b=b, T=T):
            sLDp[b].wait(ten, ev[('bldq', i)])
            sLDa[b].wait(ten, ev[('bldk', i)])
            if i == 0:
                sDVE.wait(ten, ev['ec'])
            if i >= 1:
                sDVE.wait(ten, ev[('brow', i - 1)])
            for ci in range(NCI):
                ten.matmul(p_row[0][:, 0:T], AK[:, ci, :], QFT[b][:, ci, 0:T],
                           start=(ci == 0), stop=(ci == NCI - 1))
            for ci in range(NCI):
                ins = ten.matmul(p_row[1][:, 0:T], AQ[:, ci, :], KFT[b][:, ci, 0:T],
                                 start=(ci == 0), stop=(ci == NCI - 1))
            ev[('browmm', i)] = sPE.inc(ins)
        on('tensor', f_rowmm)

        def f_brow(vector, i=i, b=b, T=T):
            sPE.wait(vector, ev[('browmm', i)])
            if i >= 2:
                sPE.wait(vector, ev[('bbc', i - 2, 1, 7)])
            V = mkV(vector)
            r = i % 2
            si_f = ROWS[:, r, 0, 0:T]
            so_f = ROWS[:, r, 1, 0:T]
            V('tensor_scalar_add', ROWS[:, r, 2, 0:T], p_row[0][:, 0:T], EC[:, 0:1])
            V('reciprocal', si_f, ROWS[:, r, 2, 0:T])
            V('tensor_copy', SIB[:, b, 0:T], si_f)
            V('tensor_scalar_add', ROWS[:, r, 3, 0:T], p_row[1][:, 0:T], EC[:, 1:2])
            V('reciprocal', so_f, ROWS[:, r, 3, 0:T])
            ev[('brow', i)] = V('tensor_copy', SOB[:, b, 0:T], so_f)
        on('vector', f_brow)

        for which in range(2):
            SB = SIB if which == 0 else SOB
            FT = QFT if which == 0 else KFT
            for ci in range(NCI):
                def f_bc(ten, i=i, b=b, T=T, ci=ci, which=which, SB=SB):
                    if ci == 0 and which == 0:
                        sDVE.wait(ten, ev[('brow', i)])
                    k = ('bstt', i, which, ci - 2)
                    if ci >= 2:
                        sDVE.wait(ten, ev[k])
                    elif which == 1:
                        sDVE.wait(ten, ev[('bstt', i, 0, 6 + ci)])
                    elif i >= 1:
                        sDVE.wait(ten, ev[('bstt', i - 1, 1, 6 + ci)])
                    ins = ten.matmul(p_bc[ci % 2][:, 0:T], vb['selb'][:, ci, :],
                                     SB[:, b, 0:T], start=True, stop=True)
                    ev[('bbc', i, which, ci)] = sPE.inc(ins)
                on('tensor', f_bc)

                def f_stt(vector, i=i, b=b, T=T, ci=ci, which=which, FT=FT):
                    sPE.wait(vector, ev[('bbc', i, which, ci)])
                    V = mkV(vector)
                    if which == 0:
                        out = QSIB[b][:, ci, 0:T]
                        if ci == 0 and i >= 2:
                            sSTp[b].wait(vector, ev[('bst', i - 2)])
                        acc = T12P[:, 1, ci, i:i + 1]
                    else:
                        out = DUMB[:, 0:T]
                        acc = T12P[:, 0, ci, i:i + 1]
                    ev[('bstt', i, which, ci)] = V(
                        'scalar_tensor_tensor', out, FT[b][:, ci, 0:T], 1.0,
                        p_bc[ci % 2][:, 0:T], ALU.mult, ALU.mult, accum_out=acc)
                on('vector', f_stt)
        ev[('bstt', i)] = ev[('bstt', i, 1, 7)]

        def f_bst(sync, i=i, b=b, t0=t0, T=T):
            sDVE.wait(sync, ev[('bstt', i, 0, 7)])
            ev[('bst', i)] = sSTp[b].inc(
                sync.dma_start(out=io['qsi'][:, :, t0:t0 + T], in_=QSIB[b][:, :, 0:T]), 16)
        on('sync', f_bst)

    bst_done = (sSTp[0].n, sSTp[1].n)

    def f_r2red(vector):
        V = mkV(vector)
        for w in range(2):
            for ci in range(NCI):
                ev['r2red'] = V('reduce_sum', T12[:, w, ci:ci + 1],
                                T12P[:, w, ci, 0:nt], mybir.AxisListType.X)
    on('vector', f_r2red)
    g2 = emit_ar('r2', T12[:], io['cc2i'], io['cc2o'], T12[:],
                 [(sDVE, sems['dve'].n)])

    def f_build_at(vector):
        sGPS.wait(vector, g2)
        V = mkV(vector)
        V('memset', AT1[:], 0.0)
        V('memset', AT2[:], 0.0)
        for ci in range(NCI):
            V('tensor_scalar_add', AT1[0:64, ci, 2 * ci:2 * ci + 1], T12[0:64, 0, ci:ci + 1], EPS_A)
            V('tensor_scalar_add', AT1[64:128, ci, 2 * ci + 1:2 * ci + 2], T12[64:128, 0, ci:ci + 1], EPS_A)
            V('tensor_scalar_add', AT2[0:64, ci, 2 * ci:2 * ci + 1], T12[0:64, 1, ci:ci + 1], EPS_A)
            ev['buildat'] = V('tensor_scalar_add', AT2[64:128, ci, 2 * ci + 1:2 * ci + 2], T12[64:128, 1, ci:ci + 1], EPS_A)
    on('vector', f_build_at)

    def f_ec2_mm(ten):
        sDVE.wait(ten, ev['buildat'])
        for ci in range(NCI):
            ten.matmul(p_row[0][0:16, 0:1], AT1[:, ci, :], vb['onesb'][:],
                       start=(ci == 0), stop=(ci == NCI - 1))
        for ci in range(NCI):
            ins = ten.matmul(p_row[1][0:16, 0:1], AT2[:, ci, :], vb['onesb'][:],
                             start=(ci == 0), stop=(ci == NCI - 1))
        ev['ec2mm'] = sPE.inc(ins)
    on('tensor', f_ec2_mm)

    def f_ec2(vector):
        sPE.wait(vector, ev['ec2mm'])
        V = mkV(vector)
        V('tensor_scalar_mul', EC[:, 2:3], p_row[0][0:16, 0:1], EPS_A)
        ev['ec2'] = V('tensor_scalar_mul', EC[:, 3:4], p_row[1][0:16, 0:1], EPS_A)
    on('vector', f_ec2)

    DBGETM = ctx.sb("c_dbgetm", [128, H], F32) if DBG else None
    KTT = [ctx.sb(f"c_ktt{b}", [128, 2, H], BF16) for b in range(2)]
    VTT = [ctx.sb(f"c_vtt{b}", [128, 2, H], BF16) for b in range(2)]
    ESB = ctx.sb("c_esb", [16, 2, TP], BF16)
    VSCB = ctx.sb("c_vscb", [128, 2, H], BF16)

    for i, (t0, T) in enumerate(tls):
        b = i % 2

        def f_cload(sync, i=i, b=b, t0=t0, T=T):
            if i >= 2:
                sPE.wait(sync, ev[('ckv', i - 2)])
                sDVE.wait(sync, ev[('cvsc', i - 2, 1)])
            elif ('bstt', nt - 1) in ev:
                sDVE.wait(sync, ev[('bstt', nt - 1)])
            tc0 = 2 * i
            ev[('cldq', i)] = sLDp[b].dma(sync, 
                out=QFT[b][:, :, 0:T], in_=io['qf'][:, :, t0:t0 + T])
            ev[('cldk', i)] = sLDa[b].dma(sync, 
                out=KFT[b][:, :, 0:T], in_=io['kf'][:, :, t0:t0 + T])
            ev[('cldkt', i)] = sLDb[b].dma(sync, 
                out=KTT[b][:], in_=io['kt'][tc0:tc0 + 2, :, :].rearrange("tc p d -> p tc d"))
            ev[('cldvt', i)] = sLDc[b].dma(sync, 
                out=VTT[b][:], in_=io['vt'][tc0:tc0 + 2, :, :].rearrange("tc p d -> p tc d"))
        on('sync', f_cload)

        def f_crowmm(ten, i=i, b=b, T=T):
            sLDp[b].wait(ten, ev[('cldq', i)])
            sLDa[b].wait(ten, ev[('cldk', i)])
            sLDb[b].wait(ten, ev[('cldkt', i)])
            if i == 0:
                sDVE.wait(ten, ev['ec2'])
            if i >= 1:
                sDVE.wait(ten, ev[('ces', i - 1)])
            for ci in range(NCI):
                ten.matmul(p_row[0][:, 0:T], AT1[:, ci, :], QFT[b][:, ci, 0:T],
                           start=(ci == 0), stop=(ci == NCI - 1))
            for ci in range(NCI):
                ins = ten.matmul(p_row[1][:, 0:T], AT2[:, ci, :], KFT[b][:, ci, 0:T],
                                 start=(ci == 0), stop=(ci == NCI - 1))
            ev[('crowmm', i)] = sPE.inc(ins)
        on('tensor', f_crowmm)

        def f_crow(vector, i=i, b=b, T=T, t0=t0):
            sPE.wait(vector, ev[('crowmm', i)])
            V = mkV(vector)
            r = i % 2
            V('tensor_scalar_add', ROWS[:, r, 0, 0:T], p_row[0][:, 0:T], EC[:, 2:3])
            ev[('cct', i)] = V('tensor_scalar', ROWS[:, r, 1, 0:T], p_row[1][:, 0:T],
                               EC[:, 3:4], None, ALU.add)
            ev[('cclip', i)] = V('tensor_scalar', ROWS[:, r, 2, 0:T], ROWS[:, r, 1, 0:T],
                                 1.0, -1.0, ALU.min, ALU.max)
        on('vector', f_crow)

        def f_cact(scalar, i=i, b=b, T=T, t0=t0):
            sDVE.wait(scalar, ev[('cclip', i)])
            A = mkA(scalar)
            r = i % 2
            ev[('csa', i)] = A('activation', SA[:, t0:t0 + T], ROWS[:, r, 0, 0:T], AF.Sigmoid)
            ev[('cexp', i)] = A('activation', ROWS[:, r, 3, 0:T], ROWS[:, r, 2, 0:T],
                                AF.Exp, accum_out=ZP[:, i:i + 1])
        on('scalar', f_cact)

        def f_ces(vector, i=i, b=b, T=T):
            sACT.wait(vector, ev[('cexp', i)])
            if i >= 2:
                sPE.wait(vector, ev[('ce', i - 2, 1)])
            r = i % 2
            ev[('ces', i)] = mkV(vector)('tensor_copy', ESB[:, b, 0:T], ROWS[:, r, 3, 0:T])
        on('vector', f_ces)

        for tch in range(2):
            def f_ce(ten, i=i, b=b, tch=tch):
                sDVE.wait(ten, ev[('ces', i)])
                if ('cvsc', i, tch - 1) in ev:
                    sDVE.wait(ten, ev[('cvsc', i, tch - 1)])
                elif i >= 1:
                    sDVE.wait(ten, ev[('cvsc', i - 1, 1)])
                ten.matmul(p_e[:, 0:512], ESB[:, b, tch * 128:(tch + 1) * 128],
                           vb['selrow'][:, 0:512], start=True, stop=True)
                ins = ten.matmul(p_e[:, 512:1024], ESB[:, b, tch * 128:(tch + 1) * 128],
                                 vb['selrow'][:, 512:1024], start=True, stop=True)
                ev[('ce', i, tch)] = sPE.inc(ins)
            on('tensor', f_ce)

            def f_cvsc(vector, i=i, b=b, tch=tch):
                sPE.wait(vector, ev[('ce', i, tch)])
                sLDc[b].wait(vector, ev[('cldvt', i)])
                ev[('cvsc', i, tch)] = mkV(vector)(
                    'tensor_mul', VSCB[:, tch, :], VTT[b][:, tch, :], p_e[:, 0:H])
            on('vector', f_cvsc)

            if DBG and i == nt - 1 and tch == 1:
                def f_dbg_etm(scalar, i=i):
                    sPE.wait(scalar, ev[('ce', i, 1)])
                    ev['dbgetm'] = mkA(scalar)('copy', DBGETM[:], p_e[:, 0:H])
                on('scalar', f_dbg_etm)

            def f_ckv(ten, i=i, b=b, tch=tch):
                sDVE.wait(ten, ev[('cvsc', i, tch)])
                for ci in range(NCI):
                    ins = ten.matmul(
                        p_kv[:, ci, :], KTT[b][:, tch, ci * 128:(ci + 1) * 128],
                        VSCB[:, tch, ci * 128:(ci + 1) * 128],
                        start=(i == 0 and tch == 0 and ci % 4 == 0),
                        stop=(i == nt - 1 and tch == 1 and ci % 4 == 3))
                ev[('ckv', i, tch)] = sPE.inc(ins)
            on('tensor', f_ckv)
        ev[('ckv', i)] = ev[('ckv', i, 1)]

    def f_kvev(scalar):
        sPE.wait(scalar, ev[('ckv', nt - 1)])
        ev['kvev'] = mkA(scalar)('copy', KVF[:], p_kv[:])
    on('scalar', f_kvev)

    def f_kvzero(vector):
        sACT.wait(vector, ev['kvev'])
        V = mkV(vector)
        for ci in range(NCI):
            V('memset', KVF[0:64, ci, 64:128], 0.0)
            V('memset', KVF[64:128, ci, 0:64], 0.0)
        for _ in (0,):
            ev['zred'] = V('reduce_sum', ZROW[:, 0:1], ZP[:, 0:nt], mybir.AxisListType.X)
    on('vector', f_kvzero)

    g3 = emit_ar('r3', KVF[:].rearrange("p c d -> p (c d)"), io['cc3i'], io['cc3o'],
                 KVG[:].rearrange("p c d -> p (c d)"), [(sDVE, sems['dve'].n)])
    g3z = emit_ar('r3z', ZROW[:, 0:1], io['cc4i'], io['cc4o'], ZROW[:, 0:1],
                  [(sGPS, g3)])

    def f_zinv(vector):
        sGPS.wait(vector, g3z)
        V = mkV(vector)
        V('reciprocal', ZROW[:, 1:2], ZROW[:, 0:1])
        V('tensor_scalar_mul', ZROW[:, 1:2], ZROW[:, 1:2], SSCALE)
        ev['zb'] = V('tensor_copy', ZB[:], ZROW[:, 1:2])
    on('vector', f_zinv)

    def f_zrep_mm(ten):
        sDVE.wait(ten, ev['zb'])
        for ci in range(NCI):
            ins = ten.matmul(p_bc[0][:, ci:ci + 1],
                             vb['selb'][:, ci, :], ZB[:], start=True, stop=True)
        ev['zrepmm'] = sPE.inc(ins)
    on('tensor', f_zrep_mm)

    def f_zrep(scalar):
        sPE.wait(scalar, ev['zrepmm'])
        ev['zrep'] = mkA(scalar)('copy', ZREP[:], p_bc[0][:, 0:NCI])
    on('scalar', f_zrep)

    def f_kvs(vector):
        sACT.wait(vector, ev['zrep'])
        V = mkV(vector)
        for ci in range(NCI):
            ev['kvs'] = V('tensor_scalar_mul', KVSB[:, ci, :], KVG[:, ci, :],
                          ZREP[:, ci:ci + 1])
    on('vector', f_kvs)

    if DBG:
        def f_dbg(g):
            sDVE.wait(g, sems['dve'].n)
            sACT.wait(g, sems['act'].n)
            for nm, ap in (("dbg_sa", SA[:]), ("dbg_kvg", KVG[:]),
                           ("dbg_kvsb", KVSB[:]), ("dbg_zrow", ZROW[:]),
                           ("dbg_zrep", ZREP[:]), ("dbg_t12", T12[:]),
                           ("dbg_at1", AT1[:]), ("dbg_at2", AT2[:]),
                           ("dbg_ec", EC[:]), ("dbg_vscb", VSCB[:]),
                           ("dbg_esb", ESB[:]), ("dbg_ktt", KTT[(nt - 1) % 2][:]),
                           ("dbg_etm", DBGETM[:])):
                sGPS.dma(g, out=io[nm], in_=ap)
        on('gpsimd', f_dbg)

    bc_scope.close()
    QST = [ctx.sb(f"d_qst{b}", [128, NCI, TP], BF16) for b in range(2)]
    XT = [ctx.sb(f"d_xt{b}", [128, NCI, TP], F32) for b in range(2)]
    SAB = ctx.sb("d_sab", [16, 2, TP], BF16)
    QSA = ctx.sb("d_qsa", [128, NCI, TP], BF16)
    OB = ctx.sb("d_ob", [128, NCI, TP], BF16)
    OX = ctx.sb("d_ox", [128, TP], F32)
    OT = [ctx.sb(f"d_ot{b}", [128, NCI, TP], F32) for b in range(2)]
    p_mm2 = [ctx.ps(f"d_pmm{k}", [128, TP]) for k in range(2)]

    for i, (t0, T) in enumerate(tls):
        b = i % 2

        def f_dload(sync, i=i, b=b, t0=t0, T=T):
            if i >= 2:
                sDVE.wait(sync, ev[('dep', i - 2)])
            else:
                sSTp[0].wait(sync, bst_done[0])
                sSTp[1].wait(sync, bst_done[1])
            ev[('dldq', i)] = sLDp[b].dma(sync, 
                out=QST[b][:, :, 0:T], in_=io['qsi'][:, :, t0:t0 + T])
            ev[('dld', i)] = sLDa[b].dma(sync, 
                out=XT[b][:, :, 0:T], in_=io['r2'][:, :, HALO + t0:HALO + t0 + T])
        on('sync', f_dload)

        def f_dsab(vector, i=i, b=b, T=T, t0=t0):
            if i == 0:
                sDVE.wait(vector, ev['kvs'])
            if i >= 1:
                sPE.wait(vector, ev[('dsabmm', i - 1, 7)])
            ev[('dsab', i)] = mkV(vector)('tensor_copy', SAB[:, b, 0:T], SA[:, t0:t0 + T])
        on('vector', f_dsab)

        for ci in range(NCI):
            def f_dbc(ten, i=i, b=b, T=T, ci=ci):
                sDVE.wait(ten, ev[('dsab', i)])
                if ci >= 2:
                    sDVE.wait(ten, ev[('dqsa', i, ci - 2)])
                elif i >= 1:
                    sDVE.wait(ten, ev[('dqsa', i - 1, 6 + ci)])
                ins = ten.matmul(p_bc[ci % 2][:, 0:T], vb['selb'][:, ci, :],
                                 SAB[:, b, 0:T], start=True, stop=True)
                ev[('dsabmm', i, ci)] = sPE.inc(ins)
            on('tensor', f_dbc)

            def f_dqsa(vector, i=i, b=b, T=T, ci=ci):
                sPE.wait(vector, ev[('dsabmm', i, ci)])
                sLDp[b].wait(vector, ev[('dldq', i)])
                if ci == 0 and i >= 1:
                    sACT.wait(vector, ev[('dob', i - 1, 7)])
                ev[('dqsa', i, ci)] = mkV(vector)(
                    'tensor_mul', QSA[:, ci, 0:T], QST[b][:, ci, 0:T], p_bc[ci % 2][:, 0:T])
            on('vector', f_dqsa)

            def f_dkv(ten, i=i, T=T, ci=ci):
                sDVE.wait(ten, ev[('dqsa', i, ci)])
                if ci >= 2:
                    sACT.wait(ten, ev[('dob', i, ci - 2)])
                elif i >= 1:
                    sACT.wait(ten, ev[('dob', i - 1, 6 + ci)])
                ins = ten.matmul(p_mm2[ci % 2][:, 0:T], KVSB[:, ci, :],
                                 QSA[:, ci, 0:T], start=True, stop=True)
                ev[('dkv', i, ci)] = sPE.inc(ins)
            on('tensor', f_dkv)

            def f_dob(scalar, i=i, T=T, ci=ci):
                sPE.wait(scalar, ev[('dkv', i, ci)])
                if ci == 0 and i >= 1:
                    sPE.wait(scalar, ev[('dproj', i - 1, 7)])
                ev[('dob', i, ci)] = mkA(scalar)('copy', OB[:, ci, 0:T],
                                                 p_mm2[ci % 2][:, 0:T])
            on('scalar', f_dob)

        for oc in range(8):
            def f_dproj(ten, i=i, b=b, T=T, oc=oc):
                if oc == 0:
                    sACT.wait(ten, ev[('dob', i, 7)])
                    sems['wld'].wait(ten, wo_ld)
                if oc >= 2:
                    sDVE.wait(ten, ev[('dep', i, oc - 2)])
                elif i >= 1:
                    sDVE.wait(ten, ev[('dep', i - 1, 6 + oc)])
                for ci in range(NCI):
                    ins = ten.matmul(p_mm2[oc % 2][:, 0:T], WO[:, ci, oc * 128:(oc + 1) * 128],
                                     OB[:, ci, 0:T], start=(ci == 0), stop=(ci == NCI - 1))
                ev[('dproj', i, oc)] = sPE.inc(ins)
            on('tensor', f_dproj)

            def f_dox(scalar, i=i, T=T, oc=oc):
                sPE.wait(scalar, ev[('dproj', i, oc)])
                if oc >= 1:
                    sDVE.wait(scalar, ev[('dep', i, oc - 1)])
                elif i >= 1:
                    sDVE.wait(scalar, ev[('dep', i - 1, 7)])
                ev[('dox', i, oc)] = mkA(scalar)(
                    'activation', OX[:, 0:T], p_mm2[oc % 2][:, 0:T], AF.Identity,
                    bias=vb['bo'][:, oc:oc + 1])
            on('scalar', f_dox)

            def f_dep(vector, i=i, b=b, T=T, oc=oc):
                sACT.wait(vector, ev[('dox', i, oc)])
                sLDa[b].wait(vector, ev[('dld', i)])
                if oc == 0 and i >= 2:
                    sSTp[b].wait(vector, ev[('dst', i - 2)])
                ev[('dep', i, oc)] = mkV(vector)(
                    'tensor_add', OT[b][:, oc, 0:T], OX[:, 0:T], XT[b][:, oc, 0:T])
            on('vector', f_dep)
        ev[('dep', i)] = ev[('dep', i, 7)]

        def f_dst(sync, i=i, b=b, t0=t0, T=T):
            sDVE.wait(sync, ev[('dep', i)])
            ev[('dst', i)] = sSTp[b].inc(
                sync.dma_start(out=io['r3'][:, :, t0:t0 + T], in_=OT[b][:, :, 0:T]), 16)
        on('sync', f_dst)

    def f_dfin(sync):
        sSTp[(nt - 1) % 2].wait(sync, ev[('dst', nt - 1)])
        sSTp[(nt - 2) % 2].wait(sync, ev[('dst', nt - 2)])
    on('sync', f_dfin)


def emit_phase_l1f(ctx, block, io, vb, sems):
    TS = min(512, TCORE)
    TT = TS
    NTT = TS // TT
    NS = TCORE // TS
    NH = 32
    sPE, sACT, sDVE, sGPS = sems['pe'], sems['act'], sems['dve'], sems['gps']
    sLDp = [sems['ld0'], sems['ld1']]
    sSTp = [sems['st0'], sems['st1']]
    sWL = [sems['wl0'], sems['wl1']]
    ev = {}

    LNB = ctx.sb("f_lnb", [128, NCI, TS], BF16)
    GX = ctx.sb("f_gx", [128, NH, TS], BF16)
    GY = ctx.sb("f_gy", [128, NH, TS], BF16)
    XT = ctx.sb("f_xt", [128, NCI, TT], F32)
    WB1 = [ctx.sb(f"f_wb1{k}", [128, NCI, 128], BF16) for k in range(2)]
    WB2 = [ctx.sb(f"f_wb2{k}", [128, NCI, 128], BF16) for k in range(2)]
    WH = [ctx.sb(f"f_wh{k}", [128, NH, 128], BF16) for k in range(2)]
    W3B = [ctx.sb(f"f_w3b{k}", [128, NH, 128], BF16) for k in range(2)]
    SILT = ctx.sb("f_silt", [128, TT], BF16)
    OX = ctx.sb("f_ox", [128, TT], F32)
    OTS = [ctx.sb(f"f_ots{k}", [128, TT], F32) for k in range(2)]
    ln = LNHelper(ctx, sems, vb, "f", TT)
    p1 = ctx.ps("f_p1", [128, TT])
    p2 = ctx.ps("f_p2", [128, TT])

    def on(name, f):
        getattr(block, name)(f)

    def mkV(vector):
        def V(fn, *args, **kw):
            sDVE.wait(vector, sDVE.n)
            return sDVE.inc(getattr(vector, fn)(*args, **kw))
        return V

    def mkA(scalar):
        def A(fn, *args, **kw):
            sACT.wait(scalar, sACT.n)
            return sACT.inc(getattr(scalar, fn)(*args, **kw))
        return A

    for s in range(NS):
        S0 = s * TS

        for tt in range(NTT):
            i = s * NTT + tt
            t0 = S0 + tt * TT

            def f_load(sync, i=i, t0=t0, s=s, tt=tt):
                if i >= 1:
                    sDVE.wait(sync, ev[('ln2', i - 1)])
                    sACT.wait(sync, ev[('sq', i - 1)])
                if s >= 1 and tt == 0:
                    sDVE.wait(sync, ev[('f4ep', s - 1, NTT - 1)])
                ev[('fxld', i)] = sLDp[0].dma(
                    sync, out=XT[:], in_=io['r3'][:, :, t0:t0 + TT])
            on('sync', f_load)

            def ld_wait(eng, i=i):
                sLDp[0].wait(eng, ev[('fxld', i)])

            def tgt(vector, V, ci, i=i, s=s, tt=tt):
                if ci == 0 and s >= 1:
                    sPE.wait(vector, ev[('f1done', s - 1)])
                return V('tensor_add', LNB[:, ci, tt * TT:(tt + 1) * TT],
                         ln.LNT[:, ci, 0:TT], ln.p_b[:, 0:TT])

            ln.emit(block, ev, i, 0, XT, TT, tgt, ld_wait)

        for oc in range(NH):
            kb = oc % 2

            def f_wl(g, s=s, oc=oc, kb=kb):
                if oc >= 2:
                    sPE.wait(g, ev[('f1mm', s, oc - 2)])
                elif s >= 1:
                    sPE.wait(g, ev[('f1mm', s - 1, NH - 2 + oc)])
                sWL[kb].dma(g, out=WB1[kb][:], in_=io['w1'][oc, :, :, :])
                ev[('f1w', s, oc)] = sWL[kb].dma(g, out=WB2[kb][:], in_=io['w2'][oc, :, :, :])
            on('gpsimd', f_wl)

            for tt in range(NTT):
                def f_mm(ten, s=s, oc=oc, tt=tt, kb=kb):
                    if tt == 1:
                        sACT.wait(ten, ev[('f1silu', s, oc, 0)])
                        sDVE.wait(ten, ev[('f1stt', s, oc, 0)])
                    if tt == 0:
                        sWL[kb].wait(ten, ev[('f1w', s, oc)])
                        if oc == 0:
                            sDVE.wait(ten, ev[('ln2', s * NTT + NTT - 1)])
                        if oc >= 1:
                            sDVE.wait(ten, ev[('f1stt', s, oc - 1, NTT - 1)])
                        elif s >= 1:
                            sDVE.wait(ten, ev[('f1stt', s - 1, NH - 1, NTT - 1)])
                    for ci in range(NCI):
                        ten.matmul(p1[:, 0:TT], WB1[kb][:, ci, :],
                                   LNB[:, ci, tt * TT:(tt + 1) * TT],
                                   start=(ci == 0), stop=(ci == NCI - 1))
                    for ci in range(NCI):
                        ins = ten.matmul(p2[:, 0:TT], WB2[kb][:, ci, :],
                                         LNB[:, ci, tt * TT:(tt + 1) * TT],
                                         start=(ci == 0), stop=(ci == NCI - 1))
                    ev[('f1mm', s, oc)] = ev[('f1mmt', s, oc, tt)] = sPE.inc(ins)
                on('tensor', f_mm)

                def f_silu(scalar, s=s, oc=oc, tt=tt):
                    sPE.wait(scalar, ev[('f1mmt', s, oc, tt)])
                    ev[('f1silu', s, oc, tt)] = mkA(scalar)(
                        'activation', SILT[:, 0:TT], p1[:, 0:TT], AF.Silu,
                        scale=vb['s1v'][:, oc:oc + 1])
                on('scalar', f_silu)

                def f_stt(vector, s=s, oc=oc, tt=tt):
                    sACT.wait(vector, ev[('f1silu', s, oc, tt)])
                    if oc == 0 and tt == 0 and s >= 1:
                        sPE.wait(vector, ev[('f4mm', s - 1, NTT - 1, 7)])
                    ev[('f1stt', s, oc, tt)] = mkV(vector)(
                        'scalar_tensor_tensor', GX[:, oc, tt * TT:(tt + 1) * TT],
                        p2[:, 0:TT], vb['s2v'][:, oc:oc + 1], SILT[:, 0:TT],
                        ALU.mult, ALU.mult)
                on('vector', f_stt)
        ev[('f1done', s)] = ev[('f1mm', s, NH - 1)]

        for stage, (SRC, DST, wkey, skey) in enumerate((
                (GX, GY, 'h1', 'h1s'), (GY, GX, 'h2', 'h2s'))):
            for oc in range(NH):
                kb = oc % 2

                def f_whl(g, s=s, oc=oc, kb=kb, stage=stage, wkey=wkey):
                    if oc >= 2:
                        sPE.wait(g, ev[('fhmm', s, stage, oc - 2)])
                    elif stage >= 1:
                        sPE.wait(g, ev[('fhmm', s, 0, NH - 2 + oc)])
                    elif s >= 1:
                        sPE.wait(g, ev[('f4mm', s - 1, NTT - 1, 6 + oc)])
                    ev[('fhw', s, stage, oc)] = sWL[kb].dma(
                        g, out=WH[kb][:], in_=io[wkey][oc, :, :, :])
                on('gpsimd', f_whl)

                for tt in range(NTT):
                    def f_hmm(ten, s=s, oc=oc, tt=tt, kb=kb, stage=stage, SRC=SRC):
                        if tt == 1:
                            sACT.wait(ten, ev[('fg', s, stage, oc, 0)])
                        if tt == 0:
                            sWL[kb].wait(ten, ev[('fhw', s, stage, oc)])
                            if oc == 0:
                                if stage == 0:
                                    sDVE.wait(ten, ev[('f1stt', s, NH - 1, NTT - 1)])
                                else:
                                    sACT.wait(ten, ev[('fg', s, 0, NH - 1, NTT - 1)])
                            if oc >= 1:
                                sACT.wait(ten, ev[('fg', s, stage, oc - 1, NTT - 1)])
                            elif stage == 1:
                                sACT.wait(ten, ev[('fg', s, 0, NH - 1, NTT - 1)])
                        for hc in range(NH):
                            ins = ten.matmul(p1[:, 0:TT], WH[kb][:, hc, :],
                                             SRC[:, hc, tt * TT:(tt + 1) * TT],
                                             start=(hc == 0), stop=(hc == NH - 1))
                        ev[('fhmm', s, stage, oc)] = ev[('fhmmt', s, stage, oc, tt)] = sPE.inc(ins)
                    on('tensor', f_hmm)

                    def f_gelu(scalar, s=s, oc=oc, tt=tt, stage=stage, DST=DST, skey=skey):
                        sPE.wait(scalar, ev[('fhmmt', s, stage, oc, tt)])
                        if oc == 0 and tt == 0:
                            if stage == 0 and s >= 1:
                                sPE.wait(scalar, ev[('fhmm', s - 1, 1, NH - 1)])
                            elif stage == 1:
                                sPE.wait(scalar, ev[('fhmm', s, 0, NH - 1)])
                        ev[('fg', s, stage, oc, tt)] = mkA(scalar)(
                            'activation', DST[:, oc, tt * TT:(tt + 1) * TT],
                            p1[:, 0:TT], AF.Gelu, scale=vb[skey][:, oc:oc + 1])
                    on('scalar', f_gelu)
        ev[('f2done', s)] = ev[('fhmm', s, 1, NH - 1)]

        for tt in range(NTT):
            t0 = S0 + tt * TT

            def f_x4(sync, s=s, tt=tt, t0=t0):
                sDVE.wait(sync, ev[('ln2', s * NTT + NTT - 1)])
                sACT.wait(sync, ev[('sq', s * NTT + NTT - 1)])
                if s + tt >= 1:
                    sDVE.wait(sync, ev[('f4ep', s, tt - 1) if tt >= 1 else ('f4ep', s - 1, NTT - 1)])
                ev[('f4x', s, tt)] = sLDp[1].dma(
                    sync, out=XT[:], in_=io['r3'][:, :, t0:t0 + TT])
            on('sync', f_x4)

            for oc in range(NCI):
                kb = oc % 2

                def f_w3l(g, s=s, tt=tt, oc=oc, kb=kb):
                    if oc >= 2:
                        sPE.wait(g, ev[('f4mm', s, tt, oc - 2)])
                    elif tt >= 1:
                        sPE.wait(g, ev[('f4mm', s, tt - 1, 6 + oc)])
                    elif s >= 1:
                        sPE.wait(g, ev[('f4mm', s - 1, NTT - 1, 6 + oc)])
                    ev[('f4w', s, tt, oc)] = sWL[kb].dma(
                        g, out=W3B[kb][:], in_=io['w3'][oc, :, :, :])
                on('gpsimd', f_w3l)

                def f_mm3(ten, s=s, tt=tt, oc=oc, kb=kb):
                    sWL[kb].wait(ten, ev[('f4w', s, tt, oc)])
                    if oc == 0 and tt == 0:
                        sACT.wait(ten, ev[('fg', s, 1, NH - 1, NTT - 1)])
                    if oc >= 1:
                        sACT.wait(ten, ev[('f4ox', s, tt, oc - 1)])
                    elif tt >= 1:
                        sACT.wait(ten, ev[('f4ox', s, tt - 1, 7)])
                    elif s >= 1:
                        sACT.wait(ten, ev[('f4ox', s - 1, NTT - 1, 7)])
                    for hc in range(NH):
                        ins = ten.matmul(p1[:, 0:TT], W3B[kb][:, hc, :],
                                         GX[:, hc, tt * TT:(tt + 1) * TT],
                                         start=(hc == 0), stop=(hc == NH - 1))
                    ev[('f4mm', s, tt, oc)] = sPE.inc(ins)
                on('tensor', f_mm3)

                def f_ox(scalar, s=s, tt=tt, oc=oc):
                    sPE.wait(scalar, ev[('f4mm', s, tt, oc)])
                    if oc >= 1:
                        sDVE.wait(scalar, ev[('f4add', s, tt, oc - 1)])
                    elif tt >= 1:
                        sDVE.wait(scalar, ev[('f4add', s, tt - 1, 7)])
                    elif s >= 1:
                        sDVE.wait(scalar, ev[('f4add', s - 1, NTT - 1, 7)])
                    ev[('f4ox', s, tt, oc)] = mkA(scalar)(
                        'activation', OX[:, 0:TT], p1[:, 0:TT], AF.Identity,
                        scale=vb['s3v'][:, oc:oc + 1])
                on('scalar', f_ox)

                def f_add(vector, s=s, tt=tt, oc=oc):
                    sACT.wait(vector, ev[('f4ox', s, tt, oc)])
                    sLDp[1].wait(vector, ev[('f4x', s, tt)])
                    ob = oc % 2
                    prev = None
                    if oc >= 2:
                        prev = (s, tt, oc - 2)
                    elif tt >= 1:
                        prev = (s, tt - 1, 6 + oc)
                    elif s >= 1:
                        prev = (s - 1, NTT - 1, 6 + oc)
                    if prev is not None:
                        sSTp[ob].wait(vector, ev[('fyst',) + prev])
                    ev[('f4add', s, tt, oc)] = ev[('f4ep', s, tt)] = mkV(vector)(
                        'tensor_add', OTS[ob][:, 0:TT], OX[:, 0:TT], XT[:, oc, 0:TT])
                on('vector', f_add)

                def f_yst(sync, s=s, tt=tt, oc=oc, t0=t0):
                    ob = oc % 2
                    sDVE.wait(sync, ev[('f4add', s, tt, oc)])
                    ev[('fyst', s, tt, oc)] = sSTp[ob].dma(
                        sync, out=io['y'][:, oc, t0:t0 + TT], in_=OTS[ob][:, 0:TT])
                on('sync', f_yst)

    def f_fin(sync):
        sSTp[0].wait(sync, ev[('fyst', NS - 1, NTT - 1, 6)])
        sSTp[1].wait(sync, ev[('fyst', NS - 1, NTT - 1, 7)])
        if DBG:
            sLDp[0].dma(sync, out=io['dbg_lnb'][:, :, 0:TS], in_=LNB[:])
            sLDp[0].dma(sync, out=io['dbg_gx'][:, :, 0:TS], in_=GX[:])
            sLDp[0].dma(sync, out=io['dbg_gy'][:, :, 0:TS], in_=GY[:])
            sLDp[0].wait(sync, sLDp[0].n)
    on('sync', f_fin)



ALL_PHASES = ('l0a', 'l0f', 'l1aA', 'l1aBCD', 'l1f')


def build_program(phases=ALL_PHASES, io_external=(), io_extin=(), cc_groups=None):
    if cc_groups is None:
        cc_groups = [[0, 1, 2, 3], [4, 5, 6, 7]]
    nc = bass.Bass("TRN2", target_bir_lowering=False)
    es = contextlib.ExitStack()
    ctx = Ctx(nc, es)
    with es:
        def dram(name, shape, dtype=F32, ext_in=False):
            if ext_in:
                kind = "ExternalInput"
            elif name in io_external or (DBG and name.startswith("dbg_")):
                kind = "ExternalOutput"
            else:
                kind = "Internal"
            return nc.dram_tensor(name, shape, dtype, kind=kind).ap()

        io = {}
        io['x'] = dram("x", [128, NCI, NTOK], ext_in=True)
        io['r1'] = dram("r1", [128, NCI, NTOK])
        io['r2'] = dram("r2", [128, NCI, NTOK], ext_in=('r2' in io_extin))
        io['qf'] = dram("qf", [128, NCI, TCORE], BF16)
        io['kf'] = dram("kf", [128, NCI, TCORE], BF16)
        io['kt'] = dram("kt", [TCORE // 128, 128, H], BF16)
        io['vt'] = dram("vt", [TCORE // 128, 128, H], BF16)
        io['qsi'] = dram("qsi", [128, NCI, TCORE], BF16)
        io['r3'] = dram("r3", [128, NCI, TCORE], ext_in=('r3' in io_extin))
        io['y'] = dram("y", [128, NCI, TCORE], ext_in=False)
        io['cc1i'] = dram("cc1i", [128, 2 * NCI])
        io['cc1o'] = dram("cc1o", [128, 2 * NCI])
        io['cc2i'] = dram("cc2i", [128, 2 * NCI])
        io['cc2o'] = dram("cc2o", [128, 2 * NCI])
        io['cc3i'] = dram("cc3i", [128, NCI * 128])
        io['cc3o'] = dram("cc3o", [128, NCI * 128])
        io['cc4i'] = dram("cc4i", [16, 1])
        io['cc4o'] = dram("cc4o", [16, 1])
        if DBG:
            io['dbg_sa'] = dram("dbg_sa", [16, TCORE], ext_in=False)
            io['dbg_kvg'] = dram("dbg_kvg", [128, NCI, 128])
            io['dbg_kvsb'] = dram("dbg_kvsb", [128, NCI, 128], BF16)
            io['dbg_zrow'] = dram("dbg_zrow", [16, 2])
            io['dbg_zrep'] = dram("dbg_zrep", [128, NCI])
            io['dbg_t12'] = dram("dbg_t12", [128, 2, NCI])
            io['dbg_at1'] = dram("dbg_at1", [128, NCI, 16], BF16)
            io['dbg_at2'] = dram("dbg_at2", [128, NCI, 16], BF16)
            io['dbg_ec'] = dram("dbg_ec", [16, 4])
            io['dbg_vscb'] = dram("dbg_vscb", [128, 2, H], BF16)
            io['dbg_esb'] = dram("dbg_esb", [16, 2, 256], BF16)
            io['dbg_ktt'] = dram("dbg_ktt", [128, 2, H], BF16)
            io['dbg_etm'] = dram("dbg_etm", [128, H])
            io['dbg_lnb'] = dram("dbg_lnb", [128, NCI, 1024], BF16)
            io['dbg_gx'] = dram("dbg_gx", [128, 32, 1024], BF16)
            io['dbg_gy'] = dram("dbg_gy", [128, 32, 1024], BF16)
            io['dbg_flnt'] = dram("dbg_flnt", [128, NCI, 512])
            io['dbg_frows'] = dram("dbg_frows", [1, 1, 5, 512])

        wnames = []
        if 'l0a' in phases:
            wnames += ['l0a_wv', 'l0a_wr', 'l0a_wo']
        if 'l0f' in phases:
            wnames += ['l0f_wk', 'l0f_wr', 'l0f_wv']
        if 'l1aA' in phases:
            wnames += ['l1a_wq', 'l1a_wk', 'l1a_wv']
        if 'l1aBCD' in phases:
            wnames += ['l1a_wo']
        dw = {n: dram(n, [128, NCI, H], ext_in=True) for n in wnames}
        if 'l1f' in phases:
            io['w1'] = dram("w1", [32, 128, NCI, 128], ext_in=True)
            io['w2'] = dram("w2", [32, 128, NCI, 128], ext_in=True)
            io['h1'] = dram("h1", [32, 128, 32, 128], ext_in=True)
            io['h2'] = dram("h2", [32, 128, 32, 128], ext_in=True)
            io['w3'] = dram("w3", [NCI, 128, 32, 128], ext_in=True)
            io['s1v'] = dram("s1v", [128, 32], ext_in=True)
            io['s2v'] = dram("s2v", [128, 32], ext_in=True)
            io['h1sv'] = dram("h1sv", [128, 32], ext_in=True)
            io['h2sv'] = dram("h2sv", [128, 32], ext_in=True)
            io['s3v'] = dram("s3v", [128, NCI], ext_in=True)
        d_ident = dram("ident", [128, 128], ext_in=True)
        d_selb = dram("selb", [NCI, 16, 128], ext_in=True)
        d_selrow = dram("selrow", [16, H], ext_in=True)

        vnames = ['td', 'tmv', 'tmr', 'mur', 'muk', 'bq', 'bk', 'bv', 'bo']
        dv = {n: dram("vec_" + n, [128, NCI], ext_in=True) for n in vnames}

        vb = {n: ctx.sb("v_" + n, [128, NCI], F32) for n in vnames}
        vb['d'] = ctx.sb("v_d", [128, NCI], F32)
        vb['dneg'] = ctx.sb("v_dneg", [128, NCI], F32)
        vb['consts'] = ctx.sb("v_consts", [128, 4], F32)
        vb['ones_inv'] = ctx.sb("v_onesinv", [128, 1], BF16)
        vb['ones1'] = ctx.sb("v_ones1", [1, 128], F32)
        vb['nones1'] = ctx.sb("v_nones1", [1, 128], F32)
        vb['idf'] = ctx.sb("v_idf", [128, 128], F32)
        vb['idb'] = ctx.sb("v_idb", [128, 128], BF16)
        vb['selbf'] = ctx.sb("v_selbf", [16, NCI, 128], F32)
        vb['selb'] = ctx.sb("v_selb", [16, NCI, 128], BF16)
        vb['selrowf'] = ctx.sb("v_selrowf", [16, H], F32)
        vb['selrow'] = ctx.sb("v_selrow", [16, H], BF16)
        vb['onesb'] = ctx.sb("v_onesb", [128, 1], BF16)
        if 'l1f' in phases:
            vb['s1v'] = ctx.sb("v_s1v", [128, 32], F32)
            vb['s2v'] = ctx.sb("v_s2v", [128, 32], F32)
            vb['h1s'] = ctx.sb("v_h1s", [128, 32], F32)
            vb['h2s'] = ctx.sb("v_h2s", [128, 32], F32)
            vb['s3v'] = ctx.sb("v_s3v", [128, NCI], F32)

        if 'l1aA' in phases:
            nt_a = len(tiles_of(TCORE, 256))
            QSP = ctx.sb("a_qsp", [128, NCI, nt_a], F32)
            KSP = ctx.sb("a_ksp", [128, NCI, nt_a], F32)

        wscope = contextlib.ExitStack()
        wscope_closed = False
        ctx.scope = wscope
        wslot = [ctx.sb(f"wslot{k}", [128, NCI, H], BF16) for k in range(4)]
        ctx.scope = es

        semnames = ('pe', 'act', 'dve', 'gps', 'wld',
                    'ld0', 'ld1', 'st0', 'st1', 'cc',
                    'lda0', 'lda1', 'ldb0', 'ldb1', 'ldc0', 'ldc1',
                    'sta0', 'sta1', 'stb0', 'stb1', 'stc0', 'stc1',
                    'wl0', 'wl1')
        sems = {k: ctx.sem("s_" + k) for k in semnames}
        ph = Phases(ctx, sems)

        with nc.Block() as blk0:
            def f_vload(s):
                for n in vnames:
                    sems['ld0'].dma(s, out=vb[n][:], in_=dv[n])
                sems['ld0'].dma(s, out=vb['idf'][:], in_=d_ident)
                sems['ld0'].dma(s, 
                    out=vb['selbf'][:], in_=d_selb.rearrange("c h j -> h c j"))
                sems['ld0'].dma(s, out=vb['selrowf'][:], in_=d_selrow)
                if 'l1f' in phases:
                    for a, bnm in (('s1v', 's1v'), ('s2v', 's2v'), ('h1s', 'h1sv'),
                                   ('h2s', 'h2sv'), ('s3v', 's3v')):
                        sems['ld0'].dma(s, out=vb[a][:], in_=io[bnm])
            blk0.sync(f_vload)

            def f_consts(v):
                sems['ld0'].wait(v, sems['ld0'].n)
                sDVE = sems['dve']
                for ins in (
                    v.memset(vb['consts'][:, 0:1], EPS_LN),
                    v.memset(vb['consts'][:, 1:2], EPS_A),
                    v.memset(vb['ones_inv'][:], 1.0 / H),
                    v.memset(vb['ones1'][:], 1.0),
                    v.memset(vb['nones1'][:], -1.0),
                    v.tensor_copy(vb['idb'][:], vb['idf'][:]),
                    v.tensor_copy(vb['selb'][:], vb['selbf'][:]),
                    v.tensor_copy(vb['selrow'][:], vb['selrowf'][:]),
                    v.memset(vb['onesb'][:], 1.0),
                ):
                    sDVE.inc(ins)
            blk0.vector(f_consts)

            def f_d(sc):
                sems['dve'].wait(sc, sems['dve'].n)
                sems['act'].inc(sc.activation(vb['dneg'][:], vb['td'][:], AF.Exp))
                sems['act'].wait(sc, sems['act'].n)
                sems['act'].inc(sc.activation(vb['d'][:], vb['dneg'][:], AF.Exp, scale=-1.0))
            blk0.scalar(f_d)
        ph.snapshot()

        if 'l0a' in phases:
            with contextlib.ExitStack() as ps:
                ctx.scope = ps
                with nc.Block() as blk:
                    ph.emit_guard(blk)
                    load_phase_weights(blk, sems, [
                        (wslot[0][:], dw['l0a_wv']),
                        (wslot[1][:], dw['l0a_wr']),
                        (wslot[2][:], dw['l0a_wo'])])
                    wv = sems['wld'].n

                    def f_w(t):
                        sems['wld'].wait(t, wv)
                    blk.tensor(f_w)
                    emit_phase_l0(ctx, blk, io['x'], io['r1'],
                                  wslot[0], wslot[1], wslot[2], vb, sems, 'attn')
                ctx.scope = es
            ph.snapshot()

        if 'l0f' in phases:
            with contextlib.ExitStack() as ps:
                ctx.scope = ps
                with nc.Block() as blk:
                    ph.emit_guard(blk)
                    load_phase_weights(blk, sems, [
                        (wslot[0][:], dw['l0f_wk']),
                        (wslot[1][:], dw['l0f_wr']),
                        (wslot[2][:], dw['l0f_wv'])])
                    wv = sems['wld'].n

                    def f_w(t):
                        sems['wld'].wait(t, wv)
                    blk.tensor(f_w)
                    emit_phase_l0(ctx, blk, io['r1'], io['r2'],
                                  wslot[0], wslot[1], wslot[2], vb, sems, 'ffn')
                ctx.scope = es
            ph.snapshot()

        if 'l1aA' in phases:
            with contextlib.ExitStack() as ps:
                ctx.scope = ps
                with nc.Block() as blk:
                    ph.emit_guard(blk)
                    wpairs = [
                        (wslot[0][:], dw['l1a_wq']),
                        (wslot[1][:], dw['l1a_wk']),
                        (wslot[2][:], dw['l1a_wv'])]
                    load_phase_weights(blk, sems, wpairs)
                    wv = sems['wld'].n

                    def f_w(t):
                        sems['wld'].wait(t, wv)
                    blk.tensor(f_w)
                    emit_phase_l1a_A(ctx, blk, io, wslot[0], wslot[1],
                                     wslot[2], vb, sems, QSP, KSP)
                ctx.scope = es
            ph.snapshot()

        if 'l1aBCD' in phases:
            wscope.close()
            wscope_closed = True
            io['dw_l1a_wo'] = dw['l1a_wo']
            with contextlib.ExitStack() as ps:
                ctx.scope = ps
                with nc.Block() as blk:
                    ph.emit_guard(blk)
                    emit_l1a_BCD(ctx, blk, io, vb, sems,
                                 QSP, KSP, nt_a, cc_groups)
                ctx.scope = es
            ph.snapshot()

        if 'l1f' in phases:
            if not wscope_closed:
                wscope.close()
                wscope_closed = True
            with contextlib.ExitStack() as ps:
                ctx.scope = ps
                with nc.Block() as blk:
                    ph.emit_guard(blk)
                    emit_phase_l1f(ctx, blk, io, vb, sems)
                ctx.scope = es
            ph.snapshot()

        return nc



def _fm(x2d):
    T = x2d.shape[0]
    return np.ascontiguousarray(x2d.T.reshape(NCI, 128, T).transpose(1, 0, 2))


def _unfm(a):
    return a.transpose(1, 0, 2).reshape(H, -1).T


def _wT(W):
    return np.ascontiguousarray(W.T.reshape(NCI, 128, W.shape[0]).transpose(1, 0, 2))


def _vec(v):
    return np.ascontiguousarray(np.asarray(v, np.float32).reshape(NCI, 128).T)


def make_core_inputs(full, phases=ALL_PHASES):
    x = np.asarray(full['L0_attn_x'])
    per = {}
    if 'l0a' in phases:
        per['l0a_wv'] = _wT(np.asarray(full['L0_attn_Wv']))
        per['l0a_wr'] = _wT(np.asarray(full['L0_attn_Wr']))
        per['l0a_wo'] = _wT(np.asarray(full['L0_attn_Wo']))
    if 'l0f' in phases:
        per['l0f_wk'] = _wT(np.asarray(full['L0_ffn_W_k']))
        per['l0f_wr'] = _wT(np.asarray(full['L0_ffn_W_r']))
        per['l0f_wv'] = _wT(np.asarray(full['L0_ffn_W_v']))
    if 'l1aA' in phases:
        per['l1a_wq'] = _wT(np.asarray(full['L1_attn_Wq']))
        per['l1a_wk'] = _wT(np.asarray(full['L1_attn_Wk']))
        per['l1a_wv'] = _wT(np.asarray(full['L1_attn_Wv']))
    if 'l1aBCD' in phases:
        per['l1a_wo'] = _wT(np.asarray(full['L1_attn_Wo']))
    per['vec_td'] = _vec(full['L0_attn_time_decay'])
    per['vec_tmv'] = _vec(full['L0_attn_time_mix_v'])
    per['vec_tmr'] = _vec(full['L0_attn_time_mix_r'])
    per['vec_mur'] = _vec(full['L0_ffn_mu_r'])
    per['vec_muk'] = _vec(full['L0_ffn_mu_k'])
    per['vec_bq'] = _vec(full['L1_attn_bq'])
    per['vec_bk'] = _vec(full['L1_attn_bk'])
    per['vec_bv'] = _vec(full['L1_attn_bv'])
    per['vec_bo'] = _vec(full['L1_attn_bo'])
    per['ident'] = np.eye(128, dtype=np.float32)
    selb = np.zeros((NCI, 16, 128), np.float32)
    for ci in range(NCI):
        selb[ci, 2 * ci, 0:64] = 1.0
        selb[ci, 2 * ci + 1, 64:128] = 1.0
    per['selb'] = selb
    selrow = np.zeros((16, H), np.float32)
    for hh in range(16):
        selrow[hh, hh * 64:(hh + 1) * 64] = 1.0
    per['selrow'] = selrow
    if 'l1f' in phases:
        def blk4(W, no):
            W = np.asarray(W, np.float32)
            return np.ascontiguousarray(
                W.reshape(no, 128, W.shape[1] // 128, 128).transpose(0, 3, 2, 1))
        per['w1'] = blk4(full['L1_ffn_w1_w'], 32)
        per['w2'] = blk4(full['L1_ffn_w2_w'], 32)
        per['h1'] = blk4(full['L1_ffn_h1_w'], 32)
        per['h2'] = blk4(full['L1_ffn_h2_w'], 32)
        per['w3'] = blk4(full['L1_ffn_w3_w'], NCI)
        per['s1v'] = np.ascontiguousarray(np.asarray(full['L1_ffn_w1_s'], np.float32).reshape(32, 128).T)
        per['s2v'] = np.ascontiguousarray(np.asarray(full['L1_ffn_w2_s'], np.float32).reshape(32, 128).T)
        per['h1sv'] = np.ascontiguousarray(np.asarray(full['L1_ffn_h1_s'], np.float32).reshape(32, 128).T)
        per['h2sv'] = np.ascontiguousarray(np.asarray(full['L1_ffn_h2_s'], np.float32).reshape(32, 128).T)
        per['s3v'] = np.ascontiguousarray(np.asarray(full['L1_ffn_w3_s'], np.float32).reshape(NCI, 128).T)
    maps = []
    for c in range(NCORES):
        bidx, s = divmod(c, 4)
        s0 = s * TCORE
        if s0 == 0:
            halo = np.zeros((HALO, H), np.float32)
        else:
            halo = x[bidx, s0 - HALO:s0]
        xs = np.concatenate([halo, x[bidx, s0:s0 + TCORE]], axis=0)
        m = {'x': _fm(xs)}
        m.update(per)
        maps.append(m)
    return maps


def kernel(**inputs):
    maps = make_core_inputs(inputs, phases=ALL_PHASES)
    nc = build_program(phases=ALL_PHASES, io_external=('y',))
    res = run_bass_kernel_spmd(nc, maps, core_ids=list(range(NCORES)))
    out = np.empty((2, 4 * TCORE, H), np.float32)
    for c in range(NCORES):
        y = np.asarray(res.results[c]['y']).reshape(128, NCI, TCORE)
        bidx, s = divmod(c, 4)
        out[bidx, s * TCORE:(s + 1) * TCORE] = _unfm(y.astype(np.float32))
    return out


# version 05
# speedup vs baseline: 1.1887x; 1.0171x over previous
import contextlib
import numpy as np
import concourse.bass as bass
import concourse.mybir as mybir
from concourse.bass_utils import run_bass_kernel_spmd

F32 = mybir.dt.float32
BF16 = mybir.dt.bfloat16
AF = mybir.ActivationFunctionType
ALU = mybir.AluOpType

H = 1024
NCI = 8
HALO = 128
TCORE = 4096
NTOK = TCORE + HALO
NCORES = 8
EPS_LN = 1e-5
EPS_A = 1e-6
SSCALE = 16384.0
DBG = False


def tiles_of(total, step, base=0):
    out = []
    t = 0
    while t < total:
        out.append((base + t, min(step, total - t)))
        t += step
    return out


class Sem:
    def __init__(self, ctx, name):
        self.h = ctx.es.enter_context(ctx.nc.semaphore(name))
        self.n = 0

    def inc(self, instr, k=1):
        instr.then_inc(self.h, k)
        self.n += k
        return self.n

    def wait(self, eng, val):
        if val > 0:
            eng.wait_ge(self.h, val)

    def dma(self, eng, *args, **kw):
        self.wait(eng, self.n)
        return self.inc(eng.dma_start(*args, **kw), 16)


class Ctx:
    def __init__(self, nc, es):
        self.nc = nc
        self.es = es
        self.scope = es

    def sem(self, name):
        return Sem(self, name)

    def sb(self, name, shape, dtype):
        return self.scope.enter_context(self.nc.sbuf_tensor(name, shape, dtype))

    def ps(self, name, shape, dtype=F32):
        return self.scope.enter_context(self.nc.psum_tensor(name, shape, dtype))


ENGS = ('sync', 'gpsimd', 'scalar', 'vector', 'tensor')


class Phases:

    def __init__(self, ctx, sems):
        self.ctx = ctx
        self.sems = sems
        self.guard = {}

    def snapshot(self):
        self.guard = {k: s.n for k, s in self.sems.items()}

    def emit_guard(self, block):
        g = dict(self.guard)

        def f(eng):
            for k, v in g.items():
                self.sems[k].wait(eng, v)
        for e in ENGS:
            getattr(block, e)(f)


def load_phase_weights(block, sems, pairs):
    sW = sems['wld']

    def f(g):
        for dst, src in pairs:
            sW.dma(g, out=dst, in_=src)
    block.gpsimd(f)
    return sW.n



def emit_phase_l0(ctx, block, d_src, d_dst, W1, W2, W3, vb, sems, kind):
    TP = 256
    tls = tiles_of(NTOK, TP)
    XT = [ctx.sb(f"{kind}_xt{b}", [128, NCI, TP], F32) for b in range(2)]
    SQB = [ctx.sb(f"{kind}_sqb{b}", [128, NCI, TP], BF16) for b in range(2)]
    XB = [ctx.sb(f"{kind}_xb{b}", [128, NCI, TP], BF16) for b in range(2)]
    LNT = ctx.sb(f"{kind}_lnt", [128, NCI, TP], F32)
    LNF = ctx.sb(f"{kind}_lnf", [128, NCI, TP + 1], F32)
    PREV = ctx.sb(f"{kind}_prev", [128, NCI, TP + 1], F32)
    UT = ctx.sb(f"{kind}_ut", [128, NCI, TP], F32)
    VIN = [ctx.sb(f"{kind}_vin{b}", [128, NCI, TP], BF16) for b in range(2)]
    RIN = [ctx.sb(f"{kind}_rin{b}", [128, NCI, TP], BF16) for b in range(2)]
    VB = [ctx.sb(f"{kind}_vb{b}", [128, NCI, TP], BF16) for b in range(2)]
    KAF = ctx.sb(f"{kind}_kaf", [128, TP], F32) if kind == 'ffn' else None
    RS = ctx.sb(f"{kind}_rs", [128, NCI, TP], F32)
    OTX = ctx.sb(f"{kind}_otx", [128, TP], F32)
    OT = [ctx.sb(f"{kind}_ot{b}", [128, NCI, TP], F32) for b in range(2)]
    CARRY = ctx.sb(f"{kind}_carry", [128, NCI], F32)
    ROWS = ctx.sb(f"{kind}_rows", [1, 2, 5, TP], F32)
    p_sum = ctx.ps(f"{kind}_psum", [1, TP])
    p_sq = ctx.ps(f"{kind}_psq", [1, TP])
    p_a = ctx.ps(f"{kind}_pa", [128, TP])
    p_b = ctx.ps(f"{kind}_pb", [128, TP])
    p_mm = [ctx.ps(f"{kind}_pmm{b}", [128, TP]) for b in range(2)]

    sPE, sACT, sDVE = sems['pe'], sems['act'], sems['dve']
    sLDp = [sems['ld0'], sems['ld1']]
    sSTp = [sems['st0'], sems['st1']]
    ev = {}

    def on(name, f):
        getattr(block, name)(f)

    def mkV(vector):
        def V(fn, *args, **kw):
            sDVE.wait(vector, sDVE.n)
            return sDVE.inc(getattr(vector, fn)(*args, **kw))
        return V

    def mkA(scalar):
        def A(fn, *args, **kw):
            sACT.wait(scalar, sACT.n)
            return sACT.inc(getattr(scalar, fn)(*args, **kw))
        return A

    eps_ln = vb['consts'][0:1, 0:1]
    if kind == 'attn':
        m1, m2c, dcol = vb['tmv'], vb['tmr'], vb['d']
    else:
        m1, m2c = vb['muk'], vb['mur']

    def f_init(vector):
        V = mkV(vector)
        V('memset', CARRY[:], 0.0)
        V('memset', PREV[:, :, 0:1], 0.0)
        ev['init'] = V('memset', LNF[:, :, 0:1], 0.0)
    on('vector', f_init)

    for i, (t0, T) in enumerate(tls):
        b = i % 2
        r = i % 2

        def f_load(sync, i=i, b=b, t0=t0, T=T):
            if i >= 2:
                sDVE.wait(sync, ev[('ep', i - 2)])
                sACT.wait(sync, ev[('sq', i - 2)])
            ev[('xld', i)] = sLDp[b].inc(
                sync.dma_start(out=XT[b][:, :, 0:T], in_=d_src[:, :, t0:t0 + T]), 16)
        on('sync', f_load)

        def f_sq(scalar, i=i, b=b, T=T):
            sLDp[b].wait(scalar, ev[('xld', i)])
            if i >= 2:
                sPE.wait(scalar, ev[('stats', i - 2)])
            ev[('sq', i)] = mkA(scalar)(
                'activation', SQB[b][:, :, 0:T], XT[b][:, :, 0:T], AF.Square)
        on('scalar', f_sq)

        def f_cast(vector, i=i, b=b, T=T):
            sLDp[b].wait(vector, ev[('xld', i)])
            if i >= 2:
                sPE.wait(vector, ev[('stats', i - 2)])
            ev[('xb', i)] = mkV(vector)(
                'tensor_copy', XB[b][:, :, 0:T], XT[b][:, :, 0:T])
        on('vector', f_cast)

        def f_stats(ten, i=i, b=b, T=T):
            sACT.wait(ten, ev[('sq', i)])
            sDVE.wait(ten, ev[('xb', i)])
            if i >= 1:
                sACT.wait(ten, ev[('m2', i - 1)])
                sDVE.wait(ten, ev[('rows', i - 1)])
            for ci in range(NCI):
                ten.matmul(p_sq[0:1, 0:T], vb['ones_inv'][:], SQB[b][:, ci, 0:T],
                           start=(ci == 0), stop=(ci == NCI - 1))
            for ci in range(NCI):
                ins = ten.matmul(p_sum[0:1, 0:T], vb['ones_inv'][:], XB[b][:, ci, 0:T],
                                 start=(ci == 0), stop=(ci == NCI - 1))
            ev[('stats', i)] = sPE.inc(ins)
        on('tensor', f_stats)

        m2 = ROWS[0:1, r, 0, 0:T]
        var = ROWS[0:1, r, 1, 0:T]
        sqv = ROWS[0:1, r, 2, 0:T]
        a_ = ROWS[0:1, r, 3, 0:T]
        mr = ROWS[0:1, r, 4, 0:T]

        def f_m2(scalar, i=i, T=T, m2=m2):
            sPE.wait(scalar, ev[('stats', i)])
            if i >= 1:
                sDVE.wait(scalar, ev[('rows', i - 1)])
            ev[('m2', i)] = mkA(scalar)('activation', m2, p_sum[0:1, 0:T], AF.Square)
        on('scalar', f_m2)

        def f_var(vector, i=i, T=T, m2=m2, var=var):
            sACT.wait(vector, ev[('m2', i)])
            ev[('var', i)] = mkV(vector)('tensor_sub', var, p_sq[0:1, 0:T], m2)
        on('vector', f_var)

        def f_sqv(scalar, i=i, T=T, var=var, sqv=sqv):
            sDVE.wait(scalar, ev[('var', i)])
            ev[('sqv', i)] = mkA(scalar)('activation', sqv, var, AF.Sqrt, bias=eps_ln)
        on('scalar', f_sqv)

        def f_rows(vector, i=i, T=T, sqv=sqv, a_=a_, mr=mr):
            sACT.wait(vector, ev[('sqv', i)])
            if i >= 1:
                sPE.wait(vector, ev[('bcast', i - 1)])
            V = mkV(vector)
            V('reciprocal', a_, sqv)
            ev[('rows', i)] = V('tensor_mul', mr, p_sum[0:1, 0:T], a_)
        on('vector', f_rows)

        def f_bcast(ten, i=i, T=T, a_=a_, mr=mr):
            sDVE.wait(ten, ev[('rows', i)])
            if i >= 1:
                sDVE.wait(ten, ev[('ln2', i - 1)])
            ten.matmul(p_a[:, 0:T], vb['ones1'][:], a_, start=True, stop=True)
            ins = ten.matmul(p_b[:, 0:T], vb['nones1'][:], mr, start=True, stop=True)
            ev[('bcast', i)] = sPE.inc(ins)
        on('tensor', f_bcast)

        def f_ln(vector, i=i, b=b, T=T):
            sPE.wait(vector, ev[('bcast', i)])
            V = mkV(vector)
            for ci in range(NCI):
                V('tensor_mul', LNT[:, ci, 0:T], XT[b][:, ci, 0:T], p_a[:, 0:T])
            for ci in range(NCI):
                ev[('ln2', i)] = V('tensor_add', LNF[:, ci, 1:T + 1],
                                   LNT[:, ci, 0:T], p_b[:, 0:T])
        on('vector', f_ln)

        if kind == 'attn':
            def f_scan(vector, i=i, T=T):
                V = mkV(vector)
                for ci in range(NCI):
                    V('tensor_copy', PREV[:, ci, 0:1], CARRY[:, ci:ci + 1])
                    V('tensor_tensor_scan',
                      PREV[:, ci, 1:T + 1],
                      dcol[:, ci:ci + 1].broadcast_to([128, T]),
                      LNF[:, ci, 1:T + 1],
                      CARRY[:, ci:ci + 1], ALU.mult, ALU.add)
                    V('tensor_copy', CARRY[:, ci:ci + 1], PREV[:, ci, T:T + 1])
            on('vector', f_scan)
            def base_ap(ci, T):
                return PREV[:, ci, 0:T]
        else:
            def base_ap(ci, T):
                return LNF[:, ci, 0:T]

        def f_mix(vector, i=i, b=b, T=T):
            if i >= 2:
                sPE.wait(vector, ev[('projr', i - 2)])
            V = mkV(vector)
            for ci in range(NCI):
                V('tensor_sub', UT[:, ci, 0:T], LNF[:, ci, 1:T + 1], base_ap(ci, T))
            for ci in range(NCI):
                V('scalar_tensor_tensor',
                  VIN[b][:, ci, 0:T], UT[:, ci, 0:T], m1[:, ci:ci + 1],
                  base_ap(ci, T), ALU.mult, ALU.add)
            for ci in range(NCI):
                ev[('mix', i)] = V('scalar_tensor_tensor',
                                   RIN[b][:, ci, 0:T], UT[:, ci, 0:T], m2c[:, ci:ci + 1],
                                   base_ap(ci, T), ALU.mult, ALU.add)
            if kind == 'ffn':
                for ci in range(NCI):
                    ev[('mix', i)] = V('tensor_copy', LNF[:, ci, 0:1], LNF[:, ci, T:T + 1])
        on('vector', f_mix)

        def f_projv_pre(ten, i=i):
            sDVE.wait(ten, ev[('mix', i)])
        on('tensor', f_projv_pre)

        for oc in range(8):
            def f_projv(ten, i=i, b=b, T=T, oc=oc):
                if oc >= 2:
                    sACT.wait(ten, ev[('evacv', i, oc - 2)])
                elif i >= 1:
                    sDVE.wait(ten, ev[('epoc', i - 1, 6 + oc)])
                for ci in range(NCI):
                    ins = ten.matmul(p_mm[oc % 2][:, 0:T], W1[:, ci, oc * 128:(oc + 1) * 128],
                                     VIN[b][:, ci, 0:T], start=(ci == 0), stop=(ci == NCI - 1))
                ev[('projv', i, oc)] = sPE.inc(ins)
            on('tensor', f_projv)

            def f_evacv(scalar, i=i, b=b, T=T, oc=oc):
                sPE.wait(scalar, ev[('projv', i, oc)])
                if oc == 0 and i >= 2:
                    sPE.wait(scalar, ev[('projo', i - 2)])
                A = mkA(scalar)
                if kind == 'attn':
                    ev[('evacv', i, oc)] = A('copy', VB[b][:, oc, 0:T], p_mm[oc % 2][:, 0:T])
                else:
                    A('activation', KAF[:, 0:T], p_mm[oc % 2][:, 0:T], AF.Relu)
                    ev[('evacv', i, oc)] = A('activation', VB[b][:, oc, 0:T],
                                             KAF[:, 0:T], AF.Square)
            on('scalar', f_evacv)

        for oc in range(8):
            def f_projr(ten, i=i, b=b, T=T, oc=oc):
                if oc >= 2:
                    sACT.wait(ten, ev[('sigr', i, oc - 2)])
                else:
                    sACT.wait(ten, ev[('evacv', i, 6 + oc)])
                for ci in range(NCI):
                    ins = ten.matmul(p_mm[oc % 2][:, 0:T], W2[:, ci, oc * 128:(oc + 1) * 128],
                                     RIN[b][:, ci, 0:T], start=(ci == 0), stop=(ci == NCI - 1))
                ev[('projr', i, oc)] = sPE.inc(ins)
            on('tensor', f_projr)

            def f_sigr(scalar, i=i, T=T, oc=oc):
                sPE.wait(scalar, ev[('projr', i, oc)])
                if oc == 0 and i >= 1:
                    sDVE.wait(scalar, ev[('ep', i - 1)])
                ev[('sigr', i, oc)] = mkA(scalar)(
                    'activation', RS[:, oc, 0:T], p_mm[oc % 2][:, 0:T], AF.Sigmoid)
            on('scalar', f_sigr)
        ev[('projr', i)] = ev[('projr', i, 7)]

        for oc in range(8):
            def f_projo(ten, i=i, b=b, T=T, oc=oc):
                if oc == 0:
                    sACT.wait(ten, ev[('evacv', i, 7)])
                if oc >= 2:
                    sDVE.wait(ten, ev[('epoc', i, oc - 2)])
                else:
                    sACT.wait(ten, ev[('sigr', i, 6 + oc)])
                for ci in range(NCI):
                    ins = ten.matmul(p_mm[oc % 2][:, 0:T], W3[:, ci, oc * 128:(oc + 1) * 128],
                                     VB[b][:, ci, 0:T], start=(ci == 0), stop=(ci == NCI - 1))
                ev[('projo', i, oc)] = sPE.inc(ins)
            on('tensor', f_projo)

            def f_ep(vector, i=i, b=b, T=T, oc=oc):
                sPE.wait(vector, ev[('projo', i, oc)])
                sACT.wait(vector, ev[('sigr', i, oc)])
                if oc == 0 and i >= 2:
                    sSTp[b].wait(vector, ev[('stored', i - 2)])
                V = mkV(vector)
                V('tensor_mul', OTX[:, 0:T], RS[:, oc, 0:T], p_mm[oc % 2][:, 0:T])
                ev[('epoc', i, oc)] = V('tensor_add', OT[b][:, oc, 0:T],
                                        OTX[:, 0:T], XT[b][:, oc, 0:T])
            on('vector', f_ep)
        ev[('ep', i)] = ev[('epoc', i, 7)]
        ev[('projo', i)] = ev[('projo', i, 7)]

        def f_store(sync, i=i, b=b, t0=t0, T=T):
            sDVE.wait(sync, ev[('ep', i)])
            ev[('stored', i)] = sSTp[b].inc(
                sync.dma_start(out=d_dst[:, :, t0:t0 + T], in_=OT[b][:, :, 0:T]), 16)
        on('sync', f_store)

    def f_fin(sync):
        n = len(tls)
        sSTp[(n - 1) % 2].wait(sync, ev[('stored', n - 1)])
        sSTp[(n - 2) % 2].wait(sync, ev[('stored', n - 2)])
    on('sync', f_fin)




class LNHelper:

    def __init__(self, ctx, sems, vb, prefix, TP):
        self.vb = vb
        self.sems = sems
        self.TP = TP
        self.SQB = [ctx.sb(f"{prefix}_sqb{b}", [128, NCI, TP], BF16) for b in range(2)]
        self.XB = [ctx.sb(f"{prefix}_xb{b}", [128, NCI, TP], BF16) for b in range(2)]
        self.LNT = ctx.sb(f"{prefix}_lnt", [128, NCI, TP], BF16)
        self.ROWS = ctx.sb(f"{prefix}_rows", [1, 1, 5, TP], F32)
        self.p_sum = ctx.ps(f"{prefix}_psum", [1, TP])
        self.p_sq = ctx.ps(f"{prefix}_psq", [1, TP])
        self.p_a = ctx.ps(f"{prefix}_pa", [128, TP])
        self.p_b = ctx.ps(f"{prefix}_pb", [128, TP])

    def emit(self, block, ev, i, b, XT_b, T, tgt, ld_wait, pe_busy_wait=None):
        sems = self.sems
        sPE, sACT, sDVE = sems['pe'], sems['act'], sems['dve']
        vb = self.vb
        SQB, XB, LNT, ROWS = self.SQB, self.XB, self.LNT, self.ROWS
        p_sum, p_sq, p_a, p_b = self.p_sum, self.p_sq, self.p_a, self.p_b
        eps_ln = vb['consts'][0:1, 0:1]
        r = 0

        def on(name, f):
            getattr(block, name)(f)

        def mkV(vector):
            def V(fn, *args, **kw):
                sDVE.wait(vector, sDVE.n)
                return sDVE.inc(getattr(vector, fn)(*args, **kw))
            return V

        def mkA(scalar):
            def A(fn, *args, **kw):
                sACT.wait(scalar, sACT.n)
                return sACT.inc(getattr(scalar, fn)(*args, **kw))
            return A

        def f_sq(scalar, i=i, b=b, T=T):
            ld_wait(scalar)
            if i >= 2:
                sPE.wait(scalar, ev[('stats', i - 2)])
            ev[('sq', i)] = mkA(scalar)(
                'activation', SQB[b][:, :, 0:T], XT_b[:, :, 0:T], AF.Square)
        on('scalar', f_sq)

        def f_cast(vector, i=i, b=b, T=T):
            ld_wait(vector)
            if i >= 2:
                sPE.wait(vector, ev[('stats', i - 2)])
            ev[('xb', i)] = mkV(vector)(
                'tensor_copy', XB[b][:, :, 0:T], XT_b[:, :, 0:T])
        on('vector', f_cast)

        def f_stats(ten, i=i, b=b, T=T):
            sACT.wait(ten, ev[('sq', i)])
            sDVE.wait(ten, ev[('xb', i)])
            if i >= 1:
                sACT.wait(ten, ev[('m2', i - 1)])
                sDVE.wait(ten, ev[('rows', i - 1)])
            if pe_busy_wait is not None:
                pe_busy_wait(ten, i)
            for ci in range(NCI):
                ten.matmul(p_sq[0:1, 0:T], vb['ones_inv'][:], SQB[b][:, ci, 0:T],
                           start=(ci == 0), stop=(ci == NCI - 1))
            for ci in range(NCI):
                ins = ten.matmul(p_sum[0:1, 0:T], vb['ones_inv'][:], XB[b][:, ci, 0:T],
                                 start=(ci == 0), stop=(ci == NCI - 1))
            ev[('stats', i)] = sPE.inc(ins)
        on('tensor', f_stats)

        m2 = ROWS[0:1, r, 0, 0:T]
        var = ROWS[0:1, r, 1, 0:T]
        sqv = ROWS[0:1, r, 2, 0:T]
        a_ = ROWS[0:1, r, 3, 0:T]
        mr = ROWS[0:1, r, 4, 0:T]

        def f_m2(scalar, i=i, T=T, m2=m2):
            sPE.wait(scalar, ev[('stats', i)])
            if i >= 1:
                sDVE.wait(scalar, ev[('rows', i - 1)])
            ev[('m2', i)] = mkA(scalar)('activation', m2, p_sum[0:1, 0:T], AF.Square)
        on('scalar', f_m2)

        def f_var(vector, i=i, T=T, m2=m2, var=var):
            sACT.wait(vector, ev[('m2', i)])
            ev[('var', i)] = mkV(vector)('tensor_sub', var, p_sq[0:1, 0:T], m2)
        on('vector', f_var)

        def f_sqv(scalar, i=i, T=T, var=var, sqv=sqv):
            sDVE.wait(scalar, ev[('var', i)])
            ev[('sqv', i)] = mkA(scalar)('activation', sqv, var, AF.Sqrt, bias=eps_ln)
        on('scalar', f_sqv)

        def f_rows(vector, i=i, T=T, sqv=sqv, a_=a_, mr=mr):
            sACT.wait(vector, ev[('sqv', i)])
            if i >= 1:
                sPE.wait(vector, ev[('bcast', i - 1)])
            V = mkV(vector)
            V('reciprocal', a_, sqv)
            ev[('rows', i)] = V('tensor_mul', mr, p_sum[0:1, 0:T], a_)
        on('vector', f_rows)

        def f_bcast(ten, i=i, T=T, a_=a_, mr=mr):
            sDVE.wait(ten, ev[('rows', i)])
            if i >= 1:
                sDVE.wait(ten, ev[('ln2', i - 1)])
            ten.matmul(p_a[:, 0:T], vb['ones1'][:], a_, start=True, stop=True)
            ins = ten.matmul(p_b[:, 0:T], vb['nones1'][:], mr, start=True, stop=True)
            ev[('bcast', i)] = sPE.inc(ins)
        on('tensor', f_bcast)

        def f_ln(vector, i=i, b=b, T=T):
            sPE.wait(vector, ev[('bcast', i)])
            V = mkV(vector)
            for ci in range(NCI):
                V('tensor_mul', LNT[:, ci, 0:T], XT_b[:, ci, 0:T], p_a[:, 0:T])
            for ci in range(NCI):
                ev[('ln2', i)] = tgt(vector, V, ci)
        on('vector', f_ln)


def emit_phase_l1a_A(ctx, block, io, WQ, WK, WV_, vb, sems, QSP, KSP):
    TP = 256
    tls = tiles_of(TCORE, TP)
    d_src = io['r2']
    XT = [ctx.sb(f"a_xt{b}", [128, NCI, TP], F32) for b in range(2)]
    LNB = [ctx.sb(f"a_lnb{b}", [128, NCI, TP], BF16) for b in range(2)]
    QFB = [ctx.sb(f"a_qfb{b}", [128, NCI, TP], BF16) for b in range(2)]
    KFB = [ctx.sb(f"a_kfb{b}", [128, NCI, TP], BF16) for b in range(2)]
    VFB = [ctx.sb(f"a_vfb{b}", [128, NCI, TP], BF16) for b in range(2)]
    KTS = [ctx.sb(f"a_kts{b}", [128, 2, H], BF16) for b in range(2)]
    VTS = [ctx.sb(f"a_vts{b}", [128, 2, H], BF16) for b in range(2)]
    nt = len(tls)
    ln = LNHelper(ctx, sems, vb, "a", TP)
    p_mm = [ctx.ps(f"a_pmm{b}", [128, TP]) for b in range(2)]
    p_tp = [ctx.ps(f"a_ptp{k}", [128, 128], BF16) for k in range(2)]

    sPE, sACT, sDVE = sems['pe'], sems['act'], sems['dve']
    sLDp = [sems['ld0'], sems['ld1']]
    sSTp = [sems['st0'], sems['st1']]
    sSTa = [sems['sta0'], sems['sta1']]
    sSTb = [sems['stb0'], sems['stb1']]
    sSTc = [sems['stc0'], sems['stc1']]
    ev = {}

    def on(name, f):
        getattr(block, name)(f)

    def mkV(vector):
        def V(fn, *args, **kw):
            sDVE.wait(vector, sDVE.n)
            return sDVE.inc(getattr(vector, fn)(*args, **kw))
        return V

    def mkA(scalar):
        def A(fn, *args, **kw):
            sACT.wait(scalar, sACT.n)
            return sACT.inc(getattr(scalar, fn)(*args, **kw))
        return A

    def bank_prev(i, st, oc):
        g = st * 8 + oc - 2
        if g >= 0:
            return i, g // 8, g % 8
        if i == 0:
            return None
        return i - 1, (24 + g) // 8, (24 + g) % 8

    EV_ENG = {0: 'act', 1: 'act', 2: 'dve'}

    for i, (t0, T) in enumerate(tls):
        b = i % 2

        def f_load(sync, i=i, b=b, t0=t0, T=T):
            if i >= 2:
                sACT.wait(sync, ev[('sq', i - 2)])
                sDVE.wait(sync, ev[('xb', i - 2)])
            ev[('xld', i)] = sLDp[b].inc(
                sync.dma_start(out=XT[b][:, :, 0:T],
                               in_=d_src[:, :, HALO + t0:HALO + t0 + T]), 16)
        on('sync', f_load)

        def ld_wait(eng, i=i, b=b):
            sLDp[b].wait(eng, ev[('xld', i)])

        def tgt(vector, V, ci, i=i, b=b, T=T):
            if ci == 0 and i >= 2:
                sPE.wait(vector, ev[('projV', i - 2)])
            return V('tensor_add', LNB[b][:, ci, 0:T], ln.LNT[:, ci, 0:T],
                     ln.p_b[:, 0:T])

        ln.emit(block, ev, i, b, XT[b], T, tgt, ld_wait)

        for st, Wmat, OUT, bias_col, acc in ((0, WQ, QFB, 'bq', QSP),
                                             (1, WK, KFB, 'bk', KSP)):
            for oc in range(8):
                def f_proj(ten, i=i, b=b, T=T, oc=oc, st=st, Wmat=Wmat):
                    if st == 0 and oc == 0:
                        sDVE.wait(ten, ev[('ln2', i)])
                    prev = bank_prev(i, st, oc)
                    if prev is not None:
                        pi, pst, poc = prev
                        s = sACT if EV_ENG[pst] == 'act' else sDVE
                        s.wait(ten, ev[('evx', pi, pst, poc)])
                    for ci in range(NCI):
                        ins = ten.matmul(p_mm[oc % 2][:, 0:T],
                                         Wmat[:, ci, oc * 128:(oc + 1) * 128],
                                         LNB[b][:, ci, 0:T],
                                         start=(ci == 0), stop=(ci == NCI - 1))
                    ev[('proj', i, st, oc)] = sPE.inc(ins)
                on('tensor', f_proj)

                def f_evac(scalar, i=i, b=b, T=T, oc=oc, st=st, OUT=OUT,
                           bias_col=bias_col, acc=acc):
                    sPE.wait(scalar, ev[('proj', i, st, oc)])
                    if st == 0 and oc == 0 and i >= 2:
                        sSTp[b].wait(scalar, ev[('stq', i - 2)])
                    if st == 1 and oc == 0 and i >= 2:
                        sSTa[b].wait(scalar, ev[('stk', i - 2)])
                    ev[('evx', i, st, oc)] = mkA(scalar)(
                        'activation', OUT[b][:, oc, 0:T], p_mm[oc % 2][:, 0:T],
                        AF.Sigmoid, bias=vb[bias_col][:, oc:oc + 1],
                        accum_out=acc[:, oc, i:i + 1])
                on('scalar', f_evac)

        for oc in range(8):
            def f_projv(ten, i=i, b=b, T=T, oc=oc):
                pi, pst, poc = bank_prev(i, 2, oc)
                s = sACT if EV_ENG[pst] == 'act' else sDVE
                s.wait(ten, ev[('evx', pi, pst, poc)])
                for ci in range(NCI):
                    ins = ten.matmul(p_mm[oc % 2][:, 0:T],
                                     WV_[:, ci, oc * 128:(oc + 1) * 128],
                                     LNB[b][:, ci, 0:T],
                                     start=(ci == 0), stop=(ci == NCI - 1))
                ev[('proj', i, 2, oc)] = sPE.inc(ins)
            on('tensor', f_projv)

            def f_evacv(vector, i=i, b=b, T=T, oc=oc):
                sPE.wait(vector, ev[('proj', i, 2, oc)])
                ev[('evx', i, 2, oc)] = mkV(vector)(
                    'tensor_scalar_add', VFB[b][:, oc, 0:T], p_mm[oc % 2][:, 0:T],
                    vb['bv'][:, oc:oc + 1])
            on('vector', f_evacv)
        ev[('projV', i)] = ev[('proj', i, 2, 7)]

        tp_seq = [(tch, sid, oc) for tch in range(2) for sid in range(2)
                  for oc in range(8)]
        for n, (tch, sid, oc) in enumerate(tp_seq):
            SRC = KFB if sid == 0 else VFB
            DST = KTS if sid == 0 else VTS

            def f_tp(ten, i=i, b=b, tch=tch, oc=oc, sid=sid, n=n, SRC=SRC):
                if n == 0:
                    sACT.wait(ten, ev[('evx', i, 1, 7)])
                    sDVE.wait(ten, ev[('evx', i, 2, 7)])
                if n >= 2:
                    sACT.wait(ten, ev[('tpc', i, n - 2)])
                elif i >= 1:
                    sACT.wait(ten, ev[('tpc', i - 1, 30 + n)])
                ins = ten.transpose(p_tp[n % 2][:],
                                    SRC[b][:, oc, tch * 128:(tch + 1) * 128],
                                    vb['idb'][:])
                ev[('tp', i, n)] = sPE.inc(ins)
            on('tensor', f_tp)

            def f_tpc(scalar, i=i, b=b, tch=tch, oc=oc, n=n, DST=DST):
                sPE.wait(scalar, ev[('tp', i, n)])
                if n == 0 and i >= 2:
                    sSTb[b].wait(scalar, ev[('stkt', i - 2)])
                    sSTc[b].wait(scalar, ev[('stvt', i - 2)])
                ev[('tpc', i, n)] = mkA(scalar)(
                    'copy', DST[b][:, tch, oc * 128:(oc + 1) * 128], p_tp[n % 2][:])
            on('scalar', f_tpc)

        def f_store(sync, i=i, b=b, t0=t0, T=T):
            sACT.wait(sync, ev[('tpc', i, 31)])
            sDVE.wait(sync, ev[('evx', i, 2, 7)])
            ev[('stq', i)] = sSTp[b].dma(sync, 
                out=io['qf'][:, :, t0:t0 + T], in_=QFB[b][:, :, 0:T])
            ev[('stk', i)] = sSTa[b].dma(sync, 
                out=io['kf'][:, :, t0:t0 + T], in_=KFB[b][:, :, 0:T])
            tc0 = 2 * i
            ev[('stkt', i)] = sSTb[b].dma(sync, 
                out=io['kt'][tc0:tc0 + 2, :, :].rearrange("tc p d -> p tc d"),
                in_=KTS[b][:])
            ev[('stvt', i)] = sSTc[b].dma(sync, 
                out=io['vt'][tc0:tc0 + 2, :, :].rearrange("tc p d -> p tc d"),
                in_=VTS[b][:])
        on('sync', f_store)

    def f_fin(sync):
        for j in (nt - 1, nt - 2):
            sSTp[j % 2].wait(sync, ev[('stq', j)])
            sSTa[j % 2].wait(sync, ev[('stk', j)])
            sSTb[j % 2].wait(sync, ev[('stkt', j)])
            sSTc[j % 2].wait(sync, ev[('stvt', j)])
    on('sync', f_fin)
    ev['nt'] = nt
    return ev


def emit_l1a_BCD(ctx, block, io, vb, sems, QSP, KSP, nt_a, cc_groups):
    TP = 256
    WO = ctx.sb("d_wo", [128, NCI, H], BF16)
    load_phase_weights(block, sems, [(WO[:], io['dw_l1a_wo'])])
    wo_ld = sems['wld'].n
    tls = tiles_of(TCORE, TP)
    nt = len(tls)
    sPE, sACT, sDVE, sGPS, sCC = (sems[k] for k in
                                  ('pe', 'act', 'dve', 'gps', 'cc'))
    sLDp = [sems['ld0'], sems['ld1']]
    sLDa = [sems['lda0'], sems['lda1']]
    sLDb = [sems['ldb0'], sems['ldb1']]
    sLDc = [sems['ldc0'], sems['ldc1']]
    sSTp = [sems['st0'], sems['st1']]
    ev = {}

    def on(name, f):
        getattr(block, name)(f)

    def mkV(vector):
        def V(fn, *args, **kw):
            sDVE.wait(vector, sDVE.n)
            return sDVE.inc(getattr(vector, fn)(*args, **kw))
        return V

    def mkA(scalar):
        def A(fn, *args, **kw):
            sACT.wait(scalar, sACT.n)
            return sACT.inc(getattr(scalar, fn)(*args, **kw))
        return A

    SUMS = ctx.sb("b_sums", [128, 2, NCI], F32)
    GS = ctx.sb("b_gs", [128, 2, NCI], F32)
    AK = ctx.sb("b_ak", [128, NCI, 16], BF16)
    AQ = ctx.sb("b_aq", [128, NCI, 16], BF16)
    EC = ctx.sb("b_ec", [16, 4], F32)
    T12 = ctx.sb("b_t12", [128, 2, NCI], F32)
    T12P = ctx.sb("b_t12p", [128, 2, NCI, nt], F32)
    AT1 = ctx.sb("b_at1", [128, NCI, 16], BF16)
    AT2 = ctx.sb("b_at2", [128, NCI, 16], BF16)
    SA = ctx.sb("c_sa", [16, TCORE], F32)
    ZP = ctx.sb("c_zp", [16, nt], F32)
    KVF = ctx.sb("c_kvf", [128, NCI, 128], F32)
    KVG = ctx.sb("c_kvg", [128, NCI, 128], F32)
    KVSB = ctx.sb("c_kvsb", [128, NCI, 128], BF16)
    ZROW = ctx.sb("c_zrow", [16, 2], F32)
    ZB = ctx.sb("c_zb", [16, 1], BF16)
    ZREP = ctx.sb("c_zrep", [128, NCI], F32)
    ROWS = ctx.sb("b_rows", [16, 2, 6, TP], F32)
    eps_a = vb['consts'][0:1, 1:2]

    def emit_ar(tag, sb_in, d_in, d_out, sb_out, wait_evs):
        def f(g):
            for s, v in wait_evs:
                s.wait(g, v)
            sGPS.dma(g, out=d_in, in_=sb_in)
            sGPS.wait(g, sGPS.n)
            sCC.inc(g.collective_compute(
                "AllReduce", ALU.add, replica_groups=cc_groups,
                ins=[d_in.opt()], outs=[d_out.opt()]))
            sCC.wait(g, sCC.n)
            sGPS.dma(g, out=sb_out, in_=d_out)
            sGPS.wait(g, sGPS.n)
        on('gpsimd', f)
        return sGPS.n

    def f_r1red(vector):
        V = mkV(vector)
        for ci in range(NCI):
            V('reduce_sum', SUMS[:, 0, ci:ci + 1], QSP[:, ci, 0:nt_a], mybir.AxisListType.X)
        for ci in range(NCI):
            ev['r1red'] = V('reduce_sum', SUMS[:, 1, ci:ci + 1], KSP[:, ci, 0:nt_a], mybir.AxisListType.X)
    on('vector', f_r1red)
    g1 = emit_ar('r1', SUMS[:], io['cc1i'], io['cc1o'], GS[:],
                 [(sDVE, sems['dve'].n)])

    def f_build_a(vector):
        sGPS.wait(vector, g1)
        V = mkV(vector)
        V('memset', AK[:], 0.0)
        V('memset', AQ[:], 0.0)
        for ci in range(NCI):
            V('tensor_scalar_add', AK[0:64, ci, 2 * ci:2 * ci + 1], GS[0:64, 1, ci:ci + 1], EPS_A)
            V('tensor_scalar_add', AK[64:128, ci, 2 * ci + 1:2 * ci + 2], GS[64:128, 1, ci:ci + 1], EPS_A)
            V('tensor_scalar_add', AQ[0:64, ci, 2 * ci:2 * ci + 1], GS[0:64, 0, ci:ci + 1], EPS_A)
            ev['builda'] = V('tensor_scalar_add', AQ[64:128, ci, 2 * ci + 1:2 * ci + 2], GS[64:128, 0, ci:ci + 1], EPS_A)
    on('vector', f_build_a)

    p_row = [ctx.ps(f"b_prow{k}", [16, TP]) for k in range(2)]
    p_bc = [ctx.ps(f"b_pbc{k}", [128, TP]) for k in range(2)]
    bc_scope = contextlib.ExitStack()
    outer_scope = ctx.scope
    ctx.scope = bc_scope
    p_e = ctx.ps("c_pe", [128, H])
    p_kv = ctx.ps("c_pkv", [128, NCI, 128])
    ctx.scope = outer_scope

    def f_ec_mm(ten):
        sDVE.wait(ten, ev['builda'])
        for ci in range(NCI):
            ten.matmul(p_row[0][0:16, 0:1], AK[:, ci, :], vb['onesb'][:],
                       start=(ci == 0), stop=(ci == NCI - 1))
        for ci in range(NCI):
            ins = ten.matmul(p_row[1][0:16, 0:1], AQ[:, ci, :], vb['onesb'][:],
                             start=(ci == 0), stop=(ci == NCI - 1))
        ev['ecmm'] = sPE.inc(ins)
    on('tensor', f_ec_mm)

    def f_ec(vector):
        sPE.wait(vector, ev['ecmm'])
        V = mkV(vector)
        V('tensor_scalar_mul', EC[:, 0:1], p_row[0][0:16, 0:1], EPS_A)
        ev['ec'] = V('tensor_scalar_mul', EC[:, 1:2], p_row[1][0:16, 0:1], EPS_A)
    on('vector', f_ec)

    QFT = [ctx.sb(f"b_qft{b}", [128, NCI, TP], BF16) for b in range(2)]
    KFT = [ctx.sb(f"b_kft{b}", [128, NCI, TP], BF16) for b in range(2)]
    SIB = ctx.sb("b_sib", [16, 2, TP], BF16)
    SOB = ctx.sb("b_sob", [16, 2, TP], BF16)
    QSIB = [ctx.sb(f"b_qsib{b}", [128, NCI, TP], BF16) for b in range(2)]
    DUMB = ctx.sb("b_dumb", [128, TP], BF16)

    for i, (t0, T) in enumerate(tls):
        b = i % 2

        def f_load(sync, i=i, b=b, t0=t0, T=T):
            if i >= 2:
                sDVE.wait(sync, ev[('bstt', i - 2)])
            ev[('bldq', i)] = sLDp[b].inc(
                sync.dma_start(out=QFT[b][:, :, 0:T], in_=io['qf'][:, :, t0:t0 + T]), 16)
            ev[('bldk', i)] = sLDa[b].inc(
                sync.dma_start(out=KFT[b][:, :, 0:T], in_=io['kf'][:, :, t0:t0 + T]), 16)
        on('sync', f_load)

        def f_rowmm(ten, i=i, b=b, T=T):
            sLDp[b].wait(ten, ev[('bldq', i)])
            sLDa[b].wait(ten, ev[('bldk', i)])
            if i == 0:
                sDVE.wait(ten, ev['ec'])
            if i >= 1:
                sDVE.wait(ten, ev[('brow', i - 1)])
            for ci in range(NCI):
                ten.matmul(p_row[0][:, 0:T], AK[:, ci, :], QFT[b][:, ci, 0:T],
                           start=(ci == 0), stop=(ci == NCI - 1))
            for ci in range(NCI):
                ins = ten.matmul(p_row[1][:, 0:T], AQ[:, ci, :], KFT[b][:, ci, 0:T],
                                 start=(ci == 0), stop=(ci == NCI - 1))
            ev[('browmm', i)] = sPE.inc(ins)
        on('tensor', f_rowmm)

        def f_brow(vector, i=i, b=b, T=T):
            sPE.wait(vector, ev[('browmm', i)])
            if i >= 2:
                sPE.wait(vector, ev[('bbc', i - 2, 1, 7)])
            V = mkV(vector)
            r = i % 2
            si_f = ROWS[:, r, 0, 0:T]
            so_f = ROWS[:, r, 1, 0:T]
            V('tensor_scalar_add', ROWS[:, r, 2, 0:T], p_row[0][:, 0:T], EC[:, 0:1])
            V('reciprocal', si_f, ROWS[:, r, 2, 0:T])
            V('tensor_copy', SIB[:, b, 0:T], si_f)
            V('tensor_scalar_add', ROWS[:, r, 3, 0:T], p_row[1][:, 0:T], EC[:, 1:2])
            V('reciprocal', so_f, ROWS[:, r, 3, 0:T])
            ev[('brow', i)] = V('tensor_copy', SOB[:, b, 0:T], so_f)
        on('vector', f_brow)

        for which in range(2):
            SB = SIB if which == 0 else SOB
            FT = QFT if which == 0 else KFT
            for ci in range(NCI):
                def f_bc(ten, i=i, b=b, T=T, ci=ci, which=which, SB=SB):
                    if ci == 0 and which == 0:
                        sDVE.wait(ten, ev[('brow', i)])
                    k = ('bstt', i, which, ci - 2)
                    if ci >= 2:
                        sDVE.wait(ten, ev[k])
                    elif which == 1:
                        sDVE.wait(ten, ev[('bstt', i, 0, 6 + ci)])
                    elif i >= 1:
                        sDVE.wait(ten, ev[('bstt', i - 1, 1, 6 + ci)])
                    ins = ten.matmul(p_bc[ci % 2][:, 0:T], vb['selb'][:, ci, :],
                                     SB[:, b, 0:T], start=True, stop=True)
                    ev[('bbc', i, which, ci)] = sPE.inc(ins)
                on('tensor', f_bc)

                def f_stt(vector, i=i, b=b, T=T, ci=ci, which=which, FT=FT):
                    sPE.wait(vector, ev[('bbc', i, which, ci)])
                    V = mkV(vector)
                    if which == 0:
                        out = QSIB[b][:, ci, 0:T]
                        if ci == 0 and i >= 2:
                            sSTp[b].wait(vector, ev[('bst', i - 2)])
                        acc = T12P[:, 1, ci, i:i + 1]
                    else:
                        out = DUMB[:, 0:T]
                        acc = T12P[:, 0, ci, i:i + 1]
                    ev[('bstt', i, which, ci)] = V(
                        'scalar_tensor_tensor', out, FT[b][:, ci, 0:T], 1.0,
                        p_bc[ci % 2][:, 0:T], ALU.mult, ALU.mult, accum_out=acc)
                on('vector', f_stt)
        ev[('bstt', i)] = ev[('bstt', i, 1, 7)]

        def f_bst(sync, i=i, b=b, t0=t0, T=T):
            sDVE.wait(sync, ev[('bstt', i, 0, 7)])
            ev[('bst', i)] = sSTp[b].inc(
                sync.dma_start(out=io['qsi'][:, :, t0:t0 + T], in_=QSIB[b][:, :, 0:T]), 16)
        on('sync', f_bst)

    bst_done = (sSTp[0].n, sSTp[1].n)

    def f_r2red(vector):
        V = mkV(vector)
        for w in range(2):
            for ci in range(NCI):
                ev['r2red'] = V('reduce_sum', T12[:, w, ci:ci + 1],
                                T12P[:, w, ci, 0:nt], mybir.AxisListType.X)
    on('vector', f_r2red)
    g2 = emit_ar('r2', T12[:], io['cc2i'], io['cc2o'], T12[:],
                 [(sDVE, sems['dve'].n)])

    def f_build_at(vector):
        sGPS.wait(vector, g2)
        V = mkV(vector)
        V('memset', AT1[:], 0.0)
        V('memset', AT2[:], 0.0)
        for ci in range(NCI):
            V('tensor_scalar_add', AT1[0:64, ci, 2 * ci:2 * ci + 1], T12[0:64, 0, ci:ci + 1], EPS_A)
            V('tensor_scalar_add', AT1[64:128, ci, 2 * ci + 1:2 * ci + 2], T12[64:128, 0, ci:ci + 1], EPS_A)
            V('tensor_scalar_add', AT2[0:64, ci, 2 * ci:2 * ci + 1], T12[0:64, 1, ci:ci + 1], EPS_A)
            ev['buildat'] = V('tensor_scalar_add', AT2[64:128, ci, 2 * ci + 1:2 * ci + 2], T12[64:128, 1, ci:ci + 1], EPS_A)
    on('vector', f_build_at)

    def f_ec2_mm(ten):
        sDVE.wait(ten, ev['buildat'])
        for ci in range(NCI):
            ten.matmul(p_row[0][0:16, 0:1], AT1[:, ci, :], vb['onesb'][:],
                       start=(ci == 0), stop=(ci == NCI - 1))
        for ci in range(NCI):
            ins = ten.matmul(p_row[1][0:16, 0:1], AT2[:, ci, :], vb['onesb'][:],
                             start=(ci == 0), stop=(ci == NCI - 1))
        ev['ec2mm'] = sPE.inc(ins)
    on('tensor', f_ec2_mm)

    def f_ec2(vector):
        sPE.wait(vector, ev['ec2mm'])
        V = mkV(vector)
        V('tensor_scalar_mul', EC[:, 2:3], p_row[0][0:16, 0:1], EPS_A)
        ev['ec2'] = V('tensor_scalar_mul', EC[:, 3:4], p_row[1][0:16, 0:1], EPS_A)
    on('vector', f_ec2)

    DBGETM = ctx.sb("c_dbgetm", [128, H], F32) if DBG else None
    KTT = [ctx.sb(f"c_ktt{b}", [128, 2, H], BF16) for b in range(2)]
    VTT = [ctx.sb(f"c_vtt{b}", [128, 2, H], BF16) for b in range(2)]
    ESB = ctx.sb("c_esb", [16, 2, TP], BF16)
    VSCB = ctx.sb("c_vscb", [128, 2, H], BF16)

    for i, (t0, T) in enumerate(tls):
        b = i % 2

        def f_cload(sync, i=i, b=b, t0=t0, T=T):
            if i >= 2:
                sPE.wait(sync, ev[('ckv', i - 2)])
                sDVE.wait(sync, ev[('cvsc', i - 2, 1)])
            elif ('bstt', nt - 1) in ev:
                sDVE.wait(sync, ev[('bstt', nt - 1)])
            tc0 = 2 * i
            ev[('cldq', i)] = sLDp[b].dma(sync, 
                out=QFT[b][:, :, 0:T], in_=io['qf'][:, :, t0:t0 + T])
            ev[('cldk', i)] = sLDa[b].dma(sync, 
                out=KFT[b][:, :, 0:T], in_=io['kf'][:, :, t0:t0 + T])
            ev[('cldkt', i)] = sLDb[b].dma(sync, 
                out=KTT[b][:], in_=io['kt'][tc0:tc0 + 2, :, :].rearrange("tc p d -> p tc d"))
            ev[('cldvt', i)] = sLDc[b].dma(sync, 
                out=VTT[b][:], in_=io['vt'][tc0:tc0 + 2, :, :].rearrange("tc p d -> p tc d"))
        on('sync', f_cload)

        def f_crowmm(ten, i=i, b=b, T=T):
            sLDp[b].wait(ten, ev[('cldq', i)])
            sLDa[b].wait(ten, ev[('cldk', i)])
            sLDb[b].wait(ten, ev[('cldkt', i)])
            if i == 0:
                sDVE.wait(ten, ev['ec2'])
            if i >= 1:
                sDVE.wait(ten, ev[('ces', i - 1)])
            for ci in range(NCI):
                ten.matmul(p_row[0][:, 0:T], AT1[:, ci, :], QFT[b][:, ci, 0:T],
                           start=(ci == 0), stop=(ci == NCI - 1))
            for ci in range(NCI):
                ins = ten.matmul(p_row[1][:, 0:T], AT2[:, ci, :], KFT[b][:, ci, 0:T],
                                 start=(ci == 0), stop=(ci == NCI - 1))
            ev[('crowmm', i)] = sPE.inc(ins)
        on('tensor', f_crowmm)

        def f_crow(vector, i=i, b=b, T=T, t0=t0):
            sPE.wait(vector, ev[('crowmm', i)])
            V = mkV(vector)
            r = i % 2
            V('tensor_scalar_add', ROWS[:, r, 0, 0:T], p_row[0][:, 0:T], EC[:, 2:3])
            ev[('cct', i)] = V('tensor_scalar', ROWS[:, r, 1, 0:T], p_row[1][:, 0:T],
                               EC[:, 3:4], None, ALU.add)
            ev[('cclip', i)] = V('tensor_scalar', ROWS[:, r, 2, 0:T], ROWS[:, r, 1, 0:T],
                                 1.0, -1.0, ALU.min, ALU.max)
        on('vector', f_crow)

        def f_cact(scalar, i=i, b=b, T=T, t0=t0):
            sDVE.wait(scalar, ev[('cclip', i)])
            A = mkA(scalar)
            r = i % 2
            ev[('csa', i)] = A('activation', SA[:, t0:t0 + T], ROWS[:, r, 0, 0:T], AF.Sigmoid)
            ev[('cexp', i)] = A('activation', ROWS[:, r, 3, 0:T], ROWS[:, r, 2, 0:T],
                                AF.Exp, accum_out=ZP[:, i:i + 1])
        on('scalar', f_cact)

        def f_ces(vector, i=i, b=b, T=T):
            sACT.wait(vector, ev[('cexp', i)])
            if i >= 2:
                sPE.wait(vector, ev[('ce', i - 2, 1)])
            r = i % 2
            ev[('ces', i)] = mkV(vector)('tensor_copy', ESB[:, b, 0:T], ROWS[:, r, 3, 0:T])
        on('vector', f_ces)

        for tch in range(2):
            def f_ce(ten, i=i, b=b, tch=tch):
                sDVE.wait(ten, ev[('ces', i)])
                if ('cvsc', i, tch - 1) in ev:
                    sDVE.wait(ten, ev[('cvsc', i, tch - 1)])
                elif i >= 1:
                    sDVE.wait(ten, ev[('cvsc', i - 1, 1)])
                ten.matmul(p_e[:, 0:512], ESB[:, b, tch * 128:(tch + 1) * 128],
                           vb['selrow'][:, 0:512], start=True, stop=True)
                ins = ten.matmul(p_e[:, 512:1024], ESB[:, b, tch * 128:(tch + 1) * 128],
                                 vb['selrow'][:, 512:1024], start=True, stop=True)
                ev[('ce', i, tch)] = sPE.inc(ins)
            on('tensor', f_ce)

            def f_cvsc(vector, i=i, b=b, tch=tch):
                sPE.wait(vector, ev[('ce', i, tch)])
                sLDc[b].wait(vector, ev[('cldvt', i)])
                ev[('cvsc', i, tch)] = mkV(vector)(
                    'tensor_mul', VSCB[:, tch, :], VTT[b][:, tch, :], p_e[:, 0:H])
            on('vector', f_cvsc)

            if DBG and i == nt - 1 and tch == 1:
                def f_dbg_etm(scalar, i=i):
                    sPE.wait(scalar, ev[('ce', i, 1)])
                    ev['dbgetm'] = mkA(scalar)('copy', DBGETM[:], p_e[:, 0:H])
                on('scalar', f_dbg_etm)

            def f_ckv(ten, i=i, b=b, tch=tch):
                sDVE.wait(ten, ev[('cvsc', i, tch)])
                for ci in range(NCI):
                    ins = ten.matmul(
                        p_kv[:, ci, :], KTT[b][:, tch, ci * 128:(ci + 1) * 128],
                        VSCB[:, tch, ci * 128:(ci + 1) * 128],
                        start=(i == 0 and tch == 0 and ci % 4 == 0),
                        stop=(i == nt - 1 and tch == 1 and ci % 4 == 3))
                ev[('ckv', i, tch)] = sPE.inc(ins)
            on('tensor', f_ckv)
        ev[('ckv', i)] = ev[('ckv', i, 1)]

    def f_kvev(scalar):
        sPE.wait(scalar, ev[('ckv', nt - 1)])
        ev['kvev'] = mkA(scalar)('copy', KVF[:], p_kv[:])
    on('scalar', f_kvev)

    def f_kvzero(vector):
        sACT.wait(vector, ev['kvev'])
        V = mkV(vector)
        for ci in range(NCI):
            V('memset', KVF[0:64, ci, 64:128], 0.0)
            V('memset', KVF[64:128, ci, 0:64], 0.0)
        for _ in (0,):
            ev['zred'] = V('reduce_sum', ZROW[:, 0:1], ZP[:, 0:nt], mybir.AxisListType.X)
    on('vector', f_kvzero)

    g3 = emit_ar('r3', KVF[:].rearrange("p c d -> p (c d)"), io['cc3i'], io['cc3o'],
                 KVG[:].rearrange("p c d -> p (c d)"), [(sDVE, sems['dve'].n)])
    g3z = emit_ar('r3z', ZROW[:, 0:1], io['cc4i'], io['cc4o'], ZROW[:, 0:1],
                  [(sGPS, g3)])

    def f_zinv(vector):
        sGPS.wait(vector, g3z)
        V = mkV(vector)
        V('reciprocal', ZROW[:, 1:2], ZROW[:, 0:1])
        V('tensor_scalar_mul', ZROW[:, 1:2], ZROW[:, 1:2], SSCALE)
        ev['zb'] = V('tensor_copy', ZB[:], ZROW[:, 1:2])
    on('vector', f_zinv)

    def f_zrep_mm(ten):
        sDVE.wait(ten, ev['zb'])
        for ci in range(NCI):
            ins = ten.matmul(p_bc[0][:, ci:ci + 1],
                             vb['selb'][:, ci, :], ZB[:], start=True, stop=True)
        ev['zrepmm'] = sPE.inc(ins)
    on('tensor', f_zrep_mm)

    def f_zrep(scalar):
        sPE.wait(scalar, ev['zrepmm'])
        ev['zrep'] = mkA(scalar)('copy', ZREP[:], p_bc[0][:, 0:NCI])
    on('scalar', f_zrep)

    def f_kvs(vector):
        sACT.wait(vector, ev['zrep'])
        V = mkV(vector)
        for ci in range(NCI):
            ev['kvs'] = V('tensor_scalar_mul', KVSB[:, ci, :], KVG[:, ci, :],
                          ZREP[:, ci:ci + 1])
    on('vector', f_kvs)

    if DBG:
        def f_dbg(g):
            sDVE.wait(g, sems['dve'].n)
            sACT.wait(g, sems['act'].n)
            for nm, ap in (("dbg_sa", SA[:]), ("dbg_kvg", KVG[:]),
                           ("dbg_kvsb", KVSB[:]), ("dbg_zrow", ZROW[:]),
                           ("dbg_zrep", ZREP[:]), ("dbg_t12", T12[:]),
                           ("dbg_at1", AT1[:]), ("dbg_at2", AT2[:]),
                           ("dbg_ec", EC[:]), ("dbg_vscb", VSCB[:]),
                           ("dbg_esb", ESB[:]), ("dbg_ktt", KTT[(nt - 1) % 2][:]),
                           ("dbg_etm", DBGETM[:])):
                sGPS.dma(g, out=io[nm], in_=ap)
        on('gpsimd', f_dbg)

    bc_scope.close()
    QST = [ctx.sb(f"d_qst{b}", [128, NCI, TP], BF16) for b in range(2)]
    XT = [ctx.sb(f"d_xt{b}", [128, NCI, TP], F32) for b in range(2)]
    SAB = ctx.sb("d_sab", [16, 2, TP], BF16)
    QSA = ctx.sb("d_qsa", [128, NCI, TP], BF16)
    OB = ctx.sb("d_ob", [128, NCI, TP], BF16)
    OX = ctx.sb("d_ox", [128, TP], F32)
    OT = [ctx.sb(f"d_ot{b}", [128, NCI, TP], F32) for b in range(2)]
    p_mm2 = [ctx.ps(f"d_pmm{k}", [128, TP]) for k in range(2)]

    for i, (t0, T) in enumerate(tls):
        b = i % 2

        def f_dload(sync, i=i, b=b, t0=t0, T=T):
            if i >= 2:
                sDVE.wait(sync, ev[('dep', i - 2)])
            else:
                sSTp[0].wait(sync, bst_done[0])
                sSTp[1].wait(sync, bst_done[1])
            ev[('dldq', i)] = sLDp[b].dma(sync, 
                out=QST[b][:, :, 0:T], in_=io['qsi'][:, :, t0:t0 + T])
            ev[('dld', i)] = sLDa[b].dma(sync, 
                out=XT[b][:, :, 0:T], in_=io['r2'][:, :, HALO + t0:HALO + t0 + T])
        on('sync', f_dload)

        def f_dsab(vector, i=i, b=b, T=T, t0=t0):
            if i == 0:
                sDVE.wait(vector, ev['kvs'])
            if i >= 1:
                sPE.wait(vector, ev[('dsabmm', i - 1, 7)])
            ev[('dsab', i)] = mkV(vector)('tensor_copy', SAB[:, b, 0:T], SA[:, t0:t0 + T])
        on('vector', f_dsab)

        for ci in range(NCI):
            def f_dbc(ten, i=i, b=b, T=T, ci=ci):
                sDVE.wait(ten, ev[('dsab', i)])
                if ci >= 2:
                    sDVE.wait(ten, ev[('dqsa', i, ci - 2)])
                elif i >= 1:
                    sDVE.wait(ten, ev[('dqsa', i - 1, 6 + ci)])
                ins = ten.matmul(p_bc[ci % 2][:, 0:T], vb['selb'][:, ci, :],
                                 SAB[:, b, 0:T], start=True, stop=True)
                ev[('dsabmm', i, ci)] = sPE.inc(ins)
            on('tensor', f_dbc)

            def f_dqsa(vector, i=i, b=b, T=T, ci=ci):
                sPE.wait(vector, ev[('dsabmm', i, ci)])
                sLDp[b].wait(vector, ev[('dldq', i)])
                if ci == 0 and i >= 1:
                    sACT.wait(vector, ev[('dob', i - 1, 7)])
                ev[('dqsa', i, ci)] = mkV(vector)(
                    'tensor_mul', QSA[:, ci, 0:T], QST[b][:, ci, 0:T], p_bc[ci % 2][:, 0:T])
            on('vector', f_dqsa)

            def f_dkv(ten, i=i, T=T, ci=ci):
                sDVE.wait(ten, ev[('dqsa', i, ci)])
                if ci >= 2:
                    sACT.wait(ten, ev[('dob', i, ci - 2)])
                elif i >= 1:
                    sACT.wait(ten, ev[('dob', i - 1, 6 + ci)])
                ins = ten.matmul(p_mm2[ci % 2][:, 0:T], KVSB[:, ci, :],
                                 QSA[:, ci, 0:T], start=True, stop=True)
                ev[('dkv', i, ci)] = sPE.inc(ins)
            on('tensor', f_dkv)

            def f_dob(scalar, i=i, T=T, ci=ci):
                sPE.wait(scalar, ev[('dkv', i, ci)])
                if ci == 0 and i >= 1:
                    sPE.wait(scalar, ev[('dproj', i - 1, 7)])
                ev[('dob', i, ci)] = mkA(scalar)('copy', OB[:, ci, 0:T],
                                                 p_mm2[ci % 2][:, 0:T])
            on('scalar', f_dob)

        for oc in range(8):
            def f_dproj(ten, i=i, b=b, T=T, oc=oc):
                if oc == 0:
                    sACT.wait(ten, ev[('dob', i, 7)])
                    sems['wld'].wait(ten, wo_ld)
                if oc >= 2:
                    sDVE.wait(ten, ev[('dep', i, oc - 2)])
                elif i >= 1:
                    sDVE.wait(ten, ev[('dep', i - 1, 6 + oc)])
                for ci in range(NCI):
                    ins = ten.matmul(p_mm2[oc % 2][:, 0:T], WO[:, ci, oc * 128:(oc + 1) * 128],
                                     OB[:, ci, 0:T], start=(ci == 0), stop=(ci == NCI - 1))
                ev[('dproj', i, oc)] = sPE.inc(ins)
            on('tensor', f_dproj)

            def f_dox(scalar, i=i, T=T, oc=oc):
                sPE.wait(scalar, ev[('dproj', i, oc)])
                if oc >= 1:
                    sDVE.wait(scalar, ev[('dep', i, oc - 1)])
                elif i >= 1:
                    sDVE.wait(scalar, ev[('dep', i - 1, 7)])
                ev[('dox', i, oc)] = mkA(scalar)(
                    'activation', OX[:, 0:T], p_mm2[oc % 2][:, 0:T], AF.Identity,
                    bias=vb['bo'][:, oc:oc + 1])
            on('scalar', f_dox)

            def f_dep(vector, i=i, b=b, T=T, oc=oc):
                sACT.wait(vector, ev[('dox', i, oc)])
                sLDa[b].wait(vector, ev[('dld', i)])
                if oc == 0 and i >= 2:
                    sSTp[b].wait(vector, ev[('dst', i - 2)])
                ev[('dep', i, oc)] = mkV(vector)(
                    'tensor_add', OT[b][:, oc, 0:T], OX[:, 0:T], XT[b][:, oc, 0:T])
            on('vector', f_dep)
        ev[('dep', i)] = ev[('dep', i, 7)]

        def f_dst(sync, i=i, b=b, t0=t0, T=T):
            sDVE.wait(sync, ev[('dep', i)])
            ev[('dst', i)] = sSTp[b].inc(
                sync.dma_start(out=io['r3'][:, :, t0:t0 + T], in_=OT[b][:, :, 0:T]), 16)
        on('sync', f_dst)

    def f_dfin(sync):
        sSTp[(nt - 1) % 2].wait(sync, ev[('dst', nt - 1)])
        sSTp[(nt - 2) % 2].wait(sync, ev[('dst', nt - 2)])
    on('sync', f_dfin)


def emit_phase_l1f(ctx, block, io, vb, sems):
    TS = min(512, TCORE)
    TT = TS
    NTT = TS // TT
    NS = TCORE // TS
    NH = 32
    sPE, sACT, sDVE, sGPS = sems['pe'], sems['act'], sems['dve'], sems['gps']
    sLDp = [sems['ld0'], sems['ld1']]
    sSTp = [sems['st0'], sems['st1']]
    sWL = [sems['wl0'], sems['wl1']]
    ev = {}

    LNB = ctx.sb("f_lnb", [128, NCI, TS], BF16)
    GX = ctx.sb("f_gx", [128, NH, TS], BF16)
    GY = ctx.sb("f_gy", [128, NH, TS], BF16)
    XT = ctx.sb("f_xt", [128, NCI, TT], F32)
    WB1 = [ctx.sb(f"f_wb1{k}", [128, NCI, 128], BF16) for k in range(2)]
    WB2 = [ctx.sb(f"f_wb2{k}", [128, NCI, 128], BF16) for k in range(2)]
    WH = [ctx.sb(f"f_wh{k}", [128, NH, 128], BF16) for k in range(2)]
    W3B = [ctx.sb(f"f_w3b{k}", [128, NH, 128], BF16) for k in range(2)]
    SILT = ctx.sb("f_silt", [128, TT], BF16)
    OX = ctx.sb("f_ox", [128, TT], F32)
    OTS = [ctx.sb(f"f_ots{k}", [128, TT], F32) for k in range(2)]
    ln = LNHelper(ctx, sems, vb, "f", TT)
    p1 = ctx.ps("f_p1", [128, TT])
    p2 = ctx.ps("f_p2", [128, TT])

    def on(name, f):
        getattr(block, name)(f)

    def mkV(vector):
        def V(fn, *args, **kw):
            sDVE.wait(vector, sDVE.n)
            return sDVE.inc(getattr(vector, fn)(*args, **kw))
        return V

    def mkA(scalar):
        def A(fn, *args, **kw):
            sACT.wait(scalar, sACT.n)
            return sACT.inc(getattr(scalar, fn)(*args, **kw))
        return A

    for s in range(NS):
        S0 = s * TS

        for tt in range(NTT):
            i = s * NTT + tt
            t0 = S0 + tt * TT

            def f_load(sync, i=i, t0=t0, s=s, tt=tt):
                if i >= 1:
                    sDVE.wait(sync, ev[('ln2', i - 1)])
                    sACT.wait(sync, ev[('sq', i - 1)])
                if s >= 1 and tt == 0:
                    sDVE.wait(sync, ev[('f4ep', s - 1, NTT - 1)])
                ev[('fxld', i)] = sLDp[0].dma(
                    sync, out=XT[:], in_=io['r3'][:, :, t0:t0 + TT])
            on('sync', f_load)

            def ld_wait(eng, i=i):
                sLDp[0].wait(eng, ev[('fxld', i)])

            def tgt(vector, V, ci, i=i, s=s, tt=tt):
                if ci == 0 and s >= 1:
                    sPE.wait(vector, ev[('f1done', s - 1)])
                return V('tensor_add', LNB[:, ci, tt * TT:(tt + 1) * TT],
                         ln.LNT[:, ci, 0:TT], ln.p_b[:, 0:TT])

            ln.emit(block, ev, i, 0, XT, TT, tgt, ld_wait)

        for oc in range(NH):
            kb = oc % 2

            def f_wl(g, s=s, oc=oc, kb=kb):
                if oc >= 2:
                    sPE.wait(g, ev[('f1mm', s, oc - 2)])
                elif s >= 1:
                    sPE.wait(g, ev[('f1mm', s - 1, NH - 2 + oc)])
                if s == 0:
                    sWL[kb].dma(g, out=WB1[kb][:], in_=io['w1'][oc, :, :, :])
                    ev[('f1w', s, oc)] = sWL[kb].dma(g, out=WB2[kb][:], in_=io['w2'][oc, :, :, :])
                    sWL[kb].dma(g, out=io['w1b'][oc, :, :, :], in_=WB1[kb][:])
                    ev[('f1wb', oc)] = sWL[kb].dma(g, out=io['w2b'][oc, :, :, :], in_=WB2[kb][:])
                else:
                    sWL[kb].dma(g, out=WB1[kb][:], in_=io['w1b'][oc, :, :, :])
                    ev[('f1w', s, oc)] = sWL[kb].dma(g, out=WB2[kb][:], in_=io['w2b'][oc, :, :, :])
            on('gpsimd', f_wl)

            for tt in range(NTT):
                def f_mm(ten, s=s, oc=oc, tt=tt, kb=kb):
                    if tt == 1:
                        sACT.wait(ten, ev[('f1silu', s, oc, 0)])
                        sDVE.wait(ten, ev[('f1stt', s, oc, 0)])
                    if tt == 0:
                        sWL[kb].wait(ten, ev[('f1w', s, oc)])
                        if oc == 0:
                            sDVE.wait(ten, ev[('ln2', s * NTT + NTT - 1)])
                        if oc >= 1:
                            sDVE.wait(ten, ev[('f1stt', s, oc - 1, NTT - 1)])
                        elif s >= 1:
                            sDVE.wait(ten, ev[('f1stt', s - 1, NH - 1, NTT - 1)])
                    for ci in range(NCI):
                        ten.matmul(p1[:, 0:TT], WB1[kb][:, ci, :],
                                   LNB[:, ci, tt * TT:(tt + 1) * TT],
                                   start=(ci == 0), stop=(ci == NCI - 1))
                    for ci in range(NCI):
                        ins = ten.matmul(p2[:, 0:TT], WB2[kb][:, ci, :],
                                         LNB[:, ci, tt * TT:(tt + 1) * TT],
                                         start=(ci == 0), stop=(ci == NCI - 1))
                    ev[('f1mm', s, oc)] = ev[('f1mmt', s, oc, tt)] = sPE.inc(ins)
                on('tensor', f_mm)

                def f_silu(scalar, s=s, oc=oc, tt=tt):
                    sPE.wait(scalar, ev[('f1mmt', s, oc, tt)])
                    ev[('f1silu', s, oc, tt)] = mkA(scalar)(
                        'activation', SILT[:, 0:TT], p1[:, 0:TT], AF.Silu,
                        scale=vb['s1v'][:, oc:oc + 1])
                on('scalar', f_silu)

                def f_stt(vector, s=s, oc=oc, tt=tt):
                    sACT.wait(vector, ev[('f1silu', s, oc, tt)])
                    if oc == 0 and tt == 0 and s >= 1:
                        sPE.wait(vector, ev[('f4mm', s - 1, NTT - 1, 7)])
                    ev[('f1stt', s, oc, tt)] = mkV(vector)(
                        'scalar_tensor_tensor', GX[:, oc, tt * TT:(tt + 1) * TT],
                        p2[:, 0:TT], vb['s2v'][:, oc:oc + 1], SILT[:, 0:TT],
                        ALU.mult, ALU.mult)
                on('vector', f_stt)
        ev[('f1done', s)] = ev[('f1mm', s, NH - 1)]

        for stage, (SRC, DST, wkey, skey) in enumerate((
                (GX, GY, 'h1', 'h1s'), (GY, GX, 'h2', 'h2s'))):
            for oc in range(NH):
                kb = oc % 2

                def f_whl(g, s=s, oc=oc, kb=kb, stage=stage, wkey=wkey):
                    if oc >= 2:
                        sPE.wait(g, ev[('fhmm', s, stage, oc - 2)])
                    elif stage >= 1:
                        sPE.wait(g, ev[('fhmm', s, 0, NH - 2 + oc)])
                    elif s >= 1:
                        sPE.wait(g, ev[('f4mm', s - 1, NTT - 1, 6 + oc)])
                    if s == 0:
                        ev[('fhw', s, stage, oc)] = sWL[kb].dma(
                            g, out=WH[kb][:], in_=io[wkey][oc, :, :, :])
                        ev[('fhwb', stage, oc)] = sWL[kb].dma(
                            g, out=io[wkey + 'b'][oc, :, :, :], in_=WH[kb][:])
                    else:
                        ev[('fhw', s, stage, oc)] = sWL[kb].dma(
                            g, out=WH[kb][:], in_=io[wkey + 'b'][oc, :, :, :])
                on('gpsimd', f_whl)

                for tt in range(NTT):
                    def f_hmm(ten, s=s, oc=oc, tt=tt, kb=kb, stage=stage, SRC=SRC):
                        if tt == 1:
                            sACT.wait(ten, ev[('fg', s, stage, oc, 0)])
                        if tt == 0:
                            sWL[kb].wait(ten, ev[('fhw', s, stage, oc)])
                            if oc == 0:
                                if stage == 0:
                                    sDVE.wait(ten, ev[('f1stt', s, NH - 1, NTT - 1)])
                                else:
                                    sACT.wait(ten, ev[('fg', s, 0, NH - 1, NTT - 1)])
                            if oc >= 1:
                                sACT.wait(ten, ev[('fg', s, stage, oc - 1, NTT - 1)])
                            elif stage == 1:
                                sACT.wait(ten, ev[('fg', s, 0, NH - 1, NTT - 1)])
                        for hc in range(NH):
                            ins = ten.matmul(p1[:, 0:TT], WH[kb][:, hc, :],
                                             SRC[:, hc, tt * TT:(tt + 1) * TT],
                                             start=(hc == 0), stop=(hc == NH - 1))
                        ev[('fhmm', s, stage, oc)] = ev[('fhmmt', s, stage, oc, tt)] = sPE.inc(ins)
                    on('tensor', f_hmm)

                    def f_gelu(scalar, s=s, oc=oc, tt=tt, stage=stage, DST=DST, skey=skey):
                        sPE.wait(scalar, ev[('fhmmt', s, stage, oc, tt)])
                        if oc == 0 and tt == 0:
                            if stage == 0 and s >= 1:
                                sPE.wait(scalar, ev[('fhmm', s - 1, 1, NH - 1)])
                            elif stage == 1:
                                sPE.wait(scalar, ev[('fhmm', s, 0, NH - 1)])
                        ev[('fg', s, stage, oc, tt)] = mkA(scalar)(
                            'activation', DST[:, oc, tt * TT:(tt + 1) * TT],
                            p1[:, 0:TT], AF.Gelu, scale=vb[skey][:, oc:oc + 1])
                    on('scalar', f_gelu)
        ev[('f2done', s)] = ev[('fhmm', s, 1, NH - 1)]

        for tt in range(NTT):
            t0 = S0 + tt * TT

            def f_x4(sync, s=s, tt=tt, t0=t0):
                sDVE.wait(sync, ev[('ln2', s * NTT + NTT - 1)])
                sACT.wait(sync, ev[('sq', s * NTT + NTT - 1)])
                if s + tt >= 1:
                    sDVE.wait(sync, ev[('f4ep', s, tt - 1) if tt >= 1 else ('f4ep', s - 1, NTT - 1)])
                ev[('f4x', s, tt)] = sLDp[1].dma(
                    sync, out=XT[:], in_=io['r3'][:, :, t0:t0 + TT])
            on('sync', f_x4)

            for oc in range(NCI):
                kb = oc % 2

                def f_w3l(g, s=s, tt=tt, oc=oc, kb=kb):
                    if oc >= 2:
                        sPE.wait(g, ev[('f4mm', s, tt, oc - 2)])
                    elif tt >= 1:
                        sPE.wait(g, ev[('f4mm', s, tt - 1, 6 + oc)])
                    elif s >= 1:
                        sPE.wait(g, ev[('f4mm', s - 1, NTT - 1, 6 + oc)])
                    if s == 0 and tt == 0:
                        sWL[kb].dma(g, out=W3B[kb][:], in_=io['w3'][oc, :, :, :])
                        ev[('f4w', s, tt, oc)] = sWL[kb].dma(
                            g, out=io['w3b'][oc, :, :, :], in_=W3B[kb][:])
                    else:
                        ev[('f4w', s, tt, oc)] = sWL[kb].dma(
                            g, out=W3B[kb][:], in_=io['w3b'][oc, :, :, :])
                on('gpsimd', f_w3l)

                def f_mm3(ten, s=s, tt=tt, oc=oc, kb=kb):
                    sWL[kb].wait(ten, ev[('f4w', s, tt, oc)])
                    if oc == 0 and tt == 0:
                        sACT.wait(ten, ev[('fg', s, 1, NH - 1, NTT - 1)])
                    if oc >= 1:
                        sACT.wait(ten, ev[('f4ox', s, tt, oc - 1)])
                    elif tt >= 1:
                        sACT.wait(ten, ev[('f4ox', s, tt - 1, 7)])
                    elif s >= 1:
                        sACT.wait(ten, ev[('f4ox', s - 1, NTT - 1, 7)])
                    for hc in range(NH):
                        ins = ten.matmul(p1[:, 0:TT], W3B[kb][:, hc, :],
                                         GX[:, hc, tt * TT:(tt + 1) * TT],
                                         start=(hc == 0), stop=(hc == NH - 1))
                    ev[('f4mm', s, tt, oc)] = sPE.inc(ins)
                on('tensor', f_mm3)

                def f_ox(scalar, s=s, tt=tt, oc=oc):
                    sPE.wait(scalar, ev[('f4mm', s, tt, oc)])
                    if oc >= 1:
                        sDVE.wait(scalar, ev[('f4add', s, tt, oc - 1)])
                    elif tt >= 1:
                        sDVE.wait(scalar, ev[('f4add', s, tt - 1, 7)])
                    elif s >= 1:
                        sDVE.wait(scalar, ev[('f4add', s - 1, NTT - 1, 7)])
                    ev[('f4ox', s, tt, oc)] = mkA(scalar)(
                        'activation', OX[:, 0:TT], p1[:, 0:TT], AF.Identity,
                        scale=vb['s3v'][:, oc:oc + 1])
                on('scalar', f_ox)

                def f_add(vector, s=s, tt=tt, oc=oc):
                    sACT.wait(vector, ev[('f4ox', s, tt, oc)])
                    sLDp[1].wait(vector, ev[('f4x', s, tt)])
                    ob = oc % 2
                    prev = None
                    if oc >= 2:
                        prev = (s, tt, oc - 2)
                    elif tt >= 1:
                        prev = (s, tt - 1, 6 + oc)
                    elif s >= 1:
                        prev = (s - 1, NTT - 1, 6 + oc)
                    if prev is not None:
                        sSTp[ob].wait(vector, ev[('fyst',) + prev])
                    ev[('f4add', s, tt, oc)] = ev[('f4ep', s, tt)] = mkV(vector)(
                        'tensor_add', OTS[ob][:, 0:TT], OX[:, 0:TT], XT[:, oc, 0:TT])
                on('vector', f_add)

                def f_yst(sync, s=s, tt=tt, oc=oc, t0=t0):
                    ob = oc % 2
                    sDVE.wait(sync, ev[('f4add', s, tt, oc)])
                    ev[('fyst', s, tt, oc)] = sSTp[ob].dma(
                        sync, out=io['y'][:, oc, t0:t0 + TT], in_=OTS[ob][:, 0:TT])
                on('sync', f_yst)

    def f_fin(sync):
        sSTp[0].wait(sync, ev[('fyst', NS - 1, NTT - 1, 6)])
        sSTp[1].wait(sync, ev[('fyst', NS - 1, NTT - 1, 7)])
        if DBG:
            sLDp[0].dma(sync, out=io['dbg_lnb'][:, :, 0:TS], in_=LNB[:])
            sLDp[0].dma(sync, out=io['dbg_gx'][:, :, 0:TS], in_=GX[:])
            sLDp[0].dma(sync, out=io['dbg_gy'][:, :, 0:TS], in_=GY[:])
            sLDp[0].wait(sync, sLDp[0].n)
    on('sync', f_fin)



ALL_PHASES = ('l0a', 'l0f', 'l1aA', 'l1aBCD', 'l1f')


def build_program(phases=ALL_PHASES, io_external=(), io_extin=(), cc_groups=None):
    if cc_groups is None:
        cc_groups = [[0, 1, 2, 3], [4, 5, 6, 7]]
    nc = bass.Bass("TRN2", target_bir_lowering=False)
    es = contextlib.ExitStack()
    ctx = Ctx(nc, es)
    with es:
        def dram(name, shape, dtype=F32, ext_in=False):
            if ext_in:
                kind = "ExternalInput"
            elif name in io_external or (DBG and name.startswith("dbg_")):
                kind = "ExternalOutput"
            else:
                kind = "Internal"
            return nc.dram_tensor(name, shape, dtype, kind=kind).ap()

        io = {}
        io['x'] = dram("x", [128, NCI, NTOK], ext_in=True)
        io['r1'] = dram("r1", [128, NCI, NTOK])
        io['r2'] = dram("r2", [128, NCI, NTOK], ext_in=('r2' in io_extin))
        io['qf'] = dram("qf", [128, NCI, TCORE], BF16)
        io['kf'] = dram("kf", [128, NCI, TCORE], BF16)
        io['kt'] = dram("kt", [TCORE // 128, 128, H], BF16)
        io['vt'] = dram("vt", [TCORE // 128, 128, H], BF16)
        io['qsi'] = dram("qsi", [128, NCI, TCORE], BF16)
        io['r3'] = dram("r3", [128, NCI, TCORE], ext_in=('r3' in io_extin))
        io['y'] = dram("y", [128, NCI, TCORE], ext_in=False)
        io['cc1i'] = dram("cc1i", [128, 2 * NCI])
        io['cc1o'] = dram("cc1o", [128, 2 * NCI])
        io['cc2i'] = dram("cc2i", [128, 2 * NCI])
        io['cc2o'] = dram("cc2o", [128, 2 * NCI])
        io['cc3i'] = dram("cc3i", [128, NCI * 128])
        io['cc3o'] = dram("cc3o", [128, NCI * 128])
        io['cc4i'] = dram("cc4i", [16, 1])
        io['cc4o'] = dram("cc4o", [16, 1])
        if DBG:
            io['dbg_sa'] = dram("dbg_sa", [16, TCORE], ext_in=False)
            io['dbg_kvg'] = dram("dbg_kvg", [128, NCI, 128])
            io['dbg_kvsb'] = dram("dbg_kvsb", [128, NCI, 128], BF16)
            io['dbg_zrow'] = dram("dbg_zrow", [16, 2])
            io['dbg_zrep'] = dram("dbg_zrep", [128, NCI])
            io['dbg_t12'] = dram("dbg_t12", [128, 2, NCI])
            io['dbg_at1'] = dram("dbg_at1", [128, NCI, 16], BF16)
            io['dbg_at2'] = dram("dbg_at2", [128, NCI, 16], BF16)
            io['dbg_ec'] = dram("dbg_ec", [16, 4])
            io['dbg_vscb'] = dram("dbg_vscb", [128, 2, H], BF16)
            io['dbg_esb'] = dram("dbg_esb", [16, 2, 256], BF16)
            io['dbg_ktt'] = dram("dbg_ktt", [128, 2, H], BF16)
            io['dbg_etm'] = dram("dbg_etm", [128, H])
            io['dbg_lnb'] = dram("dbg_lnb", [128, NCI, 1024], BF16)
            io['dbg_gx'] = dram("dbg_gx", [128, 32, 1024], BF16)
            io['dbg_gy'] = dram("dbg_gy", [128, 32, 1024], BF16)
            io['dbg_flnt'] = dram("dbg_flnt", [128, NCI, 512])
            io['dbg_frows'] = dram("dbg_frows", [1, 1, 5, 512])

        wnames = []
        if 'l0a' in phases:
            wnames += ['l0a_wv', 'l0a_wr', 'l0a_wo']
        if 'l0f' in phases:
            wnames += ['l0f_wk', 'l0f_wr', 'l0f_wv']
        if 'l1aA' in phases:
            wnames += ['l1a_wq', 'l1a_wk', 'l1a_wv']
        if 'l1aBCD' in phases:
            wnames += ['l1a_wo']
        dw = {n: dram(n, [128, NCI, H], ext_in=True) for n in wnames}
        if 'l1f' in phases:
            io['w1'] = dram("w1", [32, 128, NCI, 128], ext_in=True)
            io['w2'] = dram("w2", [32, 128, NCI, 128], ext_in=True)
            io['h1'] = dram("h1", [32, 128, 32, 128], ext_in=True)
            io['h2'] = dram("h2", [32, 128, 32, 128], ext_in=True)
            io['w3'] = dram("w3", [NCI, 128, 32, 128], ext_in=True)
            io['s1v'] = dram("s1v", [128, 32], ext_in=True)
            io['s2v'] = dram("s2v", [128, 32], ext_in=True)
            io['h1sv'] = dram("h1sv", [128, 32], ext_in=True)
            io['h2sv'] = dram("h2sv", [128, 32], ext_in=True)
            io['s3v'] = dram("s3v", [128, NCI], ext_in=True)
            io['w1b'] = dram("w1b", [32, 128, NCI, 128], BF16)
            io['w2b'] = dram("w2b", [32, 128, NCI, 128], BF16)
            io['h1b'] = dram("h1b", [32, 128, 32, 128], BF16)
            io['h2b'] = dram("h2b", [32, 128, 32, 128], BF16)
            io['w3b'] = dram("w3b", [NCI, 128, 32, 128], BF16)
        d_ident = dram("ident", [128, 128], ext_in=True)
        d_selb = dram("selb", [NCI, 16, 128], ext_in=True)
        d_selrow = dram("selrow", [16, H], ext_in=True)

        vnames = ['td', 'tmv', 'tmr', 'mur', 'muk', 'bq', 'bk', 'bv', 'bo']
        dv = {n: dram("vec_" + n, [128, NCI], ext_in=True) for n in vnames}

        vb = {n: ctx.sb("v_" + n, [128, NCI], F32) for n in vnames}
        vb['d'] = ctx.sb("v_d", [128, NCI], F32)
        vb['dneg'] = ctx.sb("v_dneg", [128, NCI], F32)
        vb['consts'] = ctx.sb("v_consts", [128, 4], F32)
        vb['ones_inv'] = ctx.sb("v_onesinv", [128, 1], BF16)
        vb['ones1'] = ctx.sb("v_ones1", [1, 128], F32)
        vb['nones1'] = ctx.sb("v_nones1", [1, 128], F32)
        vb['idf'] = ctx.sb("v_idf", [128, 128], F32)
        vb['idb'] = ctx.sb("v_idb", [128, 128], BF16)
        vb['selbf'] = ctx.sb("v_selbf", [16, NCI, 128], F32)
        vb['selb'] = ctx.sb("v_selb", [16, NCI, 128], BF16)
        vb['selrowf'] = ctx.sb("v_selrowf", [16, H], F32)
        vb['selrow'] = ctx.sb("v_selrow", [16, H], BF16)
        vb['onesb'] = ctx.sb("v_onesb", [128, 1], BF16)
        if 'l1f' in phases:
            vb['s1v'] = ctx.sb("v_s1v", [128, 32], F32)
            vb['s2v'] = ctx.sb("v_s2v", [128, 32], F32)
            vb['h1s'] = ctx.sb("v_h1s", [128, 32], F32)
            vb['h2s'] = ctx.sb("v_h2s", [128, 32], F32)
            vb['s3v'] = ctx.sb("v_s3v", [128, NCI], F32)

        if 'l1aA' in phases:
            nt_a = len(tiles_of(TCORE, 256))
            QSP = ctx.sb("a_qsp", [128, NCI, nt_a], F32)
            KSP = ctx.sb("a_ksp", [128, NCI, nt_a], F32)

        wscope = contextlib.ExitStack()
        wscope_closed = False
        ctx.scope = wscope
        wslot = [ctx.sb(f"wslot{k}", [128, NCI, H], BF16) for k in range(4)]
        ctx.scope = es

        semnames = ('pe', 'act', 'dve', 'gps', 'wld',
                    'ld0', 'ld1', 'st0', 'st1', 'cc',
                    'lda0', 'lda1', 'ldb0', 'ldb1', 'ldc0', 'ldc1',
                    'sta0', 'sta1', 'stb0', 'stb1', 'stc0', 'stc1',
                    'wl0', 'wl1')
        sems = {k: ctx.sem("s_" + k) for k in semnames}
        ph = Phases(ctx, sems)

        with nc.Block() as blk0:
            def f_vload(s):
                for n in vnames:
                    sems['ld0'].dma(s, out=vb[n][:], in_=dv[n])
                sems['ld0'].dma(s, out=vb['idf'][:], in_=d_ident)
                sems['ld0'].dma(s, 
                    out=vb['selbf'][:], in_=d_selb.rearrange("c h j -> h c j"))
                sems['ld0'].dma(s, out=vb['selrowf'][:], in_=d_selrow)
                if 'l1f' in phases:
                    for a, bnm in (('s1v', 's1v'), ('s2v', 's2v'), ('h1s', 'h1sv'),
                                   ('h2s', 'h2sv'), ('s3v', 's3v')):
                        sems['ld0'].dma(s, out=vb[a][:], in_=io[bnm])
            blk0.sync(f_vload)

            def f_consts(v):
                sems['ld0'].wait(v, sems['ld0'].n)
                sDVE = sems['dve']
                for ins in (
                    v.memset(vb['consts'][:, 0:1], EPS_LN),
                    v.memset(vb['consts'][:, 1:2], EPS_A),
                    v.memset(vb['ones_inv'][:], 1.0 / H),
                    v.memset(vb['ones1'][:], 1.0),
                    v.memset(vb['nones1'][:], -1.0),
                    v.tensor_copy(vb['idb'][:], vb['idf'][:]),
                    v.tensor_copy(vb['selb'][:], vb['selbf'][:]),
                    v.tensor_copy(vb['selrow'][:], vb['selrowf'][:]),
                    v.memset(vb['onesb'][:], 1.0),
                ):
                    sDVE.inc(ins)
            blk0.vector(f_consts)

            def f_d(sc):
                sems['dve'].wait(sc, sems['dve'].n)
                sems['act'].inc(sc.activation(vb['dneg'][:], vb['td'][:], AF.Exp))
                sems['act'].wait(sc, sems['act'].n)
                sems['act'].inc(sc.activation(vb['d'][:], vb['dneg'][:], AF.Exp, scale=-1.0))
            blk0.scalar(f_d)
        ph.snapshot()

        if 'l0a' in phases:
            with contextlib.ExitStack() as ps:
                ctx.scope = ps
                with nc.Block() as blk:
                    ph.emit_guard(blk)
                    load_phase_weights(blk, sems, [
                        (wslot[0][:], dw['l0a_wv']),
                        (wslot[1][:], dw['l0a_wr']),
                        (wslot[2][:], dw['l0a_wo'])])
                    wv = sems['wld'].n

                    def f_w(t):
                        sems['wld'].wait(t, wv)
                    blk.tensor(f_w)
                    emit_phase_l0(ctx, blk, io['x'], io['r1'],
                                  wslot[0], wslot[1], wslot[2], vb, sems, 'attn')
                ctx.scope = es
            ph.snapshot()

        if 'l0f' in phases:
            with contextlib.ExitStack() as ps:
                ctx.scope = ps
                with nc.Block() as blk:
                    ph.emit_guard(blk)
                    load_phase_weights(blk, sems, [
                        (wslot[0][:], dw['l0f_wk']),
                        (wslot[1][:], dw['l0f_wr']),
                        (wslot[2][:], dw['l0f_wv'])])
                    wv = sems['wld'].n

                    def f_w(t):
                        sems['wld'].wait(t, wv)
                    blk.tensor(f_w)
                    emit_phase_l0(ctx, blk, io['r1'], io['r2'],
                                  wslot[0], wslot[1], wslot[2], vb, sems, 'ffn')
                ctx.scope = es
            ph.snapshot()

        if 'l1aA' in phases:
            with contextlib.ExitStack() as ps:
                ctx.scope = ps
                with nc.Block() as blk:
                    ph.emit_guard(blk)
                    wpairs = [
                        (wslot[0][:], dw['l1a_wq']),
                        (wslot[1][:], dw['l1a_wk']),
                        (wslot[2][:], dw['l1a_wv'])]
                    load_phase_weights(blk, sems, wpairs)
                    wv = sems['wld'].n

                    def f_w(t):
                        sems['wld'].wait(t, wv)
                    blk.tensor(f_w)
                    emit_phase_l1a_A(ctx, blk, io, wslot[0], wslot[1],
                                     wslot[2], vb, sems, QSP, KSP)
                ctx.scope = es
            ph.snapshot()

        if 'l1aBCD' in phases:
            wscope.close()
            wscope_closed = True
            io['dw_l1a_wo'] = dw['l1a_wo']
            with contextlib.ExitStack() as ps:
                ctx.scope = ps
                with nc.Block() as blk:
                    ph.emit_guard(blk)
                    emit_l1a_BCD(ctx, blk, io, vb, sems,
                                 QSP, KSP, nt_a, cc_groups)
                ctx.scope = es
            ph.snapshot()

        if 'l1f' in phases:
            if not wscope_closed:
                wscope.close()
                wscope_closed = True
            with contextlib.ExitStack() as ps:
                ctx.scope = ps
                with nc.Block() as blk:
                    ph.emit_guard(blk)
                    emit_phase_l1f(ctx, blk, io, vb, sems)
                ctx.scope = es
            ph.snapshot()

        return nc



def _fm(x2d):
    T = x2d.shape[0]
    return np.ascontiguousarray(x2d.T.reshape(NCI, 128, T).transpose(1, 0, 2))


def _unfm(a):
    return a.transpose(1, 0, 2).reshape(H, -1).T


def _wT(W):
    return np.ascontiguousarray(W.T.reshape(NCI, 128, W.shape[0]).transpose(1, 0, 2))


def _vec(v):
    return np.ascontiguousarray(np.asarray(v, np.float32).reshape(NCI, 128).T)


def make_core_inputs(full, phases=ALL_PHASES):
    x = np.asarray(full['L0_attn_x'])
    per = {}
    if 'l0a' in phases:
        per['l0a_wv'] = _wT(np.asarray(full['L0_attn_Wv']))
        per['l0a_wr'] = _wT(np.asarray(full['L0_attn_Wr']))
        per['l0a_wo'] = _wT(np.asarray(full['L0_attn_Wo']))
    if 'l0f' in phases:
        per['l0f_wk'] = _wT(np.asarray(full['L0_ffn_W_k']))
        per['l0f_wr'] = _wT(np.asarray(full['L0_ffn_W_r']))
        per['l0f_wv'] = _wT(np.asarray(full['L0_ffn_W_v']))
    if 'l1aA' in phases:
        per['l1a_wq'] = _wT(np.asarray(full['L1_attn_Wq']))
        per['l1a_wk'] = _wT(np.asarray(full['L1_attn_Wk']))
        per['l1a_wv'] = _wT(np.asarray(full['L1_attn_Wv']))
    if 'l1aBCD' in phases:
        per['l1a_wo'] = _wT(np.asarray(full['L1_attn_Wo']))
    per['vec_td'] = _vec(full['L0_attn_time_decay'])
    per['vec_tmv'] = _vec(full['L0_attn_time_mix_v'])
    per['vec_tmr'] = _vec(full['L0_attn_time_mix_r'])
    per['vec_mur'] = _vec(full['L0_ffn_mu_r'])
    per['vec_muk'] = _vec(full['L0_ffn_mu_k'])
    per['vec_bq'] = _vec(full['L1_attn_bq'])
    per['vec_bk'] = _vec(full['L1_attn_bk'])
    per['vec_bv'] = _vec(full['L1_attn_bv'])
    per['vec_bo'] = _vec(full['L1_attn_bo'])
    per['ident'] = np.eye(128, dtype=np.float32)
    selb = np.zeros((NCI, 16, 128), np.float32)
    for ci in range(NCI):
        selb[ci, 2 * ci, 0:64] = 1.0
        selb[ci, 2 * ci + 1, 64:128] = 1.0
    per['selb'] = selb
    selrow = np.zeros((16, H), np.float32)
    for hh in range(16):
        selrow[hh, hh * 64:(hh + 1) * 64] = 1.0
    per['selrow'] = selrow
    if 'l1f' in phases:
        def blk4(W, no):
            W = np.asarray(W, np.float32)
            return np.ascontiguousarray(
                W.reshape(no, 128, W.shape[1] // 128, 128).transpose(0, 3, 2, 1))
        per['w1'] = blk4(full['L1_ffn_w1_w'], 32)
        per['w2'] = blk4(full['L1_ffn_w2_w'], 32)
        per['h1'] = blk4(full['L1_ffn_h1_w'], 32)
        per['h2'] = blk4(full['L1_ffn_h2_w'], 32)
        per['w3'] = blk4(full['L1_ffn_w3_w'], NCI)
        per['s1v'] = np.ascontiguousarray(np.asarray(full['L1_ffn_w1_s'], np.float32).reshape(32, 128).T)
        per['s2v'] = np.ascontiguousarray(np.asarray(full['L1_ffn_w2_s'], np.float32).reshape(32, 128).T)
        per['h1sv'] = np.ascontiguousarray(np.asarray(full['L1_ffn_h1_s'], np.float32).reshape(32, 128).T)
        per['h2sv'] = np.ascontiguousarray(np.asarray(full['L1_ffn_h2_s'], np.float32).reshape(32, 128).T)
        per['s3v'] = np.ascontiguousarray(np.asarray(full['L1_ffn_w3_s'], np.float32).reshape(NCI, 128).T)
    maps = []
    for c in range(NCORES):
        bidx, s = divmod(c, 4)
        s0 = s * TCORE
        if s0 == 0:
            halo = np.zeros((HALO, H), np.float32)
        else:
            halo = x[bidx, s0 - HALO:s0]
        xs = np.concatenate([halo, x[bidx, s0:s0 + TCORE]], axis=0)
        m = {'x': _fm(xs)}
        m.update(per)
        maps.append(m)
    return maps


def kernel(**inputs):
    maps = make_core_inputs(inputs, phases=ALL_PHASES)
    nc = build_program(phases=ALL_PHASES, io_external=('y',))
    res = run_bass_kernel_spmd(nc, maps, core_ids=list(range(NCORES)))
    out = np.empty((2, 4 * TCORE, H), np.float32)
    for c in range(NCORES):
        y = np.asarray(res.results[c]['y']).reshape(128, NCI, TCORE)
        bidx, s = divmod(c, 4)
        out[bidx, s * TCORE:(s + 1) * TCORE] = _unfm(y.astype(np.float32))
    return out


# version 06
# speedup vs baseline: 1.3031x; 1.0963x over previous
import contextlib
import numpy as np
import concourse.bass as bass
import concourse.mybir as mybir
from concourse.bass_utils import run_bass_kernel_spmd

F32 = mybir.dt.float32
BF16 = mybir.dt.bfloat16
AF = mybir.ActivationFunctionType
ALU = mybir.AluOpType

H = 1024
NCI = 8
HALO = 128
TCORE = 4096
NTOK = TCORE + HALO
NCORES = 8
EPS_LN = 1e-5
EPS_A = 1e-6
SSCALE = 16384.0
DBG = False


def tiles_of(total, step, base=0):
    out = []
    t = 0
    while t < total:
        out.append((base + t, min(step, total - t)))
        t += step
    return out


class Sem:
    def __init__(self, ctx, name):
        self.h = ctx.es.enter_context(ctx.nc.semaphore(name))
        self.n = 0

    def inc(self, instr, k=1):
        instr.then_inc(self.h, k)
        self.n += k
        return self.n

    def wait(self, eng, val):
        if val > 0:
            eng.wait_ge(self.h, val)

    def dma(self, eng, *args, **kw):
        self.wait(eng, self.n)
        return self.inc(eng.dma_start(*args, **kw), 16)


class Ctx:
    def __init__(self, nc, es):
        self.nc = nc
        self.es = es
        self.scope = es

    def sem(self, name):
        return Sem(self, name)

    def sb(self, name, shape, dtype):
        return self.scope.enter_context(self.nc.sbuf_tensor(name, shape, dtype))

    def ps(self, name, shape, dtype=F32):
        return self.scope.enter_context(self.nc.psum_tensor(name, shape, dtype))


ENGS = ('sync', 'gpsimd', 'scalar', 'vector', 'tensor')


class Phases:

    def __init__(self, ctx, sems):
        self.ctx = ctx
        self.sems = sems
        self.guard = {}

    def snapshot(self):
        self.guard = {k: s.n for k, s in self.sems.items()}

    def emit_guard(self, block):
        g = dict(self.guard)

        def f(eng):
            for k, v in g.items():
                self.sems[k].wait(eng, v)
        for e in ENGS:
            getattr(block, e)(f)


def load_phase_weights(block, sems, pairs):
    sW = sems['wld']

    def f(g):
        for dst, src in pairs:
            sW.dma(g, out=dst, in_=src)
    block.gpsimd(f)
    return sW.n



def emit_phase_l0(ctx, block, d_src, d_dst, W1, W2, W3, vb, sems, kind):
    TP = 256
    tls = tiles_of(NTOK, TP)
    XT = [ctx.sb(f"{kind}_xt{b}", [128, NCI, TP], F32) for b in range(2)]
    SQB = [ctx.sb(f"{kind}_sqb{b}", [128, NCI, TP], BF16) for b in range(2)]
    XB = [ctx.sb(f"{kind}_xb{b}", [128, NCI, TP], BF16) for b in range(2)]
    LNT = ctx.sb(f"{kind}_lnt", [128, NCI, TP], F32)
    LNF = ctx.sb(f"{kind}_lnf", [128, NCI, TP + 1], F32)
    PREV = ctx.sb(f"{kind}_prev", [128, NCI, TP + 1], F32)
    UT = ctx.sb(f"{kind}_ut", [128, NCI, TP], F32)
    VIN = [ctx.sb(f"{kind}_vin{b}", [128, NCI, TP], BF16) for b in range(2)]
    RIN = [ctx.sb(f"{kind}_rin{b}", [128, NCI, TP], BF16) for b in range(2)]
    VB = [ctx.sb(f"{kind}_vb{b}", [128, NCI, TP], BF16) for b in range(2)]
    KAF = ctx.sb(f"{kind}_kaf", [128, TP], F32) if kind == 'ffn' else None
    RS = ctx.sb(f"{kind}_rs", [128, NCI, TP], F32)
    OTX = ctx.sb(f"{kind}_otx", [128, TP], F32)
    OT = [ctx.sb(f"{kind}_ot{b}", [128, NCI, TP], F32) for b in range(2)]
    CARRY = ctx.sb(f"{kind}_carry", [128, NCI], F32)
    ROWS = ctx.sb(f"{kind}_rows", [1, 2, 5, TP], F32)
    p_sum = ctx.ps(f"{kind}_psum", [1, TP])
    p_sq = ctx.ps(f"{kind}_psq", [1, TP])
    p_a = ctx.ps(f"{kind}_pa", [128, TP])
    p_b = ctx.ps(f"{kind}_pb", [128, TP])
    p_mm = [ctx.ps(f"{kind}_pmm{b}", [128, TP]) for b in range(2)]

    sPE, sACT, sDVE = sems['pe'], sems['act'], sems['dve']
    sLDp = [sems['ld0'], sems['ld1']]
    sSTp = [sems['st0'], sems['st1']]
    ev = {}

    def on(name, f):
        getattr(block, name)(f)

    def mkV(vector):
        def V(fn, *args, **kw):
            sDVE.wait(vector, sDVE.n)
            return sDVE.inc(getattr(vector, fn)(*args, **kw))
        return V

    def mkA(scalar):
        def A(fn, *args, **kw):
            sACT.wait(scalar, sACT.n)
            return sACT.inc(getattr(scalar, fn)(*args, **kw))
        return A

    eps_ln = vb['consts'][0:1, 0:1]
    if kind == 'attn':
        m1, m2c, dcol = vb['tmv'], vb['tmr'], vb['d']
    else:
        m1, m2c = vb['muk'], vb['mur']

    def f_init(vector):
        V = mkV(vector)
        V('memset', CARRY[:], 0.0)
        V('memset', PREV[:, :, 0:1], 0.0)
        ev['init'] = V('memset', LNF[:, :, 0:1], 0.0)
    on('vector', f_init)

    for i, (t0, T) in enumerate(tls):
        b = i % 2
        r = i % 2

        def f_load(sync, i=i, b=b, t0=t0, T=T):
            if i >= 2:
                sDVE.wait(sync, ev[('ep', i - 2)])
                sACT.wait(sync, ev[('sq', i - 2)])
            ev[('xld', i)] = sLDp[b].inc(
                sync.dma_start(out=XT[b][:, :, 0:T], in_=d_src[:, :, t0:t0 + T]), 16)
        on('sync', f_load)

        def f_sq(scalar, i=i, b=b, T=T):
            sLDp[b].wait(scalar, ev[('xld', i)])
            if i >= 2:
                sPE.wait(scalar, ev[('stats', i - 2)])
            ev[('sq', i)] = mkA(scalar)(
                'activation', SQB[b][:, :, 0:T], XT[b][:, :, 0:T], AF.Square)
        on('scalar', f_sq)

        def f_cast(vector, i=i, b=b, T=T):
            sLDp[b].wait(vector, ev[('xld', i)])
            if i >= 2:
                sPE.wait(vector, ev[('stats', i - 2)])
            ev[('xb', i)] = mkV(vector)(
                'tensor_copy', XB[b][:, :, 0:T], XT[b][:, :, 0:T])
        on('vector', f_cast)

        def f_stats(ten, i=i, b=b, T=T):
            sACT.wait(ten, ev[('sq', i)])
            sDVE.wait(ten, ev[('xb', i)])
            if i >= 1:
                sACT.wait(ten, ev[('m2', i - 1)])
                sDVE.wait(ten, ev[('rows', i - 1)])
            for ci in range(NCI):
                ten.matmul(p_sq[0:1, 0:T], vb['ones_inv'][:], SQB[b][:, ci, 0:T],
                           start=(ci == 0), stop=(ci == NCI - 1))
            for ci in range(NCI):
                ins = ten.matmul(p_sum[0:1, 0:T], vb['ones_inv'][:], XB[b][:, ci, 0:T],
                                 start=(ci == 0), stop=(ci == NCI - 1))
            ev[('stats', i)] = sPE.inc(ins)
        on('tensor', f_stats)

        m2 = ROWS[0:1, r, 0, 0:T]
        var = ROWS[0:1, r, 1, 0:T]
        sqv = ROWS[0:1, r, 2, 0:T]
        a_ = ROWS[0:1, r, 3, 0:T]
        mr = ROWS[0:1, r, 4, 0:T]

        def f_m2(scalar, i=i, T=T, m2=m2):
            sPE.wait(scalar, ev[('stats', i)])
            if i >= 1:
                sDVE.wait(scalar, ev[('rows', i - 1)])
            ev[('m2', i)] = mkA(scalar)('activation', m2, p_sum[0:1, 0:T], AF.Square)
        on('scalar', f_m2)

        def f_var(vector, i=i, T=T, m2=m2, var=var):
            sACT.wait(vector, ev[('m2', i)])
            ev[('var', i)] = mkV(vector)('tensor_sub', var, p_sq[0:1, 0:T], m2)
        on('vector', f_var)

        def f_sqv(scalar, i=i, T=T, var=var, sqv=sqv):
            sDVE.wait(scalar, ev[('var', i)])
            ev[('sqv', i)] = mkA(scalar)('activation', sqv, var, AF.Sqrt, bias=eps_ln)
        on('scalar', f_sqv)

        def f_rows(vector, i=i, T=T, sqv=sqv, a_=a_, mr=mr):
            sACT.wait(vector, ev[('sqv', i)])
            if i >= 1:
                sPE.wait(vector, ev[('bcast', i - 1)])
            V = mkV(vector)
            V('reciprocal', a_, sqv)
            ev[('rows', i)] = V('tensor_mul', mr, p_sum[0:1, 0:T], a_)
        on('vector', f_rows)

        def f_bcast(ten, i=i, T=T, a_=a_, mr=mr):
            sDVE.wait(ten, ev[('rows', i)])
            if i >= 1:
                sDVE.wait(ten, ev[('ln2', i - 1)])
            ten.matmul(p_a[:, 0:T], vb['ones1'][:], a_, start=True, stop=True)
            ins = ten.matmul(p_b[:, 0:T], vb['nones1'][:], mr, start=True, stop=True)
            ev[('bcast', i)] = sPE.inc(ins)
        on('tensor', f_bcast)

        def f_ln(vector, i=i, b=b, T=T):
            sPE.wait(vector, ev[('bcast', i)])
            V = mkV(vector)
            for ci in range(NCI):
                V('tensor_mul', LNT[:, ci, 0:T], XT[b][:, ci, 0:T], p_a[:, 0:T])
            for ci in range(NCI):
                ev[('ln2', i)] = V('tensor_add', LNF[:, ci, 1:T + 1],
                                   LNT[:, ci, 0:T], p_b[:, 0:T])
        on('vector', f_ln)

        if kind == 'attn':
            def f_scan(vector, i=i, T=T):
                V = mkV(vector)
                for ci in range(NCI):
                    V('tensor_copy', PREV[:, ci, 0:1], CARRY[:, ci:ci + 1])
                    V('tensor_tensor_scan',
                      PREV[:, ci, 1:T + 1],
                      dcol[:, ci:ci + 1].broadcast_to([128, T]),
                      LNF[:, ci, 1:T + 1],
                      CARRY[:, ci:ci + 1], ALU.mult, ALU.add)
                    V('tensor_copy', CARRY[:, ci:ci + 1], PREV[:, ci, T:T + 1])
            on('vector', f_scan)
            def base_ap(ci, T):
                return PREV[:, ci, 0:T]
        else:
            def base_ap(ci, T):
                return LNF[:, ci, 0:T]

        def f_mix(vector, i=i, b=b, T=T):
            if i >= 2:
                sPE.wait(vector, ev[('projr', i - 2)])
            V = mkV(vector)
            for ci in range(NCI):
                V('tensor_sub', UT[:, ci, 0:T], LNF[:, ci, 1:T + 1], base_ap(ci, T))
            for ci in range(NCI):
                V('scalar_tensor_tensor',
                  VIN[b][:, ci, 0:T], UT[:, ci, 0:T], m1[:, ci:ci + 1],
                  base_ap(ci, T), ALU.mult, ALU.add)
            for ci in range(NCI):
                ev[('mix', i)] = V('scalar_tensor_tensor',
                                   RIN[b][:, ci, 0:T], UT[:, ci, 0:T], m2c[:, ci:ci + 1],
                                   base_ap(ci, T), ALU.mult, ALU.add)
            if kind == 'ffn':
                for ci in range(NCI):
                    ev[('mix', i)] = V('tensor_copy', LNF[:, ci, 0:1], LNF[:, ci, T:T + 1])
        on('vector', f_mix)

        def f_projv_pre(ten, i=i):
            sDVE.wait(ten, ev[('mix', i)])
        on('tensor', f_projv_pre)

        for oc in range(8):
            def f_projv(ten, i=i, b=b, T=T, oc=oc):
                if oc >= 2:
                    sACT.wait(ten, ev[('evacv', i, oc - 2)])
                elif i >= 1:
                    sDVE.wait(ten, ev[('epoc', i - 1, 6 + oc)])
                for ci in range(NCI):
                    ins = ten.matmul(p_mm[oc % 2][:, 0:T], W1[:, ci, oc * 128:(oc + 1) * 128],
                                     VIN[b][:, ci, 0:T], start=(ci == 0), stop=(ci == NCI - 1))
                ev[('projv', i, oc)] = sPE.inc(ins)
            on('tensor', f_projv)

            def f_evacv(scalar, i=i, b=b, T=T, oc=oc):
                sPE.wait(scalar, ev[('projv', i, oc)])
                if oc == 0 and i >= 2:
                    sPE.wait(scalar, ev[('projo', i - 2)])
                A = mkA(scalar)
                if kind == 'attn':
                    ev[('evacv', i, oc)] = A('copy', VB[b][:, oc, 0:T], p_mm[oc % 2][:, 0:T])
                else:
                    A('activation', KAF[:, 0:T], p_mm[oc % 2][:, 0:T], AF.Relu)
                    ev[('evacv', i, oc)] = A('activation', VB[b][:, oc, 0:T],
                                             KAF[:, 0:T], AF.Square)
            on('scalar', f_evacv)

        for oc in range(8):
            def f_projr(ten, i=i, b=b, T=T, oc=oc):
                if oc >= 2:
                    sACT.wait(ten, ev[('sigr', i, oc - 2)])
                else:
                    sACT.wait(ten, ev[('evacv', i, 6 + oc)])
                for ci in range(NCI):
                    ins = ten.matmul(p_mm[oc % 2][:, 0:T], W2[:, ci, oc * 128:(oc + 1) * 128],
                                     RIN[b][:, ci, 0:T], start=(ci == 0), stop=(ci == NCI - 1))
                ev[('projr', i, oc)] = sPE.inc(ins)
            on('tensor', f_projr)

            def f_sigr(scalar, i=i, T=T, oc=oc):
                sPE.wait(scalar, ev[('projr', i, oc)])
                if oc == 0 and i >= 1:
                    sDVE.wait(scalar, ev[('ep', i - 1)])
                ev[('sigr', i, oc)] = mkA(scalar)(
                    'activation', RS[:, oc, 0:T], p_mm[oc % 2][:, 0:T], AF.Sigmoid)
            on('scalar', f_sigr)
        ev[('projr', i)] = ev[('projr', i, 7)]

        for oc in range(8):
            def f_projo(ten, i=i, b=b, T=T, oc=oc):
                if oc == 0:
                    sACT.wait(ten, ev[('evacv', i, 7)])
                if oc >= 2:
                    sDVE.wait(ten, ev[('epoc', i, oc - 2)])
                else:
                    sACT.wait(ten, ev[('sigr', i, 6 + oc)])
                for ci in range(NCI):
                    ins = ten.matmul(p_mm[oc % 2][:, 0:T], W3[:, ci, oc * 128:(oc + 1) * 128],
                                     VB[b][:, ci, 0:T], start=(ci == 0), stop=(ci == NCI - 1))
                ev[('projo', i, oc)] = sPE.inc(ins)
            on('tensor', f_projo)

            def f_ep(vector, i=i, b=b, T=T, oc=oc):
                sPE.wait(vector, ev[('projo', i, oc)])
                sACT.wait(vector, ev[('sigr', i, oc)])
                if oc == 0 and i >= 2:
                    sSTp[b].wait(vector, ev[('stored', i - 2)])
                V = mkV(vector)
                V('tensor_mul', OTX[:, 0:T], RS[:, oc, 0:T], p_mm[oc % 2][:, 0:T])
                ev[('epoc', i, oc)] = V('tensor_add', OT[b][:, oc, 0:T],
                                        OTX[:, 0:T], XT[b][:, oc, 0:T])
            on('vector', f_ep)
        ev[('ep', i)] = ev[('epoc', i, 7)]
        ev[('projo', i)] = ev[('projo', i, 7)]

        def f_store(sync, i=i, b=b, t0=t0, T=T):
            sDVE.wait(sync, ev[('ep', i)])
            ev[('stored', i)] = sSTp[b].inc(
                sync.dma_start(out=d_dst[:, :, t0:t0 + T], in_=OT[b][:, :, 0:T]), 16)
        on('sync', f_store)

    def f_fin(sync):
        n = len(tls)
        sSTp[(n - 1) % 2].wait(sync, ev[('stored', n - 1)])
        sSTp[(n - 2) % 2].wait(sync, ev[('stored', n - 2)])
    on('sync', f_fin)




class LNHelper:

    def __init__(self, ctx, sems, vb, prefix, TP, nbuf=2):
        self.vb = vb
        self.sems = sems
        self.TP = TP
        self.SQB = [ctx.sb(f"{prefix}_sqb{b}", [128, NCI, TP], BF16) for b in range(nbuf)]
        self.XB = [ctx.sb(f"{prefix}_xb{b}", [128, NCI, TP], BF16) for b in range(nbuf)]
        self.LNT = ctx.sb(f"{prefix}_lnt", [128, NCI, TP], BF16)
        self.ROWS = ctx.sb(f"{prefix}_rows", [1, 1, 5, TP], F32)
        self.p_sum = ctx.ps(f"{prefix}_psum", [1, TP])
        self.p_sq = ctx.ps(f"{prefix}_psq", [1, TP])
        self.p_a = ctx.ps(f"{prefix}_pa", [128, TP])
        self.p_b = ctx.ps(f"{prefix}_pb", [128, TP])

    def emit(self, block, ev, i, b, XT_b, T, tgt, ld_wait, pe_busy_wait=None):
        sems = self.sems
        sPE, sACT, sDVE = sems['pe'], sems['act'], sems['dve']
        vb = self.vb
        SQB, XB, LNT, ROWS = self.SQB, self.XB, self.LNT, self.ROWS
        p_sum, p_sq, p_a, p_b = self.p_sum, self.p_sq, self.p_a, self.p_b
        eps_ln = vb['consts'][0:1, 0:1]
        r = 0

        def on(name, f):
            getattr(block, name)(f)

        def mkV(vector):
            def V(fn, *args, **kw):
                sDVE.wait(vector, sDVE.n)
                return sDVE.inc(getattr(vector, fn)(*args, **kw))
            return V

        def mkA(scalar):
            def A(fn, *args, **kw):
                sACT.wait(scalar, sACT.n)
                return sACT.inc(getattr(scalar, fn)(*args, **kw))
            return A

        def f_sq(scalar, i=i, b=b, T=T):
            ld_wait(scalar)
            if i >= 2:
                sPE.wait(scalar, ev[('stats', i - 2)])
            ev[('sq', i)] = mkA(scalar)(
                'activation', SQB[b][:, :, 0:T], XT_b[:, :, 0:T], AF.Square)
        on('scalar', f_sq)

        def f_cast(vector, i=i, b=b, T=T):
            ld_wait(vector)
            if i >= 2:
                sPE.wait(vector, ev[('stats', i - 2)])
            ev[('xb', i)] = mkV(vector)(
                'tensor_copy', XB[b][:, :, 0:T], XT_b[:, :, 0:T])
        on('vector', f_cast)

        def f_stats(ten, i=i, b=b, T=T):
            sACT.wait(ten, ev[('sq', i)])
            sDVE.wait(ten, ev[('xb', i)])
            if i >= 1:
                sACT.wait(ten, ev[('m2', i - 1)])
                sDVE.wait(ten, ev[('rows', i - 1)])
            if pe_busy_wait is not None:
                pe_busy_wait(ten, i)
            for ci in range(NCI):
                ten.matmul(p_sq[0:1, 0:T], vb['ones_inv'][:], SQB[b][:, ci, 0:T],
                           start=(ci == 0), stop=(ci == NCI - 1))
            for ci in range(NCI):
                ins = ten.matmul(p_sum[0:1, 0:T], vb['ones_inv'][:], XB[b][:, ci, 0:T],
                                 start=(ci == 0), stop=(ci == NCI - 1))
            ev[('stats', i)] = sPE.inc(ins)
        on('tensor', f_stats)

        m2 = ROWS[0:1, r, 0, 0:T]
        var = ROWS[0:1, r, 1, 0:T]
        sqv = ROWS[0:1, r, 2, 0:T]
        a_ = ROWS[0:1, r, 3, 0:T]
        mr = ROWS[0:1, r, 4, 0:T]

        def f_m2(scalar, i=i, T=T, m2=m2):
            sPE.wait(scalar, ev[('stats', i)])
            if i >= 1:
                sDVE.wait(scalar, ev[('rows', i - 1)])
            ev[('m2', i)] = mkA(scalar)('activation', m2, p_sum[0:1, 0:T], AF.Square)
        on('scalar', f_m2)

        def f_var(vector, i=i, T=T, m2=m2, var=var):
            sACT.wait(vector, ev[('m2', i)])
            ev[('var', i)] = mkV(vector)('tensor_sub', var, p_sq[0:1, 0:T], m2)
        on('vector', f_var)

        def f_sqv(scalar, i=i, T=T, var=var, sqv=sqv):
            sDVE.wait(scalar, ev[('var', i)])
            ev[('sqv', i)] = mkA(scalar)('activation', sqv, var, AF.Sqrt, bias=eps_ln)
        on('scalar', f_sqv)

        def f_rows(vector, i=i, T=T, sqv=sqv, a_=a_, mr=mr):
            sACT.wait(vector, ev[('sqv', i)])
            if i >= 1:
                sPE.wait(vector, ev[('bcast', i - 1)])
            V = mkV(vector)
            V('reciprocal', a_, sqv)
            ev[('rows', i)] = V('tensor_mul', mr, p_sum[0:1, 0:T], a_)
        on('vector', f_rows)

        def f_bcast(ten, i=i, T=T, a_=a_, mr=mr):
            sDVE.wait(ten, ev[('rows', i)])
            if i >= 1:
                sDVE.wait(ten, ev[('ln2', i - 1)])
            ten.matmul(p_a[:, 0:T], vb['ones1'][:], a_, start=True, stop=True)
            ins = ten.matmul(p_b[:, 0:T], vb['nones1'][:], mr, start=True, stop=True)
            ev[('bcast', i)] = sPE.inc(ins)
        on('tensor', f_bcast)

        def f_ln(vector, i=i, b=b, T=T):
            sPE.wait(vector, ev[('bcast', i)])
            V = mkV(vector)
            for ci in range(NCI):
                V('tensor_mul', LNT[:, ci, 0:T], XT_b[:, ci, 0:T], p_a[:, 0:T])
            for ci in range(NCI):
                ev[('ln2', i)] = tgt(vector, V, ci)
        on('vector', f_ln)


def emit_phase_l1a_A(ctx, block, io, WQ, WK, WV_, vb, sems, QSP, KSP):
    TP = 256
    tls = tiles_of(TCORE, TP)
    d_src = io['r2']
    XT = [ctx.sb(f"a_xt{b}", [128, NCI, TP], F32) for b in range(2)]
    LNB = [ctx.sb(f"a_lnb{b}", [128, NCI, TP], BF16) for b in range(2)]
    QFB = [ctx.sb(f"a_qfb{b}", [128, NCI, TP], BF16) for b in range(2)]
    KFB = [ctx.sb(f"a_kfb{b}", [128, NCI, TP], BF16) for b in range(2)]
    VFB = [ctx.sb(f"a_vfb{b}", [128, NCI, TP], BF16) for b in range(2)]
    KTS = [ctx.sb(f"a_kts{b}", [128, 2, H], BF16) for b in range(2)]
    VTS = [ctx.sb(f"a_vts{b}", [128, 2, H], BF16) for b in range(2)]
    nt = len(tls)
    ln = LNHelper(ctx, sems, vb, "a", TP)
    p_mm = [ctx.ps(f"a_pmm{b}", [128, TP]) for b in range(2)]
    p_tp = [ctx.ps(f"a_ptp{k}", [128, 128], BF16) for k in range(2)]

    sPE, sACT, sDVE = sems['pe'], sems['act'], sems['dve']
    sLDp = [sems['ld0'], sems['ld1']]
    sSTp = [sems['st0'], sems['st1']]
    sSTa = [sems['sta0'], sems['sta1']]
    sSTb = [sems['stb0'], sems['stb1']]
    sSTc = [sems['stc0'], sems['stc1']]
    ev = {}

    def on(name, f):
        getattr(block, name)(f)

    def mkV(vector):
        def V(fn, *args, **kw):
            sDVE.wait(vector, sDVE.n)
            return sDVE.inc(getattr(vector, fn)(*args, **kw))
        return V

    def mkA(scalar):
        def A(fn, *args, **kw):
            sACT.wait(scalar, sACT.n)
            return sACT.inc(getattr(scalar, fn)(*args, **kw))
        return A

    def bank_prev(i, st, oc):
        g = st * 8 + oc - 2
        if g >= 0:
            return i, g // 8, g % 8
        if i == 0:
            return None
        return i - 1, (24 + g) // 8, (24 + g) % 8

    EV_ENG = {0: 'act', 1: 'act', 2: 'dve'}

    for i, (t0, T) in enumerate(tls):
        b = i % 2

        def f_load(sync, i=i, b=b, t0=t0, T=T):
            if i >= 2:
                sACT.wait(sync, ev[('sq', i - 2)])
                sDVE.wait(sync, ev[('xb', i - 2)])
            ev[('xld', i)] = sLDp[b].inc(
                sync.dma_start(out=XT[b][:, :, 0:T],
                               in_=d_src[:, :, HALO + t0:HALO + t0 + T]), 16)
        on('sync', f_load)

        def ld_wait(eng, i=i, b=b):
            sLDp[b].wait(eng, ev[('xld', i)])

        def tgt(vector, V, ci, i=i, b=b, T=T):
            if ci == 0 and i >= 2:
                sPE.wait(vector, ev[('projV', i - 2)])
            return V('tensor_add', LNB[b][:, ci, 0:T], ln.LNT[:, ci, 0:T],
                     ln.p_b[:, 0:T])

        ln.emit(block, ev, i, b, XT[b], T, tgt, ld_wait)

        for st, Wmat, OUT, bias_col, acc in ((0, WQ, QFB, 'bq', QSP),
                                             (1, WK, KFB, 'bk', KSP)):
            for oc in range(8):
                def f_proj(ten, i=i, b=b, T=T, oc=oc, st=st, Wmat=Wmat):
                    if st == 0 and oc == 0:
                        sDVE.wait(ten, ev[('ln2', i)])
                    prev = bank_prev(i, st, oc)
                    if prev is not None:
                        pi, pst, poc = prev
                        s = sACT if EV_ENG[pst] == 'act' else sDVE
                        s.wait(ten, ev[('evx', pi, pst, poc)])
                    for ci in range(NCI):
                        ins = ten.matmul(p_mm[oc % 2][:, 0:T],
                                         Wmat[:, ci, oc * 128:(oc + 1) * 128],
                                         LNB[b][:, ci, 0:T],
                                         start=(ci == 0), stop=(ci == NCI - 1))
                    ev[('proj', i, st, oc)] = sPE.inc(ins)
                on('tensor', f_proj)

                def f_evac(scalar, i=i, b=b, T=T, oc=oc, st=st, OUT=OUT,
                           bias_col=bias_col, acc=acc):
                    sPE.wait(scalar, ev[('proj', i, st, oc)])
                    if st == 0 and oc == 0 and i >= 2:
                        sSTp[b].wait(scalar, ev[('stq', i - 2)])
                    if st == 1 and oc == 0 and i >= 2:
                        sSTa[b].wait(scalar, ev[('stk', i - 2)])
                    ev[('evx', i, st, oc)] = mkA(scalar)(
                        'activation', OUT[b][:, oc, 0:T], p_mm[oc % 2][:, 0:T],
                        AF.Sigmoid, bias=vb[bias_col][:, oc:oc + 1],
                        accum_out=acc[:, oc, i:i + 1])
                on('scalar', f_evac)

        for oc in range(8):
            def f_projv(ten, i=i, b=b, T=T, oc=oc):
                pi, pst, poc = bank_prev(i, 2, oc)
                s = sACT if EV_ENG[pst] == 'act' else sDVE
                s.wait(ten, ev[('evx', pi, pst, poc)])
                for ci in range(NCI):
                    ins = ten.matmul(p_mm[oc % 2][:, 0:T],
                                     WV_[:, ci, oc * 128:(oc + 1) * 128],
                                     LNB[b][:, ci, 0:T],
                                     start=(ci == 0), stop=(ci == NCI - 1))
                ev[('proj', i, 2, oc)] = sPE.inc(ins)
            on('tensor', f_projv)

            def f_evacv(vector, i=i, b=b, T=T, oc=oc):
                sPE.wait(vector, ev[('proj', i, 2, oc)])
                ev[('evx', i, 2, oc)] = mkV(vector)(
                    'tensor_scalar_add', VFB[b][:, oc, 0:T], p_mm[oc % 2][:, 0:T],
                    vb['bv'][:, oc:oc + 1])
            on('vector', f_evacv)
        ev[('projV', i)] = ev[('proj', i, 2, 7)]

        tp_seq = [(tch, sid, oc) for tch in range(2) for sid in range(2)
                  for oc in range(8)]
        for n, (tch, sid, oc) in enumerate(tp_seq):
            SRC = KFB if sid == 0 else VFB
            DST = KTS if sid == 0 else VTS

            def f_tp(ten, i=i, b=b, tch=tch, oc=oc, sid=sid, n=n, SRC=SRC):
                if n == 0:
                    sACT.wait(ten, ev[('evx', i, 1, 7)])
                    sDVE.wait(ten, ev[('evx', i, 2, 7)])
                if n >= 2:
                    sACT.wait(ten, ev[('tpc', i, n - 2)])
                elif i >= 1:
                    sACT.wait(ten, ev[('tpc', i - 1, 30 + n)])
                ins = ten.transpose(p_tp[n % 2][:],
                                    SRC[b][:, oc, tch * 128:(tch + 1) * 128],
                                    vb['idb'][:])
                ev[('tp', i, n)] = sPE.inc(ins)
            on('tensor', f_tp)

            def f_tpc(scalar, i=i, b=b, tch=tch, oc=oc, n=n, DST=DST):
                sPE.wait(scalar, ev[('tp', i, n)])
                if n == 0 and i >= 2:
                    sSTb[b].wait(scalar, ev[('stkt', i - 2)])
                    sSTc[b].wait(scalar, ev[('stvt', i - 2)])
                ev[('tpc', i, n)] = mkA(scalar)(
                    'copy', DST[b][:, tch, oc * 128:(oc + 1) * 128], p_tp[n % 2][:])
            on('scalar', f_tpc)

        def f_store(sync, i=i, b=b, t0=t0, T=T):
            sACT.wait(sync, ev[('tpc', i, 31)])
            sDVE.wait(sync, ev[('evx', i, 2, 7)])
            ev[('stq', i)] = sSTp[b].dma(sync, 
                out=io['qf'][:, :, t0:t0 + T], in_=QFB[b][:, :, 0:T])
            ev[('stk', i)] = sSTa[b].dma(sync, 
                out=io['kf'][:, :, t0:t0 + T], in_=KFB[b][:, :, 0:T])
            tc0 = 2 * i
            ev[('stkt', i)] = sSTb[b].dma(sync, 
                out=io['kt'][tc0:tc0 + 2, :, :].rearrange("tc p d -> p tc d"),
                in_=KTS[b][:])
            ev[('stvt', i)] = sSTc[b].dma(sync, 
                out=io['vt'][tc0:tc0 + 2, :, :].rearrange("tc p d -> p tc d"),
                in_=VTS[b][:])
        on('sync', f_store)

    def f_fin(sync):
        for j in (nt - 1, nt - 2):
            sSTp[j % 2].wait(sync, ev[('stq', j)])
            sSTa[j % 2].wait(sync, ev[('stk', j)])
            sSTb[j % 2].wait(sync, ev[('stkt', j)])
            sSTc[j % 2].wait(sync, ev[('stvt', j)])
    on('sync', f_fin)
    ev['nt'] = nt
    return ev


def emit_l1a_BCD(ctx, block, io, vb, sems, QSP, KSP, nt_a, cc_groups):
    TP = 256
    WO = ctx.sb("d_wo", [128, NCI, H], BF16)
    load_phase_weights(block, sems, [(WO[:], io['dw_l1a_wo'])])
    wo_ld = sems['wld'].n
    tls = tiles_of(TCORE, TP)
    nt = len(tls)
    sPE, sACT, sDVE, sGPS, sCC = (sems[k] for k in
                                  ('pe', 'act', 'dve', 'gps', 'cc'))
    sLDp = [sems['ld0'], sems['ld1']]
    sLDa = [sems['lda0'], sems['lda1']]
    sLDb = [sems['ldb0'], sems['ldb1']]
    sLDc = [sems['ldc0'], sems['ldc1']]
    sSTp = [sems['st0'], sems['st1']]
    ev = {}

    def on(name, f):
        getattr(block, name)(f)

    def mkV(vector):
        def V(fn, *args, **kw):
            sDVE.wait(vector, sDVE.n)
            return sDVE.inc(getattr(vector, fn)(*args, **kw))
        return V

    def mkA(scalar):
        def A(fn, *args, **kw):
            sACT.wait(scalar, sACT.n)
            return sACT.inc(getattr(scalar, fn)(*args, **kw))
        return A

    SUMS = ctx.sb("b_sums", [128, 2, NCI], F32)
    GS = ctx.sb("b_gs", [128, 2, NCI], F32)
    AK = ctx.sb("b_ak", [128, NCI, 16], BF16)
    AQ = ctx.sb("b_aq", [128, NCI, 16], BF16)
    EC = ctx.sb("b_ec", [16, 4], F32)
    T12 = ctx.sb("b_t12", [128, 2, NCI], F32)
    T12P = ctx.sb("b_t12p", [128, 2, NCI, nt], F32)
    AT1 = ctx.sb("b_at1", [128, NCI, 16], BF16)
    AT2 = ctx.sb("b_at2", [128, NCI, 16], BF16)
    SA = ctx.sb("c_sa", [16, TCORE], F32)
    ZP = ctx.sb("c_zp", [16, nt], F32)
    KVF = ctx.sb("c_kvf", [128, NCI, 128], F32)
    KVG = ctx.sb("c_kvg", [128, NCI, 128], F32)
    KVSB = ctx.sb("c_kvsb", [128, NCI, 128], BF16)
    ZROW = ctx.sb("c_zrow", [16, 2], F32)
    ZB = ctx.sb("c_zb", [16, 1], BF16)
    ZREP = ctx.sb("c_zrep", [128, NCI], F32)
    ROWS = ctx.sb("b_rows", [16, 2, 6, TP], F32)
    eps_a = vb['consts'][0:1, 1:2]

    def emit_ar(tag, sb_in, d_in, d_out, sb_out, wait_evs):
        def f(g):
            for s, v in wait_evs:
                s.wait(g, v)
            sGPS.dma(g, out=d_in, in_=sb_in)
            sGPS.wait(g, sGPS.n)
            sCC.inc(g.collective_compute(
                "AllReduce", ALU.add, replica_groups=cc_groups,
                ins=[d_in.opt()], outs=[d_out.opt()]))
            sCC.wait(g, sCC.n)
            sGPS.dma(g, out=sb_out, in_=d_out)
            sGPS.wait(g, sGPS.n)
        on('gpsimd', f)
        return sGPS.n

    def f_r1red(vector):
        V = mkV(vector)
        for ci in range(NCI):
            V('reduce_sum', SUMS[:, 0, ci:ci + 1], QSP[:, ci, 0:nt_a], mybir.AxisListType.X)
        for ci in range(NCI):
            ev['r1red'] = V('reduce_sum', SUMS[:, 1, ci:ci + 1], KSP[:, ci, 0:nt_a], mybir.AxisListType.X)
    on('vector', f_r1red)
    g1 = emit_ar('r1', SUMS[:], io['cc1i'], io['cc1o'], GS[:],
                 [(sDVE, sems['dve'].n)])

    def f_build_a(vector):
        sGPS.wait(vector, g1)
        V = mkV(vector)
        V('memset', AK[:], 0.0)
        V('memset', AQ[:], 0.0)
        for ci in range(NCI):
            V('tensor_scalar_add', AK[0:64, ci, 2 * ci:2 * ci + 1], GS[0:64, 1, ci:ci + 1], EPS_A)
            V('tensor_scalar_add', AK[64:128, ci, 2 * ci + 1:2 * ci + 2], GS[64:128, 1, ci:ci + 1], EPS_A)
            V('tensor_scalar_add', AQ[0:64, ci, 2 * ci:2 * ci + 1], GS[0:64, 0, ci:ci + 1], EPS_A)
            ev['builda'] = V('tensor_scalar_add', AQ[64:128, ci, 2 * ci + 1:2 * ci + 2], GS[64:128, 0, ci:ci + 1], EPS_A)
    on('vector', f_build_a)

    p_row = [ctx.ps(f"b_prow{k}", [16, TP]) for k in range(2)]
    p_bc = [ctx.ps(f"b_pbc{k}", [128, TP]) for k in range(2)]
    bc_scope = contextlib.ExitStack()
    outer_scope = ctx.scope
    ctx.scope = bc_scope
    p_e = ctx.ps("c_pe", [128, H])
    p_kv = ctx.ps("c_pkv", [128, NCI, 128])
    ctx.scope = outer_scope

    def f_ec_mm(ten):
        sDVE.wait(ten, ev['builda'])
        for ci in range(NCI):
            ten.matmul(p_row[0][0:16, 0:1], AK[:, ci, :], vb['onesb'][:],
                       start=(ci == 0), stop=(ci == NCI - 1))
        for ci in range(NCI):
            ins = ten.matmul(p_row[1][0:16, 0:1], AQ[:, ci, :], vb['onesb'][:],
                             start=(ci == 0), stop=(ci == NCI - 1))
        ev['ecmm'] = sPE.inc(ins)
    on('tensor', f_ec_mm)

    def f_ec(vector):
        sPE.wait(vector, ev['ecmm'])
        V = mkV(vector)
        V('tensor_scalar_mul', EC[:, 0:1], p_row[0][0:16, 0:1], EPS_A)
        ev['ec'] = V('tensor_scalar_mul', EC[:, 1:2], p_row[1][0:16, 0:1], EPS_A)
    on('vector', f_ec)

    QFT = [ctx.sb(f"b_qft{b}", [128, NCI, TP], BF16) for b in range(2)]
    KFT = [ctx.sb(f"b_kft{b}", [128, NCI, TP], BF16) for b in range(2)]
    SIB = ctx.sb("b_sib", [16, 2, TP], BF16)
    SOB = ctx.sb("b_sob", [16, 2, TP], BF16)
    QSIB = [ctx.sb(f"b_qsib{b}", [128, NCI, TP], BF16) for b in range(2)]
    DUMB = ctx.sb("b_dumb", [128, TP], BF16)

    for i, (t0, T) in enumerate(tls):
        b = i % 2

        def f_load(sync, i=i, b=b, t0=t0, T=T):
            if i >= 2:
                sDVE.wait(sync, ev[('bstt', i - 2)])
            ev[('bldq', i)] = sLDp[b].inc(
                sync.dma_start(out=QFT[b][:, :, 0:T], in_=io['qf'][:, :, t0:t0 + T]), 16)
            ev[('bldk', i)] = sLDa[b].inc(
                sync.dma_start(out=KFT[b][:, :, 0:T], in_=io['kf'][:, :, t0:t0 + T]), 16)
        on('sync', f_load)

        def f_rowmm(ten, i=i, b=b, T=T):
            sLDp[b].wait(ten, ev[('bldq', i)])
            sLDa[b].wait(ten, ev[('bldk', i)])
            if i == 0:
                sDVE.wait(ten, ev['ec'])
            if i >= 1:
                sDVE.wait(ten, ev[('brow', i - 1)])
            for ci in range(NCI):
                ten.matmul(p_row[0][:, 0:T], AK[:, ci, :], QFT[b][:, ci, 0:T],
                           start=(ci == 0), stop=(ci == NCI - 1))
            for ci in range(NCI):
                ins = ten.matmul(p_row[1][:, 0:T], AQ[:, ci, :], KFT[b][:, ci, 0:T],
                                 start=(ci == 0), stop=(ci == NCI - 1))
            ev[('browmm', i)] = sPE.inc(ins)
        on('tensor', f_rowmm)

        def f_brow(vector, i=i, b=b, T=T):
            sPE.wait(vector, ev[('browmm', i)])
            if i >= 2:
                sPE.wait(vector, ev[('bbc', i - 2, 1, 7)])
            V = mkV(vector)
            r = i % 2
            si_f = ROWS[:, r, 0, 0:T]
            so_f = ROWS[:, r, 1, 0:T]
            V('tensor_scalar_add', ROWS[:, r, 2, 0:T], p_row[0][:, 0:T], EC[:, 0:1])
            V('reciprocal', si_f, ROWS[:, r, 2, 0:T])
            V('tensor_copy', SIB[:, b, 0:T], si_f)
            V('tensor_scalar_add', ROWS[:, r, 3, 0:T], p_row[1][:, 0:T], EC[:, 1:2])
            V('reciprocal', so_f, ROWS[:, r, 3, 0:T])
            ev[('brow', i)] = V('tensor_copy', SOB[:, b, 0:T], so_f)
        on('vector', f_brow)

        for which in range(2):
            SB = SIB if which == 0 else SOB
            FT = QFT if which == 0 else KFT
            for ci in range(NCI):
                def f_bc(ten, i=i, b=b, T=T, ci=ci, which=which, SB=SB):
                    if ci == 0 and which == 0:
                        sDVE.wait(ten, ev[('brow', i)])
                    k = ('bstt', i, which, ci - 2)
                    if ci >= 2:
                        sDVE.wait(ten, ev[k])
                    elif which == 1:
                        sDVE.wait(ten, ev[('bstt', i, 0, 6 + ci)])
                    elif i >= 1:
                        sDVE.wait(ten, ev[('bstt', i - 1, 1, 6 + ci)])
                    ins = ten.matmul(p_bc[ci % 2][:, 0:T], vb['selb'][:, ci, :],
                                     SB[:, b, 0:T], start=True, stop=True)
                    ev[('bbc', i, which, ci)] = sPE.inc(ins)
                on('tensor', f_bc)

                def f_stt(vector, i=i, b=b, T=T, ci=ci, which=which, FT=FT):
                    sPE.wait(vector, ev[('bbc', i, which, ci)])
                    V = mkV(vector)
                    if which == 0:
                        out = QSIB[b][:, ci, 0:T]
                        if ci == 0 and i >= 2:
                            sSTp[b].wait(vector, ev[('bst', i - 2)])
                        acc = T12P[:, 1, ci, i:i + 1]
                    else:
                        out = DUMB[:, 0:T]
                        acc = T12P[:, 0, ci, i:i + 1]
                    ev[('bstt', i, which, ci)] = V(
                        'scalar_tensor_tensor', out, FT[b][:, ci, 0:T], 1.0,
                        p_bc[ci % 2][:, 0:T], ALU.mult, ALU.mult, accum_out=acc)
                on('vector', f_stt)
        ev[('bstt', i)] = ev[('bstt', i, 1, 7)]

        def f_bst(sync, i=i, b=b, t0=t0, T=T):
            sDVE.wait(sync, ev[('bstt', i, 0, 7)])
            ev[('bst', i)] = sSTp[b].inc(
                sync.dma_start(out=io['qsi'][:, :, t0:t0 + T], in_=QSIB[b][:, :, 0:T]), 16)
        on('sync', f_bst)

    bst_done = (sSTp[0].n, sSTp[1].n)

    def f_r2red(vector):
        V = mkV(vector)
        for w in range(2):
            for ci in range(NCI):
                ev['r2red'] = V('reduce_sum', T12[:, w, ci:ci + 1],
                                T12P[:, w, ci, 0:nt], mybir.AxisListType.X)
    on('vector', f_r2red)
    g2 = emit_ar('r2', T12[:], io['cc2i'], io['cc2o'], T12[:],
                 [(sDVE, sems['dve'].n)])

    def f_build_at(vector):
        sGPS.wait(vector, g2)
        V = mkV(vector)
        V('memset', AT1[:], 0.0)
        V('memset', AT2[:], 0.0)
        for ci in range(NCI):
            V('tensor_scalar_add', AT1[0:64, ci, 2 * ci:2 * ci + 1], T12[0:64, 0, ci:ci + 1], EPS_A)
            V('tensor_scalar_add', AT1[64:128, ci, 2 * ci + 1:2 * ci + 2], T12[64:128, 0, ci:ci + 1], EPS_A)
            V('tensor_scalar_add', AT2[0:64, ci, 2 * ci:2 * ci + 1], T12[0:64, 1, ci:ci + 1], EPS_A)
            ev['buildat'] = V('tensor_scalar_add', AT2[64:128, ci, 2 * ci + 1:2 * ci + 2], T12[64:128, 1, ci:ci + 1], EPS_A)
    on('vector', f_build_at)

    def f_ec2_mm(ten):
        sDVE.wait(ten, ev['buildat'])
        for ci in range(NCI):
            ten.matmul(p_row[0][0:16, 0:1], AT1[:, ci, :], vb['onesb'][:],
                       start=(ci == 0), stop=(ci == NCI - 1))
        for ci in range(NCI):
            ins = ten.matmul(p_row[1][0:16, 0:1], AT2[:, ci, :], vb['onesb'][:],
                             start=(ci == 0), stop=(ci == NCI - 1))
        ev['ec2mm'] = sPE.inc(ins)
    on('tensor', f_ec2_mm)

    def f_ec2(vector):
        sPE.wait(vector, ev['ec2mm'])
        V = mkV(vector)
        V('tensor_scalar_mul', EC[:, 2:3], p_row[0][0:16, 0:1], EPS_A)
        ev['ec2'] = V('tensor_scalar_mul', EC[:, 3:4], p_row[1][0:16, 0:1], EPS_A)
    on('vector', f_ec2)

    DBGETM = ctx.sb("c_dbgetm", [128, H], F32) if DBG else None
    KTT = [ctx.sb(f"c_ktt{b}", [128, 2, H], BF16) for b in range(2)]
    VTT = [ctx.sb(f"c_vtt{b}", [128, 2, H], BF16) for b in range(2)]
    ESB = ctx.sb("c_esb", [16, 2, TP], BF16)
    VSCB = ctx.sb("c_vscb", [128, 2, H], BF16)

    for i, (t0, T) in enumerate(tls):
        b = i % 2

        def f_cload(sync, i=i, b=b, t0=t0, T=T):
            if i >= 2:
                sPE.wait(sync, ev[('ckv', i - 2)])
                sDVE.wait(sync, ev[('cvsc', i - 2, 1)])
            elif ('bstt', nt - 1) in ev:
                sDVE.wait(sync, ev[('bstt', nt - 1)])
            tc0 = 2 * i
            ev[('cldq', i)] = sLDp[b].dma(sync, 
                out=QFT[b][:, :, 0:T], in_=io['qf'][:, :, t0:t0 + T])
            ev[('cldk', i)] = sLDa[b].dma(sync, 
                out=KFT[b][:, :, 0:T], in_=io['kf'][:, :, t0:t0 + T])
            ev[('cldkt', i)] = sLDb[b].dma(sync, 
                out=KTT[b][:], in_=io['kt'][tc0:tc0 + 2, :, :].rearrange("tc p d -> p tc d"))
            ev[('cldvt', i)] = sLDc[b].dma(sync, 
                out=VTT[b][:], in_=io['vt'][tc0:tc0 + 2, :, :].rearrange("tc p d -> p tc d"))
        on('sync', f_cload)

        def f_crowmm(ten, i=i, b=b, T=T):
            sLDp[b].wait(ten, ev[('cldq', i)])
            sLDa[b].wait(ten, ev[('cldk', i)])
            sLDb[b].wait(ten, ev[('cldkt', i)])
            if i == 0:
                sDVE.wait(ten, ev['ec2'])
            if i >= 1:
                sDVE.wait(ten, ev[('ces', i - 1)])
            for ci in range(NCI):
                ten.matmul(p_row[0][:, 0:T], AT1[:, ci, :], QFT[b][:, ci, 0:T],
                           start=(ci == 0), stop=(ci == NCI - 1))
            for ci in range(NCI):
                ins = ten.matmul(p_row[1][:, 0:T], AT2[:, ci, :], KFT[b][:, ci, 0:T],
                                 start=(ci == 0), stop=(ci == NCI - 1))
            ev[('crowmm', i)] = sPE.inc(ins)
        on('tensor', f_crowmm)

        def f_crow(vector, i=i, b=b, T=T, t0=t0):
            sPE.wait(vector, ev[('crowmm', i)])
            V = mkV(vector)
            r = i % 2
            V('tensor_scalar_add', ROWS[:, r, 0, 0:T], p_row[0][:, 0:T], EC[:, 2:3])
            ev[('cct', i)] = V('tensor_scalar', ROWS[:, r, 1, 0:T], p_row[1][:, 0:T],
                               EC[:, 3:4], None, ALU.add)
            ev[('cclip', i)] = V('tensor_scalar', ROWS[:, r, 2, 0:T], ROWS[:, r, 1, 0:T],
                                 1.0, -1.0, ALU.min, ALU.max)
        on('vector', f_crow)

        def f_cact(scalar, i=i, b=b, T=T, t0=t0):
            sDVE.wait(scalar, ev[('cclip', i)])
            A = mkA(scalar)
            r = i % 2
            ev[('csa', i)] = A('activation', SA[:, t0:t0 + T], ROWS[:, r, 0, 0:T], AF.Sigmoid)
            ev[('cexp', i)] = A('activation', ROWS[:, r, 3, 0:T], ROWS[:, r, 2, 0:T],
                                AF.Exp, accum_out=ZP[:, i:i + 1])
        on('scalar', f_cact)

        def f_ces(vector, i=i, b=b, T=T):
            sACT.wait(vector, ev[('cexp', i)])
            if i >= 2:
                sPE.wait(vector, ev[('ce', i - 2, 1)])
            r = i % 2
            ev[('ces', i)] = mkV(vector)('tensor_copy', ESB[:, b, 0:T], ROWS[:, r, 3, 0:T])
        on('vector', f_ces)

        for tch in range(2):
            def f_ce(ten, i=i, b=b, tch=tch):
                sDVE.wait(ten, ev[('ces', i)])
                if ('cvsc', i, tch - 1) in ev:
                    sDVE.wait(ten, ev[('cvsc', i, tch - 1)])
                elif i >= 1:
                    sDVE.wait(ten, ev[('cvsc', i - 1, 1)])
                ten.matmul(p_e[:, 0:512], ESB[:, b, tch * 128:(tch + 1) * 128],
                           vb['selrow'][:, 0:512], start=True, stop=True)
                ins = ten.matmul(p_e[:, 512:1024], ESB[:, b, tch * 128:(tch + 1) * 128],
                                 vb['selrow'][:, 512:1024], start=True, stop=True)
                ev[('ce', i, tch)] = sPE.inc(ins)
            on('tensor', f_ce)

            def f_cvsc(vector, i=i, b=b, tch=tch):
                sPE.wait(vector, ev[('ce', i, tch)])
                sLDc[b].wait(vector, ev[('cldvt', i)])
                ev[('cvsc', i, tch)] = mkV(vector)(
                    'tensor_mul', VSCB[:, tch, :], VTT[b][:, tch, :], p_e[:, 0:H])
            on('vector', f_cvsc)

            if DBG and i == nt - 1 and tch == 1:
                def f_dbg_etm(scalar, i=i):
                    sPE.wait(scalar, ev[('ce', i, 1)])
                    ev['dbgetm'] = mkA(scalar)('copy', DBGETM[:], p_e[:, 0:H])
                on('scalar', f_dbg_etm)

            def f_ckv(ten, i=i, b=b, tch=tch):
                sDVE.wait(ten, ev[('cvsc', i, tch)])
                for ci in range(NCI):
                    ins = ten.matmul(
                        p_kv[:, ci, :], KTT[b][:, tch, ci * 128:(ci + 1) * 128],
                        VSCB[:, tch, ci * 128:(ci + 1) * 128],
                        start=(i == 0 and tch == 0 and ci % 4 == 0),
                        stop=(i == nt - 1 and tch == 1 and ci % 4 == 3))
                ev[('ckv', i, tch)] = sPE.inc(ins)
            on('tensor', f_ckv)
        ev[('ckv', i)] = ev[('ckv', i, 1)]

    def f_kvev(scalar):
        sPE.wait(scalar, ev[('ckv', nt - 1)])
        ev['kvev'] = mkA(scalar)('copy', KVF[:], p_kv[:])
    on('scalar', f_kvev)

    def f_kvzero(vector):
        sACT.wait(vector, ev['kvev'])
        V = mkV(vector)
        for ci in range(NCI):
            V('memset', KVF[0:64, ci, 64:128], 0.0)
            V('memset', KVF[64:128, ci, 0:64], 0.0)
        for _ in (0,):
            ev['zred'] = V('reduce_sum', ZROW[:, 0:1], ZP[:, 0:nt], mybir.AxisListType.X)
    on('vector', f_kvzero)

    g3 = emit_ar('r3', KVF[:].rearrange("p c d -> p (c d)"), io['cc3i'], io['cc3o'],
                 KVG[:].rearrange("p c d -> p (c d)"), [(sDVE, sems['dve'].n)])
    g3z = emit_ar('r3z', ZROW[:, 0:1], io['cc4i'], io['cc4o'], ZROW[:, 0:1],
                  [(sGPS, g3)])

    def f_zinv(vector):
        sGPS.wait(vector, g3z)
        V = mkV(vector)
        V('reciprocal', ZROW[:, 1:2], ZROW[:, 0:1])
        V('tensor_scalar_mul', ZROW[:, 1:2], ZROW[:, 1:2], SSCALE)
        ev['zb'] = V('tensor_copy', ZB[:], ZROW[:, 1:2])
    on('vector', f_zinv)

    def f_zrep_mm(ten):
        sDVE.wait(ten, ev['zb'])
        for ci in range(NCI):
            ins = ten.matmul(p_bc[0][:, ci:ci + 1],
                             vb['selb'][:, ci, :], ZB[:], start=True, stop=True)
        ev['zrepmm'] = sPE.inc(ins)
    on('tensor', f_zrep_mm)

    def f_zrep(scalar):
        sPE.wait(scalar, ev['zrepmm'])
        ev['zrep'] = mkA(scalar)('copy', ZREP[:], p_bc[0][:, 0:NCI])
    on('scalar', f_zrep)

    def f_kvs(vector):
        sACT.wait(vector, ev['zrep'])
        V = mkV(vector)
        for ci in range(NCI):
            ev['kvs'] = V('tensor_scalar_mul', KVSB[:, ci, :], KVG[:, ci, :],
                          ZREP[:, ci:ci + 1])
    on('vector', f_kvs)

    if DBG:
        def f_dbg(g):
            sDVE.wait(g, sems['dve'].n)
            sACT.wait(g, sems['act'].n)
            for nm, ap in (("dbg_sa", SA[:]), ("dbg_kvg", KVG[:]),
                           ("dbg_kvsb", KVSB[:]), ("dbg_zrow", ZROW[:]),
                           ("dbg_zrep", ZREP[:]), ("dbg_t12", T12[:]),
                           ("dbg_at1", AT1[:]), ("dbg_at2", AT2[:]),
                           ("dbg_ec", EC[:]), ("dbg_vscb", VSCB[:]),
                           ("dbg_esb", ESB[:]), ("dbg_ktt", KTT[(nt - 1) % 2][:]),
                           ("dbg_etm", DBGETM[:])):
                sGPS.dma(g, out=io[nm], in_=ap)
        on('gpsimd', f_dbg)

    bc_scope.close()
    QST = [ctx.sb(f"d_qst{b}", [128, NCI, TP], BF16) for b in range(2)]
    XT = [ctx.sb(f"d_xt{b}", [128, NCI, TP], F32) for b in range(2)]
    SAB = ctx.sb("d_sab", [16, 2, TP], BF16)
    QSA = ctx.sb("d_qsa", [128, NCI, TP], BF16)
    OB = ctx.sb("d_ob", [128, NCI, TP], BF16)
    OX = ctx.sb("d_ox", [128, TP], F32)
    OT = [ctx.sb(f"d_ot{b}", [128, NCI, TP], F32) for b in range(2)]
    p_mm2 = [ctx.ps(f"d_pmm{k}", [128, TP]) for k in range(2)]

    for i, (t0, T) in enumerate(tls):
        b = i % 2

        def f_dload(sync, i=i, b=b, t0=t0, T=T):
            if i >= 2:
                sDVE.wait(sync, ev[('dep', i - 2)])
            else:
                sSTp[0].wait(sync, bst_done[0])
                sSTp[1].wait(sync, bst_done[1])
            ev[('dldq', i)] = sLDp[b].dma(sync, 
                out=QST[b][:, :, 0:T], in_=io['qsi'][:, :, t0:t0 + T])
            ev[('dld', i)] = sLDa[b].dma(sync, 
                out=XT[b][:, :, 0:T], in_=io['r2'][:, :, HALO + t0:HALO + t0 + T])
        on('sync', f_dload)

        def f_dsab(vector, i=i, b=b, T=T, t0=t0):
            if i == 0:
                sDVE.wait(vector, ev['kvs'])
            if i >= 1:
                sPE.wait(vector, ev[('dsabmm', i - 1, 7)])
            ev[('dsab', i)] = mkV(vector)('tensor_copy', SAB[:, b, 0:T], SA[:, t0:t0 + T])
        on('vector', f_dsab)

        for ci in range(NCI):
            def f_dbc(ten, i=i, b=b, T=T, ci=ci):
                sDVE.wait(ten, ev[('dsab', i)])
                if ci >= 2:
                    sDVE.wait(ten, ev[('dqsa', i, ci - 2)])
                elif i >= 1:
                    sDVE.wait(ten, ev[('dqsa', i - 1, 6 + ci)])
                ins = ten.matmul(p_bc[ci % 2][:, 0:T], vb['selb'][:, ci, :],
                                 SAB[:, b, 0:T], start=True, stop=True)
                ev[('dsabmm', i, ci)] = sPE.inc(ins)
            on('tensor', f_dbc)

            def f_dqsa(vector, i=i, b=b, T=T, ci=ci):
                sPE.wait(vector, ev[('dsabmm', i, ci)])
                sLDp[b].wait(vector, ev[('dldq', i)])
                if ci == 0 and i >= 1:
                    sACT.wait(vector, ev[('dob', i - 1, 7)])
                ev[('dqsa', i, ci)] = mkV(vector)(
                    'tensor_mul', QSA[:, ci, 0:T], QST[b][:, ci, 0:T], p_bc[ci % 2][:, 0:T])
            on('vector', f_dqsa)

            def f_dkv(ten, i=i, T=T, ci=ci):
                sDVE.wait(ten, ev[('dqsa', i, ci)])
                if ci >= 2:
                    sACT.wait(ten, ev[('dob', i, ci - 2)])
                elif i >= 1:
                    sACT.wait(ten, ev[('dob', i - 1, 6 + ci)])
                ins = ten.matmul(p_mm2[ci % 2][:, 0:T], KVSB[:, ci, :],
                                 QSA[:, ci, 0:T], start=True, stop=True)
                ev[('dkv', i, ci)] = sPE.inc(ins)
            on('tensor', f_dkv)

            def f_dob(scalar, i=i, T=T, ci=ci):
                sPE.wait(scalar, ev[('dkv', i, ci)])
                if ci == 0 and i >= 1:
                    sPE.wait(scalar, ev[('dproj', i - 1, 7)])
                ev[('dob', i, ci)] = mkA(scalar)('copy', OB[:, ci, 0:T],
                                                 p_mm2[ci % 2][:, 0:T])
            on('scalar', f_dob)

        for oc in range(8):
            def f_dproj(ten, i=i, b=b, T=T, oc=oc):
                if oc == 0:
                    sACT.wait(ten, ev[('dob', i, 7)])
                    sems['wld'].wait(ten, wo_ld)
                if oc >= 2:
                    sDVE.wait(ten, ev[('dep', i, oc - 2)])
                elif i >= 1:
                    sDVE.wait(ten, ev[('dep', i - 1, 6 + oc)])
                for ci in range(NCI):
                    ins = ten.matmul(p_mm2[oc % 2][:, 0:T], WO[:, ci, oc * 128:(oc + 1) * 128],
                                     OB[:, ci, 0:T], start=(ci == 0), stop=(ci == NCI - 1))
                ev[('dproj', i, oc)] = sPE.inc(ins)
            on('tensor', f_dproj)

            def f_dox(scalar, i=i, T=T, oc=oc):
                sPE.wait(scalar, ev[('dproj', i, oc)])
                if oc >= 1:
                    sDVE.wait(scalar, ev[('dep', i, oc - 1)])
                elif i >= 1:
                    sDVE.wait(scalar, ev[('dep', i - 1, 7)])
                ev[('dox', i, oc)] = mkA(scalar)(
                    'activation', OX[:, 0:T], p_mm2[oc % 2][:, 0:T], AF.Identity,
                    bias=vb['bo'][:, oc:oc + 1])
            on('scalar', f_dox)

            def f_dep(vector, i=i, b=b, T=T, oc=oc):
                sACT.wait(vector, ev[('dox', i, oc)])
                sLDa[b].wait(vector, ev[('dld', i)])
                if oc == 0 and i >= 2:
                    sSTp[b].wait(vector, ev[('dst', i - 2)])
                ev[('dep', i, oc)] = mkV(vector)(
                    'tensor_add', OT[b][:, oc, 0:T], OX[:, 0:T], XT[b][:, oc, 0:T])
            on('vector', f_dep)
        ev[('dep', i)] = ev[('dep', i, 7)]

        def f_dst(sync, i=i, b=b, t0=t0, T=T):
            sDVE.wait(sync, ev[('dep', i)])
            ev[('dst', i)] = sSTp[b].inc(
                sync.dma_start(out=io['r3'][:, :, t0:t0 + T], in_=OT[b][:, :, 0:T]), 16)
        on('sync', f_dst)

    def f_dfin(sync):
        sSTp[(nt - 1) % 2].wait(sync, ev[('dst', nt - 1)])
        sSTp[(nt - 2) % 2].wait(sync, ev[('dst', nt - 2)])
    on('sync', f_dfin)


def emit_phase_l1f(ctx, block, io, vb, sems):
    TS = min(512, TCORE)
    TT = TS
    NTT = TS // TT
    NS = TCORE // TS
    NH = 32
    sPE, sACT, sDVE, sGPS = sems['pe'], sems['act'], sems['dve'], sems['gps']
    sLDp = [sems['ld0'], sems['ld1']]
    sSTp = [sems['st0'], sems['st1']]
    sWL = [sems['wl0'], sems['wl1']]
    ev = {}

    LNB = ctx.sb("f_lnb", [128, NCI, TS], BF16)
    GX = ctx.sb("f_gx", [128, NH, TS], BF16)
    GY = ctx.sb("f_gy", [128, NH, TS], BF16)
    XT = ctx.sb("f_xt", [128, NCI, TT], F32)
    XT4 = ctx.sb("f_xt4", [128, NCI, TT], F32)
    WB1 = [ctx.sb(f"f_wb1{k}", [128, NCI, 128], BF16) for k in range(2)]
    WB2 = [ctx.sb(f"f_wb2{k}", [128, NCI, 128], BF16) for k in range(2)]
    WH = [ctx.sb(f"f_wh{k}", [128, NH, 128], BF16) for k in range(2)]
    W3B = [ctx.sb(f"f_w3b{k}", [128, NH, 128], BF16) for k in range(2)]
    SILT = [ctx.sb(f"f_silt{k}", [128, TT], BF16) for k in range(2)]
    OX = ctx.sb("f_ox", [128, TT], F32)
    OTS = [ctx.sb(f"f_ots{k}", [128, TT], F32) for k in range(2)]
    ln = LNHelper(ctx, sems, vb, "f", TT, nbuf=1)
    p1 = [ctx.ps(f"f_p1{k}", [128, TT]) for k in range(2)]
    p2 = [ctx.ps(f"f_p2{k}", [128, TT]) for k in range(2)]

    def on(name, f):
        getattr(block, name)(f)

    def mkV(vector):
        def V(fn, *args, **kw):
            sDVE.wait(vector, sDVE.n)
            return sDVE.inc(getattr(vector, fn)(*args, **kw))
        return V

    def mkA(scalar):
        def A(fn, *args, **kw):
            sACT.wait(scalar, sACT.n)
            return sACT.inc(getattr(scalar, fn)(*args, **kw))
        return A

    for s in range(NS):
        S0 = s * TS

        for tt in range(NTT):
            i = s * NTT + tt
            t0 = S0 + tt * TT

            def f_load(sync, i=i, t0=t0, s=s, tt=tt):
                if i >= 1:
                    sDVE.wait(sync, ev[('ln2', i - 1)])
                    sACT.wait(sync, ev[('sq', i - 1)])
                ev[('fxld', i)] = sLDp[0].dma(
                    sync, out=XT[:], in_=io['r3'][:, :, t0:t0 + TT])
            on('sync', f_load)

            def ld_wait(eng, i=i):
                sLDp[0].wait(eng, ev[('fxld', i)])

            def tgt(vector, V, ci, i=i, s=s, tt=tt):
                if ci == 0 and s >= 1:
                    sPE.wait(vector, ev[('f1done', s - 1)])
                return V('tensor_add', LNB[:, ci, tt * TT:(tt + 1) * TT],
                         ln.LNT[:, ci, 0:TT], ln.p_b[:, 0:TT])

            ln.emit(block, ev, i, 0, XT, TT, tgt, ld_wait)

        for oc in range(NH):
            kb = oc % 2

            def f_wl(g, s=s, oc=oc, kb=kb):
                if oc >= 2:
                    sPE.wait(g, ev[('f1mm', s, oc - 2)])
                elif s >= 1:
                    sPE.wait(g, ev[('f1mm', s - 1, NH - 2 + oc)])
                if s == 0:
                    sWL[kb].dma(g, out=WB1[kb][:], in_=io['w1'][oc, :, :, :])
                    ev[('f1w', s, oc)] = sWL[kb].dma(g, out=WB2[kb][:], in_=io['w2'][oc, :, :, :])
                    sWL[kb].dma(g, out=io['w1b'][oc, :, :, :], in_=WB1[kb][:])
                    ev[('f1wb', oc)] = sWL[kb].dma(g, out=io['w2b'][oc, :, :, :], in_=WB2[kb][:])
                else:
                    sWL[kb].dma(g, out=WB1[kb][:], in_=io['w1b'][oc, :, :, :])
                    ev[('f1w', s, oc)] = sWL[kb].dma(g, out=WB2[kb][:], in_=io['w2b'][oc, :, :, :])
            on('gpsimd', f_wl)

            for tt in range(NTT):
                def f_mm(ten, s=s, oc=oc, tt=tt, kb=kb):
                    if tt == 1:
                        sACT.wait(ten, ev[('f1silu', s, oc, 0)])
                        sDVE.wait(ten, ev[('f1stt', s, oc, 0)])
                    if tt == 0:
                        sWL[kb].wait(ten, ev[('f1w', s, oc)])
                        if oc == 0:
                            sDVE.wait(ten, ev[('ln2', s * NTT + NTT - 1)])
                        if oc >= 2:
                            sDVE.wait(ten, ev[('f1stt', s, oc - 2, NTT - 1)])
                        elif s >= 1:
                            sDVE.wait(ten, ev[('f1stt', s - 1, NH - 2 + oc, NTT - 1)])
                            sACT.wait(ten, ev[('f4ox', s - 1, NTT - 1, 7)])
                    for ci in range(NCI):
                        ten.matmul(p1[oc % 2][:, 0:TT], WB1[kb][:, ci, :],
                                   LNB[:, ci, tt * TT:(tt + 1) * TT],
                                   start=(ci == 0), stop=(ci == NCI - 1))
                    for ci in range(NCI):
                        ins = ten.matmul(p2[oc % 2][:, 0:TT], WB2[kb][:, ci, :],
                                         LNB[:, ci, tt * TT:(tt + 1) * TT],
                                         start=(ci == 0), stop=(ci == NCI - 1))
                    ev[('f1mm', s, oc)] = ev[('f1mmt', s, oc, tt)] = sPE.inc(ins)
                on('tensor', f_mm)

                def f_silu(scalar, s=s, oc=oc, tt=tt):
                    sPE.wait(scalar, ev[('f1mmt', s, oc, tt)])
                    if oc >= 2:
                        sDVE.wait(scalar, ev[('f1stt', s, oc - 2, NTT - 1)])
                    elif s >= 1:
                        sDVE.wait(scalar, ev[('f1stt', s - 1, NH - 2 + oc, NTT - 1)])
                    ev[('f1silu', s, oc, tt)] = mkA(scalar)(
                        'activation', SILT[oc % 2][:, 0:TT], p1[oc % 2][:, 0:TT], AF.Silu,
                        scale=vb['s1v'][:, oc:oc + 1])
                on('scalar', f_silu)

                def f_stt(vector, s=s, oc=oc, tt=tt):
                    sACT.wait(vector, ev[('f1silu', s, oc, tt)])
                    if oc == 0 and tt == 0 and s >= 1:
                        sPE.wait(vector, ev[('f4mm', s - 1, NTT - 1, 7)])
                    ev[('f1stt', s, oc, tt)] = mkV(vector)(
                        'scalar_tensor_tensor', GX[:, oc, tt * TT:(tt + 1) * TT],
                        p2[oc % 2][:, 0:TT], vb['s2v'][:, oc:oc + 1], SILT[oc % 2][:, 0:TT],
                        ALU.mult, ALU.mult)
                on('vector', f_stt)
        ev[('f1done', s)] = ev[('f1mm', s, NH - 1)]

        for stage, (SRC, DST, wkey, skey) in enumerate((
                (GX, GY, 'h1', 'h1s'), (GY, GX, 'h2', 'h2s'))):
            for oc in range(NH):
                kb = oc % 2

                def f_whl(g, s=s, oc=oc, kb=kb, stage=stage, wkey=wkey):
                    if oc >= 2:
                        sPE.wait(g, ev[('fhmm', s, stage, oc - 2)])
                    elif stage >= 1:
                        sPE.wait(g, ev[('fhmm', s, 0, NH - 2 + oc)])
                    elif s >= 1:
                        sPE.wait(g, ev[('f4mm', s - 1, NTT - 1, 6 + oc)])
                    if s == 0:
                        ev[('fhw', s, stage, oc)] = sWL[kb].dma(
                            g, out=WH[kb][:], in_=io[wkey][oc, :, :, :])
                        ev[('fhwb', stage, oc)] = sWL[kb].dma(
                            g, out=io[wkey + 'b'][oc, :, :, :], in_=WH[kb][:])
                    else:
                        ev[('fhw', s, stage, oc)] = sWL[kb].dma(
                            g, out=WH[kb][:], in_=io[wkey + 'b'][oc, :, :, :])
                on('gpsimd', f_whl)

                for tt in range(NTT):
                    def f_hmm(ten, s=s, oc=oc, tt=tt, kb=kb, stage=stage, SRC=SRC):
                        if tt == 1:
                            sACT.wait(ten, ev[('fg', s, stage, oc, 0)])
                        if tt == 0:
                            sWL[kb].wait(ten, ev[('fhw', s, stage, oc)])
                            if oc == 0:
                                if stage == 0:
                                    sDVE.wait(ten, ev[('f1stt', s, NH - 1, NTT - 1)])
                                else:
                                    sACT.wait(ten, ev[('fg', s, 0, NH - 1, NTT - 1)])
                            if oc >= 2:
                                sACT.wait(ten, ev[('fg', s, stage, oc - 2, NTT - 1)])
                            elif stage == 1:
                                sACT.wait(ten, ev[('fg', s, 0, NH - 2 + oc, NTT - 1)])
                            elif oc < 2:
                                sACT.wait(ten, ev[('f1silu', s, NH - 2 + oc, NTT - 1)])
                        for hc in range(NH):
                            ins = ten.matmul(p1[oc % 2][:, 0:TT], WH[kb][:, hc, :],
                                             SRC[:, hc, tt * TT:(tt + 1) * TT],
                                             start=(hc == 0), stop=(hc == NH - 1))
                        ev[('fhmm', s, stage, oc)] = ev[('fhmmt', s, stage, oc, tt)] = sPE.inc(ins)
                    on('tensor', f_hmm)

                    def f_gelu(scalar, s=s, oc=oc, tt=tt, stage=stage, DST=DST, skey=skey):
                        sPE.wait(scalar, ev[('fhmmt', s, stage, oc, tt)])
                        if oc == 0 and tt == 0:
                            if stage == 0 and s >= 1:
                                sPE.wait(scalar, ev[('fhmm', s - 1, 1, NH - 1)])
                            elif stage == 1:
                                sPE.wait(scalar, ev[('fhmm', s, 0, NH - 1)])
                        ev[('fg', s, stage, oc, tt)] = mkA(scalar)(
                            'activation', DST[:, oc, tt * TT:(tt + 1) * TT],
                            p1[oc % 2][:, 0:TT], AF.Gelu, scale=vb[skey][:, oc:oc + 1])
                    on('scalar', f_gelu)
        ev[('f2done', s)] = ev[('fhmm', s, 1, NH - 1)]

        for tt in range(NTT):
            t0 = S0 + tt * TT

            def f_x4(sync, s=s, tt=tt, t0=t0):
                if s + tt >= 1:
                    sDVE.wait(sync, ev[('f4ep', s, tt - 1) if tt >= 1 else ('f4ep', s - 1, NTT - 1)])
                ev[('f4x', s, tt)] = sLDp[1].dma(
                    sync, out=XT4[:], in_=io['r3'][:, :, t0:t0 + TT])
            on('sync', f_x4)

            for oc in range(NCI):
                kb = oc % 2

                def f_w3l(g, s=s, tt=tt, oc=oc, kb=kb):
                    if oc >= 2:
                        sPE.wait(g, ev[('f4mm', s, tt, oc - 2)])
                    elif tt >= 1:
                        sPE.wait(g, ev[('f4mm', s, tt - 1, 6 + oc)])
                    elif s >= 1:
                        sPE.wait(g, ev[('f4mm', s - 1, NTT - 1, 6 + oc)])
                    if s == 0 and tt == 0:
                        sWL[kb].dma(g, out=W3B[kb][:], in_=io['w3'][oc, :, :, :])
                        ev[('f4w', s, tt, oc)] = sWL[kb].dma(
                            g, out=io['w3b'][oc, :, :, :], in_=W3B[kb][:])
                    else:
                        ev[('f4w', s, tt, oc)] = sWL[kb].dma(
                            g, out=W3B[kb][:], in_=io['w3b'][oc, :, :, :])
                on('gpsimd', f_w3l)

                def f_mm3(ten, s=s, tt=tt, oc=oc, kb=kb):
                    sWL[kb].wait(ten, ev[('f4w', s, tt, oc)])
                    if oc == 0 and tt == 0:
                        sACT.wait(ten, ev[('fg', s, 1, NH - 1, NTT - 1)])
                    if oc >= 2:
                        sACT.wait(ten, ev[('f4ox', s, tt, oc - 2)])
                    elif tt >= 1:
                        sACT.wait(ten, ev[('f4ox', s, tt - 1, 6 + oc)])
                    elif oc < 2:
                        sACT.wait(ten, ev[('fg', s, 1, NH - 2 + oc, NTT - 1)])
                    for hc in range(NH):
                        ins = ten.matmul(p1[oc % 2][:, 0:TT], W3B[kb][:, hc, :],
                                         GX[:, hc, tt * TT:(tt + 1) * TT],
                                         start=(hc == 0), stop=(hc == NH - 1))
                    ev[('f4mm', s, tt, oc)] = sPE.inc(ins)
                on('tensor', f_mm3)

                def f_ox(scalar, s=s, tt=tt, oc=oc):
                    sPE.wait(scalar, ev[('f4mm', s, tt, oc)])
                    if oc >= 1:
                        sDVE.wait(scalar, ev[('f4add', s, tt, oc - 1)])
                    elif tt >= 1:
                        sDVE.wait(scalar, ev[('f4add', s, tt - 1, 7)])
                    elif s >= 1:
                        sDVE.wait(scalar, ev[('f4add', s - 1, NTT - 1, 7)])
                    ev[('f4ox', s, tt, oc)] = mkA(scalar)(
                        'activation', OX[:, 0:TT], p1[oc % 2][:, 0:TT], AF.Identity,
                        scale=vb['s3v'][:, oc:oc + 1])
                on('scalar', f_ox)

                def f_add(vector, s=s, tt=tt, oc=oc):
                    sACT.wait(vector, ev[('f4ox', s, tt, oc)])
                    sLDp[1].wait(vector, ev[('f4x', s, tt)])
                    ob = oc % 2
                    prev = None
                    if oc >= 2:
                        prev = (s, tt, oc - 2)
                    elif tt >= 1:
                        prev = (s, tt - 1, 6 + oc)
                    elif s >= 1:
                        prev = (s - 1, NTT - 1, 6 + oc)
                    if prev is not None:
                        sSTp[ob].wait(vector, ev[('fyst',) + prev])
                    ev[('f4add', s, tt, oc)] = ev[('f4ep', s, tt)] = mkV(vector)(
                        'tensor_add', OTS[ob][:, 0:TT], OX[:, 0:TT], XT4[:, oc, 0:TT])
                on('vector', f_add)

                def f_yst(sync, s=s, tt=tt, oc=oc, t0=t0):
                    ob = oc % 2
                    sDVE.wait(sync, ev[('f4add', s, tt, oc)])
                    ev[('fyst', s, tt, oc)] = sSTp[ob].dma(
                        sync, out=io['y'][:, oc, t0:t0 + TT], in_=OTS[ob][:, 0:TT])
                on('sync', f_yst)

    def f_fin(sync):
        sSTp[0].wait(sync, ev[('fyst', NS - 1, NTT - 1, 6)])
        sSTp[1].wait(sync, ev[('fyst', NS - 1, NTT - 1, 7)])
        if DBG:
            sLDp[0].dma(sync, out=io['dbg_lnb'][:, :, 0:TS], in_=LNB[:])
            sLDp[0].dma(sync, out=io['dbg_gx'][:, :, 0:TS], in_=GX[:])
            sLDp[0].dma(sync, out=io['dbg_gy'][:, :, 0:TS], in_=GY[:])
            sLDp[0].wait(sync, sLDp[0].n)
    on('sync', f_fin)



ALL_PHASES = ('l0a', 'l0f', 'l1aA', 'l1aBCD', 'l1f')


def build_program(phases=ALL_PHASES, io_external=(), io_extin=(), cc_groups=None):
    if cc_groups is None:
        cc_groups = [[0, 1, 2, 3], [4, 5, 6, 7]]
    nc = bass.Bass("TRN2", target_bir_lowering=False)
    es = contextlib.ExitStack()
    ctx = Ctx(nc, es)
    with es:
        def dram(name, shape, dtype=F32, ext_in=False):
            if ext_in:
                kind = "ExternalInput"
            elif name in io_external or (DBG and name.startswith("dbg_")):
                kind = "ExternalOutput"
            else:
                kind = "Internal"
            return nc.dram_tensor(name, shape, dtype, kind=kind).ap()

        io = {}
        io['x'] = dram("x", [128, NCI, NTOK], ext_in=True)
        io['r1'] = dram("r1", [128, NCI, NTOK])
        io['r2'] = dram("r2", [128, NCI, NTOK], ext_in=('r2' in io_extin))
        io['qf'] = dram("qf", [128, NCI, TCORE], BF16)
        io['kf'] = dram("kf", [128, NCI, TCORE], BF16)
        io['kt'] = dram("kt", [TCORE // 128, 128, H], BF16)
        io['vt'] = dram("vt", [TCORE // 128, 128, H], BF16)
        io['qsi'] = dram("qsi", [128, NCI, TCORE], BF16)
        io['r3'] = dram("r3", [128, NCI, TCORE], ext_in=('r3' in io_extin))
        io['y'] = dram("y", [128, NCI, TCORE], ext_in=False)
        io['cc1i'] = dram("cc1i", [128, 2 * NCI])
        io['cc1o'] = dram("cc1o", [128, 2 * NCI])
        io['cc2i'] = dram("cc2i", [128, 2 * NCI])
        io['cc2o'] = dram("cc2o", [128, 2 * NCI])
        io['cc3i'] = dram("cc3i", [128, NCI * 128])
        io['cc3o'] = dram("cc3o", [128, NCI * 128])
        io['cc4i'] = dram("cc4i", [16, 1])
        io['cc4o'] = dram("cc4o", [16, 1])
        if DBG:
            io['dbg_sa'] = dram("dbg_sa", [16, TCORE], ext_in=False)
            io['dbg_kvg'] = dram("dbg_kvg", [128, NCI, 128])
            io['dbg_kvsb'] = dram("dbg_kvsb", [128, NCI, 128], BF16)
            io['dbg_zrow'] = dram("dbg_zrow", [16, 2])
            io['dbg_zrep'] = dram("dbg_zrep", [128, NCI])
            io['dbg_t12'] = dram("dbg_t12", [128, 2, NCI])
            io['dbg_at1'] = dram("dbg_at1", [128, NCI, 16], BF16)
            io['dbg_at2'] = dram("dbg_at2", [128, NCI, 16], BF16)
            io['dbg_ec'] = dram("dbg_ec", [16, 4])
            io['dbg_vscb'] = dram("dbg_vscb", [128, 2, H], BF16)
            io['dbg_esb'] = dram("dbg_esb", [16, 2, 256], BF16)
            io['dbg_ktt'] = dram("dbg_ktt", [128, 2, H], BF16)
            io['dbg_etm'] = dram("dbg_etm", [128, H])
            io['dbg_lnb'] = dram("dbg_lnb", [128, NCI, 1024], BF16)
            io['dbg_gx'] = dram("dbg_gx", [128, 32, 1024], BF16)
            io['dbg_gy'] = dram("dbg_gy", [128, 32, 1024], BF16)
            io['dbg_flnt'] = dram("dbg_flnt", [128, NCI, 512])
            io['dbg_frows'] = dram("dbg_frows", [1, 1, 5, 512])

        wnames = []
        if 'l0a' in phases:
            wnames += ['l0a_wv', 'l0a_wr', 'l0a_wo']
        if 'l0f' in phases:
            wnames += ['l0f_wk', 'l0f_wr', 'l0f_wv']
        if 'l1aA' in phases:
            wnames += ['l1a_wq', 'l1a_wk', 'l1a_wv']
        if 'l1aBCD' in phases:
            wnames += ['l1a_wo']
        dw = {n: dram(n, [128, NCI, H], ext_in=True) for n in wnames}
        if 'l1f' in phases:
            io['w1'] = dram("w1", [32, 128, NCI, 128], ext_in=True)
            io['w2'] = dram("w2", [32, 128, NCI, 128], ext_in=True)
            io['h1'] = dram("h1", [32, 128, 32, 128], ext_in=True)
            io['h2'] = dram("h2", [32, 128, 32, 128], ext_in=True)
            io['w3'] = dram("w3", [NCI, 128, 32, 128], ext_in=True)
            io['s1v'] = dram("s1v", [128, 32], ext_in=True)
            io['s2v'] = dram("s2v", [128, 32], ext_in=True)
            io['h1sv'] = dram("h1sv", [128, 32], ext_in=True)
            io['h2sv'] = dram("h2sv", [128, 32], ext_in=True)
            io['s3v'] = dram("s3v", [128, NCI], ext_in=True)
            io['w1b'] = dram("w1b", [32, 128, NCI, 128], BF16)
            io['w2b'] = dram("w2b", [32, 128, NCI, 128], BF16)
            io['h1b'] = dram("h1b", [32, 128, 32, 128], BF16)
            io['h2b'] = dram("h2b", [32, 128, 32, 128], BF16)
            io['w3b'] = dram("w3b", [NCI, 128, 32, 128], BF16)
        d_ident = dram("ident", [128, 128], ext_in=True)
        d_selb = dram("selb", [NCI, 16, 128], ext_in=True)
        d_selrow = dram("selrow", [16, H], ext_in=True)

        vnames = ['td', 'tmv', 'tmr', 'mur', 'muk', 'bq', 'bk', 'bv', 'bo']
        dv = {n: dram("vec_" + n, [128, NCI], ext_in=True) for n in vnames}

        vb = {n: ctx.sb("v_" + n, [128, NCI], F32) for n in vnames}
        vb['d'] = ctx.sb("v_d", [128, NCI], F32)
        vb['dneg'] = ctx.sb("v_dneg", [128, NCI], F32)
        vb['consts'] = ctx.sb("v_consts", [128, 4], F32)
        vb['ones_inv'] = ctx.sb("v_onesinv", [128, 1], BF16)
        vb['ones1'] = ctx.sb("v_ones1", [1, 128], F32)
        vb['nones1'] = ctx.sb("v_nones1", [1, 128], F32)
        vb['idf'] = ctx.sb("v_idf", [128, 128], F32)
        vb['idb'] = ctx.sb("v_idb", [128, 128], BF16)
        vb['selbf'] = ctx.sb("v_selbf", [16, NCI, 128], F32)
        vb['selb'] = ctx.sb("v_selb", [16, NCI, 128], BF16)
        vb['selrowf'] = ctx.sb("v_selrowf", [16, H], F32)
        vb['selrow'] = ctx.sb("v_selrow", [16, H], BF16)
        vb['onesb'] = ctx.sb("v_onesb", [128, 1], BF16)
        if 'l1f' in phases:
            vb['s1v'] = ctx.sb("v_s1v", [128, 32], F32)
            vb['s2v'] = ctx.sb("v_s2v", [128, 32], F32)
            vb['h1s'] = ctx.sb("v_h1s", [128, 32], F32)
            vb['h2s'] = ctx.sb("v_h2s", [128, 32], F32)
            vb['s3v'] = ctx.sb("v_s3v", [128, NCI], F32)

        if 'l1aA' in phases:
            nt_a = len(tiles_of(TCORE, 256))
            QSP = ctx.sb("a_qsp", [128, NCI, nt_a], F32)
            KSP = ctx.sb("a_ksp", [128, NCI, nt_a], F32)

        wscope = contextlib.ExitStack()
        wscope_closed = False
        ctx.scope = wscope
        wslot = [ctx.sb(f"wslot{k}", [128, NCI, H], BF16) for k in range(4)]
        ctx.scope = es

        semnames = ('pe', 'act', 'dve', 'gps', 'wld',
                    'ld0', 'ld1', 'st0', 'st1', 'cc',
                    'lda0', 'lda1', 'ldb0', 'ldb1', 'ldc0', 'ldc1',
                    'sta0', 'sta1', 'stb0', 'stb1', 'stc0', 'stc1',
                    'wl0', 'wl1')
        sems = {k: ctx.sem("s_" + k) for k in semnames}
        ph = Phases(ctx, sems)

        with nc.Block() as blk0:
            def f_vload(s):
                for n in vnames:
                    sems['ld0'].dma(s, out=vb[n][:], in_=dv[n])
                sems['ld0'].dma(s, out=vb['idf'][:], in_=d_ident)
                sems['ld0'].dma(s, 
                    out=vb['selbf'][:], in_=d_selb.rearrange("c h j -> h c j"))
                sems['ld0'].dma(s, out=vb['selrowf'][:], in_=d_selrow)
                if 'l1f' in phases:
                    for a, bnm in (('s1v', 's1v'), ('s2v', 's2v'), ('h1s', 'h1sv'),
                                   ('h2s', 'h2sv'), ('s3v', 's3v')):
                        sems['ld0'].dma(s, out=vb[a][:], in_=io[bnm])
            blk0.sync(f_vload)

            def f_consts(v):
                sems['ld0'].wait(v, sems['ld0'].n)
                sDVE = sems['dve']
                for ins in (
                    v.memset(vb['consts'][:, 0:1], EPS_LN),
                    v.memset(vb['consts'][:, 1:2], EPS_A),
                    v.memset(vb['ones_inv'][:], 1.0 / H),
                    v.memset(vb['ones1'][:], 1.0),
                    v.memset(vb['nones1'][:], -1.0),
                    v.tensor_copy(vb['idb'][:], vb['idf'][:]),
                    v.tensor_copy(vb['selb'][:], vb['selbf'][:]),
                    v.tensor_copy(vb['selrow'][:], vb['selrowf'][:]),
                    v.memset(vb['onesb'][:], 1.0),
                ):
                    sDVE.inc(ins)
            blk0.vector(f_consts)

            def f_d(sc):
                sems['dve'].wait(sc, sems['dve'].n)
                sems['act'].inc(sc.activation(vb['dneg'][:], vb['td'][:], AF.Exp))
                sems['act'].wait(sc, sems['act'].n)
                sems['act'].inc(sc.activation(vb['d'][:], vb['dneg'][:], AF.Exp, scale=-1.0))
            blk0.scalar(f_d)
        ph.snapshot()

        if 'l0a' in phases:
            with contextlib.ExitStack() as ps:
                ctx.scope = ps
                with nc.Block() as blk:
                    ph.emit_guard(blk)
                    load_phase_weights(blk, sems, [
                        (wslot[0][:], dw['l0a_wv']),
                        (wslot[1][:], dw['l0a_wr']),
                        (wslot[2][:], dw['l0a_wo'])])
                    wv = sems['wld'].n

                    def f_w(t):
                        sems['wld'].wait(t, wv)
                    blk.tensor(f_w)
                    emit_phase_l0(ctx, blk, io['x'], io['r1'],
                                  wslot[0], wslot[1], wslot[2], vb, sems, 'attn')
                ctx.scope = es
            ph.snapshot()

        if 'l0f' in phases:
            with contextlib.ExitStack() as ps:
                ctx.scope = ps
                with nc.Block() as blk:
                    ph.emit_guard(blk)
                    load_phase_weights(blk, sems, [
                        (wslot[0][:], dw['l0f_wk']),
                        (wslot[1][:], dw['l0f_wr']),
                        (wslot[2][:], dw['l0f_wv'])])
                    wv = sems['wld'].n

                    def f_w(t):
                        sems['wld'].wait(t, wv)
                    blk.tensor(f_w)
                    emit_phase_l0(ctx, blk, io['r1'], io['r2'],
                                  wslot[0], wslot[1], wslot[2], vb, sems, 'ffn')
                ctx.scope = es
            ph.snapshot()

        if 'l1aA' in phases:
            with contextlib.ExitStack() as ps:
                ctx.scope = ps
                with nc.Block() as blk:
                    ph.emit_guard(blk)
                    wpairs = [
                        (wslot[0][:], dw['l1a_wq']),
                        (wslot[1][:], dw['l1a_wk']),
                        (wslot[2][:], dw['l1a_wv'])]
                    load_phase_weights(blk, sems, wpairs)
                    wv = sems['wld'].n

                    def f_w(t):
                        sems['wld'].wait(t, wv)
                    blk.tensor(f_w)
                    emit_phase_l1a_A(ctx, blk, io, wslot[0], wslot[1],
                                     wslot[2], vb, sems, QSP, KSP)
                ctx.scope = es
            ph.snapshot()

        if 'l1aBCD' in phases:
            wscope.close()
            wscope_closed = True
            io['dw_l1a_wo'] = dw['l1a_wo']
            with contextlib.ExitStack() as ps:
                ctx.scope = ps
                with nc.Block() as blk:
                    ph.emit_guard(blk)
                    emit_l1a_BCD(ctx, blk, io, vb, sems,
                                 QSP, KSP, nt_a, cc_groups)
                ctx.scope = es
            ph.snapshot()

        if 'l1f' in phases:
            if not wscope_closed:
                wscope.close()
                wscope_closed = True
            with contextlib.ExitStack() as ps:
                ctx.scope = ps
                with nc.Block() as blk:
                    ph.emit_guard(blk)
                    emit_phase_l1f(ctx, blk, io, vb, sems)
                ctx.scope = es
            ph.snapshot()

        return nc



def _fm(x2d):
    T = x2d.shape[0]
    return np.ascontiguousarray(x2d.T.reshape(NCI, 128, T).transpose(1, 0, 2))


def _unfm(a):
    return a.transpose(1, 0, 2).reshape(H, -1).T


def _wT(W):
    return np.ascontiguousarray(W.T.reshape(NCI, 128, W.shape[0]).transpose(1, 0, 2))


def _vec(v):
    return np.ascontiguousarray(np.asarray(v, np.float32).reshape(NCI, 128).T)


def make_core_inputs(full, phases=ALL_PHASES):
    x = np.asarray(full['L0_attn_x'])
    per = {}
    if 'l0a' in phases:
        per['l0a_wv'] = _wT(np.asarray(full['L0_attn_Wv']))
        per['l0a_wr'] = _wT(np.asarray(full['L0_attn_Wr']))
        per['l0a_wo'] = _wT(np.asarray(full['L0_attn_Wo']))
    if 'l0f' in phases:
        per['l0f_wk'] = _wT(np.asarray(full['L0_ffn_W_k']))
        per['l0f_wr'] = _wT(np.asarray(full['L0_ffn_W_r']))
        per['l0f_wv'] = _wT(np.asarray(full['L0_ffn_W_v']))
    if 'l1aA' in phases:
        per['l1a_wq'] = _wT(np.asarray(full['L1_attn_Wq']))
        per['l1a_wk'] = _wT(np.asarray(full['L1_attn_Wk']))
        per['l1a_wv'] = _wT(np.asarray(full['L1_attn_Wv']))
    if 'l1aBCD' in phases:
        per['l1a_wo'] = _wT(np.asarray(full['L1_attn_Wo']))
    per['vec_td'] = _vec(full['L0_attn_time_decay'])
    per['vec_tmv'] = _vec(full['L0_attn_time_mix_v'])
    per['vec_tmr'] = _vec(full['L0_attn_time_mix_r'])
    per['vec_mur'] = _vec(full['L0_ffn_mu_r'])
    per['vec_muk'] = _vec(full['L0_ffn_mu_k'])
    per['vec_bq'] = _vec(full['L1_attn_bq'])
    per['vec_bk'] = _vec(full['L1_attn_bk'])
    per['vec_bv'] = _vec(full['L1_attn_bv'])
    per['vec_bo'] = _vec(full['L1_attn_bo'])
    per['ident'] = np.eye(128, dtype=np.float32)
    selb = np.zeros((NCI, 16, 128), np.float32)
    for ci in range(NCI):
        selb[ci, 2 * ci, 0:64] = 1.0
        selb[ci, 2 * ci + 1, 64:128] = 1.0
    per['selb'] = selb
    selrow = np.zeros((16, H), np.float32)
    for hh in range(16):
        selrow[hh, hh * 64:(hh + 1) * 64] = 1.0
    per['selrow'] = selrow
    if 'l1f' in phases:
        def blk4(W, no):
            W = np.asarray(W, np.float32)
            return np.ascontiguousarray(
                W.reshape(no, 128, W.shape[1] // 128, 128).transpose(0, 3, 2, 1))
        per['w1'] = blk4(full['L1_ffn_w1_w'], 32)
        per['w2'] = blk4(full['L1_ffn_w2_w'], 32)
        per['h1'] = blk4(full['L1_ffn_h1_w'], 32)
        per['h2'] = blk4(full['L1_ffn_h2_w'], 32)
        per['w3'] = blk4(full['L1_ffn_w3_w'], NCI)
        per['s1v'] = np.ascontiguousarray(np.asarray(full['L1_ffn_w1_s'], np.float32).reshape(32, 128).T)
        per['s2v'] = np.ascontiguousarray(np.asarray(full['L1_ffn_w2_s'], np.float32).reshape(32, 128).T)
        per['h1sv'] = np.ascontiguousarray(np.asarray(full['L1_ffn_h1_s'], np.float32).reshape(32, 128).T)
        per['h2sv'] = np.ascontiguousarray(np.asarray(full['L1_ffn_h2_s'], np.float32).reshape(32, 128).T)
        per['s3v'] = np.ascontiguousarray(np.asarray(full['L1_ffn_w3_s'], np.float32).reshape(NCI, 128).T)
    maps = []
    for c in range(NCORES):
        bidx, s = divmod(c, 4)
        s0 = s * TCORE
        if s0 == 0:
            halo = np.zeros((HALO, H), np.float32)
        else:
            halo = x[bidx, s0 - HALO:s0]
        xs = np.concatenate([halo, x[bidx, s0:s0 + TCORE]], axis=0)
        m = {'x': _fm(xs)}
        m.update(per)
        maps.append(m)
    return maps


def kernel(**inputs):
    maps = make_core_inputs(inputs, phases=ALL_PHASES)
    nc = build_program(phases=ALL_PHASES, io_external=('y',))
    res = run_bass_kernel_spmd(nc, maps, core_ids=list(range(NCORES)))
    out = np.empty((2, 4 * TCORE, H), np.float32)
    for c in range(NCORES):
        y = np.asarray(res.results[c]['y']).reshape(128, NCI, TCORE)
        bidx, s = divmod(c, 4)
        out[bidx, s * TCORE:(s + 1) * TCORE] = _unfm(y.astype(np.float32))
    return out


# version 10
# speedup vs baseline: 1.3937x; 1.0695x over previous
import contextlib
import numpy as np
import concourse.bass as bass
import concourse.mybir as mybir
from concourse.bass_utils import run_bass_kernel_spmd

F32 = mybir.dt.float32
BF16 = mybir.dt.bfloat16
AF = mybir.ActivationFunctionType
ALU = mybir.AluOpType

H = 1024
NCI = 8
HALO = 128
TCORE = 4096
NTOK = TCORE + HALO
NCORES = 8
EPS_LN = 1e-5
EPS_A = 1e-6
SSCALE = 16384.0
DBG = False


def tiles_of(total, step, base=0):
    out = []
    t = 0
    while t < total:
        out.append((base + t, min(step, total - t)))
        t += step
    return out


class Sem:
    def __init__(self, ctx, name):
        self.h = ctx.es.enter_context(ctx.nc.semaphore(name))
        self.n = 0

    def inc(self, instr, k=1):
        instr.then_inc(self.h, k)
        self.n += k
        return self.n

    def wait(self, eng, val):
        if val > 0:
            eng.wait_ge(self.h, val)

    def dma(self, eng, *args, **kw):
        self.wait(eng, self.n)
        return self.inc(eng.dma_start(*args, **kw), 16)


class Ctx:
    def __init__(self, nc, es):
        self.nc = nc
        self.es = es
        self.scope = es

    def sem(self, name):
        return Sem(self, name)

    def sb(self, name, shape, dtype):
        return self.scope.enter_context(self.nc.sbuf_tensor(name, shape, dtype))

    def ps(self, name, shape, dtype=F32):
        return self.scope.enter_context(self.nc.psum_tensor(name, shape, dtype))


ENGS = ('sync', 'gpsimd', 'scalar', 'vector', 'tensor')


class Phases:

    def __init__(self, ctx, sems):
        self.ctx = ctx
        self.sems = sems
        self.guard = {}

    def snapshot(self):
        self.guard = {k: s.n for k, s in self.sems.items()}

    def emit_guard(self, block):
        g = dict(self.guard)

        def f(eng):
            for k, v in g.items():
                self.sems[k].wait(eng, v)
        for e in ENGS:
            getattr(block, e)(f)


def emit_act_preload(block, sems, func):
    def f(scalar):
        sems['act'].wait(scalar, sems['act'].n)
        sems['act'].inc(scalar.preload_act_table(func))
    block.scalar(f)


def load_phase_weights(block, sems, pairs):
    sW = sems['wld']

    def f(g):
        for dst, src in pairs:
            sW.dma(g, out=dst, in_=src)
    block.gpsimd(f)
    return sW.n



def emit_phase_l0(ctx, block, d_src, d_dst, W1, W2, W3, vb, sems, kind):
    TP = 256
    tls = tiles_of(NTOK, TP)
    XT = [ctx.sb(f"{kind}_xt{b}", [128, NCI, TP], F32) for b in range(2)]
    LNF = ctx.sb(f"{kind}_lnf", [128, NCI, TP + 1], F32)
    PREV = ctx.sb(f"{kind}_prev", [128, NCI, TP + 1], F32)
    UT = ctx.sb(f"{kind}_ut", [128, NCI, TP], F32)
    VIN = [ctx.sb(f"{kind}_vin{b}", [128, NCI, TP], BF16) for b in range(2)]
    RIN = [ctx.sb(f"{kind}_rin{b}", [128, NCI, TP], BF16) for b in range(2)]
    VB = [ctx.sb(f"{kind}_vb{b}", [128, NCI, TP], BF16) for b in range(2)]
    KAF = ctx.sb(f"{kind}_kaf", [128, TP], F32) if kind == 'ffn' else None
    RS = ctx.sb(f"{kind}_rs", [128, NCI, TP], F32)
    OTX = ctx.sb(f"{kind}_otx", [128, TP], F32)
    OT = [ctx.sb(f"{kind}_ot{b}", [128, NCI, TP], F32) for b in range(2)]
    CARRY = ctx.sb(f"{kind}_carry", [128, NCI], F32)
    ln = LNHelper(ctx, sems, vb, kind, TP, gps_apply=True)
    p_mm = [ctx.ps(f"{kind}_pmm{b}", [128, TP]) for b in range(2)]

    sPE, sACT, sDVE = sems['pe'], sems['act'], sems['dve']
    sLDp = [sems['ld0'], sems['ld1']]
    sSTp = [sems['st0'], sems['st1']]
    ev = {}

    def on(name, f):
        getattr(block, name)(f)

    def mkV(vector):
        def V(fn, *args, **kw):
            sDVE.wait(vector, sDVE.n)
            return sDVE.inc(getattr(vector, fn)(*args, **kw))
        return V

    def mkA(scalar):
        def A(fn, *args, **kw):
            sACT.wait(scalar, sACT.n)
            return sACT.inc(getattr(scalar, fn)(*args, **kw))
        return A

    def mkGPS(g):
        def G(fn, *args, **kw):
            sems['gpc'].wait(g, sems['gpc'].n)
            return sems['gpc'].inc(getattr(g, fn)(*args, **kw))
        return G

    if kind == 'attn':
        m1, m2c, dcol = vb['tmv'], vb['tmr'], vb['d']
    else:
        m1, m2c = vb['muk'], vb['mur']

    emit_act_preload(block, sems, AF.Sigmoid)

    def f_init(vector):
        V = mkV(vector)
        V('memset', CARRY[:], 0.0)
        V('memset', PREV[:, :, 0:1], 0.0)
        ev['init'] = V('memset', LNF[:, :, 0:1], 0.0)
    on('vector', f_init)

    for i, (t0, T) in enumerate(tls):
        b = i % 2

        def f_load(sync, i=i, b=b, t0=t0, T=T):
            if i >= 2:
                sDVE.wait(sync, ev[('ep', i - 2)])
                sACT.wait(sync, ev[('sq', i - 2)])
                sems['gpc'].wait(sync, ev[('ln2', i - 2)])
            ev[('xld', i)] = sLDp[b].dma(
                sync, out=XT[b][:, :, 0:T], in_=d_src[:, :, t0:t0 + T])
        on('sync', f_load)

        def ld_wait(eng, i=i, b=b):
            sLDp[b].wait(eng, ev[('xld', i)])

        def tgt(eng, V, ci, pb, i=i, b=b, T=T):
            return V('tensor_add', LNF[:, ci, 1:T + 1], ln.LNT[:, ci, 0:T], pb)

        ln.emit(block, ev, i, b, XT[b], T, tgt, ld_wait)

        if kind == 'attn':
            def f_scan(vector, i=i, T=T):
                sems['gpc'].wait(vector, ev[('ln2', i)])
                V = mkV(vector)
                for ci in range(NCI):
                    V('tensor_copy', PREV[:, ci, 0:1], CARRY[:, ci:ci + 1])
                    V('tensor_tensor_scan',
                      PREV[:, ci, 1:T + 1],
                      dcol[:, ci:ci + 1].broadcast_to([128, T]),
                      LNF[:, ci, 1:T + 1],
                      CARRY[:, ci:ci + 1], ALU.mult, ALU.add)
                    ev[('scan', i)] = V('tensor_copy', CARRY[:, ci:ci + 1], PREV[:, ci, T:T + 1])
            on('vector', f_scan)
            def base_ap(ci, T):
                return PREV[:, ci, 0:T]
        else:
            def base_ap(ci, T):
                return LNF[:, ci, 0:T]

        def f_ut(g, i=i, b=b, T=T):
            if kind == 'attn':
                sDVE.wait(g, ev[('scan', i)])
            G = mkGPS(g)
            for ci in range(NCI):
                ev[('ut', i)] = G('tensor_sub', UT[:, ci, 0:T],
                                  LNF[:, ci, 1:T + 1], base_ap(ci, T))
        on('gpsimd', f_ut)

        def f_mix(vector, i=i, b=b, T=T):
            if i >= 2:
                sPE.wait(vector, ev[('projr', i - 2)])
            sems['gpc'].wait(vector, ev[('ut', i)])
            V = mkV(vector)
            for ci in range(NCI):
                V('scalar_tensor_tensor',
                  VIN[b][:, ci, 0:T], UT[:, ci, 0:T], m1[:, ci:ci + 1],
                  base_ap(ci, T), ALU.mult, ALU.add)
            for ci in range(NCI):
                ev[('mix', i)] = V('scalar_tensor_tensor',
                                   RIN[b][:, ci, 0:T], UT[:, ci, 0:T], m2c[:, ci:ci + 1],
                                   base_ap(ci, T), ALU.mult, ALU.add)
        on('vector', f_mix)

        if kind == 'ffn':
            def f_lncarry(g, i=i, T=T):
                sDVE.wait(g, ev[('mix', i)])
                G = mkGPS(g)
                for ci in range(NCI):
                    ev[('lncarry', i)] = G('tensor_copy', LNF[:, ci, 0:1],
                                           LNF[:, ci, T:T + 1])
            on('gpsimd', f_lncarry)

        def f_projv_pre(ten, i=i):
            sDVE.wait(ten, ev[('mix', i)])
        on('tensor', f_projv_pre)

        for oc in range(8):
            def f_projv(ten, i=i, b=b, T=T, oc=oc):
                if oc >= 2:
                    sACT.wait(ten, ev[('evacv', i, oc - 2)])
                elif i >= 1:
                    sDVE.wait(ten, ev[('epoc', i - 1, 6 + oc)])
                for ci in range(NCI):
                    ins = ten.matmul(p_mm[oc % 2][:, 0:T], W1[:, ci, oc * 128:(oc + 1) * 128],
                                     VIN[b][:, ci, 0:T], start=(ci == 0), stop=(ci == NCI - 1))
                ev[('projv', i, oc)] = sPE.inc(ins)
            on('tensor', f_projv)

            def f_evacv(scalar, i=i, b=b, T=T, oc=oc):
                sPE.wait(scalar, ev[('projv', i, oc)])
                if oc == 0 and i >= 2:
                    sPE.wait(scalar, ev[('projo', i - 2)])
                A = mkA(scalar)
                if kind == 'attn':
                    ev[('evacv', i, oc)] = A('copy', VB[b][:, oc, 0:T], p_mm[oc % 2][:, 0:T])
                else:
                    A('activation', KAF[:, 0:T], p_mm[oc % 2][:, 0:T], AF.Relu)
                    ev[('evacv', i, oc)] = A('activation', VB[b][:, oc, 0:T],
                                             KAF[:, 0:T], AF.Square)
            on('scalar', f_evacv)

        for oc in range(8):
            def f_projr(ten, i=i, b=b, T=T, oc=oc):
                if oc >= 2:
                    sACT.wait(ten, ev[('sigr', i, oc - 2)])
                else:
                    sACT.wait(ten, ev[('evacv', i, 6 + oc)])
                for ci in range(NCI):
                    ins = ten.matmul(p_mm[oc % 2][:, 0:T], W2[:, ci, oc * 128:(oc + 1) * 128],
                                     RIN[b][:, ci, 0:T], start=(ci == 0), stop=(ci == NCI - 1))
                ev[('projr', i, oc)] = sPE.inc(ins)
            on('tensor', f_projr)

            def f_sigr(scalar, i=i, T=T, oc=oc):
                sPE.wait(scalar, ev[('projr', i, oc)])
                if oc == 0 and i >= 1:
                    sDVE.wait(scalar, ev[('ep', i - 1)])
                ev[('sigr', i, oc)] = mkA(scalar)(
                    'activation', RS[:, oc, 0:T], p_mm[oc % 2][:, 0:T], AF.Sigmoid)
            on('scalar', f_sigr)
        ev[('projr', i)] = ev[('projr', i, 7)]

        for oc in range(8):
            def f_projo(ten, i=i, b=b, T=T, oc=oc):
                if oc == 0:
                    sACT.wait(ten, ev[('evacv', i, 7)])
                if oc >= 2:
                    sDVE.wait(ten, ev[('epoc', i, oc - 2)])
                else:
                    sACT.wait(ten, ev[('sigr', i, 6 + oc)])
                for ci in range(NCI):
                    ins = ten.matmul(p_mm[oc % 2][:, 0:T], W3[:, ci, oc * 128:(oc + 1) * 128],
                                     VB[b][:, ci, 0:T], start=(ci == 0), stop=(ci == NCI - 1))
                ev[('projo', i, oc)] = sPE.inc(ins)
            on('tensor', f_projo)

            def f_ep(vector, i=i, b=b, T=T, oc=oc):
                sPE.wait(vector, ev[('projo', i, oc)])
                sACT.wait(vector, ev[('sigr', i, oc)])
                if oc == 0 and i >= 2:
                    sSTp[b].wait(vector, ev[('stored', i - 2)])
                V = mkV(vector)
                V('tensor_mul', OTX[:, 0:T], RS[:, oc, 0:T], p_mm[oc % 2][:, 0:T])
                ev[('epoc', i, oc)] = V('tensor_add', OT[b][:, oc, 0:T],
                                        OTX[:, 0:T], XT[b][:, oc, 0:T])
            on('vector', f_ep)
        ev[('ep', i)] = ev[('epoc', i, 7)]
        ev[('projo', i)] = ev[('projo', i, 7)]

        def f_store(sync, i=i, b=b, t0=t0, T=T):
            sDVE.wait(sync, ev[('ep', i)])
            ev[('stored', i)] = sSTp[b].inc(
                sync.dma_start(out=d_dst[:, :, t0:t0 + T], in_=OT[b][:, :, 0:T]), 16)
        on('sync', f_store)

    def f_fin(sync):
        n = len(tls)
        sSTp[(n - 1) % 2].wait(sync, ev[('stored', n - 1)])
        sSTp[(n - 2) % 2].wait(sync, ev[('stored', n - 2)])
    on('sync', f_fin)




class LNHelper:

    def __init__(self, ctx, sems, vb, prefix, TP, nbuf=2, gps_apply=False):
        self.vb = vb
        self.sems = sems
        self.TP = TP
        self.gps_apply = gps_apply
        if gps_apply:
            self.AB = ctx.sb(f"{prefix}_ab", [128, 2, 2, TP], F32)
        self.SQB = [ctx.sb(f"{prefix}_sqb{b}", [128, NCI, TP], BF16) for b in range(nbuf)]
        self.XB = [ctx.sb(f"{prefix}_xb{b}", [128, NCI, TP], BF16) for b in range(nbuf)]
        self.LNT = ctx.sb(f"{prefix}_lnt", [128, NCI, TP], BF16)
        self.ROWS = ctx.sb(f"{prefix}_rows", [1, 1, 6, TP], F32)
        self.MAGIC = ctx.sb(f"{prefix}_magic", [1, TP], mybir.dt.int32)
        self.magic_set = False
        self.p_sum = ctx.ps(f"{prefix}_psum", [1, TP])
        self.p_sq = ctx.ps(f"{prefix}_psq", [1, TP])
        self.p_a = ctx.ps(f"{prefix}_pa", [128, TP])
        self.p_b = ctx.ps(f"{prefix}_pb", [128, TP])

    def emit(self, block, ev, i, b, XT_b, T, tgt, ld_wait, pe_busy_wait=None):
        sems = self.sems
        sPE, sACT, sDVE = sems['pe'], sems['act'], sems['dve']
        sGPS = sems['gpc']
        vb = self.vb
        SQB, XB, LNT, ROWS = self.SQB, self.XB, self.LNT, self.ROWS
        p_sum, p_sq, p_a, p_b = self.p_sum, self.p_sq, self.p_a, self.p_b
        eps_ln = vb['consts'][0:1, 0:1]
        r = 0

        def on(name, f):
            getattr(block, name)(f)

        def mkV(vector):
            def V(fn, *args, **kw):
                sDVE.wait(vector, sDVE.n)
                return sDVE.inc(getattr(vector, fn)(*args, **kw))
            return V

        def mkA(scalar):
            def A(fn, *args, **kw):
                sACT.wait(scalar, sACT.n)
                return sACT.inc(getattr(scalar, fn)(*args, **kw))
            return A

        def mkG(g):
            def G(fn, *args, **kw):
                sGPS.wait(g, sGPS.n)
                return sGPS.inc(getattr(g, fn)(*args, **kw))
            return G

        def f_sq(scalar, i=i, b=b, T=T):
            ld_wait(scalar)
            if i >= 2:
                sPE.wait(scalar, ev[('stats', i - 2)])
            ev[('sq', i)] = mkA(scalar)(
                'activation', SQB[b][:, :, 0:T], XT_b[:, :, 0:T], AF.Square)
        on('scalar', f_sq)

        if self.gps_apply:
            def f_cast(g, i=i, b=b, T=T):
                ld_wait(g)
                if i >= 2:
                    sPE.wait(g, ev[('stats', i - 2)])
                ev[('xb', i)] = mkG(g)(
                    'tensor_copy', XB[b][:, :, 0:T], XT_b[:, :, 0:T])
            on('gpsimd', f_cast)
        else:
            def f_cast(vector, i=i, b=b, T=T):
                ld_wait(vector)
                if i >= 2:
                    sPE.wait(vector, ev[('stats', i - 2)])
                ev[('xb', i)] = mkV(vector)(
                    'tensor_copy', XB[b][:, :, 0:T], XT_b[:, :, 0:T])
            on('vector', f_cast)

        def f_stats(ten, i=i, b=b, T=T):
            sACT.wait(ten, ev[('sq', i)])
            (sGPS if self.gps_apply else sDVE).wait(ten, ev[('xb', i)])
            if i >= 1:
                sACT.wait(ten, ev[('m2', i - 1)])
                sDVE.wait(ten, ev[('rows', i - 1)])
            if pe_busy_wait is not None:
                pe_busy_wait(ten, i)
            for ci in range(NCI):
                ten.matmul(p_sq[0:1, 0:T], vb['ones_inv'][:], SQB[b][:, ci, 0:T],
                           start=(ci == 0), stop=(ci == NCI - 1))
            for ci in range(NCI):
                ins = ten.matmul(p_sum[0:1, 0:T], vb['ones_inv'][:], XB[b][:, ci, 0:T],
                                 start=(ci == 0), stop=(ci == NCI - 1))
            ev[('stats', i)] = sPE.inc(ins)
        on('tensor', f_stats)

        m2 = ROWS[0:1, r, 0, 0:T]
        var = ROWS[0:1, r, 1, 0:T]
        veps = ROWS[0:1, r, 2, 0:T]
        a_ = ROWS[0:1, r, 3, 0:T]
        mr = ROWS[0:1, r, 4, 0:T]
        tm = ROWS[0:1, r, 5, 0:T]

        def f_m2(scalar, i=i, T=T, m2=m2):
            sPE.wait(scalar, ev[('stats', i)])
            if i >= 1:
                sDVE.wait(scalar, ev[('rows', i - 1)])
            ev[('m2', i)] = mkA(scalar)('activation', m2, p_sum[0:1, 0:T], AF.Square)
        on('scalar', f_m2)

        def f_rows(vector, i=i, T=T, m2=m2, var=var, veps=veps, a_=a_, mr=mr, tm=tm):
            sACT.wait(vector, ev[('m2', i)])
            if i >= 1:
                sPE.wait(vector, ev[('bcast', i - 1)])
            V = mkV(vector)
            if not self.magic_set:
                V('memset', self.MAGIC[:], 0x5f3759df)
                self.magic_set = True
            V('scalar_tensor_tensor', var, m2, -1.0, p_sq[0:1, 0:T], ALU.mult, ALU.add)
            V('tensor_scalar_add', veps, var, EPS_LN)
            vi = veps.bitcast(mybir.dt.int32)
            yi = a_.bitcast(mybir.dt.int32)
            V('tensor_single_scalar', tm.bitcast(mybir.dt.int32), vi, 1,
              ALU.logical_shift_right)
            V('tensor_sub', yi, self.MAGIC[0:1, 0:T], tm.bitcast(mybir.dt.int32))
            for _ in range(3):
                V('tensor_mul', tm, a_, a_)
                V('tensor_mul', var, tm, veps)
                V('tensor_scalar', tm, var, -0.5, 1.5, ALU.mult, ALU.add)
                V('tensor_mul', m2, a_, tm)
                V('tensor_copy', a_, m2)
            ev[('rows', i)] = V('tensor_mul', mr, p_sum[0:1, 0:T], a_)
        on('vector', f_rows)

        def f_bcast(ten, i=i, T=T, a_=a_, mr=mr):
            sDVE.wait(ten, ev[('rows', i)])
            if i >= 1:
                if self.gps_apply:
                    sACT.wait(ten, ev[('ab', i - 1)])
                else:
                    sDVE.wait(ten, ev[('ln2', i - 1)])
            ten.matmul(p_a[:, 0:T], vb['ones1'][:], a_, start=True, stop=True)
            ins = ten.matmul(p_b[:, 0:T], vb['nones1'][:], mr, start=True, stop=True)
            ev[('bcast', i)] = sPE.inc(ins)
        on('tensor', f_bcast)

        if self.gps_apply:
            AB = self.AB

            def f_ab(scalar, i=i, r=r, T=T):
                sPE.wait(scalar, ev[('bcast', i)])
                if i >= 2:
                    sGPS.wait(scalar, ev[('ln2', i - 2)])
                A = mkA(scalar)
                rp = i % 2
                A('copy', AB[:, rp, 0, 0:T], p_a[:, 0:T])
                ev[('ab', i)] = A('copy', AB[:, rp, 1, 0:T], p_b[:, 0:T])
            on('scalar', f_ab)

            def f_ln(g, i=i, b=b, T=T):
                sACT.wait(g, ev[('ab', i)])
                G = mkG(g)
                rp = i % 2
                for ci in range(NCI):
                    G('tensor_mul', LNT[:, ci, 0:T], XT_b[:, ci, 0:T], AB[:, rp, 0, 0:T])
                for ci in range(NCI):
                    ev[('ln2', i)] = tgt(g, G, ci, AB[:, rp, 1, 0:T])
            on('gpsimd', f_ln)
        else:
            def f_ln(vector, i=i, b=b, T=T):
                sPE.wait(vector, ev[('bcast', i)])
                V = mkV(vector)
                for ci in range(NCI):
                    V('tensor_mul', LNT[:, ci, 0:T], XT_b[:, ci, 0:T], p_a[:, 0:T])
                for ci in range(NCI):
                    ev[('ln2', i)] = tgt(vector, V, ci, p_b[:, 0:T])
            on('vector', f_ln)


def emit_phase_l1a_A(ctx, block, io, WQ, WK, WV_, vb, sems, QSP, KSP):
    TP = 256
    tls = tiles_of(TCORE, TP)
    d_src = io['r2']
    XT = [ctx.sb(f"a_xt{b}", [128, NCI, TP], F32) for b in range(2)]
    LNB = [ctx.sb(f"a_lnb{b}", [128, NCI, TP], BF16) for b in range(2)]
    QFB = [ctx.sb(f"a_qfb{b}", [128, NCI, TP], BF16) for b in range(2)]
    KFB = [ctx.sb(f"a_kfb{b}", [128, NCI, TP], BF16) for b in range(2)]
    VFB = [ctx.sb(f"a_vfb{b}", [128, NCI, TP], BF16) for b in range(2)]
    KTS = [ctx.sb(f"a_kts{b}", [128, 2, H], BF16) for b in range(2)]
    VTS = [ctx.sb(f"a_vts{b}", [128, 2, H], BF16) for b in range(2)]
    nt = len(tls)
    ln = LNHelper(ctx, sems, vb, "a", TP)
    p_mm = [ctx.ps(f"a_pmm{b}", [128, TP]) for b in range(2)]
    p_tp = [ctx.ps(f"a_ptp{k}", [128, 128], BF16) for k in range(2)]

    sPE, sACT, sDVE = sems['pe'], sems['act'], sems['dve']
    sLDp = [sems['ld0'], sems['ld1']]
    sSTp = [sems['st0'], sems['st1']]
    sSTa = [sems['sta0'], sems['sta1']]
    sSTb = [sems['stb0'], sems['stb1']]
    sSTc = [sems['stc0'], sems['stc1']]
    ev = {}

    def on(name, f):
        getattr(block, name)(f)

    def mkV(vector):
        def V(fn, *args, **kw):
            sDVE.wait(vector, sDVE.n)
            return sDVE.inc(getattr(vector, fn)(*args, **kw))
        return V

    def mkA(scalar):
        def A(fn, *args, **kw):
            sACT.wait(scalar, sACT.n)
            return sACT.inc(getattr(scalar, fn)(*args, **kw))
        return A

    def bank_prev(i, st, oc):
        g = st * 8 + oc - 2
        if g >= 0:
            return i, g // 8, g % 8
        if i == 0:
            return None
        return i - 1, (24 + g) // 8, (24 + g) % 8

    EV_ENG = {0: 'act', 1: 'act', 2: 'dve'}
    emit_act_preload(block, sems, AF.Sigmoid)

    for i, (t0, T) in enumerate(tls):
        b = i % 2

        def f_load(sync, i=i, b=b, t0=t0, T=T):
            if i >= 2:
                sACT.wait(sync, ev[('sq', i - 2)])
                sDVE.wait(sync, ev[('xb', i - 2)])
            ev[('xld', i)] = sLDp[b].inc(
                sync.dma_start(out=XT[b][:, :, 0:T],
                               in_=d_src[:, :, HALO + t0:HALO + t0 + T]), 16)
        on('sync', f_load)

        def ld_wait(eng, i=i, b=b):
            sLDp[b].wait(eng, ev[('xld', i)])

        def tgt(eng, V, ci, pb, i=i, b=b, T=T):
            if ci == 0 and i >= 2:
                sPE.wait(eng, ev[('projV', i - 2)])
            return V('tensor_add', LNB[b][:, ci, 0:T], ln.LNT[:, ci, 0:T], pb)

        ln.emit(block, ev, i, b, XT[b], T, tgt, ld_wait)

        for st, Wmat, OUT, bias_col, acc in ((0, WQ, QFB, 'bq', QSP),
                                             (1, WK, KFB, 'bk', KSP)):
            for oc in range(8):
                def f_proj(ten, i=i, b=b, T=T, oc=oc, st=st, Wmat=Wmat):
                    if st == 0 and oc == 0:
                        sDVE.wait(ten, ev[('ln2', i)])
                    prev = bank_prev(i, st, oc)
                    if prev is not None:
                        pi, pst, poc = prev
                        s = sACT if EV_ENG[pst] == 'act' else sDVE
                        s.wait(ten, ev[('evx', pi, pst, poc)])
                    for ci in range(NCI):
                        ins = ten.matmul(p_mm[oc % 2][:, 0:T],
                                         Wmat[:, ci, oc * 128:(oc + 1) * 128],
                                         LNB[b][:, ci, 0:T],
                                         start=(ci == 0), stop=(ci == NCI - 1))
                    ev[('proj', i, st, oc)] = sPE.inc(ins)
                on('tensor', f_proj)

                def f_evac(scalar, i=i, b=b, T=T, oc=oc, st=st, OUT=OUT,
                           bias_col=bias_col, acc=acc):
                    sPE.wait(scalar, ev[('proj', i, st, oc)])
                    if st == 0 and oc == 0 and i >= 2:
                        sSTp[b].wait(scalar, ev[('stq', i - 2)])
                    if st == 1 and oc == 0 and i >= 2:
                        sSTa[b].wait(scalar, ev[('stk', i - 2)])
                    ev[('evx', i, st, oc)] = mkA(scalar)(
                        'activation', OUT[b][:, oc, 0:T], p_mm[oc % 2][:, 0:T],
                        AF.Sigmoid, bias=vb[bias_col][:, oc:oc + 1],
                        accum_out=acc[:, oc, i:i + 1])
                on('scalar', f_evac)

        for oc in range(8):
            def f_projv(ten, i=i, b=b, T=T, oc=oc):
                pi, pst, poc = bank_prev(i, 2, oc)
                s = sACT if EV_ENG[pst] == 'act' else sDVE
                s.wait(ten, ev[('evx', pi, pst, poc)])
                for ci in range(NCI):
                    ins = ten.matmul(p_mm[oc % 2][:, 0:T],
                                     WV_[:, ci, oc * 128:(oc + 1) * 128],
                                     LNB[b][:, ci, 0:T],
                                     start=(ci == 0), stop=(ci == NCI - 1))
                ev[('proj', i, 2, oc)] = sPE.inc(ins)
            on('tensor', f_projv)

            def f_evacv(vector, i=i, b=b, T=T, oc=oc):
                sPE.wait(vector, ev[('proj', i, 2, oc)])
                ev[('evx', i, 2, oc)] = mkV(vector)(
                    'tensor_scalar_add', VFB[b][:, oc, 0:T], p_mm[oc % 2][:, 0:T],
                    vb['bv'][:, oc:oc + 1])
            on('vector', f_evacv)
        ev[('projV', i)] = ev[('proj', i, 2, 7)]

        tp_seq = [(tch, sid, oc) for tch in range(2) for sid in range(2)
                  for oc in range(8)]
        for n, (tch, sid, oc) in enumerate(tp_seq):
            SRC = KFB if sid == 0 else VFB
            DST = KTS if sid == 0 else VTS

            def f_tp(ten, i=i, b=b, tch=tch, oc=oc, sid=sid, n=n, SRC=SRC):
                if n == 0:
                    sACT.wait(ten, ev[('evx', i, 1, 7)])
                    sDVE.wait(ten, ev[('evx', i, 2, 7)])
                if n >= 2:
                    sACT.wait(ten, ev[('tpc', i, n - 2)])
                elif i >= 1:
                    sACT.wait(ten, ev[('tpc', i - 1, 30 + n)])
                ins = ten.transpose(p_tp[n % 2][:],
                                    SRC[b][:, oc, tch * 128:(tch + 1) * 128],
                                    vb['idb'][:])
                ev[('tp', i, n)] = sPE.inc(ins)
            on('tensor', f_tp)

            def f_tpc(scalar, i=i, b=b, tch=tch, oc=oc, n=n, DST=DST):
                sPE.wait(scalar, ev[('tp', i, n)])
                if n == 0 and i >= 2:
                    sSTb[b].wait(scalar, ev[('stkt', i - 2)])
                    sSTc[b].wait(scalar, ev[('stvt', i - 2)])
                ev[('tpc', i, n)] = mkA(scalar)(
                    'copy', DST[b][:, tch, oc * 128:(oc + 1) * 128], p_tp[n % 2][:])
            on('scalar', f_tpc)

        def f_store(sync, i=i, b=b, t0=t0, T=T):
            sACT.wait(sync, ev[('tpc', i, 31)])
            sDVE.wait(sync, ev[('evx', i, 2, 7)])
            ev[('stq', i)] = sSTp[b].dma(sync, 
                out=io['qf'][:, :, t0:t0 + T], in_=QFB[b][:, :, 0:T])
            ev[('stk', i)] = sSTa[b].dma(sync, 
                out=io['kf'][:, :, t0:t0 + T], in_=KFB[b][:, :, 0:T])
            tc0 = 2 * i
            ev[('stkt', i)] = sSTb[b].dma(sync, 
                out=io['kt'][tc0:tc0 + 2, :, :].rearrange("tc p d -> p tc d"),
                in_=KTS[b][:])
            ev[('stvt', i)] = sSTc[b].dma(sync, 
                out=io['vt'][tc0:tc0 + 2, :, :].rearrange("tc p d -> p tc d"),
                in_=VTS[b][:])
        on('sync', f_store)

    def f_fin(sync):
        for j in (nt - 1, nt - 2):
            sSTp[j % 2].wait(sync, ev[('stq', j)])
            sSTa[j % 2].wait(sync, ev[('stk', j)])
            sSTb[j % 2].wait(sync, ev[('stkt', j)])
            sSTc[j % 2].wait(sync, ev[('stvt', j)])
    on('sync', f_fin)
    ev['nt'] = nt
    return ev


def emit_l1a_BCD(ctx, block, io, vb, sems, QSP, KSP, nt_a, cc_groups):
    TP = 256
    WO = ctx.sb("d_wo", [128, NCI, H], BF16)
    load_phase_weights(block, sems, [(WO[:], io['dw_l1a_wo'])])
    wo_ld = sems['wld'].n
    tls = tiles_of(TCORE, TP)
    nt = len(tls)
    sPE, sACT, sDVE, sGPS, sCC = (sems[k] for k in
                                  ('pe', 'act', 'dve', 'gps', 'cc'))
    sLDp = [sems['ld0'], sems['ld1']]
    sLDa = [sems['lda0'], sems['lda1']]
    sLDb = [sems['ldb0'], sems['ldb1']]
    sLDc = [sems['ldc0'], sems['ldc1']]
    sSTp = [sems['st0'], sems['st1']]
    ev = {}

    def on(name, f):
        getattr(block, name)(f)

    def mkV(vector):
        def V(fn, *args, **kw):
            sDVE.wait(vector, sDVE.n)
            return sDVE.inc(getattr(vector, fn)(*args, **kw))
        return V

    def mkA(scalar):
        def A(fn, *args, **kw):
            sACT.wait(scalar, sACT.n)
            return sACT.inc(getattr(scalar, fn)(*args, **kw))
        return A

    SUMS = ctx.sb("b_sums", [128, 2, NCI], F32)
    GS = ctx.sb("b_gs", [128, 2, NCI], F32)
    AK = ctx.sb("b_ak", [128, NCI, 16], BF16)
    AQ = ctx.sb("b_aq", [128, NCI, 16], BF16)
    EC = ctx.sb("b_ec", [16, 4], F32)
    T12 = ctx.sb("b_t12", [128, 2, NCI], F32)
    T12P = ctx.sb("b_t12p", [128, 2, NCI, nt], F32)
    AT1 = ctx.sb("b_at1", [128, NCI, 16], BF16)
    AT2 = ctx.sb("b_at2", [128, NCI, 16], BF16)
    SA = ctx.sb("c_sa", [16, TCORE], F32)
    ZP = ctx.sb("c_zp", [16, nt], F32)
    KVF = ctx.sb("c_kvf", [128, NCI, 128], F32)
    KVG = ctx.sb("c_kvg", [128, NCI, 128], F32)
    KVSB = ctx.sb("c_kvsb", [128, NCI, 128], BF16)
    ZROW = ctx.sb("c_zrow", [16, 2], F32)
    ZB = ctx.sb("c_zb", [16, 1], BF16)
    ZREP = ctx.sb("c_zrep", [128, NCI], F32)
    ROWS = ctx.sb("b_rows", [16, 2, 6, TP], F32)
    eps_a = vb['consts'][0:1, 1:2]

    emit_act_preload(block, sems, AF.Exp)

    def emit_ar(tag, sb_in, d_in, d_out, sb_out, wait_evs):
        def f(g):
            for s, v in wait_evs:
                s.wait(g, v)
            sGPS.dma(g, out=d_in, in_=sb_in)
            sGPS.wait(g, sGPS.n)
            sCC.inc(g.collective_compute(
                "AllReduce", ALU.add, replica_groups=cc_groups,
                ins=[d_in.opt()], outs=[d_out.opt()]))
            sCC.wait(g, sCC.n)
            sGPS.dma(g, out=sb_out, in_=d_out)
            sGPS.wait(g, sGPS.n)
        on('gpsimd', f)
        return sGPS.n

    def f_r1red(vector):
        V = mkV(vector)
        for ci in range(NCI):
            V('reduce_sum', SUMS[:, 0, ci:ci + 1], QSP[:, ci, 0:nt_a], mybir.AxisListType.X)
        for ci in range(NCI):
            ev['r1red'] = V('reduce_sum', SUMS[:, 1, ci:ci + 1], KSP[:, ci, 0:nt_a], mybir.AxisListType.X)
    on('vector', f_r1red)
    g1 = emit_ar('r1', SUMS[:], io['cc1i'], io['cc1o'], GS[:],
                 [(sDVE, sems['dve'].n)])

    def f_build_a(vector):
        sGPS.wait(vector, g1)
        V = mkV(vector)
        V('memset', AK[:], 0.0)
        V('memset', AQ[:], 0.0)
        for ci in range(NCI):
            V('tensor_scalar_add', AK[0:64, ci, 2 * ci:2 * ci + 1], GS[0:64, 1, ci:ci + 1], EPS_A)
            V('tensor_scalar_add', AK[64:128, ci, 2 * ci + 1:2 * ci + 2], GS[64:128, 1, ci:ci + 1], EPS_A)
            V('tensor_scalar_add', AQ[0:64, ci, 2 * ci:2 * ci + 1], GS[0:64, 0, ci:ci + 1], EPS_A)
            ev['builda'] = V('tensor_scalar_add', AQ[64:128, ci, 2 * ci + 1:2 * ci + 2], GS[64:128, 0, ci:ci + 1], EPS_A)
    on('vector', f_build_a)

    p_row = [ctx.ps(f"b_prow{k}", [16, TP]) for k in range(2)]
    p_bc = [ctx.ps(f"b_pbc{k}", [128, TP]) for k in range(2)]
    bc_scope = contextlib.ExitStack()
    outer_scope = ctx.scope
    ctx.scope = bc_scope
    p_e = ctx.ps("c_pe", [128, H])
    p_kv = ctx.ps("c_pkv", [128, NCI, 128])
    ctx.scope = outer_scope

    def f_ec_mm(ten):
        sDVE.wait(ten, ev['builda'])
        for ci in range(NCI):
            ten.matmul(p_row[0][0:16, 0:1], AK[:, ci, :], vb['onesb'][:],
                       start=(ci == 0), stop=(ci == NCI - 1))
        for ci in range(NCI):
            ins = ten.matmul(p_row[1][0:16, 0:1], AQ[:, ci, :], vb['onesb'][:],
                             start=(ci == 0), stop=(ci == NCI - 1))
        ev['ecmm'] = sPE.inc(ins)
    on('tensor', f_ec_mm)

    def f_ec(vector):
        sPE.wait(vector, ev['ecmm'])
        V = mkV(vector)
        V('tensor_scalar_mul', EC[:, 0:1], p_row[0][0:16, 0:1], EPS_A)
        ev['ec'] = V('tensor_scalar_mul', EC[:, 1:2], p_row[1][0:16, 0:1], EPS_A)
    on('vector', f_ec)

    QFT = [ctx.sb(f"b_qft{b}", [128, NCI, TP], BF16) for b in range(2)]
    KFT = [ctx.sb(f"b_kft{b}", [128, NCI, TP], BF16) for b in range(2)]
    SIB = ctx.sb("b_sib", [16, 2, TP], BF16)
    SOB = ctx.sb("b_sob", [16, 2, TP], BF16)
    QSIB = [ctx.sb(f"b_qsib{b}", [128, NCI, TP], BF16) for b in range(2)]
    DUMB = ctx.sb("b_dumb", [128, TP], BF16)

    for i, (t0, T) in enumerate(tls):
        b = i % 2

        def f_load(sync, i=i, b=b, t0=t0, T=T):
            if i >= 2:
                sDVE.wait(sync, ev[('bstt', i - 2)])
            ev[('bldq', i)] = sLDp[b].inc(
                sync.dma_start(out=QFT[b][:, :, 0:T], in_=io['qf'][:, :, t0:t0 + T]), 16)
            ev[('bldk', i)] = sLDa[b].inc(
                sync.dma_start(out=KFT[b][:, :, 0:T], in_=io['kf'][:, :, t0:t0 + T]), 16)
        on('sync', f_load)

        def f_rowmm(ten, i=i, b=b, T=T):
            sLDp[b].wait(ten, ev[('bldq', i)])
            sLDa[b].wait(ten, ev[('bldk', i)])
            if i == 0:
                sDVE.wait(ten, ev['ec'])
            if i >= 1:
                sDVE.wait(ten, ev[('brow', i - 1)])
            for ci in range(NCI):
                ten.matmul(p_row[0][:, 0:T], AK[:, ci, :], QFT[b][:, ci, 0:T],
                           start=(ci == 0), stop=(ci == NCI - 1))
            for ci in range(NCI):
                ins = ten.matmul(p_row[1][:, 0:T], AQ[:, ci, :], KFT[b][:, ci, 0:T],
                                 start=(ci == 0), stop=(ci == NCI - 1))
            ev[('browmm', i)] = sPE.inc(ins)
        on('tensor', f_rowmm)

        def f_brow(vector, i=i, b=b, T=T):
            sPE.wait(vector, ev[('browmm', i)])
            if i >= 2:
                sPE.wait(vector, ev[('bbc', i - 2, 1, 7)])
            V = mkV(vector)
            r = i % 2
            si_f = ROWS[:, r, 0, 0:T]
            so_f = ROWS[:, r, 1, 0:T]
            V('tensor_scalar_add', ROWS[:, r, 2, 0:T], p_row[0][:, 0:T], EC[:, 0:1])
            V('reciprocal', si_f, ROWS[:, r, 2, 0:T])
            V('tensor_copy', SIB[:, b, 0:T], si_f)
            V('tensor_scalar_add', ROWS[:, r, 3, 0:T], p_row[1][:, 0:T], EC[:, 1:2])
            V('reciprocal', so_f, ROWS[:, r, 3, 0:T])
            ev[('brow', i)] = V('tensor_copy', SOB[:, b, 0:T], so_f)
        on('vector', f_brow)

        for which in range(2):
            SB = SIB if which == 0 else SOB
            FT = QFT if which == 0 else KFT
            for ci in range(NCI):
                def f_bc(ten, i=i, b=b, T=T, ci=ci, which=which, SB=SB):
                    if ci == 0 and which == 0:
                        sDVE.wait(ten, ev[('brow', i)])
                    k = ('bstt', i, which, ci - 2)
                    if ci >= 2:
                        sDVE.wait(ten, ev[k])
                    elif which == 1:
                        sDVE.wait(ten, ev[('bstt', i, 0, 6 + ci)])
                    elif i >= 1:
                        sDVE.wait(ten, ev[('bstt', i - 1, 1, 6 + ci)])
                    ins = ten.matmul(p_bc[ci % 2][:, 0:T], vb['selb'][:, ci, :],
                                     SB[:, b, 0:T], start=True, stop=True)
                    ev[('bbc', i, which, ci)] = sPE.inc(ins)
                on('tensor', f_bc)

                def f_stt(vector, i=i, b=b, T=T, ci=ci, which=which, FT=FT):
                    sPE.wait(vector, ev[('bbc', i, which, ci)])
                    V = mkV(vector)
                    if which == 0:
                        out = QSIB[b][:, ci, 0:T]
                        if ci == 0 and i >= 2:
                            sSTp[b].wait(vector, ev[('bst', i - 2)])
                        acc = T12P[:, 1, ci, i:i + 1]
                    else:
                        out = DUMB[:, 0:T]
                        acc = T12P[:, 0, ci, i:i + 1]
                    ev[('bstt', i, which, ci)] = V(
                        'scalar_tensor_tensor', out, FT[b][:, ci, 0:T], 1.0,
                        p_bc[ci % 2][:, 0:T], ALU.mult, ALU.mult, accum_out=acc)
                on('vector', f_stt)
        ev[('bstt', i)] = ev[('bstt', i, 1, 7)]

        def f_bst(sync, i=i, b=b, t0=t0, T=T):
            sDVE.wait(sync, ev[('bstt', i, 0, 7)])
            ev[('bst', i)] = sSTp[b].inc(
                sync.dma_start(out=io['qsi'][:, :, t0:t0 + T], in_=QSIB[b][:, :, 0:T]), 16)
        on('sync', f_bst)

    bst_done = (sSTp[0].n, sSTp[1].n)

    def f_r2red(vector):
        V = mkV(vector)
        for w in range(2):
            for ci in range(NCI):
                ev['r2red'] = V('reduce_sum', T12[:, w, ci:ci + 1],
                                T12P[:, w, ci, 0:nt], mybir.AxisListType.X)
    on('vector', f_r2red)
    g2 = emit_ar('r2', T12[:], io['cc2i'], io['cc2o'], T12[:],
                 [(sDVE, sems['dve'].n)])

    def f_build_at(vector):
        sGPS.wait(vector, g2)
        V = mkV(vector)
        V('memset', AT1[:], 0.0)
        V('memset', AT2[:], 0.0)
        for ci in range(NCI):
            V('tensor_scalar_add', AT1[0:64, ci, 2 * ci:2 * ci + 1], T12[0:64, 0, ci:ci + 1], EPS_A)
            V('tensor_scalar_add', AT1[64:128, ci, 2 * ci + 1:2 * ci + 2], T12[64:128, 0, ci:ci + 1], EPS_A)
            V('tensor_scalar_add', AT2[0:64, ci, 2 * ci:2 * ci + 1], T12[0:64, 1, ci:ci + 1], EPS_A)
            ev['buildat'] = V('tensor_scalar_add', AT2[64:128, ci, 2 * ci + 1:2 * ci + 2], T12[64:128, 1, ci:ci + 1], EPS_A)
    on('vector', f_build_at)

    def f_ec2_mm(ten):
        sDVE.wait(ten, ev['buildat'])
        for ci in range(NCI):
            ten.matmul(p_row[0][0:16, 0:1], AT1[:, ci, :], vb['onesb'][:],
                       start=(ci == 0), stop=(ci == NCI - 1))
        for ci in range(NCI):
            ins = ten.matmul(p_row[1][0:16, 0:1], AT2[:, ci, :], vb['onesb'][:],
                             start=(ci == 0), stop=(ci == NCI - 1))
        ev['ec2mm'] = sPE.inc(ins)
    on('tensor', f_ec2_mm)

    def f_ec2(vector):
        sPE.wait(vector, ev['ec2mm'])
        V = mkV(vector)
        V('tensor_scalar_mul', EC[:, 2:3], p_row[0][0:16, 0:1], EPS_A)
        ev['ec2'] = V('tensor_scalar_mul', EC[:, 3:4], p_row[1][0:16, 0:1], EPS_A)
    on('vector', f_ec2)

    DBGETM = ctx.sb("c_dbgetm", [128, H], F32) if DBG else None
    KTT = [ctx.sb(f"c_ktt{b}", [128, 2, H], BF16) for b in range(2)]
    VTT = [ctx.sb(f"c_vtt{b}", [128, 2, H], BF16) for b in range(2)]
    ESB = ctx.sb("c_esb", [16, 2, TP], BF16)
    VSCB = ctx.sb("c_vscb", [128, 2, H], BF16)

    for i, (t0, T) in enumerate(tls):
        b = i % 2

        def f_cload(sync, i=i, b=b, t0=t0, T=T):
            if i >= 2:
                sPE.wait(sync, ev[('ckv', i - 2)])
                sDVE.wait(sync, ev[('cvsc', i - 2, 1)])
            elif ('bstt', nt - 1) in ev:
                sDVE.wait(sync, ev[('bstt', nt - 1)])
            tc0 = 2 * i
            ev[('cldq', i)] = sLDp[b].dma(sync, 
                out=QFT[b][:, :, 0:T], in_=io['qf'][:, :, t0:t0 + T])
            ev[('cldk', i)] = sLDa[b].dma(sync, 
                out=KFT[b][:, :, 0:T], in_=io['kf'][:, :, t0:t0 + T])
            ev[('cldkt', i)] = sLDb[b].dma(sync, 
                out=KTT[b][:], in_=io['kt'][tc0:tc0 + 2, :, :].rearrange("tc p d -> p tc d"))
            ev[('cldvt', i)] = sLDc[b].dma(sync, 
                out=VTT[b][:], in_=io['vt'][tc0:tc0 + 2, :, :].rearrange("tc p d -> p tc d"))
        on('sync', f_cload)

        def f_crowmm(ten, i=i, b=b, T=T):
            sLDp[b].wait(ten, ev[('cldq', i)])
            sLDa[b].wait(ten, ev[('cldk', i)])
            sLDb[b].wait(ten, ev[('cldkt', i)])
            if i == 0:
                sDVE.wait(ten, ev['ec2'])
            if i >= 1:
                sDVE.wait(ten, ev[('ces', i - 1)])
            for ci in range(NCI):
                ten.matmul(p_row[0][:, 0:T], AT1[:, ci, :], QFT[b][:, ci, 0:T],
                           start=(ci == 0), stop=(ci == NCI - 1))
            for ci in range(NCI):
                ins = ten.matmul(p_row[1][:, 0:T], AT2[:, ci, :], KFT[b][:, ci, 0:T],
                                 start=(ci == 0), stop=(ci == NCI - 1))
            ev[('crowmm', i)] = sPE.inc(ins)
        on('tensor', f_crowmm)

        def f_crow(vector, i=i, b=b, T=T, t0=t0):
            sPE.wait(vector, ev[('crowmm', i)])
            V = mkV(vector)
            r = i % 2
            V('tensor_scalar_add', ROWS[:, r, 0, 0:T], p_row[0][:, 0:T], EC[:, 2:3])
            ev[('cct', i)] = V('tensor_scalar', ROWS[:, r, 1, 0:T], p_row[1][:, 0:T],
                               EC[:, 3:4], None, ALU.add)
            ev[('cclip', i)] = V('tensor_scalar', ROWS[:, r, 2, 0:T], ROWS[:, r, 1, 0:T],
                                 1.0, -1.0, ALU.min, ALU.max)
        on('vector', f_crow)

        def f_cact(scalar, i=i, b=b, T=T, t0=t0):
            sDVE.wait(scalar, ev[('cclip', i)])
            A = mkA(scalar)
            r = i % 2
            ev[('ctanh', i)] = A('activation', ROWS[:, r, 4, 0:T], ROWS[:, r, 0, 0:T],
                                 AF.Tanh, scale=0.5)
            ev[('cexp', i)] = A('activation', ROWS[:, r, 3, 0:T], ROWS[:, r, 2, 0:T],
                                AF.Exp, accum_out=ZP[:, i:i + 1])
        on('scalar', f_cact)

        def f_ces(vector, i=i, b=b, T=T, t0=t0):
            sACT.wait(vector, ev[('cexp', i)])
            if i >= 2:
                sPE.wait(vector, ev[('ce', i - 2, 1)])
            r = i % 2
            V = mkV(vector)
            ev[('csa', i)] = V('tensor_scalar', SA[:, t0:t0 + T], ROWS[:, r, 4, 0:T],
                               0.5, 0.5, ALU.mult, ALU.add)
            ev[('ces', i)] = V('tensor_copy', ESB[:, b, 0:T], ROWS[:, r, 3, 0:T])
        on('vector', f_ces)

        for tch in range(2):
            def f_ce(ten, i=i, b=b, tch=tch):
                sDVE.wait(ten, ev[('ces', i)])
                if ('cvsc', i, tch - 1) in ev:
                    sDVE.wait(ten, ev[('cvsc', i, tch - 1)])
                elif i >= 1:
                    sDVE.wait(ten, ev[('cvsc', i - 1, 1)])
                ten.matmul(p_e[:, 0:512], ESB[:, b, tch * 128:(tch + 1) * 128],
                           vb['selrow'][:, 0:512], start=True, stop=True)
                ins = ten.matmul(p_e[:, 512:1024], ESB[:, b, tch * 128:(tch + 1) * 128],
                                 vb['selrow'][:, 512:1024], start=True, stop=True)
                ev[('ce', i, tch)] = sPE.inc(ins)
            on('tensor', f_ce)

            def f_cvsc(vector, i=i, b=b, tch=tch):
                sPE.wait(vector, ev[('ce', i, tch)])
                sLDc[b].wait(vector, ev[('cldvt', i)])
                ev[('cvsc', i, tch)] = mkV(vector)(
                    'tensor_mul', VSCB[:, tch, :], VTT[b][:, tch, :], p_e[:, 0:H])
            on('vector', f_cvsc)

            if DBG and i == nt - 1 and tch == 1:
                def f_dbg_etm(scalar, i=i):
                    sPE.wait(scalar, ev[('ce', i, 1)])
                    ev['dbgetm'] = mkA(scalar)('copy', DBGETM[:], p_e[:, 0:H])
                on('scalar', f_dbg_etm)

            def f_ckv(ten, i=i, b=b, tch=tch):
                sDVE.wait(ten, ev[('cvsc', i, tch)])
                for ci in range(NCI):
                    ins = ten.matmul(
                        p_kv[:, ci, :], KTT[b][:, tch, ci * 128:(ci + 1) * 128],
                        VSCB[:, tch, ci * 128:(ci + 1) * 128],
                        start=(i == 0 and tch == 0 and ci % 4 == 0),
                        stop=(i == nt - 1 and tch == 1 and ci % 4 == 3))
                ev[('ckv', i, tch)] = sPE.inc(ins)
            on('tensor', f_ckv)
        ev[('ckv', i)] = ev[('ckv', i, 1)]

    def f_kvev(scalar):
        sPE.wait(scalar, ev[('ckv', nt - 1)])
        ev['kvev'] = mkA(scalar)('copy', KVF[:], p_kv[:])
    on('scalar', f_kvev)

    def f_kvzero(vector):
        sACT.wait(vector, ev['kvev'])
        V = mkV(vector)
        for ci in range(NCI):
            V('memset', KVF[0:64, ci, 64:128], 0.0)
            V('memset', KVF[64:128, ci, 0:64], 0.0)
        for _ in (0,):
            ev['zred'] = V('reduce_sum', ZROW[:, 0:1], ZP[:, 0:nt], mybir.AxisListType.X)
    on('vector', f_kvzero)

    g3 = emit_ar('r3', KVF[:].rearrange("p c d -> p (c d)"), io['cc3i'], io['cc3o'],
                 KVG[:].rearrange("p c d -> p (c d)"), [(sDVE, sems['dve'].n)])
    g3z = emit_ar('r3z', ZROW[:, 0:1], io['cc4i'], io['cc4o'], ZROW[:, 0:1],
                  [(sGPS, g3)])

    def f_zinv(vector):
        sGPS.wait(vector, g3z)
        V = mkV(vector)
        V('reciprocal', ZROW[:, 1:2], ZROW[:, 0:1])
        V('tensor_scalar_mul', ZROW[:, 1:2], ZROW[:, 1:2], SSCALE)
        ev['zb'] = V('tensor_copy', ZB[:], ZROW[:, 1:2])
    on('vector', f_zinv)

    def f_zrep_mm(ten):
        sDVE.wait(ten, ev['zb'])
        for ci in range(NCI):
            ins = ten.matmul(p_bc[0][:, ci:ci + 1],
                             vb['selb'][:, ci, :], ZB[:], start=True, stop=True)
        ev['zrepmm'] = sPE.inc(ins)
    on('tensor', f_zrep_mm)

    def f_zrep(scalar):
        sPE.wait(scalar, ev['zrepmm'])
        ev['zrep'] = mkA(scalar)('copy', ZREP[:], p_bc[0][:, 0:NCI])
    on('scalar', f_zrep)

    def f_kvs(vector):
        sACT.wait(vector, ev['zrep'])
        V = mkV(vector)
        for ci in range(NCI):
            ev['kvs'] = V('tensor_scalar_mul', KVSB[:, ci, :], KVG[:, ci, :],
                          ZREP[:, ci:ci + 1])
    on('vector', f_kvs)

    if DBG:
        def f_dbg(g):
            sDVE.wait(g, sems['dve'].n)
            sACT.wait(g, sems['act'].n)
            for nm, ap in (("dbg_sa", SA[:]), ("dbg_kvg", KVG[:]),
                           ("dbg_kvsb", KVSB[:]), ("dbg_zrow", ZROW[:]),
                           ("dbg_zrep", ZREP[:]), ("dbg_t12", T12[:]),
                           ("dbg_at1", AT1[:]), ("dbg_at2", AT2[:]),
                           ("dbg_ec", EC[:]), ("dbg_vscb", VSCB[:]),
                           ("dbg_esb", ESB[:]), ("dbg_ktt", KTT[(nt - 1) % 2][:]),
                           ("dbg_etm", DBGETM[:])):
                sGPS.dma(g, out=io[nm], in_=ap)
        on('gpsimd', f_dbg)

    bc_scope.close()
    QST = [ctx.sb(f"d_qst{b}", [128, NCI, TP], BF16) for b in range(2)]
    XT = [ctx.sb(f"d_xt{b}", [128, NCI, TP], F32) for b in range(2)]
    SAB = ctx.sb("d_sab", [16, 2, TP], BF16)
    QSA = ctx.sb("d_qsa", [128, NCI, TP], BF16)
    OB = ctx.sb("d_ob", [128, NCI, TP], BF16)
    OX = ctx.sb("d_ox", [128, TP], F32)
    OT = [ctx.sb(f"d_ot{b}", [128, NCI, TP], F32) for b in range(2)]
    p_mm2 = [ctx.ps(f"d_pmm{k}", [128, TP]) for k in range(2)]

    for i, (t0, T) in enumerate(tls):
        b = i % 2

        def f_dload(sync, i=i, b=b, t0=t0, T=T):
            if i >= 2:
                sDVE.wait(sync, ev[('dep', i - 2)])
            else:
                sSTp[0].wait(sync, bst_done[0])
                sSTp[1].wait(sync, bst_done[1])
            ev[('dldq', i)] = sLDp[b].dma(sync, 
                out=QST[b][:, :, 0:T], in_=io['qsi'][:, :, t0:t0 + T])
            ev[('dld', i)] = sLDa[b].dma(sync, 
                out=XT[b][:, :, 0:T], in_=io['r2'][:, :, HALO + t0:HALO + t0 + T])
        on('sync', f_dload)

        def f_dsab(vector, i=i, b=b, T=T, t0=t0):
            if i == 0:
                sDVE.wait(vector, ev['kvs'])
            if i >= 1:
                sPE.wait(vector, ev[('dsabmm', i - 1, 7)])
            ev[('dsab', i)] = mkV(vector)('tensor_copy', SAB[:, b, 0:T], SA[:, t0:t0 + T])
        on('vector', f_dsab)

        for ci in range(NCI):
            def f_dbc(ten, i=i, b=b, T=T, ci=ci):
                sDVE.wait(ten, ev[('dsab', i)])
                if ci >= 2:
                    sDVE.wait(ten, ev[('dqsa', i, ci - 2)])
                elif i >= 1:
                    sDVE.wait(ten, ev[('dqsa', i - 1, 6 + ci)])
                ins = ten.matmul(p_bc[ci % 2][:, 0:T], vb['selb'][:, ci, :],
                                 SAB[:, b, 0:T], start=True, stop=True)
                ev[('dsabmm', i, ci)] = sPE.inc(ins)
            on('tensor', f_dbc)

            def f_dqsa(vector, i=i, b=b, T=T, ci=ci):
                sPE.wait(vector, ev[('dsabmm', i, ci)])
                sLDp[b].wait(vector, ev[('dldq', i)])
                if ci == 0 and i >= 1:
                    sACT.wait(vector, ev[('dob', i - 1, 7)])
                ev[('dqsa', i, ci)] = mkV(vector)(
                    'tensor_mul', QSA[:, ci, 0:T], QST[b][:, ci, 0:T], p_bc[ci % 2][:, 0:T])
            on('vector', f_dqsa)

            def f_dkv(ten, i=i, T=T, ci=ci):
                sDVE.wait(ten, ev[('dqsa', i, ci)])
                if ci >= 2:
                    sACT.wait(ten, ev[('dob', i, ci - 2)])
                elif i >= 1:
                    sACT.wait(ten, ev[('dob', i - 1, 6 + ci)])
                ins = ten.matmul(p_mm2[ci % 2][:, 0:T], KVSB[:, ci, :],
                                 QSA[:, ci, 0:T], start=True, stop=True)
                ev[('dkv', i, ci)] = sPE.inc(ins)
            on('tensor', f_dkv)

            def f_dob(scalar, i=i, T=T, ci=ci):
                sPE.wait(scalar, ev[('dkv', i, ci)])
                if ci == 0 and i >= 1:
                    sPE.wait(scalar, ev[('dproj', i - 1, 7)])
                ev[('dob', i, ci)] = mkA(scalar)('copy', OB[:, ci, 0:T],
                                                 p_mm2[ci % 2][:, 0:T])
            on('scalar', f_dob)

        for oc in range(8):
            def f_dproj(ten, i=i, b=b, T=T, oc=oc):
                if oc == 0:
                    sACT.wait(ten, ev[('dob', i, 7)])
                    sems['wld'].wait(ten, wo_ld)
                if oc >= 2:
                    sDVE.wait(ten, ev[('dep', i, oc - 2)])
                elif i >= 1:
                    sDVE.wait(ten, ev[('dep', i - 1, 6 + oc)])
                for ci in range(NCI):
                    ins = ten.matmul(p_mm2[oc % 2][:, 0:T], WO[:, ci, oc * 128:(oc + 1) * 128],
                                     OB[:, ci, 0:T], start=(ci == 0), stop=(ci == NCI - 1))
                ev[('dproj', i, oc)] = sPE.inc(ins)
            on('tensor', f_dproj)

            def f_dox(scalar, i=i, T=T, oc=oc):
                sPE.wait(scalar, ev[('dproj', i, oc)])
                if oc >= 1:
                    sDVE.wait(scalar, ev[('dep', i, oc - 1)])
                elif i >= 1:
                    sDVE.wait(scalar, ev[('dep', i - 1, 7)])
                ev[('dox', i, oc)] = mkA(scalar)(
                    'activation', OX[:, 0:T], p_mm2[oc % 2][:, 0:T], AF.Identity,
                    bias=vb['bo'][:, oc:oc + 1])
            on('scalar', f_dox)

            def f_dep(vector, i=i, b=b, T=T, oc=oc):
                sACT.wait(vector, ev[('dox', i, oc)])
                sLDa[b].wait(vector, ev[('dld', i)])
                if oc == 0 and i >= 2:
                    sSTp[b].wait(vector, ev[('dst', i - 2)])
                ev[('dep', i, oc)] = mkV(vector)(
                    'tensor_add', OT[b][:, oc, 0:T], OX[:, 0:T], XT[b][:, oc, 0:T])
            on('vector', f_dep)
        ev[('dep', i)] = ev[('dep', i, 7)]

        def f_dst(sync, i=i, b=b, t0=t0, T=T):
            sDVE.wait(sync, ev[('dep', i)])
            ev[('dst', i)] = sSTp[b].inc(
                sync.dma_start(out=io['r3'][:, :, t0:t0 + T], in_=OT[b][:, :, 0:T]), 16)
        on('sync', f_dst)

    def f_dfin(sync):
        sSTp[(nt - 1) % 2].wait(sync, ev[('dst', nt - 1)])
        sSTp[(nt - 2) % 2].wait(sync, ev[('dst', nt - 2)])
    on('sync', f_dfin)


def emit_phase_l1f(ctx, block, io, vb, sems):
    TS = min(512, TCORE)
    TT = TS
    NTT = TS // TT
    NS = TCORE // TS
    NH = 32
    sPE, sACT, sDVE, sGPS = sems['pe'], sems['act'], sems['dve'], sems['gps']
    sLDp = [sems['ld0'], sems['ld1']]
    sSTp = [sems['st0'], sems['st1']]
    sWL = [sems['wl0'], sems['wl1']]
    ev = {}

    LNB = ctx.sb("f_lnb", [128, NCI, TS], BF16)
    GX = ctx.sb("f_gx", [128, NH, TS], BF16)
    GY = ctx.sb("f_gy", [128, NH, TS], BF16)
    XT = ctx.sb("f_xt", [128, NCI, TT], F32)
    XT4 = ctx.sb("f_xt4", [128, NCI, TT], F32)
    WB1 = [ctx.sb(f"f_wb1{k}", [128, NCI, 128], BF16) for k in range(2)]
    WB2 = [ctx.sb(f"f_wb2{k}", [128, NCI, 128], BF16) for k in range(2)]
    WH = [ctx.sb(f"f_wh{k}", [128, NH, 128], BF16) for k in range(2)]
    W3B = [ctx.sb(f"f_w3b{k}", [128, NH, 128], BF16) for k in range(2)]
    SILT = [ctx.sb(f"f_silt{k}", [128, TT], BF16) for k in range(2)]
    OX = ctx.sb("f_ox", [128, TT], F32)
    OTS = [ctx.sb(f"f_ots{k}", [128, TT], F32) for k in range(2)]
    ln = LNHelper(ctx, sems, vb, "f", TT, nbuf=1)
    p1 = [ctx.ps(f"f_p1{k}", [128, TT]) for k in range(2)]
    p2 = [ctx.ps(f"f_p2{k}", [128, TT]) for k in range(2)]

    def on(name, f):
        getattr(block, name)(f)

    def mkV(vector):
        def V(fn, *args, **kw):
            sDVE.wait(vector, sDVE.n)
            return sDVE.inc(getattr(vector, fn)(*args, **kw))
        return V

    def mkA(scalar):
        def A(fn, *args, **kw):
            sACT.wait(scalar, sACT.n)
            return sACT.inc(getattr(scalar, fn)(*args, **kw))
        return A

    for s in range(NS):
        S0 = s * TS

        for tt in range(NTT):
            i = s * NTT + tt
            t0 = S0 + tt * TT

            def f_load(sync, i=i, t0=t0, s=s, tt=tt):
                if i >= 1:
                    sDVE.wait(sync, ev[('ln2', i - 1)])
                    sACT.wait(sync, ev[('sq', i - 1)])
                ev[('fxld', i)] = sLDp[0].dma(
                    sync, out=XT[:], in_=io['r3'][:, :, t0:t0 + TT])
            on('sync', f_load)

            def ld_wait(eng, i=i):
                sLDp[0].wait(eng, ev[('fxld', i)])

            def tgt(eng, V, ci, pb, i=i, s=s, tt=tt):
                if ci == 0 and s >= 1:
                    sPE.wait(eng, ev[('f1done', s - 1)])
                return V('tensor_add', LNB[:, ci, tt * TT:(tt + 1) * TT],
                         ln.LNT[:, ci, 0:TT], pb)

            ln.emit(block, ev, i, 0, XT, TT, tgt, ld_wait)

        emit_act_preload(block, sems, AF.Silu)
        for oc in range(NH):
            kb = oc % 2

            def f_wl(g, s=s, oc=oc, kb=kb):
                if oc >= 2:
                    sPE.wait(g, ev[('f1mm', s, oc - 2)])
                elif s >= 1:
                    sPE.wait(g, ev[('f1mm', s - 1, NH - 2 + oc)])
                if s == 0:
                    sWL[kb].dma(g, out=WB1[kb][:], in_=io['w1'][oc, :, :, :])
                    ev[('f1w', s, oc)] = sWL[kb].dma(g, out=WB2[kb][:], in_=io['w2'][oc, :, :, :])
                    sWL[kb].dma(g, out=io['w1b'][oc, :, :, :], in_=WB1[kb][:])
                    ev[('f1wb', oc)] = sWL[kb].dma(g, out=io['w2b'][oc, :, :, :], in_=WB2[kb][:])
                else:
                    sWL[kb].dma(g, out=WB1[kb][:], in_=io['w1b'][oc, :, :, :])
                    ev[('f1w', s, oc)] = sWL[kb].dma(g, out=WB2[kb][:], in_=io['w2b'][oc, :, :, :])
            on('gpsimd', f_wl)

            for tt in range(NTT):
                def f_mm(ten, s=s, oc=oc, tt=tt, kb=kb):
                    if tt == 1:
                        sACT.wait(ten, ev[('f1silu', s, oc, 0)])
                        sDVE.wait(ten, ev[('f1stt', s, oc, 0)])
                    if tt == 0:
                        sWL[kb].wait(ten, ev[('f1w', s, oc)])
                        if oc == 0:
                            sDVE.wait(ten, ev[('ln2', s * NTT + NTT - 1)])
                        if oc >= 2:
                            sDVE.wait(ten, ev[('f1stt', s, oc - 2, NTT - 1)])
                        elif s >= 1:
                            sDVE.wait(ten, ev[('f1stt', s - 1, NH - 2 + oc, NTT - 1)])
                            sACT.wait(ten, ev[('f4ox', s - 1, NTT - 1, 7)])
                    for ci in range(NCI):
                        ten.matmul(p1[oc % 2][:, 0:TT], WB1[kb][:, ci, :],
                                   LNB[:, ci, tt * TT:(tt + 1) * TT],
                                   start=(ci == 0), stop=(ci == NCI - 1))
                    for ci in range(NCI):
                        ins = ten.matmul(p2[oc % 2][:, 0:TT], WB2[kb][:, ci, :],
                                         LNB[:, ci, tt * TT:(tt + 1) * TT],
                                         start=(ci == 0), stop=(ci == NCI - 1))
                    ev[('f1mm', s, oc)] = ev[('f1mmt', s, oc, tt)] = sPE.inc(ins)
                on('tensor', f_mm)

                def f_silu(scalar, s=s, oc=oc, tt=tt):
                    sPE.wait(scalar, ev[('f1mmt', s, oc, tt)])
                    if oc >= 2:
                        sDVE.wait(scalar, ev[('f1stt', s, oc - 2, NTT - 1)])
                    elif s >= 1:
                        sDVE.wait(scalar, ev[('f1stt', s - 1, NH - 2 + oc, NTT - 1)])
                    ev[('f1silu', s, oc, tt)] = mkA(scalar)(
                        'activation', SILT[oc % 2][:, 0:TT], p1[oc % 2][:, 0:TT], AF.Silu,
                        scale=vb['s1v'][:, oc:oc + 1])
                on('scalar', f_silu)

                def f_stt(vector, s=s, oc=oc, tt=tt):
                    sACT.wait(vector, ev[('f1silu', s, oc, tt)])
                    if oc == 0 and tt == 0 and s >= 1:
                        sPE.wait(vector, ev[('f4mm', s - 1, NTT - 1, 7)])
                    ev[('f1stt', s, oc, tt)] = mkV(vector)(
                        'scalar_tensor_tensor', GX[:, oc, tt * TT:(tt + 1) * TT],
                        p2[oc % 2][:, 0:TT], vb['s2v'][:, oc:oc + 1], SILT[oc % 2][:, 0:TT],
                        ALU.mult, ALU.mult)
                on('vector', f_stt)
        ev[('f1done', s)] = ev[('f1mm', s, NH - 1)]

        emitted_gelu_preload = False
        for stage, (SRC, DST, wkey, skey) in enumerate((
                (GX, GY, 'h1', 'h1s'), (GY, GX, 'h2', 'h2s'))):
            if not emitted_gelu_preload:
                emit_act_preload(block, sems, AF.Gelu)
                emitted_gelu_preload = True
            for oc in range(NH):
                kb = oc % 2

                def f_whl(g, s=s, oc=oc, kb=kb, stage=stage, wkey=wkey):
                    if oc >= 2:
                        sPE.wait(g, ev[('fhmm', s, stage, oc - 2)])
                    elif stage >= 1:
                        sPE.wait(g, ev[('fhmm', s, 0, NH - 2 + oc)])
                    elif s >= 1:
                        sPE.wait(g, ev[('f4mm', s - 1, NTT - 1, 6 + oc)])
                    if s == 0:
                        ev[('fhw', s, stage, oc)] = sWL[kb].dma(
                            g, out=WH[kb][:], in_=io[wkey][oc, :, :, :])
                        ev[('fhwb', stage, oc)] = sWL[kb].dma(
                            g, out=io[wkey + 'b'][oc, :, :, :], in_=WH[kb][:])
                    else:
                        ev[('fhw', s, stage, oc)] = sWL[kb].dma(
                            g, out=WH[kb][:], in_=io[wkey + 'b'][oc, :, :, :])
                on('gpsimd', f_whl)

                for tt in range(NTT):
                    def f_hmm(ten, s=s, oc=oc, tt=tt, kb=kb, stage=stage, SRC=SRC):
                        if tt == 1:
                            sACT.wait(ten, ev[('fg', s, stage, oc, 0)])
                        if tt == 0:
                            sWL[kb].wait(ten, ev[('fhw', s, stage, oc)])
                            if oc == 0:
                                if stage == 0:
                                    sDVE.wait(ten, ev[('f1stt', s, NH - 1, NTT - 1)])
                                else:
                                    sACT.wait(ten, ev[('fg', s, 0, NH - 1, NTT - 1)])
                            if oc >= 2:
                                sACT.wait(ten, ev[('fg', s, stage, oc - 2, NTT - 1)])
                            elif stage == 1:
                                sACT.wait(ten, ev[('fg', s, 0, NH - 2 + oc, NTT - 1)])
                            elif oc < 2:
                                sACT.wait(ten, ev[('f1silu', s, NH - 2 + oc, NTT - 1)])
                        for hc in range(NH):
                            ins = ten.matmul(p1[oc % 2][:, 0:TT], WH[kb][:, hc, :],
                                             SRC[:, hc, tt * TT:(tt + 1) * TT],
                                             start=(hc == 0), stop=(hc == NH - 1))
                        ev[('fhmm', s, stage, oc)] = ev[('fhmmt', s, stage, oc, tt)] = sPE.inc(ins)
                    on('tensor', f_hmm)

                    def f_gelu(scalar, s=s, oc=oc, tt=tt, stage=stage, DST=DST, skey=skey):
                        sPE.wait(scalar, ev[('fhmmt', s, stage, oc, tt)])
                        if oc == 0 and tt == 0:
                            if stage == 0 and s >= 1:
                                sPE.wait(scalar, ev[('fhmm', s - 1, 1, NH - 1)])
                            elif stage == 1:
                                sPE.wait(scalar, ev[('fhmm', s, 0, NH - 1)])
                        ev[('fg', s, stage, oc, tt)] = mkA(scalar)(
                            'activation', DST[:, oc, tt * TT:(tt + 1) * TT],
                            p1[oc % 2][:, 0:TT], AF.Gelu, scale=vb[skey][:, oc:oc + 1])
                    on('scalar', f_gelu)
        ev[('f2done', s)] = ev[('fhmm', s, 1, NH - 1)]

        for tt in range(NTT):
            t0 = S0 + tt * TT

            def f_x4(sync, s=s, tt=tt, t0=t0):
                if s + tt >= 1:
                    sDVE.wait(sync, ev[('f4ep', s, tt - 1) if tt >= 1 else ('f4ep', s - 1, NTT - 1)])
                ev[('f4x', s, tt)] = sLDp[1].dma(
                    sync, out=XT4[:], in_=io['r3'][:, :, t0:t0 + TT])
            on('sync', f_x4)

            for oc in range(NCI):
                kb = oc % 2

                def f_w3l(g, s=s, tt=tt, oc=oc, kb=kb):
                    if oc >= 2:
                        sPE.wait(g, ev[('f4mm', s, tt, oc - 2)])
                    elif tt >= 1:
                        sPE.wait(g, ev[('f4mm', s, tt - 1, 6 + oc)])
                    elif s >= 1:
                        sPE.wait(g, ev[('f4mm', s - 1, NTT - 1, 6 + oc)])
                    if s == 0 and tt == 0:
                        sWL[kb].dma(g, out=W3B[kb][:], in_=io['w3'][oc, :, :, :])
                        ev[('f4w', s, tt, oc)] = sWL[kb].dma(
                            g, out=io['w3b'][oc, :, :, :], in_=W3B[kb][:])
                    else:
                        ev[('f4w', s, tt, oc)] = sWL[kb].dma(
                            g, out=W3B[kb][:], in_=io['w3b'][oc, :, :, :])
                on('gpsimd', f_w3l)

                def f_mm3(ten, s=s, tt=tt, oc=oc, kb=kb):
                    sWL[kb].wait(ten, ev[('f4w', s, tt, oc)])
                    if oc == 0 and tt == 0:
                        sACT.wait(ten, ev[('fg', s, 1, NH - 1, NTT - 1)])
                    if oc >= 2:
                        sACT.wait(ten, ev[('f4ox', s, tt, oc - 2)])
                    elif tt >= 1:
                        sACT.wait(ten, ev[('f4ox', s, tt - 1, 6 + oc)])
                    elif oc < 2:
                        sACT.wait(ten, ev[('fg', s, 1, NH - 2 + oc, NTT - 1)])
                    for hc in range(NH):
                        ins = ten.matmul(p1[oc % 2][:, 0:TT], W3B[kb][:, hc, :],
                                         GX[:, hc, tt * TT:(tt + 1) * TT],
                                         start=(hc == 0), stop=(hc == NH - 1))
                    ev[('f4mm', s, tt, oc)] = sPE.inc(ins)
                on('tensor', f_mm3)

                def f_ox(scalar, s=s, tt=tt, oc=oc):
                    sPE.wait(scalar, ev[('f4mm', s, tt, oc)])
                    if oc >= 1:
                        sDVE.wait(scalar, ev[('f4add', s, tt, oc - 1)])
                    elif tt >= 1:
                        sDVE.wait(scalar, ev[('f4add', s, tt - 1, 7)])
                    elif s >= 1:
                        sDVE.wait(scalar, ev[('f4add', s - 1, NTT - 1, 7)])
                    ev[('f4ox', s, tt, oc)] = mkA(scalar)(
                        'activation', OX[:, 0:TT], p1[oc % 2][:, 0:TT], AF.Identity,
                        scale=vb['s3v'][:, oc:oc + 1])
                on('scalar', f_ox)

                def f_add(vector, s=s, tt=tt, oc=oc):
                    sACT.wait(vector, ev[('f4ox', s, tt, oc)])
                    sLDp[1].wait(vector, ev[('f4x', s, tt)])
                    ob = oc % 2
                    prev = None
                    if oc >= 2:
                        prev = (s, tt, oc - 2)
                    elif tt >= 1:
                        prev = (s, tt - 1, 6 + oc)
                    elif s >= 1:
                        prev = (s - 1, NTT - 1, 6 + oc)
                    if prev is not None:
                        sSTp[ob].wait(vector, ev[('fyst',) + prev])
                    ev[('f4add', s, tt, oc)] = ev[('f4ep', s, tt)] = mkV(vector)(
                        'tensor_add', OTS[ob][:, 0:TT], OX[:, 0:TT], XT4[:, oc, 0:TT])
                on('vector', f_add)

                def f_yst(sync, s=s, tt=tt, oc=oc, t0=t0):
                    ob = oc % 2
                    sDVE.wait(sync, ev[('f4add', s, tt, oc)])
                    ev[('fyst', s, tt, oc)] = sSTp[ob].dma(
                        sync, out=io['y'][:, oc, t0:t0 + TT], in_=OTS[ob][:, 0:TT])
                on('sync', f_yst)

    def f_fin(sync):
        sSTp[0].wait(sync, ev[('fyst', NS - 1, NTT - 1, 6)])
        sSTp[1].wait(sync, ev[('fyst', NS - 1, NTT - 1, 7)])
        if DBG:
            sLDp[0].dma(sync, out=io['dbg_lnb'][:, :, 0:TS], in_=LNB[:])
            sLDp[0].dma(sync, out=io['dbg_gx'][:, :, 0:TS], in_=GX[:])
            sLDp[0].dma(sync, out=io['dbg_gy'][:, :, 0:TS], in_=GY[:])
            sLDp[0].wait(sync, sLDp[0].n)
    on('sync', f_fin)



ALL_PHASES = ('l0a', 'l0f', 'l1aA', 'l1aBCD', 'l1f')


def build_program(phases=ALL_PHASES, io_external=(), io_extin=(), cc_groups=None):
    if cc_groups is None:
        cc_groups = [[0, 1, 2, 3], [4, 5, 6, 7]]
    nc = bass.Bass("TRN2", target_bir_lowering=False)
    es = contextlib.ExitStack()
    ctx = Ctx(nc, es)
    with es:
        def dram(name, shape, dtype=F32, ext_in=False):
            if ext_in:
                kind = "ExternalInput"
            elif name in io_external or (DBG and name.startswith("dbg_")):
                kind = "ExternalOutput"
            else:
                kind = "Internal"
            return nc.dram_tensor(name, shape, dtype, kind=kind).ap()

        io = {}
        io['x'] = dram("x", [128, NCI, NTOK], ext_in=True)
        io['r1'] = dram("r1", [128, NCI, NTOK])
        io['r2'] = dram("r2", [128, NCI, NTOK], ext_in=('r2' in io_extin))
        io['qf'] = dram("qf", [128, NCI, TCORE], BF16)
        io['kf'] = dram("kf", [128, NCI, TCORE], BF16)
        io['kt'] = dram("kt", [TCORE // 128, 128, H], BF16)
        io['vt'] = dram("vt", [TCORE // 128, 128, H], BF16)
        io['qsi'] = dram("qsi", [128, NCI, TCORE], BF16)
        io['r3'] = dram("r3", [128, NCI, TCORE], ext_in=('r3' in io_extin))
        io['y'] = dram("y", [128, NCI, TCORE], ext_in=False)
        io['cc1i'] = dram("cc1i", [128, 2 * NCI])
        io['cc1o'] = dram("cc1o", [128, 2 * NCI])
        io['cc2i'] = dram("cc2i", [128, 2 * NCI])
        io['cc2o'] = dram("cc2o", [128, 2 * NCI])
        io['cc3i'] = dram("cc3i", [128, NCI * 128])
        io['cc3o'] = dram("cc3o", [128, NCI * 128])
        io['cc4i'] = dram("cc4i", [16, 1])
        io['cc4o'] = dram("cc4o", [16, 1])
        if DBG:
            io['dbg_sa'] = dram("dbg_sa", [16, TCORE], ext_in=False)
            io['dbg_kvg'] = dram("dbg_kvg", [128, NCI, 128])
            io['dbg_kvsb'] = dram("dbg_kvsb", [128, NCI, 128], BF16)
            io['dbg_zrow'] = dram("dbg_zrow", [16, 2])
            io['dbg_zrep'] = dram("dbg_zrep", [128, NCI])
            io['dbg_t12'] = dram("dbg_t12", [128, 2, NCI])
            io['dbg_at1'] = dram("dbg_at1", [128, NCI, 16], BF16)
            io['dbg_at2'] = dram("dbg_at2", [128, NCI, 16], BF16)
            io['dbg_ec'] = dram("dbg_ec", [16, 4])
            io['dbg_vscb'] = dram("dbg_vscb", [128, 2, H], BF16)
            io['dbg_esb'] = dram("dbg_esb", [16, 2, 256], BF16)
            io['dbg_ktt'] = dram("dbg_ktt", [128, 2, H], BF16)
            io['dbg_etm'] = dram("dbg_etm", [128, H])
            io['dbg_lnb'] = dram("dbg_lnb", [128, NCI, 1024], BF16)
            io['dbg_gx'] = dram("dbg_gx", [128, 32, 1024], BF16)
            io['dbg_gy'] = dram("dbg_gy", [128, 32, 1024], BF16)
            io['dbg_flnt'] = dram("dbg_flnt", [128, NCI, 512])
            io['dbg_frows'] = dram("dbg_frows", [1, 1, 5, 512])

        wnames = []
        if 'l0a' in phases:
            wnames += ['l0a_wv', 'l0a_wr', 'l0a_wo']
        if 'l0f' in phases:
            wnames += ['l0f_wk', 'l0f_wr', 'l0f_wv']
        if 'l1aA' in phases:
            wnames += ['l1a_wq', 'l1a_wk', 'l1a_wv']
        if 'l1aBCD' in phases:
            wnames += ['l1a_wo']
        dw = {n: dram(n, [128, NCI, H], ext_in=True) for n in wnames}
        if 'l1f' in phases:
            io['w1'] = dram("w1", [32, 128, NCI, 128], ext_in=True)
            io['w2'] = dram("w2", [32, 128, NCI, 128], ext_in=True)
            io['h1'] = dram("h1", [32, 128, 32, 128], ext_in=True)
            io['h2'] = dram("h2", [32, 128, 32, 128], ext_in=True)
            io['w3'] = dram("w3", [NCI, 128, 32, 128], ext_in=True)
            io['s1v'] = dram("s1v", [128, 32], ext_in=True)
            io['s2v'] = dram("s2v", [128, 32], ext_in=True)
            io['h1sv'] = dram("h1sv", [128, 32], ext_in=True)
            io['h2sv'] = dram("h2sv", [128, 32], ext_in=True)
            io['s3v'] = dram("s3v", [128, NCI], ext_in=True)
            io['w1b'] = dram("w1b", [32, 128, NCI, 128], BF16)
            io['w2b'] = dram("w2b", [32, 128, NCI, 128], BF16)
            io['h1b'] = dram("h1b", [32, 128, 32, 128], BF16)
            io['h2b'] = dram("h2b", [32, 128, 32, 128], BF16)
            io['w3b'] = dram("w3b", [NCI, 128, 32, 128], BF16)
        d_ident = dram("ident", [128, 128], ext_in=True)
        d_selb = dram("selb", [NCI, 16, 128], ext_in=True)
        d_selrow = dram("selrow", [16, H], ext_in=True)

        vnames = ['td', 'tmv', 'tmr', 'mur', 'muk', 'bq', 'bk', 'bv', 'bo']
        dv = {n: dram("vec_" + n, [128, NCI], ext_in=True) for n in vnames}

        vb = {n: ctx.sb("v_" + n, [128, NCI], F32) for n in vnames}
        vb['d'] = ctx.sb("v_d", [128, NCI], F32)
        vb['dneg'] = ctx.sb("v_dneg", [128, NCI], F32)
        vb['consts'] = ctx.sb("v_consts", [128, 4], F32)
        vb['ones_inv'] = ctx.sb("v_onesinv", [128, 1], BF16)
        vb['ones1'] = ctx.sb("v_ones1", [1, 128], F32)
        vb['nones1'] = ctx.sb("v_nones1", [1, 128], F32)
        vb['idf'] = ctx.sb("v_idf", [128, 128], F32)
        vb['idb'] = ctx.sb("v_idb", [128, 128], BF16)
        vb['selbf'] = ctx.sb("v_selbf", [16, NCI, 128], F32)
        vb['selb'] = ctx.sb("v_selb", [16, NCI, 128], BF16)
        vb['selrowf'] = ctx.sb("v_selrowf", [16, H], F32)
        vb['selrow'] = ctx.sb("v_selrow", [16, H], BF16)
        vb['onesb'] = ctx.sb("v_onesb", [128, 1], BF16)
        if 'l1f' in phases:
            vb['s1v'] = ctx.sb("v_s1v", [128, 32], F32)
            vb['s2v'] = ctx.sb("v_s2v", [128, 32], F32)
            vb['h1s'] = ctx.sb("v_h1s", [128, 32], F32)
            vb['h2s'] = ctx.sb("v_h2s", [128, 32], F32)
            vb['s3v'] = ctx.sb("v_s3v", [128, NCI], F32)

        if 'l1aA' in phases:
            nt_a = len(tiles_of(TCORE, 256))
            QSP = ctx.sb("a_qsp", [128, NCI, nt_a], F32)
            KSP = ctx.sb("a_ksp", [128, NCI, nt_a], F32)

        wscope = contextlib.ExitStack()
        wscope_closed = False
        ctx.scope = wscope
        wslot = [ctx.sb(f"wslot{k}", [128, NCI, H], BF16) for k in range(4)]
        ctx.scope = es

        semnames = ('pe', 'act', 'dve', 'gps', 'wld',
                    'ld0', 'ld1', 'st0', 'st1', 'cc',
                    'lda0', 'lda1', 'ldb0', 'ldb1', 'ldc0', 'ldc1',
                    'sta0', 'sta1', 'stb0', 'stb1', 'stc0', 'stc1',
                    'wl0', 'wl1', 'gpc')
        sems = {k: ctx.sem("s_" + k) for k in semnames}
        ph = Phases(ctx, sems)

        with nc.Block() as blk0:
            def f_vload(s):
                for n in vnames:
                    sems['ld0'].dma(s, out=vb[n][:], in_=dv[n])
                sems['ld0'].dma(s, out=vb['idf'][:], in_=d_ident)
                sems['ld0'].dma(s, 
                    out=vb['selbf'][:], in_=d_selb.rearrange("c h j -> h c j"))
                sems['ld0'].dma(s, out=vb['selrowf'][:], in_=d_selrow)
                if 'l1f' in phases:
                    for a, bnm in (('s1v', 's1v'), ('s2v', 's2v'), ('h1s', 'h1sv'),
                                   ('h2s', 'h2sv'), ('s3v', 's3v')):
                        sems['ld0'].dma(s, out=vb[a][:], in_=io[bnm])
            blk0.sync(f_vload)

            def f_consts(v):
                sems['ld0'].wait(v, sems['ld0'].n)
                sDVE = sems['dve']
                for ins in (
                    v.memset(vb['consts'][:, 0:1], EPS_LN),
                    v.memset(vb['consts'][:, 1:2], EPS_A),
                    v.memset(vb['ones_inv'][:], 1.0 / H),
                    v.memset(vb['ones1'][:], 1.0),
                    v.memset(vb['nones1'][:], -1.0),
                    v.tensor_copy(vb['idb'][:], vb['idf'][:]),
                    v.tensor_copy(vb['selb'][:], vb['selbf'][:]),
                    v.tensor_copy(vb['selrow'][:], vb['selrowf'][:]),
                    v.memset(vb['onesb'][:], 1.0),
                ):
                    sDVE.inc(ins)
            blk0.vector(f_consts)

            def f_d(sc):
                sems['act'].wait(sc, sems['act'].n)
                sems['act'].inc(sc.preload_act_table(AF.Exp))
                sems['dve'].wait(sc, sems['dve'].n)
                sems['act'].inc(sc.activation(vb['dneg'][:], vb['td'][:], AF.Exp))
                sems['act'].wait(sc, sems['act'].n)
                sems['act'].inc(sc.activation(vb['d'][:], vb['dneg'][:], AF.Exp, scale=-1.0))
            blk0.scalar(f_d)
        ph.snapshot()

        if 'l0a' in phases:
            with contextlib.ExitStack() as ps:
                ctx.scope = ps
                with nc.Block() as blk:
                    ph.emit_guard(blk)
                    load_phase_weights(blk, sems, [
                        (wslot[0][:], dw['l0a_wv']),
                        (wslot[1][:], dw['l0a_wr']),
                        (wslot[2][:], dw['l0a_wo'])])
                    wv = sems['wld'].n

                    def f_w(t):
                        sems['wld'].wait(t, wv)
                    blk.tensor(f_w)
                    emit_phase_l0(ctx, blk, io['x'], io['r1'],
                                  wslot[0], wslot[1], wslot[2], vb, sems, 'attn')
                ctx.scope = es
            ph.snapshot()

        if 'l0f' in phases:
            with contextlib.ExitStack() as ps:
                ctx.scope = ps
                with nc.Block() as blk:
                    ph.emit_guard(blk)
                    load_phase_weights(blk, sems, [
                        (wslot[0][:], dw['l0f_wk']),
                        (wslot[1][:], dw['l0f_wr']),
                        (wslot[2][:], dw['l0f_wv'])])
                    wv = sems['wld'].n

                    def f_w(t):
                        sems['wld'].wait(t, wv)
                    blk.tensor(f_w)
                    emit_phase_l0(ctx, blk, io['r1'], io['r2'],
                                  wslot[0], wslot[1], wslot[2], vb, sems, 'ffn')
                ctx.scope = es
            ph.snapshot()

        if 'l1aA' in phases:
            with contextlib.ExitStack() as ps:
                ctx.scope = ps
                with nc.Block() as blk:
                    ph.emit_guard(blk)
                    wpairs = [
                        (wslot[0][:], dw['l1a_wq']),
                        (wslot[1][:], dw['l1a_wk']),
                        (wslot[2][:], dw['l1a_wv'])]
                    load_phase_weights(blk, sems, wpairs)
                    wv = sems['wld'].n

                    def f_w(t):
                        sems['wld'].wait(t, wv)
                    blk.tensor(f_w)
                    emit_phase_l1a_A(ctx, blk, io, wslot[0], wslot[1],
                                     wslot[2], vb, sems, QSP, KSP)
                ctx.scope = es
            ph.snapshot()

        if 'l1aBCD' in phases:
            wscope.close()
            wscope_closed = True
            io['dw_l1a_wo'] = dw['l1a_wo']
            with contextlib.ExitStack() as ps:
                ctx.scope = ps
                with nc.Block() as blk:
                    ph.emit_guard(blk)
                    emit_l1a_BCD(ctx, blk, io, vb, sems,
                                 QSP, KSP, nt_a, cc_groups)
                ctx.scope = es
            ph.snapshot()

        if 'l1f' in phases:
            if not wscope_closed:
                wscope.close()
                wscope_closed = True
            with contextlib.ExitStack() as ps:
                ctx.scope = ps
                with nc.Block() as blk:
                    ph.emit_guard(blk)
                    emit_phase_l1f(ctx, blk, io, vb, sems)
                ctx.scope = es
            ph.snapshot()

        return nc



def _fm(x2d):
    T = x2d.shape[0]
    return np.ascontiguousarray(x2d.T.reshape(NCI, 128, T).transpose(1, 0, 2))


def _unfm(a):
    return a.transpose(1, 0, 2).reshape(H, -1).T


def _wT(W):
    return np.ascontiguousarray(W.T.reshape(NCI, 128, W.shape[0]).transpose(1, 0, 2))


def _vec(v):
    return np.ascontiguousarray(np.asarray(v, np.float32).reshape(NCI, 128).T)


def make_core_inputs(full, phases=ALL_PHASES):
    x = np.asarray(full['L0_attn_x'])
    per = {}
    if 'l0a' in phases:
        per['l0a_wv'] = _wT(np.asarray(full['L0_attn_Wv']))
        per['l0a_wr'] = _wT(np.asarray(full['L0_attn_Wr']))
        per['l0a_wo'] = _wT(np.asarray(full['L0_attn_Wo']))
    if 'l0f' in phases:
        per['l0f_wk'] = _wT(np.asarray(full['L0_ffn_W_k']))
        per['l0f_wr'] = _wT(np.asarray(full['L0_ffn_W_r']))
        per['l0f_wv'] = _wT(np.asarray(full['L0_ffn_W_v']))
    if 'l1aA' in phases:
        per['l1a_wq'] = _wT(np.asarray(full['L1_attn_Wq']))
        per['l1a_wk'] = _wT(np.asarray(full['L1_attn_Wk']))
        per['l1a_wv'] = _wT(np.asarray(full['L1_attn_Wv']))
    if 'l1aBCD' in phases:
        per['l1a_wo'] = _wT(np.asarray(full['L1_attn_Wo']))
    per['vec_td'] = _vec(full['L0_attn_time_decay'])
    per['vec_tmv'] = _vec(full['L0_attn_time_mix_v'])
    per['vec_tmr'] = _vec(full['L0_attn_time_mix_r'])
    per['vec_mur'] = _vec(full['L0_ffn_mu_r'])
    per['vec_muk'] = _vec(full['L0_ffn_mu_k'])
    per['vec_bq'] = _vec(full['L1_attn_bq'])
    per['vec_bk'] = _vec(full['L1_attn_bk'])
    per['vec_bv'] = _vec(full['L1_attn_bv'])
    per['vec_bo'] = _vec(full['L1_attn_bo'])
    per['ident'] = np.eye(128, dtype=np.float32)
    selb = np.zeros((NCI, 16, 128), np.float32)
    for ci in range(NCI):
        selb[ci, 2 * ci, 0:64] = 1.0
        selb[ci, 2 * ci + 1, 64:128] = 1.0
    per['selb'] = selb
    selrow = np.zeros((16, H), np.float32)
    for hh in range(16):
        selrow[hh, hh * 64:(hh + 1) * 64] = 1.0
    per['selrow'] = selrow
    if 'l1f' in phases:
        def blk4(W, no):
            W = np.asarray(W, np.float32)
            return np.ascontiguousarray(
                W.reshape(no, 128, W.shape[1] // 128, 128).transpose(0, 3, 2, 1))
        per['w1'] = blk4(full['L1_ffn_w1_w'], 32)
        per['w2'] = blk4(full['L1_ffn_w2_w'], 32)
        per['h1'] = blk4(full['L1_ffn_h1_w'], 32)
        per['h2'] = blk4(full['L1_ffn_h2_w'], 32)
        per['w3'] = blk4(full['L1_ffn_w3_w'], NCI)
        per['s1v'] = np.ascontiguousarray(np.asarray(full['L1_ffn_w1_s'], np.float32).reshape(32, 128).T)
        per['s2v'] = np.ascontiguousarray(np.asarray(full['L1_ffn_w2_s'], np.float32).reshape(32, 128).T)
        per['h1sv'] = np.ascontiguousarray(np.asarray(full['L1_ffn_h1_s'], np.float32).reshape(32, 128).T)
        per['h2sv'] = np.ascontiguousarray(np.asarray(full['L1_ffn_h2_s'], np.float32).reshape(32, 128).T)
        per['s3v'] = np.ascontiguousarray(np.asarray(full['L1_ffn_w3_s'], np.float32).reshape(NCI, 128).T)
    maps = []
    for c in range(NCORES):
        bidx, s = divmod(c, 4)
        s0 = s * TCORE
        if s0 == 0:
            halo = np.zeros((HALO, H), np.float32)
        else:
            halo = x[bidx, s0 - HALO:s0]
        xs = np.concatenate([halo, x[bidx, s0:s0 + TCORE]], axis=0)
        m = {'x': _fm(xs)}
        m.update(per)
        maps.append(m)
    return maps


def kernel(**inputs):
    maps = make_core_inputs(inputs, phases=ALL_PHASES)
    nc = build_program(phases=ALL_PHASES, io_external=('y',))
    res = run_bass_kernel_spmd(nc, maps, core_ids=list(range(NCORES)))
    out = np.empty((2, 4 * TCORE, H), np.float32)
    for c in range(NCORES):
        y = np.asarray(res.results[c]['y']).reshape(128, NCI, TCORE)
        bidx, s = divmod(c, 4)
        out[bidx, s * TCORE:(s + 1) * TCORE] = _unfm(y.astype(np.float32))
    return out
